# version 04
# baseline (speedup 1.0000x reference)
import contextlib
from collections import defaultdict

import numpy as np
import ml_dtypes

import concourse.bass as bass
import concourse.mybir as mybir
from concourse.bass_utils import run_bass_kernel_spmd

F32 = mybir.dt.float32
BF16 = mybir.dt.bfloat16
AF = mybir.ActivationFunctionType
ALU = mybir.AluOpType
AX = mybir.AxisListType.X

NCORE = 8
B, S, DM = 2, 8192, 1024
TPC = 2048
HALO = 128
TH = TPC + 2 * HALO
NCH = DM // 128

L0_H, L0_KV, L0_D, L0_W = 16, 2, 64, 256
HLIST = [(p % 2) * 8 + p // 2 for p in range(16)]

MOE_E, MOE_K = 16, 4
GLU_LIMIT, GLU_ALPHA = 7.0, 1.702

L1_M, L1_DH = 266, 64
L1_EPS = 1e-4
L1_RATIO = float(L1_M) ** -0.5
L1_DN = float(L1_DH) ** -0.25

ENG_NAME = {"v": "vector", "s": "scalar", "t": "tensor", "g": "gpsimd", "y": "sync"}


class TK:

    def __init__(self, nc, block, es):
        self.nc = nc
        self.block = block
        self.sem = {e: es.enter_context(nc.semaphore(f"tk_{e}")) for e in ENG_NAME}
        self.dsem = {q: es.enter_context(nc.semaphore(f"tkd_{q}")) for q in ("g", "y")}
        self.cnt = {e: 0 for e in ENG_NAME}
        self.dcnt = {q: 0 for q in ("g", "y")}
        self.known = {e: defaultdict(int) for e in ENG_NAME}
        self.writer = {}
        self.readers = defaultdict(list)

    def _needs(self, e, reads, writes):
        need = {}

        def add(tok):
            if tok is None:
                return
            kind, src, c = tok
            pk = (kind, src)
            if need.get(pk, 0) < c:
                need[pk] = c

        for k in reads:
            add(self.writer.get(k))
        for k in writes:
            add(self.writer.get(k))
            for tok in self.readers.get(k, ()):
                add(tok)
        out = []
        kn = self.known[e]
        for pk in sorted(need, key=lambda p: (p[0], str(p[1]))):
            c = need[pk]
            if kn[pk] >= c:
                continue
            out.append((pk, c))
        return out

    def _emit(self, e, fn, needs, inc_q):
        sem, dsem = self.sem, self.dsem

        def body(eng):
            for (kind, src), c in needs:
                if kind == "e":
                    eng.wait_ge(sem[src], c)
                else:
                    eng.wait_ge(dsem[src], 16 * c)
            ins = fn(eng)
            if inc_q is None:
                ins.then_inc(sem[e], 1)
            else:
                ins.then_inc(dsem[inc_q], 16)

        getattr(self.block, ENG_NAME[e])(body)
        kn = self.known[e]
        for pk, c in needs:
            if kn[pk] < c:
                kn[pk] = c

    def op(self, e, fn, reads=(), writes=()):
        needs = self._needs(e, reads, writes)
        self._emit(e, fn, needs, None)
        self.cnt[e] += 1
        tok = ("e", e, self.cnt[e])
        for k in reads:
            self.readers[k].append(tok)
        for k in writes:
            self.writer[k] = tok
            self.readers[k] = [tok]
        return tok

    DMA_WINDOW = 8

    def dma(self, q, fn, reads=(), writes=()):
        needs = self._needs(q, reads, writes)
        lag = self.dcnt[q] + 1 - self.DMA_WINDOW
        if lag > 0 and self.known[q][("d", q)] < lag:
            have = {pk: c for pk, c in needs}
            if have.get(("d", q), 0) < lag:
                have[("d", q)] = lag
            needs = sorted(have.items(), key=lambda kv: (kv[0][0], str(kv[0][1])))
        self._emit(q, fn, needs, q)
        self.dcnt[q] += 1
        tok = ("d", q, self.dcnt[q])
        for k in reads:
            self.readers[k].append(tok)
        for k in writes:
            self.writer[k] = tok
            self.readers[k] = [tok]
        return tok

    def barrier(self):
        snap_e = dict(self.cnt)
        snap_d = dict(self.dcnt)
        for e in ENG_NAME:
            needs = []
            for o in ENG_NAME:
                if snap_e[o] > self.known[e][("e", o)]:
                    needs.append((("e", o), snap_e[o]))
            for q in ("g", "y"):
                if snap_d[q] > self.known[e][("d", q)]:
                    needs.append((("d", q), snap_d[q]))
            if not needs:
                continue

            def wrapped(eng, needs=needs):
                for (kind, src), c in needs:
                    if kind == "e":
                        eng.wait_ge(self.sem[src], c)
                    else:
                        eng.wait_ge(self.dsem[src], 16 * c)

            getattr(self.block, ENG_NAME[e])(wrapped)
            for pk, c in needs:
                if self.known[e][pk] < c:
                    self.known[e][pk] = c



def _prep_shared(I):
    d = {}
    perm = np.concatenate([np.arange(h * 64, (h + 1) * 64) for h in HLIST])
    d["l0_wq"] = np.ascontiguousarray(np.asarray(I["L0_attn_Wq"], np.float32)[:, perm])
    d["l0_bq"] = np.ascontiguousarray(np.asarray(I["L0_attn_bq"], np.float32)[perm][None, :])
    d["l0_wk"] = np.ascontiguousarray(np.asarray(I["L0_attn_Wk"], np.float32))
    d["l0_bk"] = np.ascontiguousarray(np.asarray(I["L0_attn_bk"], np.float32)[None, :])
    d["l0_wv"] = np.ascontiguousarray(np.asarray(I["L0_attn_Wv"], np.float32))
    d["l0_bv"] = np.ascontiguousarray(np.asarray(I["L0_attn_bv"], np.float32)[None, :])
    d["l0_wo"] = np.ascontiguousarray(np.asarray(I["L0_attn_Wo"], np.float32)[perm, :])
    d["l0_bo"] = np.ascontiguousarray(np.asarray(I["L0_attn_bo"], np.float32)[None, :])

    d["moe_scale"] = np.ascontiguousarray(
        np.asarray(I["L0_ffn_norm_scale"], np.float32).reshape(NCH, 128).T)
    d["moe_wg"] = np.ascontiguousarray(np.asarray(I["L0_ffn_wg"], np.float32))
    d["moe_bg"] = np.ascontiguousarray(np.asarray(I["L0_ffn_bg"], np.float32)[None, :])
    w1 = np.asarray(I["L0_ffn_w1"], np.float32)
    b1 = np.asarray(I["L0_ffn_b1"], np.float32)
    w1p = np.empty((MOE_E, 1024, 2048), np.float32)
    b1p = np.empty((MOE_E, 1, 2048), np.float32)
    for e in range(MOE_E):
        a, bb = w1[e][:, 0::2], w1[e][:, 1::2]
        ba, bbv = b1[e][0::2], b1[e][1::2]
        for j in range(8):
            w1p[e][:, j * 256:j * 256 + 128] = a[:, j * 128:(j + 1) * 128]
            w1p[e][:, j * 256 + 128:(j + 1) * 256] = bb[:, j * 128:(j + 1) * 128]
            b1p[e][0, j * 256:j * 256 + 128] = ba[j * 128:(j + 1) * 128]
            b1p[e][0, j * 256 + 128:(j + 1) * 256] = bbv[j * 128:(j + 1) * 128]
    d["moe_w1"] = w1p
    d["moe_b1"] = b1p
    d["moe_w2"] = np.ascontiguousarray(np.asarray(I["L0_ffn_w2"], np.float32))
    d["moe_b2"] = np.ascontiguousarray(np.asarray(I["L0_ffn_b2"], np.float32))

    for nm in ("Wq", "Wk", "Wv", "Wo", "Wfc"):
        d[f"l1_{nm.lower()}"] = np.ascontiguousarray(np.asarray(I[f"L1_attn_{nm}"], np.float32).T)
    for nm in ("bq", "bk", "bv", "bo", "bfc"):
        d[f"l1_{nm}"] = np.ascontiguousarray(np.asarray(I[f"L1_attn_{nm}"], np.float32)[None, :])
    projT = np.asarray(I["L1_attn_proj"], np.float32).T
    d["l1_projT"] = np.ascontiguousarray(np.concatenate([projT, projT], axis=0))

    d["f_w1"] = np.ascontiguousarray(np.asarray(I["L1_ffn_w1_w"], np.float32).T)
    d["f_w2"] = np.ascontiguousarray(np.asarray(I["L1_ffn_w2_w"], np.float32).T)
    d["f_h1"] = np.ascontiguousarray(np.asarray(I["L1_ffn_h1_w"], np.float32).T)
    d["f_h2"] = np.ascontiguousarray(np.asarray(I["L1_ffn_h2_w"], np.float32).T)
    d["f_w3"] = np.ascontiguousarray(np.asarray(I["L1_ffn_w3_w"], np.float32).T)
    d["f_s1"] = np.ascontiguousarray(np.asarray(I["L1_ffn_w1_s"], np.float32).reshape(-1, 128).T)
    d["f_s2"] = np.ascontiguousarray(np.asarray(I["L1_ffn_w2_s"], np.float32).reshape(-1, 128).T)
    d["f_sh1"] = np.ascontiguousarray(np.asarray(I["L1_ffn_h1_s"], np.float32).reshape(-1, 128).T)
    d["f_sh2"] = np.ascontiguousarray(np.asarray(I["L1_ffn_h2_s"], np.float32).reshape(-1, 128).T)
    d["f_s3"] = np.ascontiguousarray(np.asarray(I["L1_ffn_w3_s"], np.float32).reshape(-1, 128).T)

    def _pstream(w, kchunks):
        K, O = w.shape
        assert K == kchunks * 128
        return np.ascontiguousarray(w.reshape(kchunks, 128, O).transpose(1, 0, 2).reshape(128, kchunks * O))

    def _ostream(w, kchunks, osz=128):
        K, O = w.shape
        noc = O // osz
        a = w.reshape(kchunks, 128, noc, osz).transpose(2, 1, 0, 3)
        return np.ascontiguousarray(a.reshape(noc, 128, kchunks * osz))

    d["l0_wq_s"] = _pstream(d["l0_wq"], NCH)
    d["l0_wk_s"] = _pstream(d["l0_wk"], NCH)
    d["l0_wv_s"] = _pstream(d["l0_wv"], NCH)
    d["l0_wo_s"] = _pstream(d["l0_wo"], NCH)
    d["moe_wg_s"] = _pstream(d["moe_wg"], NCH)
    d["moe_w1_s"] = np.ascontiguousarray(np.stack(
        [np.stack([_pstream(d["moe_w1"][e][:, j * 256:(j + 1) * 256], NCH) for j in range(8)])
         for e in range(MOE_E)]))
    d["moe_w2_s"] = np.ascontiguousarray(np.stack(
        [_pstream(d["moe_w2"][e], NCH) for e in range(MOE_E)]))
    for nm in ("l1_wq", "l1_wk", "l1_wv", "l1_wo", "l1_wfc"):
        d[nm + "_s"] = _pstream(d[nm], NCH)
    d["f_w1_s"] = _ostream(d["f_w1"], NCH)
    d["f_w2_s"] = _ostream(d["f_w2"], NCH)
    d["f_h1_s"] = _ostream(d["f_h1"], 32)
    d["f_h2_s"] = _ostream(d["f_h2"], 32)
    d["f_w3_s"] = _ostream(d["f_w3"], 32)

    d["eye"] = np.eye(128, dtype=np.float32)
    d["ones_row"] = np.ones((1, 512), np.float32)
    r = np.arange(128)[:, None]
    u = np.arange(384)[None, :]
    d["l0mask"] = ((u >= r) & (u < r + 256)).astype(np.float32)
    return d


def _prep_core(I, c):
    xf = np.asarray(I["L0_attn_x"], np.float32).reshape(B * S, DM)
    t0 = c * TPC
    seq = t0 // S
    s_lo, s_hi = seq * S, (seq + 1) * S
    lo, hi = t0 - HALO, t0 + TPC + HALO
    pad_l, pad_r = max(0, s_lo - lo), max(0, hi - s_hi)
    sl = xf[max(lo, s_lo):min(hi, s_hi)]
    shard = np.concatenate(
        [np.zeros((pad_l, DM), np.float32), sl, np.zeros((pad_r, DM), np.float32)], axis=0)
    kvmask = np.ones((1, TH), np.float32)
    if pad_l:
        kvmask[0, :pad_l] = 0.0
    if pad_r:
        kvmask[0, TH - pad_r:] = 0.0
    hsel = np.zeros((1, 2), np.float32)
    hsel[0, seq] = 1.0
    ccmaxbias = np.full((2, 16), -1.0e30, np.float32)
    ccmaxbias[seq, :] = 0.0
    return {
        "x_fm": np.ascontiguousarray(shard.T),
        "kvmask": kvmask,
        "hsel": hsel,
        "hselc": np.ascontiguousarray(hsel.T),
        "ccmaxbias": ccmaxbias,
    }



class Ctx:
    pass


def _setup_common(nc, es, block, shared, want):
    cx = Ctx()
    cx.nc = nc
    cx.es = es
    cx.tk = TK(nc, block, es)
    cx.dram = {}
    for nm in want:
        arr = shared[nm]
        dt = BF16 if arr.dtype == ml_dtypes.bfloat16 else F32
        cx.dram[nm] = nc.dram_tensor(nm, list(arr.shape), dt, kind="ExternalInput").ap()

    cx._scopes = [es]
    cx.sb = lambda name, shape, dt: cx._scopes[-1].enter_context(nc.sbuf_tensor(name, shape, dt))

    def _push():
        cx._scopes.append(contextlib.ExitStack())

    def _pop():
        cx._scopes.pop().close()
        cx.tk.barrier()
    cx.push_scope = _push
    cx.pop_scope = _pop
    cx.eye = cx.sb("c_eye", [128, 128], F32)
    cx.eye16 = cx.sb("c_eye16", [128, 128], BF16)
    cx.ones_row = cx.sb("c_ones_row", [1, 512], F32)
    cx.ones_col16 = cx.sb("c_ones16", [128, 1], BF16)
    tk = cx.tk
    tk.dma("y", lambda e: e.dma_start(out=cx.eye[:, :], in_=cx.dram["eye"][:, :]), writes=["c:eye"])
    tk.dma("y", lambda e: e.dma_start(out=cx.ones_row[:, :], in_=cx.dram["ones_row"][:, :]), writes=["c:ones_row"])
    tk.op("v", lambda e: e.tensor_copy(cx.eye16[:, :], cx.eye[:, :]), reads=["c:eye"], writes=["c:eye16"])
    tk.op("v", lambda e: e.memset(cx.ones_col16[:, :], 1.0), writes=["c:ones_col16"])
    cx.ones_row16 = cx.sb("c_ones_row16", [1, 512], BF16)
    tk.op("v", lambda e: e.memset(cx.ones_row16[:, :], 1.0), writes=["c:ones_row16"])

    cx.pf = [es.enter_context(nc.psum_tensor(f"pf{i}", [128, 512], F32)) for i in range(6)]
    cx.pb = [es.enter_context(nc.psum_tensor(f"pb{i}", [128, 1024], BF16)) for i in range(2)]
    cx.ln_init = False
    return cx


def _chunk(t, kc, width, c0=0, c1=None):
    if c1 is None:
        c1 = width
    return t[:, kc * width + c0: kc * width + c1]



def emit_ln(cx, src_sb, src_key, src_width, src_c0, ntok, dst_sb, dst_key, dst_width, dst_c0):
    nc, tk = cx.nc, cx.tk
    if not cx.ln_init:
        cx.ln_scr = cx.sb("ln_scr", [128, 2 * 512], F32)
        cx.ln_rows = cx.sb("ln_rows", [1, 4 * 512], F32)
        cx.ln_ones_col = cx.sb("ln_ones_col", [128, 1], F32)
        cx.ln_eps = cx.sb("ln_eps", [1, 1], F32)
        tk.op("v", lambda e: e.memset(cx.ln_ones_col[:, :], 1.0), writes=["ln:ones_col"])
        tk.op("v", lambda e: e.memset(cx.ln_eps[:, :], 1e-5), writes=["ln:eps"])
        cx.ln_init = True
    scr, rows = cx.ln_scr, cx.ln_rows

    def sch(kc, a, b):
        return src_sb[:, kc * src_width + src_c0 + a: kc * src_width + src_c0 + b]

    def dch(kc, a, b):
        return dst_sb[:, kc * dst_width + dst_c0 + a: kc * dst_width + dst_c0 + b]

    for ti in range((ntok + 511) // 512):
        c0 = ti * 512
        w = min(512, ntok - c0)
        ps_s, ps_q, ps_b, ps_c = cx.pf[0], cx.pf[1], cx.pf[2], cx.pf[3]

        def mm_sum(e, c0=c0, w=w):
            ins = None
            for kc in range(NCH):
                ins = e.matmul(ps_s[0:1, 0:w], cx.ln_ones_col[:, :], sch(kc, c0, c0 + w),
                               start=(kc == 0), stop=(kc == NCH - 1))
            return ins
        tk.op("t", mm_sum, reads=[src_key, "ln:ones_col"], writes=["ps:0"])
        for kc in range(NCH):
            par = kc % 2
            tk.op("s", lambda e, kc=kc, par=par, c0=c0, w=w: e.activation(
                scr[:, par * 512:par * 512 + w], sch(kc, c0, c0 + w), AF.Square),
                reads=[src_key], writes=[f"ln:scr{par}"])
            tk.op("t", lambda e, kc=kc, par=par, w=w: e.matmul(
                ps_q[0:1, 0:w], cx.ln_ones_col[:, :], scr[:, par * 512:par * 512 + w],
                start=(kc == 0), stop=(kc == NCH - 1)),
                reads=[f"ln:scr{par}", "ln:ones_col"], writes=["ps:1"])
        tk.op("v", lambda e, w=w: e.tensor_scalar(
            rows[0:1, 0:w], ps_s[0:1, 0:w], 1.0 / DM, None, ALU.mult),
            reads=["ps:0"], writes=["ln:m"])
        tk.op("v", lambda e, w=w: e.tensor_mul(
            rows[0:1, 512:512 + w], rows[0:1, 0:w], rows[0:1, 0:w]),
            reads=["ln:m"], writes=["ln:msq"])
        tk.op("v", lambda e, w=w: e.tensor_scalar(
            rows[0:1, 1024:1024 + w], ps_q[0:1, 0:w], 1.0 / DM, None, ALU.mult),
            reads=["ps:1"], writes=["ln:var"])
        tk.op("v", lambda e, w=w: e.tensor_sub(
            rows[0:1, 1024:1024 + w], rows[0:1, 1024:1024 + w], rows[0:1, 512:512 + w]),
            reads=["ln:var", "ln:msq"], writes=["ln:var"])
        tk.op("s", lambda e, w=w: e.activation(
            rows[0:1, 1536:1536 + w], rows[0:1, 1024:1024 + w], AF.Sqrt,
            bias=cx.ln_eps[0:1, 0:1]),
            reads=["ln:var", "ln:eps"], writes=["ln:sd"])
        tk.op("v", lambda e, w=w: e.reciprocal(
            rows[0:1, 1536:1536 + w], rows[0:1, 1536:1536 + w]),
            reads=["ln:sd"], writes=["ln:sd"])
        tk.op("t", lambda e, w=w: e.matmul(
            ps_b[:, 0:w], cx.ones_row[0:1, 0:128], rows[0:1, 1536:1536 + w],
            start=True, stop=True), reads=["ln:sd", "c:ones_row"], writes=["ps:2"])
        tk.op("v", lambda e, w=w: e.tensor_mul(
            rows[0:1, 512:512 + w], rows[0:1, 0:w], rows[0:1, 1536:1536 + w]),
            reads=["ln:m", "ln:sd"], writes=["ln:msq"])
        tk.op("t", lambda e, w=w: e.matmul(
            ps_c[:, 0:w], cx.ones_row[0:1, 0:128], rows[0:1, 512:512 + w],
            start=True, stop=True), reads=["ln:msq", "c:ones_row"], writes=["ps:3"])
        for kc in range(NCH):
            par = kc % 2
            tk.op("v", lambda e, kc=kc, par=par, c0=c0, w=w: e.tensor_mul(
                scr[:, par * 512:par * 512 + w], sch(kc, c0, c0 + w), ps_b[:, 0:w]),
                reads=[src_key, "ps:2"], writes=[f"ln:scr{par}"])
            tk.op("v", lambda e, kc=kc, par=par, c0=c0, w=w: e.tensor_sub(
                dch(kc, c0, c0 + w), scr[:, par * 512:par * 512 + w], ps_c[:, 0:w]),
                reads=[f"ln:scr{par}", "ps:3"], writes=[dst_key])


def emit_l0_attn(cx, xn_sb, xn_key, x_sb, x_key, tiles=None, dbg=None):
    nc, tk, sb = cx.nc, cx.tk, cx.sb
    wq = sb("l0_wq_sb", [128, NCH * 1024], BF16)
    wo = sb("l0_wo_sb", [128, NCH * 1024], BF16)
    wk = sb("l0_wk_sb", [128, NCH * 128], BF16)
    wv = sb("l0_wv_sb", [128, NCH * 128], BF16)
    brow = sb("l0_brow", [1, 2304], F32)
    kvm = sb("l0_kvm", [1, TH], F32)
    msk = sb("l0_msk", [128, 384], F32)
    kbf = sb("l0_k", [128, TH], BF16)
    vtm = sb("l0_v", [128, 18 * 128], BF16)
    qt = sb("l0_qt", [128, 2 * 1024], BF16)
    at = sb("l0_at", [128, 2 * 1024], BF16)
    pr = sb("l0_pr", [128, 2 * 384], F32)
    prn = sb("l0_prn", [128, 2 * 384], BF16)
    prT = sb("l0_prT", [128, 2 * 384], BF16)
    rowm = sb("l0_rowm", [128, 8], F32)

    tk.dma("g", lambda e: e.dma_start(out=wq[:, :], in_=cx.dram["l0_wq_s"][:, :]), writes=["w:l0q"])
    tk.dma("g", lambda e: e.dma_start(out=wk[:, :], in_=cx.dram["l0_wk_s"][:, :]), writes=["w:l0k"])
    tk.dma("g", lambda e: e.dma_start(out=wv[:, :], in_=cx.dram["l0_wv_s"][:, :]), writes=["w:l0v"])
    tk.dma("g", lambda e: e.dma_start(out=wo[:, :], in_=cx.dram["l0_wo_s"][:, :]), writes=["w:l0o"])
    tk.dma("y", lambda e: e.dma_start(out=brow[0:1, 0:1024], in_=cx.dram["l0_bq"][:, :]), writes=["b:l0"])
    tk.dma("y", lambda e: e.dma_start(out=brow[0:1, 1024:1152], in_=cx.dram["l0_bk"][:, :]), writes=["b:l0"])
    tk.dma("y", lambda e: e.dma_start(out=brow[0:1, 1152:1280], in_=cx.dram["l0_bv"][:, :]), writes=["b:l0"])
    tk.dma("y", lambda e: e.dma_start(out=brow[0:1, 1280:2304], in_=cx.dram["l0_bo"][:, :]), writes=["b:l0"])
    tk.dma("y", lambda e: e.dma_start(out=kvm[:, :], in_=cx.dram["kvmask"][:, :]), writes=["b:kvm"])
    tk.dma("y", lambda e: e.dma_start(out=msk[:, :], in_=cx.dram["l0mask"][:, :]), writes=["b:msk"])

    for ti in range(5):
        c0 = ti * 512
        w = min(512, TH - c0)
        pi = ti % 2
        ps = cx.pf[pi]

        def mm_k(e, c0=c0, w=w, ps=ps):
            for kc in range(NCH):
                e.matmul(ps[:, 0:w], wk[:, kc * 128:(kc + 1) * 128],
                         _chunk(xn_sb, kc, TH, c0, c0 + w), start=(kc == 0), stop=False)
            return e.matmul(ps[:, 0:w], brow[0:1, 1024:1152], kvm[0:1, c0:c0 + w],
                            start=False, stop=True)
        tk.op("t", mm_k, reads=[xn_key, "w:l0k", "b:l0", "b:kvm"], writes=[f"ps:{pi}"])
        tk.op("s", lambda e, c0=c0, w=w, ps=ps: e.copy(kbf[:, c0:c0 + w], ps[:, 0:w]),
              reads=[f"ps:{pi}"], writes=["l0:k"])
    for j in range(18):
        c0 = j * 128
        pi = 2 + j % 2
        ps = cx.pf[pi]

        def mm_v(e, c0=c0, ps=ps):
            for kc in range(NCH):
                e.matmul(ps[:, 0:128], _chunk(xn_sb, kc, TH, c0, c0 + 128),
                         wv[:, kc * 128:(kc + 1) * 128], start=(kc == 0), stop=False)
            return e.matmul(ps[:, 0:128], kvm[0:1, c0:c0 + 128], brow[0:1, 1152:1280],
                            start=False, stop=True)
        tk.op("t", mm_v, reads=[xn_key, "w:l0v", "b:l0", "b:kvm"], writes=[f"ps:{pi}"])
        tk.op("s", lambda e, j=j, ps=ps: e.copy(vtm[:, j * 128:(j + 1) * 128], ps[:, 0:128]),
              reads=[f"ps:{pi}"], writes=["l0:v"])

    if tiles is None:
        tiles = range(16)
    if dbg is not None:
        tk.dma("y", lambda e: e.dma_start(out=dbg["k"][:, :], in_=kbf[:, :]), reads=["l0:k"], writes=["dbg"])
        tk.dma("y", lambda e: e.dma_start(out=dbg["v"][:, :], in_=vtm[:, :]), reads=["l0:v"], writes=["dbg"])
    for ti in tiles:
        par = ti % 2
        a = ti * 128
        ah = a + HALO
        for qc in range(NCH):
            pi = 4 + qc % 2
            ps = cx.pf[pi]

            def mm_qc(e, qc=qc, ah=ah, ps=ps):
                for kc in range(NCH):
                    e.matmul(ps[:, 0:128],
                             wq[:, kc * 1024 + qc * 128: kc * 1024 + (qc + 1) * 128],
                             _chunk(xn_sb, kc, TH, ah, ah + 128), start=(kc == 0), stop=False)
                return e.matmul(ps[:, 0:128], brow[0:1, qc * 128:(qc + 1) * 128],
                                cx.ones_row[0:1, 0:128], start=False, stop=True)
            tk.op("t", mm_qc, reads=[xn_key, "w:l0q", "b:l0", "c:ones_row"], writes=[f"ps:{pi}"])
            tk.op("s", lambda e, qc=qc, par=par, ps=ps: e.copy(
                qt[:, par * 1024 + qc * 128: par * 1024 + (qc + 1) * 128], ps[:, 0:128]),
                reads=[f"ps:{pi}"], writes=[f"l0:qt{par}"])
        for h in range(16):
            qc, base = h // 2, 64 * (h % 2)
            hp = h % 2
            ps_s = cx.pf[hp]
            ps_t = cx.pb[hp]
            ps_o = cx.pf[2 + hp]
            tk.op("t", lambda e, qc=qc, base=base, par=par, a=a, ps_s=ps_s: e.matmul(
                ps_s[0:128, 0:384],
                qt[base:base + 64, par * 1024 + qc * 128: par * 1024 + (qc + 1) * 128],
                kbf[base:base + 64, a:a + 384], start=True, stop=True),
                reads=[f"l0:qt{par}", "l0:k"], writes=[f"ps:{hp}"])
            if dbg is not None and ti == list(tiles)[0] and h == 0 and "sc" in dbg:
                tk.op("v", lambda e, ps_s=ps_s: e.tensor_copy(pr[:, 384:768], ps_s[:, 0:384]),
                      reads=["ps:0"], writes=["l0:scdump"])
                tk.dma("y", lambda e: e.dma_start(out=dbg["sc"][:, :], in_=pr[:, 384:768]),
                       reads=["l0:scdump"], writes=["dbg"])
            tk.op("v", lambda e, hp=hp, ps_s=ps_s: e.reduce_max(
                rowm[:, hp:hp + 1], ps_s[:, 0:384], axis=AX),
                reads=[f"ps:{hp}"], writes=[f"l0:rm{hp}"])
            tk.op("s", lambda e, hp=hp: e.mul(
                rowm[:, 2 + hp:3 + hp], rowm[:, hp:hp + 1], -1.0),
                reads=[f"l0:rm{hp}"], writes=[f"l0:nrm{hp}"])
            tk.op("s", lambda e, hp=hp, ps_s=ps_s: e.activation(
                pr[:, hp * 384:(hp + 1) * 384], ps_s[:, 0:384], AF.Exp,
                bias=rowm[:, 2 + hp:3 + hp]),
                reads=[f"ps:{hp}", f"l0:nrm{hp}"], writes=[f"l0:pr{hp}"])
            tk.op("v", lambda e, hp=hp: e.scalar_tensor_tensor(
                pr[:, hp * 384:(hp + 1) * 384], pr[:, hp * 384:(hp + 1) * 384],
                1.0, msk[:, 0:384], ALU.mult, ALU.mult,
                accum_out=rowm[:, 4 + hp:5 + hp]),
                reads=[f"l0:pr{hp}", "b:msk"], writes=[f"l0:pr{hp}", f"l0:rs{hp}"])
            tk.op("v", lambda e, hp=hp: e.reciprocal(
                rowm[:, 6 + hp:7 + hp], rowm[:, 4 + hp:5 + hp]),
                reads=[f"l0:rs{hp}"], writes=[f"l0:ri{hp}"])
            tk.op("v", lambda e, hp=hp: e.tensor_scalar(
                prn[:, hp * 384:(hp + 1) * 384], pr[:, hp * 384:(hp + 1) * 384],
                rowm[:, 6 + hp:7 + hp], None, ALU.mult),
                reads=[f"l0:pr{hp}", f"l0:ri{hp}"], writes=[f"l0:prn{hp}"])

            def mm_tr(e, hp=hp, ps_t=ps_t):
                ins = None
                for uc in range(3):
                    ins = e.transpose(ps_t[0:128, uc * 128:(uc + 1) * 128],
                                      prn[:, hp * 384 + uc * 128: hp * 384 + (uc + 1) * 128],
                                      cx.eye16[:, :])
                return ins
            tk.op("t", mm_tr, reads=[f"l0:prn{hp}", "c:eye16"], writes=[f"ps:b{hp}"])
            tk.op("s", lambda e, hp=hp, ps_t=ps_t: e.copy(
                prT[:, hp * 384:(hp + 1) * 384], ps_t[0:128, 0:384]),
                reads=[f"ps:b{hp}"], writes=[f"l0:prT{hp}"])

            def mm_pv(e, hp=hp, base=base, a=a, ps_o=ps_o):
                ins = None
                for uc in range(3):
                    t0 = a // 128 + uc
                    ins = e.matmul(
                        ps_o[0:64, 0:128],
                        vtm[:, t0 * 128 + base: t0 * 128 + base + 64],
                        prT[:, hp * 384 + uc * 128: hp * 384 + (uc + 1) * 128],
                        start=(uc == 0), stop=(uc == 2))
                return ins
            tk.op("t", mm_pv, reads=["l0:v", f"l0:prT{hp}"], writes=[f"ps:{2 + hp}"])
            tk.op("s", lambda e, qc=qc, base=base, par=par, hp=hp, ps_o=ps_o: e.copy(
                at[base:base + 64, par * 1024 + qc * 128: par * 1024 + (qc + 1) * 128],
                ps_o[0:64, 0:128]),
                reads=[f"ps:{2 + hp}"], writes=[f"l0:at{par}"])
        if dbg is not None and ti == list(tiles)[0]:
            tk.dma("y", lambda e, par=par: e.dma_start(out=dbg["qt"][:, :], in_=qt[:, par * 1024:(par + 1) * 1024]), reads=[f"l0:qt{par}"], writes=["dbg"])
            tk.dma("y", lambda e: e.dma_start(out=dbg["pr"][:, :], in_=pr[:, 0:384]), reads=["l0:pr0"], writes=["dbg"])
            tk.dma("y", lambda e: e.dma_start(out=dbg["prn"][:, :], in_=prn[:, 0:384]), reads=["l0:prn0"], writes=["dbg"])
            tk.dma("y", lambda e: e.dma_start(out=dbg["prT"][:, :], in_=prT[:, 0:384]), reads=["l0:prT0"], writes=["dbg"])
            tk.dma("y", lambda e, par=par: e.dma_start(out=dbg["at"][:, :], in_=at[:, par * 1024:(par + 1) * 1024]), reads=[f"l0:at{par}"], writes=["dbg"])
            tk.dma("y", lambda e: e.dma_start(out=dbg["rowm"][:, :], in_=rowm[:, :]), reads=["l0:rs0", "l0:ri0", "l0:rm0"], writes=["dbg"])
        for oc in range(NCH):
            pi = 4 + oc % 2
            ps = cx.pf[pi]

            def mm_wo(e, oc=oc, par=par, ps=ps):
                for kc in range(NCH):
                    e.matmul(ps[:, 0:128],
                             wo[:, kc * 1024 + oc * 128: kc * 1024 + (oc + 1) * 128],
                             at[:, par * 1024 + kc * 128: par * 1024 + (kc + 1) * 128],
                             start=(kc == 0), stop=False)
                return e.matmul(ps[:, 0:128], brow[0:1, 1280 + oc * 128:1280 + (oc + 1) * 128],
                                cx.ones_row[0:1, 0:128], start=False, stop=True)
            tk.op("t", mm_wo, reads=[f"l0:at{par}", "w:l0o", "b:l0", "c:ones_row"],
                  writes=[f"ps:{pi}"])
            tk.op("v", lambda e, oc=oc, a=a, ps=ps, pi=pi: e.tensor_add(
                _chunk(x_sb, oc, TPC, a, a + 128), _chunk(x_sb, oc, TPC, a, a + 128),
                ps[:, 0:128]),
                reads=[f"ps:{pi}", x_key], writes=[x_key])



L0A_SHARED = ["l0_wq_s", "l0_bq", "l0_wk_s", "l0_bk", "l0_wv_s", "l0_bv", "l0_wo_s", "l0_bo",
              "eye", "ones_row", "l0mask"]


def build_l0a():
    nc = bass.Bass("TRN2", target_bir_lowering=False)
    es = contextlib.ExitStack()
    with es:
        x_in = nc.dram_tensor("x_fm", [DM, TH], F32, kind="ExternalInput").ap()
        kvm_in = nc.dram_tensor("kvmask", [1, TH], F32, kind="ExternalInput").ap()
        xn_out = nc.dram_tensor("xn_out", [DM, TH], BF16, kind="ExternalOutput").ap()
        x1p_out = nc.dram_tensor("x1p_out", [DM, TPC], F32, kind="ExternalOutput").ap()

        block = es.enter_context(nc.Block())
        cx = _setup_common(nc, es, block, _SHARED_CACHE, L0A_SHARED)
        cx.dram["kvmask"] = kvm_in
        tk = cx.tk

        x_sb = cx.sb("x_sb", [128, NCH * TPC], F32)
        xh_sb = cx.sb("xh_sb", [128, NCH * 256], F32)
        xn_sb = cx.sb("xn_sb", [128, NCH * TH], BF16)

        for kc in range(NCH):
            tk.dma("y", lambda e, kc=kc: e.dma_start(
                out=_chunk(x_sb, kc, TPC), in_=x_in[kc * 128:(kc + 1) * 128, HALO:HALO + TPC]),
                writes=["x"])
            tk.dma("y", lambda e, kc=kc: e.dma_start(
                out=_chunk(xh_sb, kc, 256, 0, 128), in_=x_in[kc * 128:(kc + 1) * 128, 0:HALO]),
                writes=["xh"])
            tk.dma("y", lambda e, kc=kc: e.dma_start(
                out=_chunk(xh_sb, kc, 256, 128, 256),
                in_=x_in[kc * 128:(kc + 1) * 128, TH - HALO:TH]),
                writes=["xh"])

        emit_ln(cx, x_sb, "x", TPC, 0, TPC, xn_sb, "xn", TH, HALO)
        emit_ln(cx, xh_sb, "xh", 256, 0, 128, xn_sb, "xn", TH, 0)
        emit_ln(cx, xh_sb, "xh", 256, 128, 128, xn_sb, "xn", TH, TH - HALO)

        emit_l0_attn(cx, xn_sb, "xn", x_sb, "x")

        for kc in range(NCH):
            tk.dma("y", lambda e, kc=kc: e.dma_start(
                out=xn_out[kc * 128:(kc + 1) * 128, :], in_=_chunk(xn_sb, kc, TH)),
                reads=["xn"], writes=["o:xn"])
            tk.dma("y", lambda e, kc=kc: e.dma_start(
                out=x1p_out[kc * 128:(kc + 1) * 128, :], in_=_chunk(x_sb, kc, TPC)),
                reads=["x"], writes=["o:x1p"])

        n = tk.dcnt["y"]
        block.sync(lambda eng: eng.wait_ge(tk.dsem["y"], 16 * n))
    return nc




MOE_SHARED = ["moe_scale", "moe_wg_s", "moe_bg", "moe_w1_s", "moe_b1", "moe_w2_s", "moe_b2",
              "eye", "ones_row"]


def emit_moe(cx, xn_sb, xn_key, xnw, xc0, x_sb, x_key, experts=range(MOE_E), do_h=True, do_w2=True):
    nc, tk, sb = cx.nc, cx.tk, cx.sb
    E = MOE_E

    def xch(kc, a, b):
        return xn_sb[:, kc * xnw + xc0 + a: kc * xnw + xc0 + b]

    wgb = sb("moe_wg_sb", [128, NCH * 16], BF16)
    bgr = sb("moe_bg_sb", [1, 16], F32)
    sclc = sb("moe_scl_sb", [128, NCH], F32)
    b2sb = sb("moe_b2_sb", [16, 1024], BF16)
    w1b = sb("moe_w1_sb", [128, 2 * NCH * 256], BF16)
    b1r = sb("moe_b1_sb", [1, 2048], BF16)
    w2b = sb("moe_w2_sb", [128, NCH * 1024], BF16)
    ssb = sb("moe_s_sb", [128, 8 * 1024], BF16)
    acc = sb("moe_acc", [128, 8 * 1024], F32)
    gsc = sb("moe_gsc", [128, 64], F32)
    lgs = sb("moe_lg", [128, 16], F32)
    gts = sb("moe_gts", [128, 16 * 16], F32)
    gTb = sb("moe_gT", [16, 16 * 128], BF16)
    msc = sb("moe_msc", [128, 4 * 512], F32)
    c7 = sb("moe_c7", [128, 1], F32)
    tk.op("v", lambda e: e.memset(c7[:, :], 7.0), writes=["moe:c7"])

    tk.dma("g", lambda e: e.dma_start(out=wgb[:, :], in_=cx.dram["moe_wg_s"][:, :]), writes=["w:wg"])
    tk.dma("y", lambda e: e.dma_start(out=bgr[:, :], in_=cx.dram["moe_bg"][:, :]), writes=["b:bg"])
    tk.dma("y", lambda e: e.dma_start(out=sclc[:, :], in_=cx.dram["moe_scale"][:, :]), writes=["b:scl"])
    tk.dma("g", lambda e: e.dma_start(out=b2sb[:, :], in_=cx.dram["moe_b2"][:, :]), writes=["w:b2"])

    for kc in range(NCH):
        for ti in range(4):
            c0 = ti * 512
            tk.op("v", lambda e, kc=kc, c0=c0: e.tensor_add(
                _chunk(x_sb, kc, TPC, c0, c0 + 512), _chunk(x_sb, kc, TPC, c0, c0 + 512),
                xch(kc, c0, c0 + 512)),
                reads=[x_key, xn_key], writes=[x_key])

    scr, rows = cx.ln_scr, cx.ln_rows
    for ti in range(4):
        c0 = ti * 512
        w = 512
        ps_q, ps_b = cx.pf[1], cx.pf[2]
        for kc in range(NCH):
            par = kc % 2
            tk.op("s", lambda e, kc=kc, par=par, c0=c0, w=w: e.activation(
                scr[:, par * 512:par * 512 + w], xch(kc, c0, c0 + w), AF.Square),
                reads=[xn_key], writes=[f"ln:scr{par}"])
            tk.op("t", lambda e, kc=kc, par=par, w=w: e.matmul(
                ps_q[0:1, 0:w], cx.ln_ones_col[:, :], scr[:, par * 512:par * 512 + w],
                start=(kc == 0), stop=(kc == NCH - 1)),
                reads=[f"ln:scr{par}", "ln:ones_col"], writes=["ps:1"])
        tk.op("v", lambda e, w=w: e.tensor_scalar(
            rows[0:1, 0:w], ps_q[0:1, 0:w], 1.0 / DM, None, ALU.mult),
            reads=["ps:1"], writes=["ln:m"])
        tk.op("s", lambda e, w=w: e.activation(
            rows[0:1, 512:512 + w], rows[0:1, 0:w], AF.Sqrt, bias=cx.ln_eps[0:1, 0:1]),
            reads=["ln:m", "ln:eps"], writes=["ln:sd"])
        tk.op("v", lambda e, w=w: e.reciprocal(
            rows[0:1, 512:512 + w], rows[0:1, 512:512 + w]),
            reads=["ln:sd"], writes=["ln:sd"])
        tk.op("t", lambda e, w=w: e.matmul(
            ps_b[:, 0:w], cx.ones_row[0:1, 0:128], rows[0:1, 512:512 + w],
            start=True, stop=True), reads=["ln:sd", "c:ones_row"], writes=["ps:2"])
        for kc in range(NCH):
            par = kc % 2
            tk.op("v", lambda e, kc=kc, par=par, c0=c0, w=w: e.tensor_mul(
                scr[:, par * 512:par * 512 + w], xch(kc, c0, c0 + w), ps_b[:, 0:w]),
                reads=[xn_key, "ps:2"], writes=[f"ln:scr{par}"])
            tk.op("v", lambda e, kc=kc, par=par, c0=c0, w=w: e.tensor_scalar(
                xch(kc, c0, c0 + w), scr[:, par * 512:par * 512 + w],
                sclc[:, kc:kc + 1], None, ALU.mult),
                reads=[f"ln:scr{par}", "b:scl"], writes=[xn_key])

    for tg in range(16):
        t0 = tg * 128
        ps = cx.pf[3]

        def mm_lg(e, t0=t0, ps=ps):
            for kc in range(NCH):
                e.matmul(ps[:, 0:16], xch(kc, t0, t0 + 128), wgb[:, kc * 16:(kc + 1) * 16],
                         start=(kc == 0), stop=False)
            return e.matmul(ps[:, 0:16], cx.ones_row[0:1, 0:128], bgr[0:1, 0:16],
                            start=False, stop=True)
        tk.op("t", mm_lg, reads=[xn_key, "w:wg", "b:bg", "c:ones_row"], writes=["ps:3"])
        tk.op("v", lambda e, ps=ps: e.tensor_copy(lgs[:, :], ps[:, 0:16]),
              reads=["ps:3"], writes=["moe:lg"])
        tk.op("v", lambda e: e.tensor_copy(gsc[:, 0:16], lgs[:, :]),
              reads=["moe:lg"], writes=["moe:cur"])
        tk.op("v", lambda e: e.reduce_max(gsc[:, 32:33], gsc[:, 0:16], axis=AX),
              reads=["moe:cur"], writes=["moe:m1"])
        tk.op("s", lambda e: e.mul(gsc[:, 34:35], gsc[:, 32:33], -1.0),
              reads=["moe:m1"], writes=["moe:nm1"])
        for it in range(3):
            mkc = 32 if it == 0 else 33
            tk.op("v", lambda e, mkc=mkc: e.tensor_scalar(
                gsc[:, 16:32], gsc[:, 0:16], gsc[:, mkc:mkc + 1], None, ALU.is_ge),
                reads=["moe:cur", "moe:m1" if mkc == 32 else "moe:mk"], writes=["moe:mask"])
            tk.op("v", lambda e: e.scalar_tensor_tensor(
                gsc[:, 0:16], gsc[:, 16:32], -1.0e9, gsc[:, 0:16], ALU.mult, ALU.add),
                reads=["moe:mask", "moe:cur"], writes=["moe:cur"])
            tk.op("v", lambda e: e.reduce_max(gsc[:, 33:34], gsc[:, 0:16], axis=AX),
                  reads=["moe:cur"], writes=["moe:mk"])
        tk.op("v", lambda e: e.tensor_scalar(
            gsc[:, 16:32], lgs[:, :], gsc[:, 33:34], None, ALU.is_ge),
            reads=["moe:lg", "moe:mk"], writes=["moe:mask"])
        tk.op("s", lambda e: e.activation(
            gsc[:, 0:16], lgs[:, :], AF.Exp, bias=gsc[:, 34:35]),
            reads=["moe:lg", "moe:nm1"], writes=["moe:cur"])
        tk.op("v", lambda e: e.scalar_tensor_tensor(
            gsc[:, 0:16], gsc[:, 0:16], 1.0, gsc[:, 16:32], ALU.mult, ALU.mult,
            accum_out=gsc[:, 35:36]),
            reads=["moe:cur", "moe:mask"], writes=["moe:cur", "moe:zs"])
        tk.op("v", lambda e: e.reciprocal(gsc[:, 36:37], gsc[:, 35:36]),
              reads=["moe:zs"], writes=["moe:zi"])
        tk.op("v", lambda e, tg=tg: e.tensor_scalar(
            gts[:, tg * 16:(tg + 1) * 16], gsc[:, 0:16], gsc[:, 36:37], None, ALU.mult),
            reads=["moe:cur", "moe:zi"], writes=["moe:gts"])
        tk.op("t", lambda e, tg=tg, ps=ps: e.transpose(
            ps[0:16, 0:128], gts[:, tg * 16:(tg + 1) * 16], cx.eye[:, :]),
            reads=["moe:gts", "c:eye"], writes=["ps:3"])
        tk.op("s", lambda e, tg=tg, ps=ps: e.copy(
            gTb[0:16, tg * 128:(tg + 1) * 128], ps[0:16, 0:128]),
            reads=["ps:3"], writes=["moe:gT"])

    if getattr(cx, "moe_dbg", None) is not None:
        tk.dma("y", lambda e: e.dma_start(out=cx.moe_dbg[0][:, :], in_=gts[:, :]),
               reads=["moe:gts"], writes=["dbg"])
    for half in range(2):
        hoff = half * 1024
        for tsl in range(8):
            tg = half * 8 + tsl
            for h2 in range(2):
                ps = cx.pf[4 + h2 % 2]
                tk.op("t", lambda e, tg=tg, h2=h2, ps=ps: e.matmul(
                    ps[:, 0:512], gTb[0:16, tg * 128:(tg + 1) * 128],
                    b2sb[0:16, h2 * 512:(h2 + 1) * 512], start=True, stop=True),
                    reads=["moe:gT", "w:b2"], writes=[f"ps:{4 + h2 % 2}"])
                tk.op("v", lambda e, tsl=tsl, h2=h2, ps=ps: e.tensor_copy(
                    acc[:, tsl * 1024 + h2 * 512: tsl * 1024 + (h2 + 1) * 512],
                    ps[:, 0:512]),
                    reads=[f"ps:{4 + h2 % 2}"], writes=["moe:acc"])
        for ex in experts:
            wpar = ex % 2
            tk.dma("g", lambda e, ex=ex: e.dma_start(
                out=w2b[:, :], in_=cx.dram["moe_w2_s"][ex]),
                writes=["w:w2"])
            tk.dma("g", lambda e, ex=ex: e.dma_start(
                out=b1r[0:1, :], in_=cx.dram["moe_b1"][ex]),
                writes=["b:b1"])
            for j in (range(8) if do_h else []):
                jpar = (ex * 8 + j) % 2
                tk.dma("g", lambda e, ex=ex, j=j, jpar=jpar: e.dma_start(
                    out=w1b[:, jpar * 2048:(jpar + 1) * 2048],
                    in_=cx.dram["moe_w1_s"][ex, j]),
                    writes=[f"w:w1{jpar}"])
                for ttl in range(2):
                    tc0 = hoff + ttl * 512
                    ppar = ttl % 2
                    ps_a, ps_b2 = cx.pf[ppar], cx.pf[2 + ppar]

                    def mm_h(e, j=j, jpar=jpar, wpar=wpar, tc0=tc0, ps_a=ps_a, ps_b2=ps_b2):
                        for kc in range(NCH):
                            e.matmul(ps_a[:, 0:512],
                                     w1b[:, jpar * 2048 + kc * 256: jpar * 2048 + kc * 256 + 128],
                                     xch(kc, tc0, tc0 + 512), start=(kc == 0), stop=False)
                        e.matmul(ps_a[:, 0:512], b1r[0:1, j * 256: j * 256 + 128],
                                 cx.ones_row16[0:1, 0:512], start=False, stop=True)
                        for kc in range(NCH):
                            e.matmul(ps_b2[:, 0:512],
                                     w1b[:, jpar * 2048 + kc * 256 + 128: jpar * 2048 + (kc + 1) * 256],
                                     xch(kc, tc0, tc0 + 512), start=(kc == 0), stop=False)
                        return e.matmul(ps_b2[:, 0:512],
                                        b1r[0:1, j * 256 + 128: (j + 1) * 256],
                                        cx.ones_row16[0:1, 0:512], start=False, stop=True)
                    tk.op("t", mm_h, reads=[xn_key, f"w:w1{jpar}", "b:b1", "c:ones_row16"],
                          writes=[f"ps:{ppar}", f"ps:{2 + ppar}"])
                    sa = 1024 * ppar
                    tk.op("v", lambda e, sa=sa, ps_a=ps_a: e.tensor_scalar(
                        msc[:, sa:sa + 512], ps_a[:, 0:512], 7.0, None, ALU.min),
                        reads=[f"ps:{ppar}"], writes=[f"moe:s0{ppar}"])
                    tk.op("s", lambda e, sa=sa: e.activation(
                        msc[:, sa + 512:sa + 1024], msc[:, sa:sa + 512], AF.Silu, scale=GLU_ALPHA),
                        reads=[f"moe:s0{ppar}"], writes=[f"moe:s1{ppar}"])
                    tk.op("s", lambda e, sa=sa, ps_b2=ps_b2: e.activation(
                        msc[:, sa:sa + 512], ps_b2[:, 0:512], AF.Relu, bias=c7[:, 0:1],
                        scale=-1.0),
                        reads=[f"ps:{2 + ppar}", "moe:c7", f"moe:s0{ppar}"], writes=[f"moe:s0{ppar}"])
                    tk.op("v", lambda e, sa=sa: e.tensor_scalar(
                        msc[:, sa:sa + 512], msc[:, sa:sa + 512],
                        14.0, -1.0 / GLU_ALPHA, ALU.min, ALU.mult),
                        reads=[f"moe:s0{ppar}"], writes=[f"moe:s0{ppar}"])
                    tk.op("v", lambda e, sa=sa, j=j, tc0=tc0, ttl=ttl: e.scalar_tensor_tensor(
                        ssb[:, j * 1024 + ttl * 512: j * 1024 + (ttl + 1) * 512],
                        msc[:, sa:sa + 512], 8.0 / GLU_ALPHA,
                        msc[:, sa + 512:sa + 1024], ALU.add, ALU.mult),
                        reads=[f"moe:s0{ppar}", f"moe:s1{ppar}"], writes=["moe:s"])
            for ttl in (range(2) if do_w2 else []):
                for tsub in range(4):
                    tsl = ttl * 4 + tsub
                    tg = half * 8 + tsl
                    for h2 in range(2):
                        ppar = (tsl * 2 + h2) % 2
                        ps = cx.pf[4 + ppar]

                        def mm_y(e, ttl=ttl, tsub=tsub, h2=h2, wpar=wpar, ps=ps):
                            ins = None
                            for j in range(8):
                                ins = e.matmul(
                                    ps[:, 0:512],
                                    ssb[:, j * 1024 + ttl * 512 + tsub * 128: j * 1024 + ttl * 512 + (tsub + 1) * 128],
                                    w2b[:, j * 1024 + h2 * 512: j * 1024 + (h2 + 1) * 512],
                                    start=(j == 0), stop=(j == 7))
                            return ins
                        tk.op("t", mm_y, reads=["moe:s", "w:w2"], writes=[f"ps:{4 + ppar}"])
                        tk.op("v", lambda e, tsl=tsl, tg=tg, h2=h2, ex=ex, ps=ps: e.scalar_tensor_tensor(
                            acc[:, tsl * 1024 + h2 * 512: tsl * 1024 + (h2 + 1) * 512],
                            ps[:, 0:512], gts[:, tg * 16 + ex: tg * 16 + ex + 1],
                            acc[:, tsl * 1024 + h2 * 512: tsl * 1024 + (h2 + 1) * 512],
                            ALU.mult, ALU.add),
                            reads=[f"ps:{4 + ppar}", "moe:gts", "moe:acc"], writes=["moe:acc"])
        for tsl in range(8):
            for oc in range(NCH):
                ppar = (tsl * 8 + oc) % 2
                ps = cx.pf[ppar]
                tk.op("t", lambda e, tsl=tsl, oc=oc, ps=ps: e.transpose(
                    ps[0:128, 0:128], acc[:, tsl * 1024 + oc * 128: tsl * 1024 + (oc + 1) * 128],
                    cx.eye[:, :]),
                    reads=["moe:acc", "c:eye"], writes=[f"ps:{ppar}"])
                tk.op("v", lambda e, tsl=tsl, oc=oc, hoff=hoff, ps=ps, ppar=ppar: e.tensor_add(
                    _chunk(x_sb, oc, TPC, hoff + tsl * 128, hoff + (tsl + 1) * 128),
                    _chunk(x_sb, oc, TPC, hoff + tsl * 128, hoff + (tsl + 1) * 128),
                    ps[0:128, 0:128]),
                    reads=[f"ps:{ppar}", x_key], writes=[x_key])


def build_l0b():
    nc = bass.Bass("TRN2", target_bir_lowering=False)
    es = contextlib.ExitStack()
    with es:
        xn_in = nc.dram_tensor("xn_in", [DM, TPC], BF16, kind="ExternalInput").ap()
        x1p_in = nc.dram_tensor("x1p_in", [DM, TPC], F32, kind="ExternalInput").ap()
        x1_out = nc.dram_tensor("x1_out", [DM, TPC], F32, kind="ExternalOutput").ap()

        block = es.enter_context(nc.Block())
        cx = _setup_common(nc, es, block, _SHARED_CACHE, MOE_SHARED)
        tk = cx.tk
        cx.ln_scr = cx.sb("ln_scr", [128, 2 * 512], F32)
        cx.ln_rows = cx.sb("ln_rows", [1, 2 * 512], F32)
        cx.ln_ones_col = cx.sb("ln_ones_col", [128, 1], F32)
        cx.ln_eps = cx.sb("ln_eps", [1, 1], F32)
        tk.op("v", lambda e: e.memset(cx.ln_ones_col[:, :], 1.0), writes=["ln:ones_col"])
        tk.op("v", lambda e: e.memset(cx.ln_eps[:, :], 1e-5), writes=["ln:eps"])
        cx.ln_init = True

        x_sb = cx.sb("x_sb", [128, NCH * TPC], F32)
        xn_sb = cx.sb("xn_sb", [128, NCH * TPC], BF16)
        for kc in range(NCH):
            tk.dma("y", lambda e, kc=kc: e.dma_start(
                out=_chunk(x_sb, kc, TPC), in_=x1p_in[kc * 128:(kc + 1) * 128, :]), writes=["x"])
            tk.dma("y", lambda e, kc=kc: e.dma_start(
                out=_chunk(xn_sb, kc, TPC), in_=xn_in[kc * 128:(kc + 1) * 128, :]), writes=["xn"])

        emit_moe(cx, xn_sb, "xn", TPC, 0, x_sb, "x")

        for kc in range(NCH):
            tk.dma("y", lambda e, kc=kc: e.dma_start(
                out=x1_out[kc * 128:(kc + 1) * 128, :], in_=_chunk(x_sb, kc, TPC)),
                reads=["x"], writes=["o:x1"])
        n = tk.dcnt["y"]
        block.sync(lambda eng: eng.wait_ge(tk.dsem["y"], 16 * n))
    return nc




L1A_SHARED = ["l1_wq_s", "l1_bq", "l1_wk_s", "l1_bk", "l1_wv_s", "l1_bv", "l1_wo_s", "l1_bo",
              "l1_wfc_s", "l1_bfc", "l1_projT", "eye", "ones_row"]

CTXW = 3 * 65
STG_W = 16 * CTXW


def emit_l1_attn(cx, x_in_ap, xn_sb, xn_key, xnw, xc0, y0_sb, y0_key, core_dram, x_dep=(), cc=True):
    nc, tk, sb = cx.nc, cx.tk, cx.sb
    DN, RATIO, EPS = L1_DN, L1_RATIO, L1_EPS

    def xch(kc, a, b):
        return xn_sb[:, kc * xnw + xc0 + a: kc * xnw + xc0 + b]

    RG = [[0, 1, 2, 3], [4, 5, 6, 7]]
    ccm_in = nc.dram_tensor("ccm_in", [1, 16], F32)
    ccm_out = nc.dram_tensor("ccm_out", [1, 16], F32)
    ccs_in = nc.dram_tensor("ccs_in", [128, STG_W], F32)
    ccs_out = nc.dram_tensor("ccs_out", [128, STG_W], F32)

    wA = sb("l1_wA", [128, NCH * 1024], BF16)
    wB = sb("l1_wB", [128, NCH * 1024], BF16)
    br = sb("l1_br", [1, 2048], BF16)
    projT = sb("l1_projT_sb", [128, 266], BF16)
    vtm = sb("l1_v", [128, 16 * 1024], BF16)
    kbuf = sb("l1_kbuf", [128, NCH * TPC], BF16)
    xt = sb("l1_xt", [128, NCH * 512], F32)
    cx.l1_xt = xt
    diagk = sb("l1_diagk", [128, 256], F32)
    colmax = sb("l1_colmax", [128, 16], F32)
    stg = sb("l1_stg", [128, STG_W], F32)
    cx.l1_stg_dbg = stg
    stg2 = sb("l1_stg2", [16, 256], F32)
    ctxb = sb("l1_ctx", [128, 16 * 192], BF16)
    ksumb = sb("l1_ksum", [128, 48], BF16)
    mxb = sb("l1_mxb", [128, 16], F32)
    rsc = sb("l1_rsc", [128, 16], F32)
    rrow = sb("l1_rrow", [1, 512], F32)
    exs = sb("l1_exs", [128, 2 * 266], F32)
    kpb = sb("l1_kpb", [128, 2 * 266], BF16)
    qpT = sb("l1_qpT", [128, 2 * 384], BF16)
    qps = sb("l1_qps", [128, 2 * 384], BF16)
    sq64 = sb("l1_sq64", [128, 2 * 128], F32)
    qt = sb("l1_qt", [128, 2 * 1024], BF16)
    ath = sb("l1_ath", [128, 2 * 1024], BF16)
    msel = sb("l1_msel", [1, 16], F32)

    tk.dma("g", lambda e: e.dma_start(out=projT[:, :], in_=cx.dram["l1_projT"][:, :]), writes=["l1:projT"])

    for tt in range(4):
        c0 = tt * 512
        for kc in range(NCH):
            tk.dma("y", lambda e, kc=kc, c0=c0: e.dma_start(
                out=_chunk(xt, kc, 512), in_=x_in_ap[kc * 128:(kc + 1) * 128, c0:c0 + 512]),
                reads=list(x_dep), writes=["l1:xt"])
        emit_ln(cx, xt, "l1:xt", 512, 0, 512, xn_sb, xn_key, xnw, xc0 + c0)

    tk.dma("g", lambda e: e.dma_start(
        out=wA[:, :], in_=cx.dram["l1_wv_s"][:, :]), writes=["w:A"])
    tk.dma("g", lambda e: e.dma_start(out=br[0:1, 0:1024], in_=cx.dram["l1_bv"][:, :]), writes=["b:r0"])
    for j in range(16):
        c0 = j * 128
        for vh in range(2):
            pi = vh % 2
            ps = cx.pf[pi]

            def mm_v(e, c0=c0, vh=vh, ps=ps):
                for kc in range(NCH):
                    e.matmul(ps[:, 0:512], xch(kc, c0, c0 + 128),
                             wA[:, kc * 1024 + vh * 512: kc * 1024 + (vh + 1) * 512],
                             start=(kc == 0), stop=False)
                return e.matmul(ps[:, 0:512], cx.ones_row16[0:1, 0:128],
                                br[0:1, vh * 512:(vh + 1) * 512], start=False, stop=True)
            tk.op("t", mm_v, reads=[xn_key, "w:A", "b:r0", "c:ones_row16"], writes=[f"ps:{pi}"])
            tk.op("s", lambda e, j=j, vh=vh, ps=ps: e.copy(
                vtm[:, j * 1024 + vh * 512: j * 1024 + (vh + 1) * 512], ps[:, 0:512]),
                reads=[f"ps:{pi}"], writes=["l1:v"])

    tk.dma("g", lambda e: e.dma_start(
        out=wA[:, :], in_=cx.dram["l1_wk_s"][:, :]), writes=["w:A"])
    tk.dma("g", lambda e: e.dma_start(out=br[0:1, 0:1024], in_=cx.dram["l1_bk"][:, :]), writes=["b:r0"])
    for ti4 in range(4):
        c0 = ti4 * 512
        for oc in range(NCH):
            pi = 2 + oc % 2
            ps = cx.pf[pi]

            def mm_k(e, c0=c0, oc=oc, ps=ps):
                for kc in range(NCH):
                    e.matmul(ps[:, 0:512], wA[:, kc * 1024 + oc * 128: kc * 1024 + (oc + 1) * 128],
                             xch(kc, c0, c0 + 512), start=(kc == 0), stop=False)
                return e.matmul(ps[:, 0:512], br[0:1, oc * 128:(oc + 1) * 128],
                                cx.ones_row16[0:1, 0:512], start=False, stop=True)
            tk.op("t", mm_k, reads=[xn_key, "w:A", "b:r0", "c:ones_row16"], writes=[f"ps:{pi}"])
            tk.op("s", lambda e, c0=c0, oc=oc, ps=ps: e.copy(
                kbuf[:, oc * TPC + c0: oc * TPC + c0 + 512], ps[:, 0:512]),
                reads=[f"ps:{pi}"], writes=["kbuf"])

    tk.op("v", lambda e: e.memset(colmax[:, :], -3.0e38), writes=["l1:colmax"])
    tk.op("v", lambda e: e.memset(stg[:, :], 0.0), writes=["l1:stg"])
    for ti in range(16):
        for h in range(16):
            base = 64 * (h % 2)
            kcol = (h // 2) * TPC + ti * 128
            pi = (ti * 16 + h) % 2
            ps_td = cx.pf[pi]
            ps_d = cx.pf[2 + pi]
            tk.op("t", lambda e, base=base, kcol=kcol, ps_td=ps_td: e.matmul(
                ps_td[0:128, 0:266], kbuf[base:base + 64, kcol:kcol + 128],
                projT[base:base + 64, 0:266], start=True, stop=True),
                reads=["kbuf", "l1:projT"], writes=[f"ps:{pi}"])
            tk.op("v", lambda e, pi=pi, ps_td=ps_td: e.reduce_max(
                rsc[:, pi:pi + 1], ps_td[:, 0:266], axis=AX),
                reads=[f"ps:{pi}"], writes=[f"l1:rm{pi}"])
            tk.op("v", lambda e, h=h, pi=pi: e.tensor_max(
                colmax[:, h:h + 1], colmax[:, h:h + 1], rsc[:, pi:pi + 1]),
                reads=[f"l1:rm{pi}", "l1:colmax"], writes=["l1:colmax"])
            tk.op("s", lambda e, base=base, kcol=kcol, pi=pi: e.activation(
                sq64[0:64, pi * 128:pi * 128 + 128], kbuf[base:base + 64, kcol:kcol + 128],
                AF.Square),
                reads=["kbuf"], writes=[f"l1:sq{pi}"])
            tk.op("t", lambda e, pi=pi, ps_d=ps_d: e.matmul(
                ps_d[0:128, 0:1], sq64[0:64, pi * 128:pi * 128 + 128], cx.ln_ones_col[0:64, 0:1],
                start=True, stop=True),
                reads=[f"l1:sq{pi}", "ln:ones_col"], writes=[f"ps:{2 + pi}"])
            tk.op("v", lambda e, ti=ti, h=h, pi=pi, ps_d=ps_d: e.tensor_scalar(
                diagk[:, ti * 16 + h: ti * 16 + h + 1], ps_d[:, 0:1], 0.5 * DN * DN, None,
                ALU.mult),
                reads=[f"ps:{2 + pi}"], writes=["l1:diagk"])

    tk.op("t", lambda e: e.transpose(cx.pf[4][0:16, 0:128], colmax[:, :], cx.eye[:, :]),
          reads=["l1:colmax", "c:eye"], writes=["ps:4"])
    tk.op("v", lambda e: e.tensor_copy(stg2[0:16, 0:128], cx.pf[4][0:16, 0:128]),
          reads=["ps:4"], writes=["l1:stg2"])
    tk.op("v", lambda e: e.reduce_max(stg2[0:16, 128:129], stg2[0:16, 0:128], axis=AX),
          reads=["l1:stg2"], writes=["l1:stg2"])
    tk.op("t", lambda e: e.transpose(cx.pf[4][0:1, 0:16], stg2[0:16, 128:129], cx.eye[0:16, 0:16]),
          reads=["l1:stg2", "c:eye"], writes=["ps:4"])
    tk.op("v", lambda e: e.tensor_copy(msel[0:1, 0:16], cx.pf[4][0:1, 0:16]),
          reads=["ps:4"], writes=["l1:msel"])
    if cc:
        tk.dma("g", lambda e: e.dma_start(out=ccm_in.ap()[:, :], in_=msel[0:1, 0:16]),
               reads=["l1:msel"], writes=["d:ccm_in"])
        tk.barrier()
        tk.op("g", lambda e: e.collective_compute(
            "AllReduce", ALU.max, replica_groups=RG,
            ins=[ccm_in.ap().opt()], outs=[ccm_out.ap().opt()]),
            reads=["d:ccm_in"], writes=["d:ccm_out"])
        tk.dma("g", lambda e: e.dma_start(out=msel[0:1, 0:16], in_=ccm_out.ap()[0:1, :]),
               reads=["d:ccm_out"], writes=["l1:msel"])
    for h in range(16):
        tk.op("t", lambda e, h=h: e.matmul(
            cx.pf[4][0:128, 0:1], cx.ones_row[0:1, 0:128], msel[0:1, h:h + 1],
            start=True, stop=True),
            reads=["l1:msel", "c:ones_row"], writes=["ps:4"])
        tk.op("v", lambda e, h=h: e.tensor_scalar(
            mxb[:, h:h + 1], cx.pf[4][0:128, 0:1], DN, None, ALU.mult),
            reads=["ps:4"], writes=["l1:mxb"])

    for h in range(16):
        base = 64 * (h % 2)
        ps_ks = cx.pf[4]
        for ti in range(16):
            kcol = (h // 2) * TPC + ti * 128
            pi = ti % 2
            ps_td = cx.pf[pi]
            tk.op("t", lambda e, base=base, kcol=kcol, ps_td=ps_td: e.matmul(
                ps_td[0:128, 0:266], kbuf[base:base + 64, kcol:kcol + 128],
                projT[base:base + 64, 0:266], start=True, stop=True),
                reads=["kbuf", "l1:projT"], writes=[f"ps:{pi}"])
            tk.op("v", lambda e, ti=ti, h=h, pi=pi: e.scalar_tensor_tensor(
                rsc[:, 2 + pi:3 + pi], diagk[:, ti * 16 + h: ti * 16 + h + 1], -1.0,
                mxb[:, h:h + 1], ALU.mult, ALU.subtract),
                reads=["l1:diagk", "l1:mxb"], writes=[f"l1:bc{pi}"])
            tk.op("s", lambda e, pi=pi, ps_td=ps_td: e.activation(
                exs[:, pi * 266:(pi + 1) * 266], ps_td[:, 0:266], AF.Exp,
                bias=rsc[:, 2 + pi:3 + pi], scale=DN),
                reads=[f"ps:{pi}", f"l1:bc{pi}"], writes=[f"l1:ex{pi}"])
            tk.op("v", lambda e, pi=pi: e.tensor_scalar(
                kpb[:, pi * 266:(pi + 1) * 266], exs[:, pi * 266:(pi + 1) * 266],
                RATIO, RATIO * EPS, ALU.mult, ALU.add),
                reads=[f"l1:ex{pi}"], writes=[f"l1:kp{pi}"])

            CTX_BANK = [2, 3, 5]

            def mm_acc2(e, ti=ti, h=h, pi=pi, ps_ks=ps_ks):
                st = (ti == 0)
                sp = (ti == 15)
                e.matmul(ps_ks[0:1, 0:266], cx.ones_col16[:, 0:1],
                         kpb[:, pi * 266:(pi + 1) * 266], start=st, stop=sp)
                ins = None
                for mc in range(3):
                    msz = 128 if mc < 2 else 10
                    ins = e.matmul(cx.pf[CTX_BANK[mc]][0:msz, 0:64],
                                   kpb[:, pi * 266 + mc * 128: pi * 266 + mc * 128 + msz],
                                   vtm[:, ti * 1024 + h * 64: ti * 1024 + (h + 1) * 64],
                                   start=st, stop=sp)
                return ins
            tk.op("t", mm_acc2,
                  reads=[f"l1:kp{pi}", "l1:v", "c:ones_col16", "ps:4", "ps:2", "ps:3", "ps:5"],
                  writes=["ps:4", "ps:2", "ps:3", "ps:5"])
        tk.op("v", lambda e, h=h: e.tensor_copy(rrow[0:1, 32:298], ps_ks[0:1, 0:266]),
              reads=["ps:4"], writes=["l1:ksrow"])
        for mc in range(3):
            msz = 128 if mc < 2 else 10
            col = h * CTXW + mc * 65
            bk = [2, 3, 5][mc]
            tk.op("v", lambda e, mc=mc, msz=msz, col=col, bk=bk: e.tensor_copy(
                stg[0:msz, col:col + 64], cx.pf[bk][0:msz, 0:64]),
                reads=[f"ps:{bk}"], writes=["l1:stg"])
            tk.op("t", lambda e, mc=mc, msz=msz: e.matmul(
                cx.pf[0][0:msz, 0:1], rrow[0:1, 32 + mc * 128: 32 + mc * 128 + msz],
                cx.ones_row[0:1, 0:1], start=True, stop=True),
                reads=["l1:ksrow", "c:ones_row"], writes=["ps:0"])
            tk.op("v", lambda e, mc=mc, msz=msz, col=col: e.tensor_copy(
                stg[0:msz, col + 64:col + 65], cx.pf[0][0:msz, 0:1]),
                reads=["ps:0"], writes=["l1:stg"])

    if cc:
        tk.dma("g", lambda e: e.dma_start(out=ccs_in.ap()[:, :], in_=stg[:, :]),
               reads=["l1:stg"], writes=["d:ccs_in"])
        tk.barrier()
        tk.op("g", lambda e: e.collective_compute(
            "AllReduce", ALU.add, replica_groups=RG,
            ins=[ccs_in.ap().opt()], outs=[ccs_out.ap().opt()]),
            reads=["d:ccs_in"], writes=["d:ccs_out"])
        tk.dma("g", lambda e: e.dma_start(out=stg[:, :], in_=ccs_out.ap()[:, :]),
               reads=["d:ccs_out"], writes=["l1:stg"])
    for h in range(16):
        for mc in range(3):
            msz = 128 if mc < 2 else 10
            col = h * CTXW + mc * 65
            tk.op("v", lambda e, h=h, mc=mc, msz=msz, col=col: e.tensor_copy(
                ctxb[0:msz, h * 192 + mc * 64: h * 192 + (mc + 1) * 64],
                stg[0:msz, col:col + 64]),
                reads=["l1:stg"], writes=["l1:ctxb"])
            tk.op("v", lambda e, h=h, mc=mc, msz=msz, col=col: e.tensor_copy(
                ksumb[0:msz, h * 3 + mc: h * 3 + mc + 1], stg[0:msz, col + 64:col + 65]),
                reads=["l1:stg"], writes=["l1:ksumb"])

    tk.dma("g", lambda e: e.dma_start(
        out=wA[:, :], in_=cx.dram["l1_wq_s"][:, :]), writes=["w:A"])
    tk.dma("g", lambda e: e.dma_start(
        out=wB[:, :], in_=cx.dram["l1_wo_s"][:, :]), writes=["w:B"])
    tk.dma("g", lambda e: e.dma_start(out=br[0:1, 0:1024], in_=cx.dram["l1_bq"][:, :]), writes=["b:r0"])
    tk.dma("g", lambda e: e.dma_start(out=br[0:1, 1024:2048], in_=cx.dram["l1_bo"][:, :]), writes=["b:r1"])
    for ti in range(16):
        par = ti % 2
        tc0 = ti * 128
        for qc in range(NCH):
            pi = qc % 2
            ps = cx.pf[pi]

            def mm_q(e, qc=qc, tc0=tc0, ps=ps):
                for kc in range(NCH):
                    e.matmul(ps[:, 0:128], wA[:, kc * 1024 + qc * 128: kc * 1024 + (qc + 1) * 128],
                             xch(kc, tc0, tc0 + 128), start=(kc == 0), stop=False)
                return e.matmul(ps[:, 0:128], br[0:1, qc * 128:(qc + 1) * 128],
                                cx.ones_row16[0:1, 0:128], start=False, stop=True)
            tk.op("t", mm_q, reads=[xn_key, "w:A", "b:r0", "c:ones_row16"], writes=[f"ps:{pi}"])
            tk.op("s", lambda e, qc=qc, par=par, ps=ps, pi=pi: e.copy(
                qt[:, par * 1024 + qc * 128: par * 1024 + (qc + 1) * 128], ps[:, 0:128]),
                reads=[f"ps:{pi}"], writes=[f"l1:qt{par}"])
        for h in range(16):
            base = 64 * (h % 2)
            qcol = par * 1024 + (h // 2) * 128
            hp = h % 2
            ps_td = cx.pf[hp]
            ps_d = cx.pf[2 + hp]
            ps_dv = cx.pf[4]
            ps_o = cx.pf[5]
            ps_tr = cx.pb[hp]
            tk.op("t", lambda e, base=base, qcol=qcol, ps_td=ps_td: e.matmul(
                ps_td[0:128, 0:266], qt[base:base + 64, qcol:qcol + 128],
                projT[base:base + 64, 0:266], start=True, stop=True),
                reads=[f"l1:qt{par}", "l1:projT"], writes=[f"ps:{hp}"])
            tk.op("s", lambda e, base=base, qcol=qcol, hp=hp: e.activation(
                sq64[0:64, hp * 128:hp * 128 + 128], qt[base:base + 64, qcol:qcol + 128],
                AF.Square),
                reads=[f"l1:qt{par}"], writes=[f"l1:sq{hp}"])
            tk.op("t", lambda e, hp=hp, ps_d=ps_d: e.matmul(
                ps_d[0:128, 0:1], sq64[0:64, hp * 128:hp * 128 + 128], cx.ln_ones_col[0:64, 0:1],
                start=True, stop=True),
                reads=[f"l1:sq{hp}", "ln:ones_col"], writes=[f"ps:{2 + hp}"])
            tk.op("v", lambda e, hp=hp, ps_d=ps_d: e.tensor_scalar(
                rsc[:, 4 + hp:5 + hp], ps_d[:, 0:1], 0.5 * DN * DN, None, ALU.mult),
                reads=[f"ps:{2 + hp}"], writes=[f"l1:dq{hp}"])
            tk.op("v", lambda e, hp=hp, ps_td=ps_td: e.reduce_max(
                rsc[:, 6 + hp:7 + hp], ps_td[:, 0:266], axis=AX),
                reads=[f"ps:{hp}"], writes=[f"l1:qm{hp}"])
            tk.op("v", lambda e, hp=hp: e.scalar_tensor_tensor(
                rsc[:, 8 + hp:9 + hp], rsc[:, 6 + hp:7 + hp], -DN, rsc[:, 4 + hp:5 + hp],
                ALU.mult, ALU.subtract),
                reads=[f"l1:qm{hp}", f"l1:dq{hp}"], writes=[f"l1:qb{hp}"])
            tk.op("s", lambda e, hp=hp, ps_td=ps_td: e.activation(
                exs[:, hp * 266:(hp + 1) * 266], ps_td[:, 0:266], AF.Exp,
                bias=rsc[:, 8 + hp:9 + hp], scale=DN),
                reads=[f"ps:{hp}", f"l1:qb{hp}"], writes=[f"l1:ex{hp}"])
            tk.op("v", lambda e, hp=hp: e.tensor_scalar(
                kpb[:, hp * 266:(hp + 1) * 266], exs[:, hp * 266:(hp + 1) * 266],
                RATIO, RATIO * EPS, ALU.mult, ALU.add),
                reads=[f"l1:ex{hp}"], writes=[f"l1:kp{hp}"])

            def mm_tr(e, hp=hp, ps_tr=ps_tr):
                ins = None
                for mc in range(3):
                    msz = 128 if mc < 2 else 10
                    ins = e.transpose(ps_tr[0:msz, mc * 128:mc * 128 + 128],
                                      kpb[:, hp * 266 + mc * 128: hp * 266 + mc * 128 + msz],
                                      cx.eye16[:, :])
                return ins
            tk.op("t", mm_tr, reads=[f"l1:kp{hp}", "c:eye16"], writes=[f"ps:b{hp}"])
            tk.op("s", lambda e, hp=hp, ps_tr=ps_tr: e.copy(
                qpT[:, hp * 384:(hp + 1) * 384], ps_tr[0:128, 0:384]),
                reads=[f"ps:b{hp}"], writes=[f"l1:qpT{hp}"])

            def mm_den(e, h=h, hp=hp, ps_dv=ps_dv):
                ins = None
                for mc in range(3):
                    msz = 128 if mc < 2 else 10
                    ins = e.matmul(ps_dv[0:1, 0:128], ksumb[0:msz, h * 3 + mc: h * 3 + mc + 1],
                                   qpT[0:msz, hp * 384 + mc * 128: hp * 384 + (mc + 1) * 128],
                                   start=(mc == 0), stop=(mc == 2))
                return ins
            tk.op("t", mm_den, reads=["l1:ksumb", f"l1:qpT{hp}"], writes=["ps:4"])
            tk.op("v", lambda e, ps_dv=ps_dv: e.tensor_copy(
                rrow[0:1, 304:432], ps_dv[0:1, 0:128]),
                reads=["ps:4"], writes=["l1:den"])
            tk.op("v", lambda e: e.reciprocal(rrow[0:1, 304:432], rrow[0:1, 304:432]),
                  reads=["l1:den"], writes=["l1:den"])
            tk.op("t", lambda e, ps_dv=ps_dv: e.matmul(
                ps_dv[0:128, 0:128], cx.ones_row[0:1, 0:128], rrow[0:1, 304:432],
                start=True, stop=True),
                reads=["l1:den", "c:ones_row"], writes=["ps:4"])
            for mc in range(3):
                msz = 128 if mc < 2 else 10
                tk.op("v", lambda e, hp=hp, mc=mc, msz=msz, ps_dv=ps_dv: e.tensor_mul(
                    qps[0:msz, hp * 384 + mc * 128: hp * 384 + (mc + 1) * 128],
                    qpT[0:msz, hp * 384 + mc * 128: hp * 384 + (mc + 1) * 128],
                    ps_dv[0:msz, 0:128]),
                    reads=[f"l1:qpT{hp}", "ps:4"], writes=[f"l1:qps{hp}"])

            def mm_pv(e, h=h, hp=hp, ps_o=ps_o):
                ins = None
                for mc in range(3):
                    msz = 128 if mc < 2 else 10
                    ins = e.matmul(ps_o[0:64, 0:128],
                                   ctxb[0:msz, h * 192 + mc * 64: h * 192 + (mc + 1) * 64],
                                   qps[0:msz, hp * 384 + mc * 128: hp * 384 + (mc + 1) * 128],
                                   start=(mc == 0), stop=(mc == 2))
                return ins
            tk.op("t", mm_pv, reads=["l1:ctxb", f"l1:qps{hp}"], writes=["ps:5"])
            tk.op("s", lambda e, h=h, base=base, par=par, ps_o=ps_o: e.copy(
                ath[base:base + 64, par * 1024 + (h // 2) * 128: par * 1024 + (h // 2) * 128 + 128],
                ps_o[0:64, 0:128]),
                reads=["ps:5"], writes=[f"l1:ath{par}"])
        for oc in range(NCH):
            pi = oc % 2
            ps = cx.pf[pi]

            def mm_wo(e, oc=oc, par=par, ps=ps):
                for kc in range(NCH):
                    e.matmul(ps[:, 0:128], wB[:, kc * 1024 + oc * 128: kc * 1024 + (oc + 1) * 128],
                             ath[:, par * 1024 + kc * 128: par * 1024 + (kc + 1) * 128],
                             start=(kc == 0), stop=False)
                return e.matmul(ps[:, 0:128], br[0:1, 1024 + oc * 128:1024 + (oc + 1) * 128],
                                cx.ones_row16[0:1, 0:128], start=False, stop=True)
            tk.op("t", mm_wo, reads=[f"l1:ath{par}", "w:B", "b:r1", "c:ones_row16"],
                  writes=[f"ps:{pi}"])
            tk.op("s", lambda e, oc=oc, tc0=tc0, ps=ps, pi=pi: e.copy(
                kbuf[:, oc * TPC + tc0: oc * TPC + tc0 + 128], ps[:, 0:128]),
                reads=[f"ps:{pi}"], writes=["kbuf"])

    tk.dma("g", lambda e: e.dma_start(
        out=wA[:, :], in_=cx.dram["l1_wfc_s"][:, :]), writes=["w:A"])
    tk.dma("g", lambda e: e.dma_start(out=br[0:1, 0:1024], in_=cx.dram["l1_bfc"][:, :]), writes=["b:r0"])
    for tt in range(4):
        c0 = tt * 512
        for kc in range(NCH):
            tk.dma("y", lambda e, kc=kc, c0=c0: e.dma_start(
                out=_chunk(xt, kc, 512), in_=x_in_ap[kc * 128:(kc + 1) * 128, c0:c0 + 512]),
                reads=list(x_dep), writes=["l1:xt"])
        for oc in range(NCH):
            pi = oc % 2
            ps = cx.pf[pi]

            def mm_fc(e, oc=oc, c0=c0, ps=ps):
                for kc in range(NCH):
                    e.matmul(ps[:, 0:512], wA[:, kc * 1024 + oc * 128: kc * 1024 + (oc + 1) * 128],
                             kbuf[:, kc * TPC + c0: kc * TPC + c0 + 512],
                             start=(kc == 0), stop=False)
                return e.matmul(ps[:, 0:512], br[0:1, oc * 128:(oc + 1) * 128],
                                cx.ones_row16[0:1, 0:512], start=False, stop=True)
            tk.op("t", mm_fc, reads=["kbuf", "w:A", "b:r0", "c:ones_row16"], writes=[f"ps:{pi}"])
            tk.op("v", lambda e, oc=oc, ps=ps, pi=pi: e.scalar_tensor_tensor(
                _chunk(xt, oc, 512), ps[:, 0:512], 0.0, _chunk(xt, oc, 512),
                ALU.max, ALU.add),
                reads=[f"ps:{pi}", "l1:xt"], writes=["l1:xt"])
        cx.l1_y0_store(tt)


def build_l1a():
    nc = bass.Bass("TRN2", target_bir_lowering=False)
    es = contextlib.ExitStack()
    with es:
        x1_in = nc.dram_tensor("x1_in", [DM, TPC], F32, kind="ExternalInput").ap()
        hsel_in = nc.dram_tensor("hsel", [1, 2], F32, kind="ExternalInput").ap()
        hselc_in = nc.dram_tensor("hselc", [2, 1], F32, kind="ExternalInput").ap()
        ccmb_in = nc.dram_tensor("ccmaxbias", [2, 16], F32, kind="ExternalInput").ap()
        xn_out = nc.dram_tensor("xn1_out", [DM, TPC], BF16, kind="ExternalOutput").ap()
        y0_out = nc.dram_tensor("y0_out", [DM, TPC], F32, kind="ExternalOutput").ap()

        block = es.enter_context(nc.Block())
        cx = _setup_common(nc, es, block, _SHARED_CACHE, L1A_SHARED)
        tk = cx.tk
        cx.ln_scr = cx.sb("ln_scr", [128, 2 * 512], F32)
        cx.ln_rows = cx.sb("ln_rows", [1, 4 * 512], F32)
        cx.ln_ones_col = cx.sb("ln_ones_col", [128, 1], F32)
        cx.ln_eps = cx.sb("ln_eps", [1, 1], F32)
        tk.op("v", lambda e: e.memset(cx.ln_ones_col[:, :], 1.0), writes=["ln:ones_col"])
        tk.op("v", lambda e: e.memset(cx.ln_eps[:, :], 1e-5), writes=["ln:eps"])
        cx.ln_init = True

        xn_sb = cx.sb("xn_sb", [128, NCH * TPC], BF16)

        def y0_store(tt):
            c0 = tt * 512
            for kc in range(NCH):
                tk.dma("y", lambda e, kc=kc, c0=c0: e.dma_start(
                    out=y0_out[kc * 128:(kc + 1) * 128, c0:c0 + 512],
                    in_=_chunk(cx.l1_xt, kc, 512)),
                    reads=["l1:xt"], writes=["o:y0"])
        cx.l1_y0_store = y0_store

        emit_l1_attn(cx, x1_in, xn_sb, "xn", TPC, 0, None, None,
                     {"hsel": hsel_in, "hselc": hselc_in, "ccmaxbias": ccmb_in})

        for kc in range(NCH):
            tk.dma("y", lambda e, kc=kc: e.dma_start(
                out=xn_out[kc * 128:(kc + 1) * 128, :], in_=_chunk(xn_sb, kc, TPC)),
                reads=["xn"], writes=["o:xn"])
        n = tk.dcnt["y"]
        block.sync(lambda eng: eng.wait_ge(tk.dsem["y"], 16 * n))
    return nc




L1B_SHARED = ["f_w1_s", "f_w2_s", "f_h1_s", "f_h2_s", "f_w3_s", "f_s1", "f_s2", "f_sh1", "f_sh2",
              "f_s3", "eye", "ones_row"]


def emit_l1_ffn(cx, xn_sb, xn_key, xnw, xc0, y0_in_ap, out_store, y_dep=()):
    nc, tk, sb = cx.nc, cx.tk, cx.sb

    def xch(kc, a, b):
        return xn_sb[:, kc * xnw + xc0 + a: kc * xnw + xc0 + b]

    hs = sb("f_h", [128, 32 * 1024], BF16)
    gs = sb("f_g", [128, 32 * 1024], BF16)
    wch = sb("f_wch", [128, 2 * 4096], BF16)
    yt = sb("f_yt", [128, NCH * 512], F32)
    cx.f_yt = yt
    usc = yt
    scl = sb("f_scl", [128, 4 * 32 + 8], F32)

    tk.dma("y", lambda e: e.dma_start(out=scl[:, 0:32], in_=cx.dram["f_s1"][:, :]), writes=["f:scl"])
    tk.dma("y", lambda e: e.dma_start(out=scl[:, 32:64], in_=cx.dram["f_s2"][:, :]), writes=["f:scl"])
    tk.dma("y", lambda e: e.dma_start(out=scl[:, 64:96], in_=cx.dram["f_sh1"][:, :]), writes=["f:scl"])
    tk.dma("y", lambda e: e.dma_start(out=scl[:, 96:128], in_=cx.dram["f_sh2"][:, :]), writes=["f:scl"])
    tk.dma("y", lambda e: e.dma_start(out=scl[:, 128:136], in_=cx.dram["f_s3"][:, :]), writes=["f:scl"])

    for half in range(2):
        hoff = half * 1024
        for oc in range(32):
            par = oc % 2
            tk.dma("g", lambda e, oc=oc, par=par: e.dma_start(
                out=wch[:, par * 4096:par * 4096 + 1024],
                in_=cx.dram["f_w1_s"][oc]),
                writes=[f"f:wh{par}"])
            tk.dma("g", lambda e, oc=oc, par=par: e.dma_start(
                out=wch[:, par * 4096 + 1024:par * 4096 + 2048],
                in_=cx.dram["f_w2_s"][oc]),
                writes=[f"f:wh{par}"])
            for tt in range(2):
                tc0 = hoff + tt * 512
                ppar = tt % 2
                ps_a, ps_b = cx.pf[ppar], cx.pf[2 + ppar]

                def mm_h(e, oc=oc, par=par, tc0=tc0, ps_a=ps_a, ps_b=ps_b):
                    for kc in range(NCH):
                        e.matmul(ps_a[:, 0:512], wch[:, par * 4096 + kc * 128: par * 4096 + (kc + 1) * 128],
                                 xch(kc, tc0, tc0 + 512), start=(kc == 0), stop=(kc == NCH - 1))
                    ins = None
                    for kc in range(NCH):
                        ins = e.matmul(ps_b[:, 0:512], wch[:, par * 4096 + 1024 + kc * 128: par * 4096 + 1024 + (kc + 1) * 128],
                                       xch(kc, tc0, tc0 + 512), start=(kc == 0), stop=(kc == NCH - 1))
                    return ins
                tk.op("t", mm_h, reads=[xn_key, f"f:wh{par}"],
                      writes=[f"ps:{ppar}", f"ps:{2 + ppar}"])
                tk.op("s", lambda e, oc=oc, ppar=ppar, ps_a=ps_a: e.activation(
                    usc[:, ppar * 512:(ppar + 1) * 512], ps_a[:, 0:512], AF.Silu,
                    scale=scl[:, oc:oc + 1]),
                    reads=[f"ps:{ppar}", "f:scl"], writes=["f:yt"])
                tk.op("v", lambda e, oc=oc, tt=tt, ppar=ppar, ps_b=ps_b: e.scalar_tensor_tensor(
                    hs[:, oc * 1024 + tt * 512: oc * 1024 + (tt + 1) * 512],
                    ps_b[:, 0:512], scl[:, 32 + oc:33 + oc], usc[:, ppar * 512:(ppar + 1) * 512],
                    ALU.mult, ALU.mult),
                    reads=[f"ps:{2 + ppar}", "f:scl", "f:yt"], writes=["f:h"])
        for stage, (wnm, snm, soff, src_key, sbuf, dst_key, dbuf) in enumerate([
                ("f_h1", "sh1", 64, "f:h", hs, "f:g", gs),
                ("f_h2", "sh2", 96, "f:g", gs, "f:h", hs)]):
            for oc in range(32):
                par = oc % 2
                tk.dma("g", lambda e, oc=oc, par=par, wnm=wnm: e.dma_start(
                    out=wch[:, par * 4096:(par + 1) * 4096],
                    in_=cx.dram[wnm + "_s"][oc]),
                    writes=[f"f:wh{par}"])
                for tt in range(2):
                    ppar = tt % 2
                    ps = cx.pf[4 + ppar]

                    def mm_g(e, oc=oc, par=par, tt=tt, sbuf=sbuf, ps=ps):
                        ins = None
                        for kc in range(32):
                            ins = e.matmul(ps[:, 0:512],
                                           wch[:, par * 4096 + kc * 128: par * 4096 + (kc + 1) * 128],
                                           sbuf[:, kc * 1024 + tt * 512: kc * 1024 + (tt + 1) * 512],
                                           start=(kc == 0), stop=(kc == 31))
                        return ins
                    tk.op("t", mm_g, reads=[src_key, f"f:wh{par}"], writes=[f"ps:{4 + ppar}"])
                    tk.op("s", lambda e, oc=oc, tt=tt, soff=soff, dbuf=dbuf, ps=ps: e.activation(
                        dbuf[:, oc * 1024 + tt * 512: oc * 1024 + (tt + 1) * 512],
                        ps[:, 0:512], AF.Gelu, scale=scl[:, soff + oc:soff + oc + 1]),
                        reads=[f"ps:{4 + ppar}", "f:scl"], writes=[dst_key])
        for tt in range(2):
            tc0 = hoff + tt * 512
            for kc in range(NCH):
                tk.dma("y", lambda e, kc=kc, tc0=tc0: e.dma_start(
                    out=_chunk(yt, kc, 512), in_=y0_in_ap[kc * 128:(kc + 1) * 128, tc0:tc0 + 512]),
                    reads=list(y_dep), writes=["f:yt"])
            for oc in range(NCH):
                par = oc % 2
                tk.dma("g", lambda e, oc=oc, par=par: e.dma_start(
                    out=wch[:, par * 4096:(par + 1) * 4096],
                    in_=cx.dram["f_w3_s"][oc]),
                    writes=[f"f:wh{par}"])
                ps = cx.pf[par]

                def mm_o(e, oc=oc, par=par, tt=tt, ps=ps):
                    ins = None
                    for kc in range(32):
                        ins = e.matmul(ps[:, 0:512],
                                       wch[:, par * 4096 + kc * 128: par * 4096 + (kc + 1) * 128],
                                       hs[:, kc * 1024 + tt * 512: kc * 1024 + (tt + 1) * 512],
                                       start=(kc == 0), stop=(kc == 31))
                    return ins
                tk.op("t", mm_o, reads=["f:h", f"f:wh{par}"], writes=[f"ps:{par}"])
                tk.op("v", lambda e, oc=oc, par=par, ps=ps: e.scalar_tensor_tensor(
                    _chunk(yt, oc, 512), ps[:, 0:512], scl[:, 128 + oc:129 + oc],
                    _chunk(yt, oc, 512), ALU.mult, ALU.add),
                    reads=[f"ps:{par}", "f:scl", "f:yt"], writes=["f:yt"])
            out_store(half, tt)


def build_l1b():
    nc = bass.Bass("TRN2", target_bir_lowering=False)
    es = contextlib.ExitStack()
    with es:
        xn_in = nc.dram_tensor("xn1_in", [DM, TPC], BF16, kind="ExternalInput").ap()
        y0_in = nc.dram_tensor("y0_in", [DM, TPC], F32, kind="ExternalInput").ap()
        y_out = nc.dram_tensor("y_out", [DM, TPC], F32, kind="ExternalOutput").ap()

        block = es.enter_context(nc.Block())
        cx = _setup_common(nc, es, block, _SHARED_CACHE, L1B_SHARED)
        tk = cx.tk

        xn_sb = cx.sb("xn_sb", [128, NCH * TPC], BF16)
        for kc in range(NCH):
            tk.dma("y", lambda e, kc=kc: e.dma_start(
                out=_chunk(xn_sb, kc, TPC), in_=xn_in[kc * 128:(kc + 1) * 128, :]), writes=["xn"])

        def out_store(half, tt):
            tc0 = half * 1024 + tt * 512
            for kc in range(NCH):
                tk.dma("y", lambda e, kc=kc, tc0=tc0: e.dma_start(
                    out=y_out[kc * 128:(kc + 1) * 128, tc0:tc0 + 512],
                    in_=_chunk(cx.f_yt, kc, 512)),
                    reads=["f:yt"], writes=["o:y"])

        emit_l1_ffn(cx, xn_sb, "xn", TPC, 0, y0_in, out_store)
        n = tk.dcnt["y"]
        block.sync(lambda eng: eng.wait_ge(tk.dsem["y"], 16 * n))
    return nc


_SHARED_CACHE = None


def _set_shared(shared):
    global _SHARED_CACHE
    _SHARED_CACHE = shared




MEGA_SHARED = sorted(set(L0A_SHARED + MOE_SHARED + L1A_SHARED + L1B_SHARED))


def build_mega(dbg=False, cc=True):
    nc = bass.Bass("TRN2", target_bir_lowering=False)
    es = contextlib.ExitStack()
    with es:
        x_in = nc.dram_tensor("x_fm", [DM, TH], F32, kind="ExternalInput").ap()
        kvm_in = nc.dram_tensor("kvmask", [1, TH], F32, kind="ExternalInput").ap()
        y_out = nc.dram_tensor("y_out", [DM, TPC], F32, kind="ExternalOutput").ap()
        kind = dict(kind="ExternalOutput") if dbg else {}
        x1s = nc.dram_tensor("x1s", [DM, TPC], F32, **kind)
        y0s = nc.dram_tensor("y0s", [DM, TPC], F32, **kind)
        if dbg:
            xn_dbg = nc.dram_tensor("xn_dbg", [DM, TPC], BF16, kind="ExternalOutput").ap()
            stg_dbg = nc.dram_tensor("stg_dbg", [128, STG_W], F32, kind="ExternalOutput").ap()
            gts_dbg = nc.dram_tensor("gts_dbg", [128, 256], F32, kind="ExternalOutput").ap()
            x1c_dbg = nc.dram_tensor("x1c_dbg", [DM, TPC], F32, kind="ExternalOutput").ap()
            xn2_dbg = nc.dram_tensor("xn2_dbg", [DM, TPC], BF16, kind="ExternalOutput").ap()

        block = es.enter_context(nc.Block())
        cx = _setup_common(nc, es, block, _SHARED_CACHE, MEGA_SHARED)
        cx.dram["kvmask"] = kvm_in
        tk = cx.tk

        awu = cx.sb("actwarm", [1, 16], F32)
        tk.op("v", lambda e: e.memset(awu[:, :], 1.0), writes=["actwarm"])
        for fn in (AF.Sqrt, AF.Exp, AF.Silu, AF.Gelu):
            tk.op("s", lambda e, fn=fn: e.activation(awu[:, :], awu[:, :], fn),
                  reads=["actwarm"], writes=["actwarm"])
            tk.barrier()

        ccw_in = nc.dram_tensor("ccw_in", [1, 16], F32)
        ccw_out = nc.dram_tensor("ccw_out", [1, 16], F32)
        wup = cx.sb("warmup", [1, 16], F32)
        tk.op("v", lambda e: e.memset(wup[:, :], 1.0), writes=["wup"])
        tk.dma("g", lambda e: e.dma_start(out=ccw_in.ap()[:, :], in_=wup[:, :]),
               reads=["wup"], writes=["d:ccw"])
        tk.op("g", lambda e: e.collective_compute(
            "AllReduce", ALU.add, replica_groups=[[0, 1, 2, 3], [4, 5, 6, 7]],
            ins=[ccw_in.ap().opt()], outs=[ccw_out.ap().opt()]),
            reads=["d:ccw"], writes=["d:ccwo"])
        tk.dma("g", lambda e: e.dma_start(out=wup[:, :], in_=ccw_out.ap()[:, :]),
               reads=["d:ccwo"], writes=["wup"])
        tk.barrier()

        xn_sb = cx.sb("xn_sb", [128, NCH * TH], BF16)
        cx.push_scope()
        cx.ln_scr = cx.sb("ln_scr", [128, 2 * 512], F32)
        cx.ln_rows = cx.sb("ln_rows", [1, 4 * 512], F32)
        cx.ln_ones_col = cx.sb("ln_ones_col", [128, 1], F32)
        cx.ln_eps = cx.sb("ln_eps", [1, 1], F32)
        tk.op("v", lambda e: e.memset(cx.ln_ones_col[:, :], 1.0), writes=["ln:ones_col"])
        tk.op("v", lambda e: e.memset(cx.ln_eps[:, :], 1e-5), writes=["ln:eps"])
        cx.ln_init = True

        cx.push_scope()
        x_sb = cx.sb("x_sb", [128, NCH * TPC], F32)
        cx.push_scope()
        xh_sb = cx.sb("xh_sb", [128, NCH * 256], F32)
        for kc in range(NCH):
            tk.dma("y", lambda e, kc=kc: e.dma_start(
                out=_chunk(x_sb, kc, TPC), in_=x_in[kc * 128:(kc + 1) * 128, HALO:HALO + TPC]),
                writes=["x"])
            tk.dma("y", lambda e, kc=kc: e.dma_start(
                out=_chunk(xh_sb, kc, 256, 0, 128), in_=x_in[kc * 128:(kc + 1) * 128, 0:HALO]),
                writes=["xh"])
            tk.dma("y", lambda e, kc=kc: e.dma_start(
                out=_chunk(xh_sb, kc, 256, 128, 256),
                in_=x_in[kc * 128:(kc + 1) * 128, TH - HALO:TH]),
                writes=["xh"])
        emit_ln(cx, x_sb, "x", TPC, 0, TPC, xn_sb, "xn", TH, HALO)
        emit_ln(cx, xh_sb, "xh", 256, 0, 128, xn_sb, "xn", TH, 0)
        emit_ln(cx, xh_sb, "xh", 256, 128, 128, xn_sb, "xn", TH, TH - HALO)
        emit_l0_attn(cx, xn_sb, "xn", x_sb, "x")
        cx.pop_scope()

        cx.push_scope()
        cx.moe_dbg = (gts_dbg,) if dbg else None
        emit_moe(cx, xn_sb, "xn", TH, HALO, x_sb, "x")
        if dbg:
            for kc in range(NCH):
                tk.dma("y", lambda e, kc=kc: e.dma_start(
                    out=xn2_dbg[kc * 128:(kc + 1) * 128, :],
                    in_=xn_sb[:, kc * TH + HALO: kc * TH + HALO + TPC]),
                    reads=["xn"], writes=["dbg"])
        for kc in range(NCH):
            tk.dma("y", lambda e, kc=kc: e.dma_start(
                out=x1s.ap()[kc * 128:(kc + 1) * 128, :], in_=_chunk(x_sb, kc, TPC)),
                reads=["x"], writes=["d:x1s"])
        if dbg:
            tk.dma("y", lambda e: e.dma_start(out=x1c_dbg[:, :], in_=x1s.ap()[:, :]),
                   reads=["d:x1s"], writes=["dbg"])
        cx.pop_scope()
        cx.pop_scope()

        cx.push_scope()

        def y0_store(tt):
            c0 = tt * 512
            for kc in range(NCH):
                tk.dma("y", lambda e, kc=kc, c0=c0: e.dma_start(
                    out=y0s.ap()[kc * 128:(kc + 1) * 128, c0:c0 + 512],
                    in_=_chunk(cx.l1_xt, kc, 512)),
                    reads=["l1:xt"], writes=["d:y0s"])
        cx.l1_y0_store = y0_store
        emit_l1_attn(cx, x1s.ap(), xn_sb, "xn", TH, HALO, None, None, {},
                     x_dep=("d:x1s",), cc=cc)
        if dbg:
            for kc in range(NCH):
                tk.dma("y", lambda e, kc=kc: e.dma_start(
                    out=xn_dbg[kc * 128:(kc + 1) * 128, :],
                    in_=xn_sb[:, kc * TH + HALO: kc * TH + HALO + TPC]),
                    reads=["xn"], writes=["dbg"])
            tk.dma("y", lambda e: e.dma_start(out=stg_dbg[:, :], in_=cx.l1_stg_dbg[:, :]),
                   reads=["l1:stg"], writes=["dbg"])
        cx.pop_scope()
        cx.pop_scope()

        cx.push_scope()

        def out_store(half, tt):
            tc0 = half * 1024 + tt * 512
            for kc in range(NCH):
                tk.dma("y", lambda e, kc=kc, tc0=tc0: e.dma_start(
                    out=y_out[kc * 128:(kc + 1) * 128, tc0:tc0 + 512],
                    in_=_chunk(cx.f_yt, kc, 512)),
                    reads=["f:yt"], writes=["o:y"])
        emit_l1_ffn(cx, xn_sb, "xn", TH, HALO, y0s.ap(), out_store, y_dep=("d:y0s",))
        cx.pop_scope()

        ny = tk.dcnt["y"]
        ng = tk.dcnt["g"]
        def fin(eng):
            eng.wait_ge(tk.dsem["y"], 16 * ny)
            eng.wait_ge(tk.dsem["g"], 16 * ng)
        block.sync(fin)
    return nc


def kernel(**inputs):
    shared = _prep_shared(inputs)
    _set_shared(shared)
    cores = list(range(NCORE))
    nc = _get_prog("MEGA", build_mega)
    maps = []
    for c in cores:
        pc = _prep_core(inputs, c)
        m = {nm: shared[nm] for nm in MEGA_SHARED}
        m["x_fm"] = pc["x_fm"]
        m["kvmask"] = pc["kvmask"]
        maps.append(m)
    res = run_bass_kernel_spmd(nc, maps, core_ids=cores)
    out = np.empty((B * S, DM), np.float32)
    for c in cores:
        out[c * TPC:(c + 1) * TPC] = np.asarray(res.results[c]["y_out"]).T
    return out.reshape(B, S, DM)




def build_mega_l0(**moekw):
    nc = bass.Bass("TRN2", target_bir_lowering=False)
    es = contextlib.ExitStack()
    with es:
        x_in = nc.dram_tensor("x_fm", [DM, TH], F32, kind="ExternalInput").ap()
        kvm_in = nc.dram_tensor("kvmask", [1, TH], F32, kind="ExternalInput").ap()
        y_out = nc.dram_tensor("y_out", [DM, TPC], F32, kind="ExternalOutput").ap()

        block = es.enter_context(nc.Block())
        cx = _setup_common(nc, es, block, _SHARED_CACHE, MEGA_SHARED)
        cx.dram["kvmask"] = kvm_in
        tk = cx.tk
        xn_sb = cx.sb("xn_sb", [128, NCH * TH], BF16)
        cx.push_scope()
        cx.ln_scr = cx.sb("ln_scr", [128, 2 * 512], F32)
        cx.ln_rows = cx.sb("ln_rows", [1, 4 * 512], F32)
        cx.ln_ones_col = cx.sb("ln_ones_col", [128, 1], F32)
        cx.ln_eps = cx.sb("ln_eps", [1, 1], F32)
        tk.op("v", lambda e: e.memset(cx.ln_ones_col[:, :], 1.0), writes=["ln:ones_col"])
        tk.op("v", lambda e: e.memset(cx.ln_eps[:, :], 1e-5), writes=["ln:eps"])
        cx.ln_init = True
        cx.push_scope()
        x_sb = cx.sb("x_sb", [128, NCH * TPC], F32)
        cx.push_scope()
        xh_sb = cx.sb("xh_sb", [128, NCH * 256], F32)
        for kc in range(NCH):
            tk.dma("y", lambda e, kc=kc: e.dma_start(
                out=_chunk(x_sb, kc, TPC), in_=x_in[kc * 128:(kc + 1) * 128, HALO:HALO + TPC]),
                writes=["x"])
            tk.dma("y", lambda e, kc=kc: e.dma_start(
                out=_chunk(xh_sb, kc, 256, 0, 128), in_=x_in[kc * 128:(kc + 1) * 128, 0:HALO]),
                writes=["xh"])
            tk.dma("y", lambda e, kc=kc: e.dma_start(
                out=_chunk(xh_sb, kc, 256, 128, 256),
                in_=x_in[kc * 128:(kc + 1) * 128, TH - HALO:TH]),
                writes=["xh"])
        emit_ln(cx, x_sb, "x", TPC, 0, TPC, xn_sb, "xn", TH, HALO)
        emit_ln(cx, xh_sb, "xh", 256, 0, 128, xn_sb, "xn", TH, 0)
        emit_ln(cx, xh_sb, "xh", 256, 128, 128, xn_sb, "xn", TH, TH - HALO)
        emit_l0_attn(cx, xn_sb, "xn", x_sb, "x")
        cx.pop_scope()
        cx.push_scope()
        emit_moe(cx, xn_sb, "xn", TH, HALO, x_sb, "x", **moekw)
        for kc in range(NCH):
            tk.dma("y", lambda e, kc=kc: e.dma_start(
                out=y_out[kc * 128:(kc + 1) * 128, :], in_=_chunk(x_sb, kc, TPC)),
                reads=["x"], writes=["o:y"])
        cx.pop_scope()
        cx.pop_scope()
        cx.pop_scope()
        ny = tk.dcnt["y"]
        ng = tk.dcnt["g"]
        def fin(eng):
            eng.wait_ge(tk.dsem["y"], 16 * ny)
            eng.wait_ge(tk.dsem["g"], 16 * ng)
        block.sync(fin)
    return nc


_PROGS = {}


def _get_prog(name, builder):
    if name not in _PROGS:
        _PROGS[name] = builder()
    return _PROGS[name]


def kernel_v1(**inputs):
    shared = _prep_shared(inputs)
    _set_shared(shared)
    cores = list(range(NCORE))
    pcs = [_prep_core(inputs, c) for c in cores]

    ncA = _get_prog("A", build_l0a)
    mapsA = []
    for c in cores:
        m = {nm: shared[nm] for nm in L0A_SHARED}
        m["x_fm"] = pcs[c]["x_fm"]
        m["kvmask"] = pcs[c]["kvmask"]
        mapsA.append(m)
    resA = run_bass_kernel_spmd(ncA, mapsA, core_ids=cores)

    ncB = _get_prog("B", build_l0b)
    mapsB = []
    for c in cores:
        m = {nm: shared[nm] for nm in MOE_SHARED}
        m["xn_in"] = np.ascontiguousarray(
            np.asarray(resA.results[c]["xn_out"])[:, HALO:HALO + TPC])
        m["x1p_in"] = np.ascontiguousarray(np.asarray(resA.results[c]["x1p_out"]))
        mapsB.append(m)
    resB = run_bass_kernel_spmd(ncB, mapsB, core_ids=cores)

    ncC = _get_prog("C", build_l1a)
    mapsC = []
    for c in cores:
        m = {nm: shared[nm] for nm in L1A_SHARED}
        m["x1_in"] = np.ascontiguousarray(np.asarray(resB.results[c]["x1_out"]))
        m["hsel"] = pcs[c]["hsel"]
        m["hselc"] = pcs[c]["hselc"]
        m["ccmaxbias"] = pcs[c]["ccmaxbias"]
        mapsC.append(m)
    resC = run_bass_kernel_spmd(ncC, mapsC, core_ids=cores)

    ncD = _get_prog("D", build_l1b)
    mapsD = []
    for c in cores:
        m = {nm: shared[nm] for nm in L1B_SHARED}
        m["xn1_in"] = np.ascontiguousarray(np.asarray(resC.results[c]["xn1_out"]))
        m["y0_in"] = np.ascontiguousarray(np.asarray(resC.results[c]["y0_out"]))
        mapsD.append(m)
    resD = run_bass_kernel_spmd(ncD, mapsD, core_ids=cores)

    out = np.empty((B * S, DM), np.float32)
    for c in cores:
        out[c * TPC:(c + 1) * TPC] = np.asarray(resD.results[c]["y_out"]).T
    return out.reshape(B, S, DM)


# version 05
# speedup vs baseline: 1.2739x; 1.2739x over previous
import contextlib
from collections import defaultdict

import numpy as np
import ml_dtypes

import concourse.bass as bass
import concourse.mybir as mybir
from concourse.bass_utils import run_bass_kernel_spmd

F32 = mybir.dt.float32
BF16 = mybir.dt.bfloat16
AF = mybir.ActivationFunctionType
ALU = mybir.AluOpType
AX = mybir.AxisListType.X

NCORE = 8
B, S, DM = 2, 8192, 1024
TPC = 2048
HALO = 128
TH = TPC + 2 * HALO
NCH = DM // 128

L0_H, L0_KV, L0_D, L0_W = 16, 2, 64, 256
HLIST = [(p % 2) * 8 + p // 2 for p in range(16)]

MOE_E, MOE_K = 16, 4
GLU_LIMIT, GLU_ALPHA = 7.0, 1.702

L1_M, L1_DH = 266, 64
L1_EPS = 1e-4
L1_RATIO = float(L1_M) ** -0.5
L1_DN = float(L1_DH) ** -0.25

ENG_NAME = {"v": "vector", "s": "scalar", "t": "tensor", "g": "gpsimd", "y": "sync"}


class TK:

    def __init__(self, nc, block, es):
        self.nc = nc
        self.block = block
        self.sem = {e: es.enter_context(nc.semaphore(f"tk_{e}")) for e in ENG_NAME}
        self.dsem = {q: es.enter_context(nc.semaphore(f"tkd_{q}")) for q in ("g", "y")}
        self.cnt = {e: 0 for e in ENG_NAME}
        self.dcnt = {q: 0 for q in ("g", "y")}
        self.known = {e: defaultdict(int) for e in ENG_NAME}
        self.writer = {}
        self.readers = defaultdict(list)
        self.pending = {e: [] for e in ENG_NAME}

    def flush(self):
        for e in ENG_NAME:
            bodies = self.pending[e]
            if not bodies:
                continue

            def run_all(eng, bodies=bodies):
                for b in bodies:
                    b(eng)
            getattr(self.block, ENG_NAME[e])(run_all)
            self.pending[e] = []

    def _needs(self, e, reads, writes):
        need = {}

        def add(tok):
            if tok is None:
                return
            kind, src, c = tok
            pk = (kind, src)
            if need.get(pk, 0) < c:
                need[pk] = c

        for k in reads:
            add(self.writer.get(k))
        for k in writes:
            add(self.writer.get(k))
            for tok in self.readers.get(k, ()):
                add(tok)
        out = []
        kn = self.known[e]
        for pk in sorted(need, key=lambda p: (p[0], str(p[1]))):
            c = need[pk]
            if kn[pk] >= c:
                continue
            out.append((pk, c))
        return out

    def _emit(self, e, fn, needs, inc_q):
        sem, dsem = self.sem, self.dsem

        def body(eng):
            for (kind, src), c in needs:
                if kind == "e":
                    eng.wait_ge(sem[src], c)
                else:
                    eng.wait_ge(dsem[src], 16 * c)
            ins = fn(eng)
            if inc_q is None:
                ins.then_inc(sem[e], 1)
            else:
                ins.then_inc(dsem[inc_q], 16)

        self.pending[e].append(body)
        kn = self.known[e]
        for pk, c in needs:
            if kn[pk] < c:
                kn[pk] = c

    def op(self, e, fn, reads=(), writes=()):
        needs = self._needs(e, reads, writes)
        self._emit(e, fn, needs, None)
        self.cnt[e] += 1
        tok = ("e", e, self.cnt[e])
        for k in reads:
            self.readers[k].append(tok)
        for k in writes:
            self.writer[k] = tok
            self.readers[k] = [tok]
        return tok

    DMA_WINDOW = 8

    def dma(self, q, fn, reads=(), writes=()):
        needs = self._needs(q, reads, writes)
        lag = self.dcnt[q] + 1 - self.DMA_WINDOW
        if lag > 0 and self.known[q][("d", q)] < lag:
            have = {pk: c for pk, c in needs}
            if have.get(("d", q), 0) < lag:
                have[("d", q)] = lag
            needs = sorted(have.items(), key=lambda kv: (kv[0][0], str(kv[0][1])))
        self._emit(q, fn, needs, q)
        self.dcnt[q] += 1
        tok = ("d", q, self.dcnt[q])
        for k in reads:
            self.readers[k].append(tok)
        for k in writes:
            self.writer[k] = tok
            self.readers[k] = [tok]
        return tok

    def barrier(self):
        snap_e = dict(self.cnt)
        snap_d = dict(self.dcnt)
        for e in ENG_NAME:
            needs = []
            for o in ENG_NAME:
                if snap_e[o] > self.known[e][("e", o)]:
                    needs.append((("e", o), snap_e[o]))
            for q in ("g", "y"):
                if snap_d[q] > self.known[e][("d", q)]:
                    needs.append((("d", q), snap_d[q]))
            if not needs:
                continue

            def wrapped(eng, needs=needs):
                for (kind, src), c in needs:
                    if kind == "e":
                        eng.wait_ge(self.sem[src], c)
                    else:
                        eng.wait_ge(self.dsem[src], 16 * c)

            self.pending[e].append(wrapped)
            for pk, c in needs:
                if self.known[e][pk] < c:
                    self.known[e][pk] = c



def _prep_shared(I):
    d = {}
    perm = np.concatenate([np.arange(h * 64, (h + 1) * 64) for h in HLIST])
    d["l0_wq"] = np.ascontiguousarray(np.asarray(I["L0_attn_Wq"], np.float32)[:, perm])
    d["l0_bq"] = np.ascontiguousarray(np.asarray(I["L0_attn_bq"], np.float32)[perm][None, :])
    d["l0_wk"] = np.ascontiguousarray(np.asarray(I["L0_attn_Wk"], np.float32))
    d["l0_bk"] = np.ascontiguousarray(np.asarray(I["L0_attn_bk"], np.float32)[None, :])
    d["l0_wv"] = np.ascontiguousarray(np.asarray(I["L0_attn_Wv"], np.float32))
    d["l0_bv"] = np.ascontiguousarray(np.asarray(I["L0_attn_bv"], np.float32)[None, :])
    d["l0_wo"] = np.ascontiguousarray(np.asarray(I["L0_attn_Wo"], np.float32)[perm, :])
    d["l0_bo"] = np.ascontiguousarray(np.asarray(I["L0_attn_bo"], np.float32)[None, :])

    d["moe_scale"] = np.ascontiguousarray(
        np.asarray(I["L0_ffn_norm_scale"], np.float32).reshape(NCH, 128).T)
    d["moe_wg"] = np.ascontiguousarray(np.asarray(I["L0_ffn_wg"], np.float32))
    d["moe_bg"] = np.ascontiguousarray(np.asarray(I["L0_ffn_bg"], np.float32)[None, :])
    w1 = np.asarray(I["L0_ffn_w1"], np.float32)
    b1 = np.asarray(I["L0_ffn_b1"], np.float32)
    w1p = np.empty((MOE_E, 1024, 2048), np.float32)
    b1p = np.empty((MOE_E, 1, 2048), np.float32)
    for e in range(MOE_E):
        a, bb = w1[e][:, 0::2], w1[e][:, 1::2]
        ba, bbv = b1[e][0::2], b1[e][1::2]
        for j in range(8):
            w1p[e][:, j * 256:j * 256 + 128] = a[:, j * 128:(j + 1) * 128]
            w1p[e][:, j * 256 + 128:(j + 1) * 256] = bb[:, j * 128:(j + 1) * 128]
            b1p[e][0, j * 256:j * 256 + 128] = ba[j * 128:(j + 1) * 128]
            b1p[e][0, j * 256 + 128:(j + 1) * 256] = bbv[j * 128:(j + 1) * 128]
    d["moe_w1"] = w1p
    d["moe_b1"] = b1p
    d["moe_w2"] = np.ascontiguousarray(np.asarray(I["L0_ffn_w2"], np.float32))
    d["moe_b2"] = np.ascontiguousarray(np.asarray(I["L0_ffn_b2"], np.float32))

    for nm in ("Wq", "Wk", "Wv", "Wo", "Wfc"):
        d[f"l1_{nm.lower()}"] = np.ascontiguousarray(np.asarray(I[f"L1_attn_{nm}"], np.float32).T)
    for nm in ("bq", "bk", "bv", "bo", "bfc"):
        d[f"l1_{nm}"] = np.ascontiguousarray(np.asarray(I[f"L1_attn_{nm}"], np.float32)[None, :])
    projT = np.asarray(I["L1_attn_proj"], np.float32).T
    d["l1_projT"] = np.ascontiguousarray(np.concatenate([projT, projT], axis=0))

    d["f_w1"] = np.ascontiguousarray(np.asarray(I["L1_ffn_w1_w"], np.float32).T)
    d["f_w2"] = np.ascontiguousarray(np.asarray(I["L1_ffn_w2_w"], np.float32).T)
    d["f_h1"] = np.ascontiguousarray(np.asarray(I["L1_ffn_h1_w"], np.float32).T)
    d["f_h2"] = np.ascontiguousarray(np.asarray(I["L1_ffn_h2_w"], np.float32).T)
    d["f_w3"] = np.ascontiguousarray(np.asarray(I["L1_ffn_w3_w"], np.float32).T)
    d["f_s1"] = np.ascontiguousarray(np.asarray(I["L1_ffn_w1_s"], np.float32).reshape(-1, 128).T)
    d["f_s2"] = np.ascontiguousarray(np.asarray(I["L1_ffn_w2_s"], np.float32).reshape(-1, 128).T)
    d["f_sh1"] = np.ascontiguousarray(np.asarray(I["L1_ffn_h1_s"], np.float32).reshape(-1, 128).T)
    d["f_sh2"] = np.ascontiguousarray(np.asarray(I["L1_ffn_h2_s"], np.float32).reshape(-1, 128).T)
    d["f_s3"] = np.ascontiguousarray(np.asarray(I["L1_ffn_w3_s"], np.float32).reshape(-1, 128).T)

    def _pstream(w, kchunks):
        K, O = w.shape
        assert K == kchunks * 128
        return np.ascontiguousarray(w.reshape(kchunks, 128, O).transpose(1, 0, 2).reshape(128, kchunks * O))

    def _ostream(w, kchunks, osz=128):
        K, O = w.shape
        noc = O // osz
        a = w.reshape(kchunks, 128, noc, osz).transpose(2, 1, 0, 3)
        return np.ascontiguousarray(a.reshape(noc, 128, kchunks * osz))

    d["l0_wq_s"] = _pstream(d["l0_wq"], NCH)
    d["l0_wk_s"] = _pstream(d["l0_wk"], NCH)
    d["l0_wv_s"] = _pstream(d["l0_wv"], NCH)
    d["l0_wo_s"] = _pstream(d["l0_wo"], NCH)
    d["moe_wg_s"] = _pstream(d["moe_wg"], NCH)
    d["moe_w1_s"] = np.ascontiguousarray(np.stack(
        [np.stack([_pstream(d["moe_w1"][e][:, j * 256:(j + 1) * 256], NCH) for j in range(8)])
         for e in range(MOE_E)]))
    d["moe_w2_s"] = np.ascontiguousarray(np.stack(
        [_pstream(d["moe_w2"][e], NCH) for e in range(MOE_E)]))
    for nm in ("l1_wq", "l1_wk", "l1_wv", "l1_wo", "l1_wfc"):
        d[nm + "_s"] = _pstream(d[nm], NCH)
    d["f_w1_s"] = _ostream(d["f_w1"], NCH)
    d["f_w2_s"] = _ostream(d["f_w2"], NCH)
    d["f_h1_s"] = _ostream(d["f_h1"], 32)
    d["f_h2_s"] = _ostream(d["f_h2"], 32)
    d["f_w3_s"] = _ostream(d["f_w3"], 32)

    d["eye"] = np.eye(128, dtype=np.float32)
    d["ones_row"] = np.ones((1, 512), np.float32)
    r = np.arange(128)[:, None]
    u = np.arange(384)[None, :]
    d["l0mask"] = ((u >= r) & (u < r + 256)).astype(np.float32)
    return d


def _prep_core(I, c):
    xf = np.asarray(I["L0_attn_x"], np.float32).reshape(B * S, DM)
    t0 = c * TPC
    seq = t0 // S
    s_lo, s_hi = seq * S, (seq + 1) * S
    lo, hi = t0 - HALO, t0 + TPC + HALO
    pad_l, pad_r = max(0, s_lo - lo), max(0, hi - s_hi)
    sl = xf[max(lo, s_lo):min(hi, s_hi)]
    shard = np.concatenate(
        [np.zeros((pad_l, DM), np.float32), sl, np.zeros((pad_r, DM), np.float32)], axis=0)
    kvmask = np.ones((1, TH), np.float32)
    if pad_l:
        kvmask[0, :pad_l] = 0.0
    if pad_r:
        kvmask[0, TH - pad_r:] = 0.0
    hsel = np.zeros((1, 2), np.float32)
    hsel[0, seq] = 1.0
    ccmaxbias = np.full((2, 16), -1.0e30, np.float32)
    ccmaxbias[seq, :] = 0.0
    return {
        "x_fm": np.ascontiguousarray(shard.T),
        "kvmask": kvmask,
        "hsel": hsel,
        "hselc": np.ascontiguousarray(hsel.T),
        "ccmaxbias": ccmaxbias,
    }



class Ctx:
    pass


def _setup_common(nc, es, block, shared, want):
    cx = Ctx()
    cx.nc = nc
    cx.es = es
    cx.tk = TK(nc, block, es)
    cx.dram = {}
    for nm in want:
        arr = shared[nm]
        dt = BF16 if arr.dtype == ml_dtypes.bfloat16 else F32
        cx.dram[nm] = nc.dram_tensor(nm, list(arr.shape), dt, kind="ExternalInput").ap()

    cx._scopes = [es]
    cx.sb = lambda name, shape, dt: cx._scopes[-1].enter_context(nc.sbuf_tensor(name, shape, dt))

    def _push():
        cx._scopes.append(contextlib.ExitStack())

    def _pop():
        cx._scopes.pop().close()
        cx.tk.barrier()
    cx.push_scope = _push
    cx.pop_scope = _pop
    cx.eye = cx.sb("c_eye", [128, 128], F32)
    cx.eye16 = cx.sb("c_eye16", [128, 128], BF16)
    cx.ones_row = cx.sb("c_ones_row", [1, 512], F32)
    cx.ones_col16 = cx.sb("c_ones16", [128, 1], BF16)
    tk = cx.tk
    tk.dma("y", lambda e: e.dma_start(out=cx.eye[:, :], in_=cx.dram["eye"][:, :]), writes=["c:eye"])
    tk.dma("y", lambda e: e.dma_start(out=cx.ones_row[:, :], in_=cx.dram["ones_row"][:, :]), writes=["c:ones_row"])
    tk.op("v", lambda e: e.tensor_copy(cx.eye16[:, :], cx.eye[:, :]), reads=["c:eye"], writes=["c:eye16"])
    tk.op("v", lambda e: e.memset(cx.ones_col16[:, :], 1.0), writes=["c:ones_col16"])
    cx.ones_row16 = cx.sb("c_ones_row16", [1, 512], BF16)
    tk.op("v", lambda e: e.memset(cx.ones_row16[:, :], 1.0), writes=["c:ones_row16"])

    cx.pf = [es.enter_context(nc.psum_tensor(f"pf{i}", [128, 512], F32)) for i in range(6)]
    cx.pb = [es.enter_context(nc.psum_tensor(f"pb{i}", [128, 1024], BF16)) for i in range(2)]
    cx.ln_init = False
    return cx


def _chunk(t, kc, width, c0=0, c1=None):
    if c1 is None:
        c1 = width
    return t[:, kc * width + c0: kc * width + c1]



def emit_ln(cx, src_sb, src_key, src_width, src_c0, ntok, dst_sb, dst_key, dst_width, dst_c0):
    nc, tk = cx.nc, cx.tk
    if not cx.ln_init:
        cx.ln_scr = cx.sb("ln_scr", [128, 2 * 512], F32)
        cx.ln_rows = cx.sb("ln_rows", [1, 4 * 512], F32)
        cx.ln_ones_col = cx.sb("ln_ones_col", [128, 1], F32)
        cx.ln_eps = cx.sb("ln_eps", [1, 1], F32)
        tk.op("v", lambda e: e.memset(cx.ln_ones_col[:, :], 1.0), writes=["ln:ones_col"])
        tk.op("v", lambda e: e.memset(cx.ln_eps[:, :], 1e-5), writes=["ln:eps"])
        cx.ln_init = True
    scr, rows = cx.ln_scr, cx.ln_rows

    def sch(kc, a, b):
        return src_sb[:, kc * src_width + src_c0 + a: kc * src_width + src_c0 + b]

    def dch(kc, a, b):
        return dst_sb[:, kc * dst_width + dst_c0 + a: kc * dst_width + dst_c0 + b]

    for ti in range((ntok + 511) // 512):
        c0 = ti * 512
        w = min(512, ntok - c0)
        ps_s, ps_q, ps_b, ps_c = cx.pf[0], cx.pf[1], cx.pf[2], cx.pf[3]

        def mm_sum(e, c0=c0, w=w):
            ins = None
            for kc in range(NCH):
                ins = e.matmul(ps_s[0:1, 0:w], cx.ln_ones_col[:, :], sch(kc, c0, c0 + w),
                               start=(kc == 0), stop=(kc == NCH - 1))
            return ins
        tk.op("t", mm_sum, reads=[src_key, "ln:ones_col"], writes=["ps:0"])
        for kc in range(NCH):
            par = kc % 2
            tk.op("s", lambda e, kc=kc, par=par, c0=c0, w=w: e.activation(
                scr[:, par * 512:par * 512 + w], sch(kc, c0, c0 + w), AF.Square),
                reads=[src_key], writes=[f"ln:scr{par}"])
            tk.op("t", lambda e, kc=kc, par=par, w=w: e.matmul(
                ps_q[0:1, 0:w], cx.ln_ones_col[:, :], scr[:, par * 512:par * 512 + w],
                start=(kc == 0), stop=(kc == NCH - 1)),
                reads=[f"ln:scr{par}", "ln:ones_col"], writes=["ps:1"])
        tk.op("v", lambda e, w=w: e.tensor_scalar(
            rows[0:1, 0:w], ps_s[0:1, 0:w], 1.0 / DM, None, ALU.mult),
            reads=["ps:0"], writes=["ln:m"])
        tk.op("v", lambda e, w=w: e.tensor_mul(
            rows[0:1, 512:512 + w], rows[0:1, 0:w], rows[0:1, 0:w]),
            reads=["ln:m"], writes=["ln:msq"])
        tk.op("v", lambda e, w=w: e.tensor_scalar(
            rows[0:1, 1024:1024 + w], ps_q[0:1, 0:w], 1.0 / DM, None, ALU.mult),
            reads=["ps:1"], writes=["ln:var"])
        tk.op("v", lambda e, w=w: e.tensor_sub(
            rows[0:1, 1024:1024 + w], rows[0:1, 1024:1024 + w], rows[0:1, 512:512 + w]),
            reads=["ln:var", "ln:msq"], writes=["ln:var"])
        tk.op("s", lambda e, w=w: e.activation(
            rows[0:1, 1536:1536 + w], rows[0:1, 1024:1024 + w], AF.Sqrt,
            bias=cx.ln_eps[0:1, 0:1]),
            reads=["ln:var", "ln:eps"], writes=["ln:sd"])
        tk.op("v", lambda e, w=w: e.reciprocal(
            rows[0:1, 1536:1536 + w], rows[0:1, 1536:1536 + w]),
            reads=["ln:sd"], writes=["ln:sd"])
        tk.op("t", lambda e, w=w: e.matmul(
            ps_b[:, 0:w], cx.ones_row[0:1, 0:128], rows[0:1, 1536:1536 + w],
            start=True, stop=True), reads=["ln:sd", "c:ones_row"], writes=["ps:2"])
        tk.op("v", lambda e, w=w: e.tensor_mul(
            rows[0:1, 512:512 + w], rows[0:1, 0:w], rows[0:1, 1536:1536 + w]),
            reads=["ln:m", "ln:sd"], writes=["ln:msq"])
        tk.op("t", lambda e, w=w: e.matmul(
            ps_c[:, 0:w], cx.ones_row[0:1, 0:128], rows[0:1, 512:512 + w],
            start=True, stop=True), reads=["ln:msq", "c:ones_row"], writes=["ps:3"])
        for kc in range(NCH):
            par = kc % 2
            tk.op("v", lambda e, kc=kc, par=par, c0=c0, w=w: e.tensor_mul(
                scr[:, par * 512:par * 512 + w], sch(kc, c0, c0 + w), ps_b[:, 0:w]),
                reads=[src_key, "ps:2"], writes=[f"ln:scr{par}"])
            tk.op("v", lambda e, kc=kc, par=par, c0=c0, w=w: e.tensor_sub(
                dch(kc, c0, c0 + w), scr[:, par * 512:par * 512 + w], ps_c[:, 0:w]),
                reads=[f"ln:scr{par}", "ps:3"], writes=[dst_key])


def emit_l0_attn(cx, xn_sb, xn_key, x_sb, x_key, tiles=None, dbg=None):
    nc, tk, sb = cx.nc, cx.tk, cx.sb
    wq = sb("l0_wq_sb", [128, NCH * 1024], BF16)
    wo = sb("l0_wo_sb", [128, NCH * 1024], BF16)
    wk = sb("l0_wk_sb", [128, NCH * 128], BF16)
    wv = sb("l0_wv_sb", [128, NCH * 128], BF16)
    brow = sb("l0_brow", [1, 2304], F32)
    kvm = sb("l0_kvm", [1, TH], F32)
    msk = sb("l0_msk", [128, 384], F32)
    kbf = sb("l0_k", [128, TH], BF16)
    vtm = sb("l0_v", [128, 18 * 128], BF16)
    qt = sb("l0_qt", [128, 2 * 1024], BF16)
    at = sb("l0_at", [128, 2 * 1024], BF16)
    pr = sb("l0_pr", [128, 2 * 384], F32)
    prn = sb("l0_prn", [128, 2 * 384], BF16)
    prT = sb("l0_prT", [128, 2 * 384], BF16)
    rowm = sb("l0_rowm", [128, 8], F32)

    tk.dma("g", lambda e: e.dma_start(out=wq[:, :], in_=cx.dram["l0_wq_s"][:, :]), writes=["w:l0q"])
    tk.dma("g", lambda e: e.dma_start(out=wk[:, :], in_=cx.dram["l0_wk_s"][:, :]), writes=["w:l0k"])
    tk.dma("g", lambda e: e.dma_start(out=wv[:, :], in_=cx.dram["l0_wv_s"][:, :]), writes=["w:l0v"])
    tk.dma("g", lambda e: e.dma_start(out=wo[:, :], in_=cx.dram["l0_wo_s"][:, :]), writes=["w:l0o"])
    tk.dma("y", lambda e: e.dma_start(out=brow[0:1, 0:1024], in_=cx.dram["l0_bq"][:, :]), writes=["b:l0"])
    tk.dma("y", lambda e: e.dma_start(out=brow[0:1, 1024:1152], in_=cx.dram["l0_bk"][:, :]), writes=["b:l0"])
    tk.dma("y", lambda e: e.dma_start(out=brow[0:1, 1152:1280], in_=cx.dram["l0_bv"][:, :]), writes=["b:l0"])
    tk.dma("y", lambda e: e.dma_start(out=brow[0:1, 1280:2304], in_=cx.dram["l0_bo"][:, :]), writes=["b:l0"])
    tk.dma("y", lambda e: e.dma_start(out=kvm[:, :], in_=cx.dram["kvmask"][:, :]), writes=["b:kvm"])
    tk.dma("y", lambda e: e.dma_start(out=msk[:, :], in_=cx.dram["l0mask"][:, :]), writes=["b:msk"])

    for ti in range(5):
        c0 = ti * 512
        w = min(512, TH - c0)
        pi = ti % 2
        ps = cx.pf[pi]

        def mm_k(e, c0=c0, w=w, ps=ps):
            for kc in range(NCH):
                e.matmul(ps[:, 0:w], wk[:, kc * 128:(kc + 1) * 128],
                         _chunk(xn_sb, kc, TH, c0, c0 + w), start=(kc == 0), stop=False)
            return e.matmul(ps[:, 0:w], brow[0:1, 1024:1152], kvm[0:1, c0:c0 + w],
                            start=False, stop=True)
        tk.op("t", mm_k, reads=[xn_key, "w:l0k", "b:l0", "b:kvm"], writes=[f"ps:{pi}"])
        tk.op("s", lambda e, c0=c0, w=w, ps=ps: e.copy(kbf[:, c0:c0 + w], ps[:, 0:w]),
              reads=[f"ps:{pi}"], writes=["l0:k"])
    for j in range(18):
        c0 = j * 128
        pi = 2 + j % 2
        ps = cx.pf[pi]

        def mm_v(e, c0=c0, ps=ps):
            for kc in range(NCH):
                e.matmul(ps[:, 0:128], _chunk(xn_sb, kc, TH, c0, c0 + 128),
                         wv[:, kc * 128:(kc + 1) * 128], start=(kc == 0), stop=False)
            return e.matmul(ps[:, 0:128], kvm[0:1, c0:c0 + 128], brow[0:1, 1152:1280],
                            start=False, stop=True)
        tk.op("t", mm_v, reads=[xn_key, "w:l0v", "b:l0", "b:kvm"], writes=[f"ps:{pi}"])
        tk.op("s", lambda e, j=j, ps=ps: e.copy(vtm[:, j * 128:(j + 1) * 128], ps[:, 0:128]),
              reads=[f"ps:{pi}"], writes=["l0:v"])

    if tiles is None:
        tiles = range(16)
    if dbg is not None:
        tk.dma("y", lambda e: e.dma_start(out=dbg["k"][:, :], in_=kbf[:, :]), reads=["l0:k"], writes=["dbg"])
        tk.dma("y", lambda e: e.dma_start(out=dbg["v"][:, :], in_=vtm[:, :]), reads=["l0:v"], writes=["dbg"])
    for ti in tiles:
        par = ti % 2
        a = ti * 128
        ah = a + HALO
        for qc in range(NCH):
            pi = 4 + qc % 2
            ps = cx.pf[pi]

            def mm_qc(e, qc=qc, ah=ah, ps=ps):
                for kc in range(NCH):
                    e.matmul(ps[:, 0:128],
                             wq[:, kc * 1024 + qc * 128: kc * 1024 + (qc + 1) * 128],
                             _chunk(xn_sb, kc, TH, ah, ah + 128), start=(kc == 0), stop=False)
                return e.matmul(ps[:, 0:128], brow[0:1, qc * 128:(qc + 1) * 128],
                                cx.ones_row[0:1, 0:128], start=False, stop=True)
            tk.op("t", mm_qc, reads=[xn_key, "w:l0q", "b:l0", "c:ones_row"], writes=[f"ps:{pi}"])
            tk.op("s", lambda e, qc=qc, par=par, ps=ps: e.copy(
                qt[:, par * 1024 + qc * 128: par * 1024 + (qc + 1) * 128], ps[:, 0:128]),
                reads=[f"ps:{pi}"], writes=[f"l0:qt{par}"])
        for h in range(16):
            qc, base = h // 2, 64 * (h % 2)
            hp = h % 2
            ps_s = cx.pf[hp]
            ps_t = cx.pb[hp]
            ps_o = cx.pf[2 + hp]
            tk.op("t", lambda e, qc=qc, base=base, par=par, a=a, ps_s=ps_s: e.matmul(
                ps_s[0:128, 0:384],
                qt[base:base + 64, par * 1024 + qc * 128: par * 1024 + (qc + 1) * 128],
                kbf[base:base + 64, a:a + 384], start=True, stop=True),
                reads=[f"l0:qt{par}", "l0:k"], writes=[f"ps:{hp}"])
            if dbg is not None and ti == list(tiles)[0] and h == 0 and "sc" in dbg:
                tk.op("v", lambda e, ps_s=ps_s: e.tensor_copy(pr[:, 384:768], ps_s[:, 0:384]),
                      reads=["ps:0"], writes=["l0:scdump"])
                tk.dma("y", lambda e: e.dma_start(out=dbg["sc"][:, :], in_=pr[:, 384:768]),
                       reads=["l0:scdump"], writes=["dbg"])
            tk.op("v", lambda e, hp=hp, ps_s=ps_s: e.reduce_max(
                rowm[:, hp:hp + 1], ps_s[:, 0:384], axis=AX),
                reads=[f"ps:{hp}"], writes=[f"l0:rm{hp}"])
            tk.op("s", lambda e, hp=hp: e.mul(
                rowm[:, 2 + hp:3 + hp], rowm[:, hp:hp + 1], -1.0),
                reads=[f"l0:rm{hp}"], writes=[f"l0:nrm{hp}"])
            tk.op("s", lambda e, hp=hp, ps_s=ps_s: e.activation(
                pr[:, hp * 384:(hp + 1) * 384], ps_s[:, 0:384], AF.Exp,
                bias=rowm[:, 2 + hp:3 + hp]),
                reads=[f"ps:{hp}", f"l0:nrm{hp}"], writes=[f"l0:pr{hp}"])
            tk.op("v", lambda e, hp=hp: e.scalar_tensor_tensor(
                pr[:, hp * 384:(hp + 1) * 384], pr[:, hp * 384:(hp + 1) * 384],
                1.0, msk[:, 0:384], ALU.mult, ALU.mult,
                accum_out=rowm[:, 4 + hp:5 + hp]),
                reads=[f"l0:pr{hp}", "b:msk"], writes=[f"l0:pr{hp}", f"l0:rs{hp}"])
            tk.op("v", lambda e, hp=hp: e.reciprocal(
                rowm[:, 6 + hp:7 + hp], rowm[:, 4 + hp:5 + hp]),
                reads=[f"l0:rs{hp}"], writes=[f"l0:ri{hp}"])
            tk.op("v", lambda e, hp=hp: e.tensor_scalar(
                prn[:, hp * 384:(hp + 1) * 384], pr[:, hp * 384:(hp + 1) * 384],
                rowm[:, 6 + hp:7 + hp], None, ALU.mult),
                reads=[f"l0:pr{hp}", f"l0:ri{hp}"], writes=[f"l0:prn{hp}"])

            def mm_tr(e, hp=hp, ps_t=ps_t):
                ins = None
                for uc in range(3):
                    ins = e.transpose(ps_t[0:128, uc * 128:(uc + 1) * 128],
                                      prn[:, hp * 384 + uc * 128: hp * 384 + (uc + 1) * 128],
                                      cx.eye16[:, :])
                return ins
            tk.op("t", mm_tr, reads=[f"l0:prn{hp}", "c:eye16"], writes=[f"ps:b{hp}"])
            tk.op("s", lambda e, hp=hp, ps_t=ps_t: e.copy(
                prT[:, hp * 384:(hp + 1) * 384], ps_t[0:128, 0:384]),
                reads=[f"ps:b{hp}"], writes=[f"l0:prT{hp}"])

            def mm_pv(e, hp=hp, base=base, a=a, ps_o=ps_o):
                ins = None
                for uc in range(3):
                    t0 = a // 128 + uc
                    ins = e.matmul(
                        ps_o[0:64, 0:128],
                        vtm[:, t0 * 128 + base: t0 * 128 + base + 64],
                        prT[:, hp * 384 + uc * 128: hp * 384 + (uc + 1) * 128],
                        start=(uc == 0), stop=(uc == 2))
                return ins
            tk.op("t", mm_pv, reads=["l0:v", f"l0:prT{hp}"], writes=[f"ps:{2 + hp}"])
            tk.op("s", lambda e, qc=qc, base=base, par=par, hp=hp, ps_o=ps_o: e.copy(
                at[base:base + 64, par * 1024 + qc * 128: par * 1024 + (qc + 1) * 128],
                ps_o[0:64, 0:128]),
                reads=[f"ps:{2 + hp}"], writes=[f"l0:at{par}"])
        if dbg is not None and ti == list(tiles)[0]:
            tk.dma("y", lambda e, par=par: e.dma_start(out=dbg["qt"][:, :], in_=qt[:, par * 1024:(par + 1) * 1024]), reads=[f"l0:qt{par}"], writes=["dbg"])
            tk.dma("y", lambda e: e.dma_start(out=dbg["pr"][:, :], in_=pr[:, 0:384]), reads=["l0:pr0"], writes=["dbg"])
            tk.dma("y", lambda e: e.dma_start(out=dbg["prn"][:, :], in_=prn[:, 0:384]), reads=["l0:prn0"], writes=["dbg"])
            tk.dma("y", lambda e: e.dma_start(out=dbg["prT"][:, :], in_=prT[:, 0:384]), reads=["l0:prT0"], writes=["dbg"])
            tk.dma("y", lambda e, par=par: e.dma_start(out=dbg["at"][:, :], in_=at[:, par * 1024:(par + 1) * 1024]), reads=[f"l0:at{par}"], writes=["dbg"])
            tk.dma("y", lambda e: e.dma_start(out=dbg["rowm"][:, :], in_=rowm[:, :]), reads=["l0:rs0", "l0:ri0", "l0:rm0"], writes=["dbg"])
        for oc in range(NCH):
            pi = 4 + oc % 2
            ps = cx.pf[pi]

            def mm_wo(e, oc=oc, par=par, ps=ps):
                for kc in range(NCH):
                    e.matmul(ps[:, 0:128],
                             wo[:, kc * 1024 + oc * 128: kc * 1024 + (oc + 1) * 128],
                             at[:, par * 1024 + kc * 128: par * 1024 + (kc + 1) * 128],
                             start=(kc == 0), stop=False)
                return e.matmul(ps[:, 0:128], brow[0:1, 1280 + oc * 128:1280 + (oc + 1) * 128],
                                cx.ones_row[0:1, 0:128], start=False, stop=True)
            tk.op("t", mm_wo, reads=[f"l0:at{par}", "w:l0o", "b:l0", "c:ones_row"],
                  writes=[f"ps:{pi}"])
            tk.op("v", lambda e, oc=oc, a=a, ps=ps, pi=pi: e.tensor_add(
                _chunk(x_sb, oc, TPC, a, a + 128), _chunk(x_sb, oc, TPC, a, a + 128),
                ps[:, 0:128]),
                reads=[f"ps:{pi}", x_key], writes=[x_key])



L0A_SHARED = ["l0_wq_s", "l0_bq", "l0_wk_s", "l0_bk", "l0_wv_s", "l0_bv", "l0_wo_s", "l0_bo",
              "eye", "ones_row", "l0mask"]


def build_l0a():
    nc = bass.Bass("TRN2", target_bir_lowering=False)
    es = contextlib.ExitStack()
    with es:
        x_in = nc.dram_tensor("x_fm", [DM, TH], F32, kind="ExternalInput").ap()
        kvm_in = nc.dram_tensor("kvmask", [1, TH], F32, kind="ExternalInput").ap()
        xn_out = nc.dram_tensor("xn_out", [DM, TH], BF16, kind="ExternalOutput").ap()
        x1p_out = nc.dram_tensor("x1p_out", [DM, TPC], F32, kind="ExternalOutput").ap()

        block = es.enter_context(nc.Block())
        cx = _setup_common(nc, es, block, _SHARED_CACHE, L0A_SHARED)
        cx.dram["kvmask"] = kvm_in
        tk = cx.tk

        x_sb = cx.sb("x_sb", [128, NCH * TPC], F32)
        xh_sb = cx.sb("xh_sb", [128, NCH * 256], F32)
        xn_sb = cx.sb("xn_sb", [128, NCH * TH], BF16)

        for kc in range(NCH):
            tk.dma("y", lambda e, kc=kc: e.dma_start(
                out=_chunk(x_sb, kc, TPC), in_=x_in[kc * 128:(kc + 1) * 128, HALO:HALO + TPC]),
                writes=["x"])
            tk.dma("y", lambda e, kc=kc: e.dma_start(
                out=_chunk(xh_sb, kc, 256, 0, 128), in_=x_in[kc * 128:(kc + 1) * 128, 0:HALO]),
                writes=["xh"])
            tk.dma("y", lambda e, kc=kc: e.dma_start(
                out=_chunk(xh_sb, kc, 256, 128, 256),
                in_=x_in[kc * 128:(kc + 1) * 128, TH - HALO:TH]),
                writes=["xh"])

        emit_ln(cx, x_sb, "x", TPC, 0, TPC, xn_sb, "xn", TH, HALO)
        emit_ln(cx, xh_sb, "xh", 256, 0, 128, xn_sb, "xn", TH, 0)
        emit_ln(cx, xh_sb, "xh", 256, 128, 128, xn_sb, "xn", TH, TH - HALO)

        emit_l0_attn(cx, xn_sb, "xn", x_sb, "x")

        for kc in range(NCH):
            tk.dma("y", lambda e, kc=kc: e.dma_start(
                out=xn_out[kc * 128:(kc + 1) * 128, :], in_=_chunk(xn_sb, kc, TH)),
                reads=["xn"], writes=["o:xn"])
            tk.dma("y", lambda e, kc=kc: e.dma_start(
                out=x1p_out[kc * 128:(kc + 1) * 128, :], in_=_chunk(x_sb, kc, TPC)),
                reads=["x"], writes=["o:x1p"])

        n = tk.dcnt["y"]
        tk.flush()
        block.sync(lambda eng: eng.wait_ge(tk.dsem["y"], 16 * n))
    return nc




MOE_SHARED = ["moe_scale", "moe_wg_s", "moe_bg", "moe_w1_s", "moe_b1", "moe_w2_s", "moe_b2",
              "eye", "ones_row"]


def emit_moe(cx, xn_sb, xn_key, xnw, xc0, x_sb, x_key, experts=range(MOE_E), do_h=True, do_w2=True):
    nc, tk, sb = cx.nc, cx.tk, cx.sb
    E = MOE_E

    def xch(kc, a, b):
        return xn_sb[:, kc * xnw + xc0 + a: kc * xnw + xc0 + b]

    wgb = sb("moe_wg_sb", [128, NCH * 16], BF16)
    bgr = sb("moe_bg_sb", [1, 16], F32)
    sclc = sb("moe_scl_sb", [128, NCH], F32)
    b2sb = sb("moe_b2_sb", [16, 1024], BF16)
    w1b = sb("moe_w1_sb", [128, 2 * NCH * 256], BF16)
    b1r = sb("moe_b1_sb", [1, 2048], BF16)
    w2b = sb("moe_w2_sb", [128, NCH * 1024], BF16)
    ssb = sb("moe_s_sb", [128, 8 * 1024], BF16)
    acc = sb("moe_acc", [128, 8 * 1024], F32)
    gsc = sb("moe_gsc", [128, 64], F32)
    lgs = sb("moe_lg", [128, 16], F32)
    gts = sb("moe_gts", [128, 16 * 16], F32)
    gTb = sb("moe_gT", [16, 16 * 128], BF16)
    msc = sb("moe_msc", [128, 4 * 512], F32)
    c7 = sb("moe_c7", [128, 1], F32)
    tk.op("v", lambda e: e.memset(c7[:, :], 7.0), writes=["moe:c7"])

    tk.dma("g", lambda e: e.dma_start(out=wgb[:, :], in_=cx.dram["moe_wg_s"][:, :]), writes=["w:wg"])
    tk.dma("y", lambda e: e.dma_start(out=bgr[:, :], in_=cx.dram["moe_bg"][:, :]), writes=["b:bg"])
    tk.dma("y", lambda e: e.dma_start(out=sclc[:, :], in_=cx.dram["moe_scale"][:, :]), writes=["b:scl"])
    tk.dma("g", lambda e: e.dma_start(out=b2sb[:, :], in_=cx.dram["moe_b2"][:, :]), writes=["w:b2"])

    for kc in range(NCH):
        for ti in range(4):
            c0 = ti * 512
            tk.op("v", lambda e, kc=kc, c0=c0: e.tensor_add(
                _chunk(x_sb, kc, TPC, c0, c0 + 512), _chunk(x_sb, kc, TPC, c0, c0 + 512),
                xch(kc, c0, c0 + 512)),
                reads=[x_key, xn_key], writes=[x_key])

    scr, rows = cx.ln_scr, cx.ln_rows
    for ti in range(4):
        c0 = ti * 512
        w = 512
        ps_q, ps_b = cx.pf[1], cx.pf[2]
        for kc in range(NCH):
            par = kc % 2
            tk.op("s", lambda e, kc=kc, par=par, c0=c0, w=w: e.activation(
                scr[:, par * 512:par * 512 + w], xch(kc, c0, c0 + w), AF.Square),
                reads=[xn_key], writes=[f"ln:scr{par}"])
            tk.op("t", lambda e, kc=kc, par=par, w=w: e.matmul(
                ps_q[0:1, 0:w], cx.ln_ones_col[:, :], scr[:, par * 512:par * 512 + w],
                start=(kc == 0), stop=(kc == NCH - 1)),
                reads=[f"ln:scr{par}", "ln:ones_col"], writes=["ps:1"])
        tk.op("v", lambda e, w=w: e.tensor_scalar(
            rows[0:1, 0:w], ps_q[0:1, 0:w], 1.0 / DM, None, ALU.mult),
            reads=["ps:1"], writes=["ln:m"])
        tk.op("s", lambda e, w=w: e.activation(
            rows[0:1, 512:512 + w], rows[0:1, 0:w], AF.Sqrt, bias=cx.ln_eps[0:1, 0:1]),
            reads=["ln:m", "ln:eps"], writes=["ln:sd"])
        tk.op("v", lambda e, w=w: e.reciprocal(
            rows[0:1, 512:512 + w], rows[0:1, 512:512 + w]),
            reads=["ln:sd"], writes=["ln:sd"])
        tk.op("t", lambda e, w=w: e.matmul(
            ps_b[:, 0:w], cx.ones_row[0:1, 0:128], rows[0:1, 512:512 + w],
            start=True, stop=True), reads=["ln:sd", "c:ones_row"], writes=["ps:2"])
        for kc in range(NCH):
            par = kc % 2
            tk.op("v", lambda e, kc=kc, par=par, c0=c0, w=w: e.tensor_mul(
                scr[:, par * 512:par * 512 + w], xch(kc, c0, c0 + w), ps_b[:, 0:w]),
                reads=[xn_key, "ps:2"], writes=[f"ln:scr{par}"])
            tk.op("v", lambda e, kc=kc, par=par, c0=c0, w=w: e.tensor_scalar(
                xch(kc, c0, c0 + w), scr[:, par * 512:par * 512 + w],
                sclc[:, kc:kc + 1], None, ALU.mult),
                reads=[f"ln:scr{par}", "b:scl"], writes=[xn_key])

    for tg in range(16):
        t0 = tg * 128
        ps = cx.pf[3]

        def mm_lg(e, t0=t0, ps=ps):
            for kc in range(NCH):
                e.matmul(ps[:, 0:16], xch(kc, t0, t0 + 128), wgb[:, kc * 16:(kc + 1) * 16],
                         start=(kc == 0), stop=False)
            return e.matmul(ps[:, 0:16], cx.ones_row[0:1, 0:128], bgr[0:1, 0:16],
                            start=False, stop=True)
        tk.op("t", mm_lg, reads=[xn_key, "w:wg", "b:bg", "c:ones_row"], writes=["ps:3"])
        tk.op("v", lambda e, ps=ps: e.tensor_copy(lgs[:, :], ps[:, 0:16]),
              reads=["ps:3"], writes=["moe:lg"])
        tk.op("v", lambda e: e.tensor_copy(gsc[:, 0:16], lgs[:, :]),
              reads=["moe:lg"], writes=["moe:cur"])
        tk.op("v", lambda e: e.reduce_max(gsc[:, 32:33], gsc[:, 0:16], axis=AX),
              reads=["moe:cur"], writes=["moe:m1"])
        tk.op("s", lambda e: e.mul(gsc[:, 34:35], gsc[:, 32:33], -1.0),
              reads=["moe:m1"], writes=["moe:nm1"])
        for it in range(3):
            mkc = 32 if it == 0 else 33
            tk.op("v", lambda e, mkc=mkc: e.tensor_scalar(
                gsc[:, 16:32], gsc[:, 0:16], gsc[:, mkc:mkc + 1], None, ALU.is_ge),
                reads=["moe:cur", "moe:m1" if mkc == 32 else "moe:mk"], writes=["moe:mask"])
            tk.op("v", lambda e: e.scalar_tensor_tensor(
                gsc[:, 0:16], gsc[:, 16:32], -1.0e9, gsc[:, 0:16], ALU.mult, ALU.add),
                reads=["moe:mask", "moe:cur"], writes=["moe:cur"])
            tk.op("v", lambda e: e.reduce_max(gsc[:, 33:34], gsc[:, 0:16], axis=AX),
                  reads=["moe:cur"], writes=["moe:mk"])
        tk.op("v", lambda e: e.tensor_scalar(
            gsc[:, 16:32], lgs[:, :], gsc[:, 33:34], None, ALU.is_ge),
            reads=["moe:lg", "moe:mk"], writes=["moe:mask"])
        tk.op("s", lambda e: e.activation(
            gsc[:, 0:16], lgs[:, :], AF.Exp, bias=gsc[:, 34:35]),
            reads=["moe:lg", "moe:nm1"], writes=["moe:cur"])
        tk.op("v", lambda e: e.scalar_tensor_tensor(
            gsc[:, 0:16], gsc[:, 0:16], 1.0, gsc[:, 16:32], ALU.mult, ALU.mult,
            accum_out=gsc[:, 35:36]),
            reads=["moe:cur", "moe:mask"], writes=["moe:cur", "moe:zs"])
        tk.op("v", lambda e: e.reciprocal(gsc[:, 36:37], gsc[:, 35:36]),
              reads=["moe:zs"], writes=["moe:zi"])
        tk.op("v", lambda e, tg=tg: e.tensor_scalar(
            gts[:, tg * 16:(tg + 1) * 16], gsc[:, 0:16], gsc[:, 36:37], None, ALU.mult),
            reads=["moe:cur", "moe:zi"], writes=["moe:gts"])
        tk.op("t", lambda e, tg=tg, ps=ps: e.transpose(
            ps[0:16, 0:128], gts[:, tg * 16:(tg + 1) * 16], cx.eye[:, :]),
            reads=["moe:gts", "c:eye"], writes=["ps:3"])
        tk.op("s", lambda e, tg=tg, ps=ps: e.copy(
            gTb[0:16, tg * 128:(tg + 1) * 128], ps[0:16, 0:128]),
            reads=["ps:3"], writes=["moe:gT"])

    if getattr(cx, "moe_dbg", None) is not None:
        tk.dma("y", lambda e: e.dma_start(out=cx.moe_dbg[0][:, :], in_=gts[:, :]),
               reads=["moe:gts"], writes=["dbg"])
    for half in range(2):
        hoff = half * 1024
        for tsl in range(8):
            tg = half * 8 + tsl
            for h2 in range(2):
                ps = cx.pf[4 + h2 % 2]
                tk.op("t", lambda e, tg=tg, h2=h2, ps=ps: e.matmul(
                    ps[:, 0:512], gTb[0:16, tg * 128:(tg + 1) * 128],
                    b2sb[0:16, h2 * 512:(h2 + 1) * 512], start=True, stop=True),
                    reads=["moe:gT", "w:b2"], writes=[f"ps:{4 + h2 % 2}"])
                tk.op("v", lambda e, tsl=tsl, h2=h2, ps=ps: e.tensor_copy(
                    acc[:, tsl * 1024 + h2 * 512: tsl * 1024 + (h2 + 1) * 512],
                    ps[:, 0:512]),
                    reads=[f"ps:{4 + h2 % 2}"], writes=["moe:acc"])
        for ex in experts:
            wpar = ex % 2
            tk.dma("g", lambda e, ex=ex: e.dma_start(
                out=w2b[:, :], in_=cx.dram["moe_w2_s"][ex]),
                writes=["w:w2"])
            tk.dma("g", lambda e, ex=ex: e.dma_start(
                out=b1r[0:1, :], in_=cx.dram["moe_b1"][ex]),
                writes=["b:b1"])
            for j in (range(8) if do_h else []):
                jpar = (ex * 8 + j) % 2
                tk.dma("g", lambda e, ex=ex, j=j, jpar=jpar: e.dma_start(
                    out=w1b[:, jpar * 2048:(jpar + 1) * 2048],
                    in_=cx.dram["moe_w1_s"][ex, j]),
                    writes=[f"w:w1{jpar}"])
                for ttl in range(2):
                    tc0 = hoff + ttl * 512
                    ppar = ttl % 2
                    ps_a, ps_b2 = cx.pf[ppar], cx.pf[2 + ppar]

                    def mm_h(e, j=j, jpar=jpar, wpar=wpar, tc0=tc0, ps_a=ps_a, ps_b2=ps_b2):
                        for kc in range(NCH):
                            e.matmul(ps_a[:, 0:512],
                                     w1b[:, jpar * 2048 + kc * 256: jpar * 2048 + kc * 256 + 128],
                                     xch(kc, tc0, tc0 + 512), start=(kc == 0), stop=False)
                        e.matmul(ps_a[:, 0:512], b1r[0:1, j * 256: j * 256 + 128],
                                 cx.ones_row16[0:1, 0:512], start=False, stop=True)
                        for kc in range(NCH):
                            e.matmul(ps_b2[:, 0:512],
                                     w1b[:, jpar * 2048 + kc * 256 + 128: jpar * 2048 + (kc + 1) * 256],
                                     xch(kc, tc0, tc0 + 512), start=(kc == 0), stop=False)
                        return e.matmul(ps_b2[:, 0:512],
                                        b1r[0:1, j * 256 + 128: (j + 1) * 256],
                                        cx.ones_row16[0:1, 0:512], start=False, stop=True)
                    tk.op("t", mm_h, reads=[xn_key, f"w:w1{jpar}", "b:b1", "c:ones_row16"],
                          writes=[f"ps:{ppar}", f"ps:{2 + ppar}"])
                    sa = 1024 * ppar
                    tk.op("v", lambda e, sa=sa, ps_a=ps_a: e.tensor_scalar(
                        msc[:, sa:sa + 512], ps_a[:, 0:512], 7.0, None, ALU.min),
                        reads=[f"ps:{ppar}"], writes=[f"moe:s0{ppar}"])
                    tk.op("s", lambda e, sa=sa: e.activation(
                        msc[:, sa + 512:sa + 1024], msc[:, sa:sa + 512], AF.Silu, scale=GLU_ALPHA),
                        reads=[f"moe:s0{ppar}"], writes=[f"moe:s1{ppar}"])
                    tk.op("s", lambda e, sa=sa, ps_b2=ps_b2: e.activation(
                        msc[:, sa:sa + 512], ps_b2[:, 0:512], AF.Relu, bias=c7[:, 0:1],
                        scale=-1.0),
                        reads=[f"ps:{2 + ppar}", "moe:c7", f"moe:s0{ppar}"], writes=[f"moe:s0{ppar}"])
                    tk.op("v", lambda e, sa=sa: e.tensor_scalar(
                        msc[:, sa:sa + 512], msc[:, sa:sa + 512],
                        14.0, -1.0 / GLU_ALPHA, ALU.min, ALU.mult),
                        reads=[f"moe:s0{ppar}"], writes=[f"moe:s0{ppar}"])
                    tk.op("v", lambda e, sa=sa, j=j, tc0=tc0, ttl=ttl: e.scalar_tensor_tensor(
                        ssb[:, j * 1024 + ttl * 512: j * 1024 + (ttl + 1) * 512],
                        msc[:, sa:sa + 512], 8.0 / GLU_ALPHA,
                        msc[:, sa + 512:sa + 1024], ALU.add, ALU.mult),
                        reads=[f"moe:s0{ppar}", f"moe:s1{ppar}"], writes=["moe:s"])
            for ttl in (range(2) if do_w2 else []):
                for tsub in range(4):
                    tsl = ttl * 4 + tsub
                    tg = half * 8 + tsl
                    for h2 in range(2):
                        ppar = (tsl * 2 + h2) % 2
                        ps = cx.pf[4 + ppar]

                        def mm_y(e, ttl=ttl, tsub=tsub, h2=h2, wpar=wpar, ps=ps):
                            ins = None
                            for j in range(8):
                                ins = e.matmul(
                                    ps[:, 0:512],
                                    ssb[:, j * 1024 + ttl * 512 + tsub * 128: j * 1024 + ttl * 512 + (tsub + 1) * 128],
                                    w2b[:, j * 1024 + h2 * 512: j * 1024 + (h2 + 1) * 512],
                                    start=(j == 0), stop=(j == 7))
                            return ins
                        tk.op("t", mm_y, reads=["moe:s", "w:w2"], writes=[f"ps:{4 + ppar}"])
                        tk.op("v", lambda e, tsl=tsl, tg=tg, h2=h2, ex=ex, ps=ps: e.scalar_tensor_tensor(
                            acc[:, tsl * 1024 + h2 * 512: tsl * 1024 + (h2 + 1) * 512],
                            ps[:, 0:512], gts[:, tg * 16 + ex: tg * 16 + ex + 1],
                            acc[:, tsl * 1024 + h2 * 512: tsl * 1024 + (h2 + 1) * 512],
                            ALU.mult, ALU.add),
                            reads=[f"ps:{4 + ppar}", "moe:gts", "moe:acc"], writes=["moe:acc"])
        for tsl in range(8):
            for oc in range(NCH):
                ppar = (tsl * 8 + oc) % 2
                ps = cx.pf[ppar]
                tk.op("t", lambda e, tsl=tsl, oc=oc, ps=ps: e.transpose(
                    ps[0:128, 0:128], acc[:, tsl * 1024 + oc * 128: tsl * 1024 + (oc + 1) * 128],
                    cx.eye[:, :]),
                    reads=["moe:acc", "c:eye"], writes=[f"ps:{ppar}"])
                tk.op("v", lambda e, tsl=tsl, oc=oc, hoff=hoff, ps=ps, ppar=ppar: e.tensor_add(
                    _chunk(x_sb, oc, TPC, hoff + tsl * 128, hoff + (tsl + 1) * 128),
                    _chunk(x_sb, oc, TPC, hoff + tsl * 128, hoff + (tsl + 1) * 128),
                    ps[0:128, 0:128]),
                    reads=[f"ps:{ppar}", x_key], writes=[x_key])


def build_l0b():
    nc = bass.Bass("TRN2", target_bir_lowering=False)
    es = contextlib.ExitStack()
    with es:
        xn_in = nc.dram_tensor("xn_in", [DM, TPC], BF16, kind="ExternalInput").ap()
        x1p_in = nc.dram_tensor("x1p_in", [DM, TPC], F32, kind="ExternalInput").ap()
        x1_out = nc.dram_tensor("x1_out", [DM, TPC], F32, kind="ExternalOutput").ap()

        block = es.enter_context(nc.Block())
        cx = _setup_common(nc, es, block, _SHARED_CACHE, MOE_SHARED)
        tk = cx.tk
        cx.ln_scr = cx.sb("ln_scr", [128, 2 * 512], F32)
        cx.ln_rows = cx.sb("ln_rows", [1, 2 * 512], F32)
        cx.ln_ones_col = cx.sb("ln_ones_col", [128, 1], F32)
        cx.ln_eps = cx.sb("ln_eps", [1, 1], F32)
        tk.op("v", lambda e: e.memset(cx.ln_ones_col[:, :], 1.0), writes=["ln:ones_col"])
        tk.op("v", lambda e: e.memset(cx.ln_eps[:, :], 1e-5), writes=["ln:eps"])
        cx.ln_init = True

        x_sb = cx.sb("x_sb", [128, NCH * TPC], F32)
        xn_sb = cx.sb("xn_sb", [128, NCH * TPC], BF16)
        for kc in range(NCH):
            tk.dma("y", lambda e, kc=kc: e.dma_start(
                out=_chunk(x_sb, kc, TPC), in_=x1p_in[kc * 128:(kc + 1) * 128, :]), writes=["x"])
            tk.dma("y", lambda e, kc=kc: e.dma_start(
                out=_chunk(xn_sb, kc, TPC), in_=xn_in[kc * 128:(kc + 1) * 128, :]), writes=["xn"])

        emit_moe(cx, xn_sb, "xn", TPC, 0, x_sb, "x")

        for kc in range(NCH):
            tk.dma("y", lambda e, kc=kc: e.dma_start(
                out=x1_out[kc * 128:(kc + 1) * 128, :], in_=_chunk(x_sb, kc, TPC)),
                reads=["x"], writes=["o:x1"])
        n = tk.dcnt["y"]
        tk.flush()
        block.sync(lambda eng: eng.wait_ge(tk.dsem["y"], 16 * n))
    return nc




L1A_SHARED = ["l1_wq_s", "l1_bq", "l1_wk_s", "l1_bk", "l1_wv_s", "l1_bv", "l1_wo_s", "l1_bo",
              "l1_wfc_s", "l1_bfc", "l1_projT", "eye", "ones_row"]

CTXW = 3 * 65
STG_W = 16 * CTXW


def emit_l1_attn(cx, x_in_ap, xn_sb, xn_key, xnw, xc0, y0_sb, y0_key, core_dram, x_dep=(), cc=True):
    nc, tk, sb = cx.nc, cx.tk, cx.sb
    DN, RATIO, EPS = L1_DN, L1_RATIO, L1_EPS

    def xch(kc, a, b):
        return xn_sb[:, kc * xnw + xc0 + a: kc * xnw + xc0 + b]

    RG = [[0, 1, 2, 3], [4, 5, 6, 7]]
    ccm_in = nc.dram_tensor("ccm_in", [1, 16], F32)
    ccm_out = nc.dram_tensor("ccm_out", [1, 16], F32)
    ccs_in = nc.dram_tensor("ccs_in", [128, STG_W], F32)
    ccs_out = nc.dram_tensor("ccs_out", [128, STG_W], F32)

    wA = sb("l1_wA", [128, NCH * 1024], BF16)
    wB = sb("l1_wB", [128, NCH * 1024], BF16)
    br = sb("l1_br", [1, 2048], BF16)
    projT = sb("l1_projT_sb", [128, 266], BF16)
    vtm = sb("l1_v", [128, 16 * 1024], BF16)
    kbuf = sb("l1_kbuf", [128, NCH * TPC], BF16)
    xt = sb("l1_xt", [128, NCH * 512], F32)
    cx.l1_xt = xt
    diagk = sb("l1_diagk", [128, 256], F32)
    colmax = sb("l1_colmax", [128, 16], F32)
    stg = sb("l1_stg", [128, STG_W], F32)
    cx.l1_stg_dbg = stg
    stg2 = sb("l1_stg2", [16, 256], F32)
    ctxb = sb("l1_ctx", [128, 16 * 192], BF16)
    ksumb = sb("l1_ksum", [128, 48], BF16)
    mxb = sb("l1_mxb", [128, 16], F32)
    rsc = sb("l1_rsc", [128, 16], F32)
    rrow = sb("l1_rrow", [1, 512], F32)
    exs = sb("l1_exs", [128, 2 * 266], F32)
    kpb = sb("l1_kpb", [128, 2 * 266], BF16)
    qpT = sb("l1_qpT", [128, 2 * 384], BF16)
    qps = sb("l1_qps", [128, 2 * 384], BF16)
    sq64 = sb("l1_sq64", [128, 2 * 128], F32)
    qt = sb("l1_qt", [128, 2 * 1024], BF16)
    ath = sb("l1_ath", [128, 2 * 1024], BF16)
    msel = sb("l1_msel", [1, 16], F32)

    tk.dma("g", lambda e: e.dma_start(out=projT[:, :], in_=cx.dram["l1_projT"][:, :]), writes=["l1:projT"])

    for tt in range(4):
        c0 = tt * 512
        for kc in range(NCH):
            tk.dma("y", lambda e, kc=kc, c0=c0: e.dma_start(
                out=_chunk(xt, kc, 512), in_=x_in_ap[kc * 128:(kc + 1) * 128, c0:c0 + 512]),
                reads=list(x_dep), writes=["l1:xt"])
        emit_ln(cx, xt, "l1:xt", 512, 0, 512, xn_sb, xn_key, xnw, xc0 + c0)

    tk.dma("g", lambda e: e.dma_start(
        out=wA[:, :], in_=cx.dram["l1_wv_s"][:, :]), writes=["w:A"])
    tk.dma("g", lambda e: e.dma_start(out=br[0:1, 0:1024], in_=cx.dram["l1_bv"][:, :]), writes=["b:r0"])
    for j in range(16):
        c0 = j * 128
        for vh in range(2):
            pi = vh % 2
            ps = cx.pf[pi]

            def mm_v(e, c0=c0, vh=vh, ps=ps):
                for kc in range(NCH):
                    e.matmul(ps[:, 0:512], xch(kc, c0, c0 + 128),
                             wA[:, kc * 1024 + vh * 512: kc * 1024 + (vh + 1) * 512],
                             start=(kc == 0), stop=False)
                return e.matmul(ps[:, 0:512], cx.ones_row16[0:1, 0:128],
                                br[0:1, vh * 512:(vh + 1) * 512], start=False, stop=True)
            tk.op("t", mm_v, reads=[xn_key, "w:A", "b:r0", "c:ones_row16"], writes=[f"ps:{pi}"])
            tk.op("s", lambda e, j=j, vh=vh, ps=ps: e.copy(
                vtm[:, j * 1024 + vh * 512: j * 1024 + (vh + 1) * 512], ps[:, 0:512]),
                reads=[f"ps:{pi}"], writes=["l1:v"])

    tk.dma("g", lambda e: e.dma_start(
        out=wA[:, :], in_=cx.dram["l1_wk_s"][:, :]), writes=["w:A"])
    tk.dma("g", lambda e: e.dma_start(out=br[0:1, 0:1024], in_=cx.dram["l1_bk"][:, :]), writes=["b:r0"])
    for ti4 in range(4):
        c0 = ti4 * 512
        for oc in range(NCH):
            pi = 2 + oc % 2
            ps = cx.pf[pi]

            def mm_k(e, c0=c0, oc=oc, ps=ps):
                for kc in range(NCH):
                    e.matmul(ps[:, 0:512], wA[:, kc * 1024 + oc * 128: kc * 1024 + (oc + 1) * 128],
                             xch(kc, c0, c0 + 512), start=(kc == 0), stop=False)
                return e.matmul(ps[:, 0:512], br[0:1, oc * 128:(oc + 1) * 128],
                                cx.ones_row16[0:1, 0:512], start=False, stop=True)
            tk.op("t", mm_k, reads=[xn_key, "w:A", "b:r0", "c:ones_row16"], writes=[f"ps:{pi}"])
            tk.op("s", lambda e, c0=c0, oc=oc, ps=ps: e.copy(
                kbuf[:, oc * TPC + c0: oc * TPC + c0 + 512], ps[:, 0:512]),
                reads=[f"ps:{pi}"], writes=["kbuf"])

    tk.op("v", lambda e: e.memset(colmax[:, :], -3.0e38), writes=["l1:colmax"])
    tk.op("v", lambda e: e.memset(stg[:, :], 0.0), writes=["l1:stg"])
    for ti in range(16):
        for h in range(16):
            base = 64 * (h % 2)
            kcol = (h // 2) * TPC + ti * 128
            pi = (ti * 16 + h) % 2
            ps_td = cx.pf[pi]
            ps_d = cx.pf[2 + pi]
            tk.op("t", lambda e, base=base, kcol=kcol, ps_td=ps_td: e.matmul(
                ps_td[0:128, 0:266], kbuf[base:base + 64, kcol:kcol + 128],
                projT[base:base + 64, 0:266], start=True, stop=True),
                reads=["kbuf", "l1:projT"], writes=[f"ps:{pi}"])
            tk.op("v", lambda e, pi=pi, ps_td=ps_td: e.reduce_max(
                rsc[:, pi:pi + 1], ps_td[:, 0:266], axis=AX),
                reads=[f"ps:{pi}"], writes=[f"l1:rm{pi}"])
            tk.op("v", lambda e, h=h, pi=pi: e.tensor_max(
                colmax[:, h:h + 1], colmax[:, h:h + 1], rsc[:, pi:pi + 1]),
                reads=[f"l1:rm{pi}", "l1:colmax"], writes=["l1:colmax"])
            tk.op("s", lambda e, base=base, kcol=kcol, pi=pi: e.activation(
                sq64[0:64, pi * 128:pi * 128 + 128], kbuf[base:base + 64, kcol:kcol + 128],
                AF.Square),
                reads=["kbuf"], writes=[f"l1:sq{pi}"])
            tk.op("t", lambda e, pi=pi, ps_d=ps_d: e.matmul(
                ps_d[0:128, 0:1], sq64[0:64, pi * 128:pi * 128 + 128], cx.ln_ones_col[0:64, 0:1],
                start=True, stop=True),
                reads=[f"l1:sq{pi}", "ln:ones_col"], writes=[f"ps:{2 + pi}"])
            tk.op("v", lambda e, ti=ti, h=h, pi=pi, ps_d=ps_d: e.tensor_scalar(
                diagk[:, ti * 16 + h: ti * 16 + h + 1], ps_d[:, 0:1], 0.5 * DN * DN, None,
                ALU.mult),
                reads=[f"ps:{2 + pi}"], writes=["l1:diagk"])

    tk.op("t", lambda e: e.transpose(cx.pf[4][0:16, 0:128], colmax[:, :], cx.eye[:, :]),
          reads=["l1:colmax", "c:eye"], writes=["ps:4"])
    tk.op("v", lambda e: e.tensor_copy(stg2[0:16, 0:128], cx.pf[4][0:16, 0:128]),
          reads=["ps:4"], writes=["l1:stg2"])
    tk.op("v", lambda e: e.reduce_max(stg2[0:16, 128:129], stg2[0:16, 0:128], axis=AX),
          reads=["l1:stg2"], writes=["l1:stg2"])
    tk.op("t", lambda e: e.transpose(cx.pf[4][0:1, 0:16], stg2[0:16, 128:129], cx.eye[0:16, 0:16]),
          reads=["l1:stg2", "c:eye"], writes=["ps:4"])
    tk.op("v", lambda e: e.tensor_copy(msel[0:1, 0:16], cx.pf[4][0:1, 0:16]),
          reads=["ps:4"], writes=["l1:msel"])
    if cc:
        tk.dma("g", lambda e: e.dma_start(out=ccm_in.ap()[:, :], in_=msel[0:1, 0:16]),
               reads=["l1:msel"], writes=["d:ccm_in"])
        tk.barrier()
        tk.op("g", lambda e: e.collective_compute(
            "AllReduce", ALU.max, replica_groups=RG,
            ins=[ccm_in.ap().opt()], outs=[ccm_out.ap().opt()]),
            reads=["d:ccm_in"], writes=["d:ccm_out"])
        tk.dma("g", lambda e: e.dma_start(out=msel[0:1, 0:16], in_=ccm_out.ap()[0:1, :]),
               reads=["d:ccm_out"], writes=["l1:msel"])
    for h in range(16):
        tk.op("t", lambda e, h=h: e.matmul(
            cx.pf[4][0:128, 0:1], cx.ones_row[0:1, 0:128], msel[0:1, h:h + 1],
            start=True, stop=True),
            reads=["l1:msel", "c:ones_row"], writes=["ps:4"])
        tk.op("v", lambda e, h=h: e.tensor_scalar(
            mxb[:, h:h + 1], cx.pf[4][0:128, 0:1], DN, None, ALU.mult),
            reads=["ps:4"], writes=["l1:mxb"])

    for h in range(16):
        base = 64 * (h % 2)
        ps_ks = cx.pf[4]
        for ti in range(16):
            kcol = (h // 2) * TPC + ti * 128
            pi = ti % 2
            ps_td = cx.pf[pi]
            tk.op("t", lambda e, base=base, kcol=kcol, ps_td=ps_td: e.matmul(
                ps_td[0:128, 0:266], kbuf[base:base + 64, kcol:kcol + 128],
                projT[base:base + 64, 0:266], start=True, stop=True),
                reads=["kbuf", "l1:projT"], writes=[f"ps:{pi}"])
            tk.op("v", lambda e, ti=ti, h=h, pi=pi: e.scalar_tensor_tensor(
                rsc[:, 2 + pi:3 + pi], diagk[:, ti * 16 + h: ti * 16 + h + 1], -1.0,
                mxb[:, h:h + 1], ALU.mult, ALU.subtract),
                reads=["l1:diagk", "l1:mxb"], writes=[f"l1:bc{pi}"])
            tk.op("s", lambda e, pi=pi, ps_td=ps_td: e.activation(
                exs[:, pi * 266:(pi + 1) * 266], ps_td[:, 0:266], AF.Exp,
                bias=rsc[:, 2 + pi:3 + pi], scale=DN),
                reads=[f"ps:{pi}", f"l1:bc{pi}"], writes=[f"l1:ex{pi}"])
            tk.op("v", lambda e, pi=pi: e.tensor_scalar(
                kpb[:, pi * 266:(pi + 1) * 266], exs[:, pi * 266:(pi + 1) * 266],
                RATIO, RATIO * EPS, ALU.mult, ALU.add),
                reads=[f"l1:ex{pi}"], writes=[f"l1:kp{pi}"])

            CTX_BANK = [2, 3, 5]

            def mm_acc2(e, ti=ti, h=h, pi=pi, ps_ks=ps_ks):
                st = (ti == 0)
                sp = (ti == 15)
                e.matmul(ps_ks[0:1, 0:266], cx.ones_col16[:, 0:1],
                         kpb[:, pi * 266:(pi + 1) * 266], start=st, stop=sp)
                ins = None
                for mc in range(3):
                    msz = 128 if mc < 2 else 10
                    ins = e.matmul(cx.pf[CTX_BANK[mc]][0:msz, 0:64],
                                   kpb[:, pi * 266 + mc * 128: pi * 266 + mc * 128 + msz],
                                   vtm[:, ti * 1024 + h * 64: ti * 1024 + (h + 1) * 64],
                                   start=st, stop=sp)
                return ins
            tk.op("t", mm_acc2,
                  reads=[f"l1:kp{pi}", "l1:v", "c:ones_col16", "ps:4", "ps:2", "ps:3", "ps:5"],
                  writes=["ps:4", "ps:2", "ps:3", "ps:5"])
        tk.op("v", lambda e, h=h: e.tensor_copy(rrow[0:1, 32:298], ps_ks[0:1, 0:266]),
              reads=["ps:4"], writes=["l1:ksrow"])
        for mc in range(3):
            msz = 128 if mc < 2 else 10
            col = h * CTXW + mc * 65
            bk = [2, 3, 5][mc]
            tk.op("v", lambda e, mc=mc, msz=msz, col=col, bk=bk: e.tensor_copy(
                stg[0:msz, col:col + 64], cx.pf[bk][0:msz, 0:64]),
                reads=[f"ps:{bk}"], writes=["l1:stg"])
            tk.op("t", lambda e, mc=mc, msz=msz: e.matmul(
                cx.pf[0][0:msz, 0:1], rrow[0:1, 32 + mc * 128: 32 + mc * 128 + msz],
                cx.ones_row[0:1, 0:1], start=True, stop=True),
                reads=["l1:ksrow", "c:ones_row"], writes=["ps:0"])
            tk.op("v", lambda e, mc=mc, msz=msz, col=col: e.tensor_copy(
                stg[0:msz, col + 64:col + 65], cx.pf[0][0:msz, 0:1]),
                reads=["ps:0"], writes=["l1:stg"])

    if cc:
        tk.dma("g", lambda e: e.dma_start(out=ccs_in.ap()[:, :], in_=stg[:, :]),
               reads=["l1:stg"], writes=["d:ccs_in"])
        tk.barrier()
        tk.op("g", lambda e: e.collective_compute(
            "AllReduce", ALU.add, replica_groups=RG,
            ins=[ccs_in.ap().opt()], outs=[ccs_out.ap().opt()]),
            reads=["d:ccs_in"], writes=["d:ccs_out"])
        tk.dma("g", lambda e: e.dma_start(out=stg[:, :], in_=ccs_out.ap()[:, :]),
               reads=["d:ccs_out"], writes=["l1:stg"])
    for h in range(16):
        for mc in range(3):
            msz = 128 if mc < 2 else 10
            col = h * CTXW + mc * 65
            tk.op("v", lambda e, h=h, mc=mc, msz=msz, col=col: e.tensor_copy(
                ctxb[0:msz, h * 192 + mc * 64: h * 192 + (mc + 1) * 64],
                stg[0:msz, col:col + 64]),
                reads=["l1:stg"], writes=["l1:ctxb"])
            tk.op("v", lambda e, h=h, mc=mc, msz=msz, col=col: e.tensor_copy(
                ksumb[0:msz, h * 3 + mc: h * 3 + mc + 1], stg[0:msz, col + 64:col + 65]),
                reads=["l1:stg"], writes=["l1:ksumb"])

    tk.dma("g", lambda e: e.dma_start(
        out=wA[:, :], in_=cx.dram["l1_wq_s"][:, :]), writes=["w:A"])
    tk.dma("g", lambda e: e.dma_start(
        out=wB[:, :], in_=cx.dram["l1_wo_s"][:, :]), writes=["w:B"])
    tk.dma("g", lambda e: e.dma_start(out=br[0:1, 0:1024], in_=cx.dram["l1_bq"][:, :]), writes=["b:r0"])
    tk.dma("g", lambda e: e.dma_start(out=br[0:1, 1024:2048], in_=cx.dram["l1_bo"][:, :]), writes=["b:r1"])
    for ti in range(16):
        par = ti % 2
        tc0 = ti * 128
        for qc in range(NCH):
            pi = qc % 2
            ps = cx.pf[pi]

            def mm_q(e, qc=qc, tc0=tc0, ps=ps):
                for kc in range(NCH):
                    e.matmul(ps[:, 0:128], wA[:, kc * 1024 + qc * 128: kc * 1024 + (qc + 1) * 128],
                             xch(kc, tc0, tc0 + 128), start=(kc == 0), stop=False)
                return e.matmul(ps[:, 0:128], br[0:1, qc * 128:(qc + 1) * 128],
                                cx.ones_row16[0:1, 0:128], start=False, stop=True)
            tk.op("t", mm_q, reads=[xn_key, "w:A", "b:r0", "c:ones_row16"], writes=[f"ps:{pi}"])
            tk.op("s", lambda e, qc=qc, par=par, ps=ps, pi=pi: e.copy(
                qt[:, par * 1024 + qc * 128: par * 1024 + (qc + 1) * 128], ps[:, 0:128]),
                reads=[f"ps:{pi}"], writes=[f"l1:qt{par}"])
        for h in range(16):
            base = 64 * (h % 2)
            qcol = par * 1024 + (h // 2) * 128
            hp = h % 2
            ps_td = cx.pf[hp]
            ps_d = cx.pf[2 + hp]
            ps_dv = cx.pf[4]
            ps_o = cx.pf[5]
            ps_tr = cx.pb[hp]
            tk.op("t", lambda e, base=base, qcol=qcol, ps_td=ps_td: e.matmul(
                ps_td[0:128, 0:266], qt[base:base + 64, qcol:qcol + 128],
                projT[base:base + 64, 0:266], start=True, stop=True),
                reads=[f"l1:qt{par}", "l1:projT"], writes=[f"ps:{hp}"])
            tk.op("s", lambda e, base=base, qcol=qcol, hp=hp: e.activation(
                sq64[0:64, hp * 128:hp * 128 + 128], qt[base:base + 64, qcol:qcol + 128],
                AF.Square),
                reads=[f"l1:qt{par}"], writes=[f"l1:sq{hp}"])
            tk.op("t", lambda e, hp=hp, ps_d=ps_d: e.matmul(
                ps_d[0:128, 0:1], sq64[0:64, hp * 128:hp * 128 + 128], cx.ln_ones_col[0:64, 0:1],
                start=True, stop=True),
                reads=[f"l1:sq{hp}", "ln:ones_col"], writes=[f"ps:{2 + hp}"])
            tk.op("v", lambda e, hp=hp, ps_d=ps_d: e.tensor_scalar(
                rsc[:, 4 + hp:5 + hp], ps_d[:, 0:1], 0.5 * DN * DN, None, ALU.mult),
                reads=[f"ps:{2 + hp}"], writes=[f"l1:dq{hp}"])
            tk.op("v", lambda e, hp=hp, ps_td=ps_td: e.reduce_max(
                rsc[:, 6 + hp:7 + hp], ps_td[:, 0:266], axis=AX),
                reads=[f"ps:{hp}"], writes=[f"l1:qm{hp}"])
            tk.op("v", lambda e, hp=hp: e.scalar_tensor_tensor(
                rsc[:, 8 + hp:9 + hp], rsc[:, 6 + hp:7 + hp], -DN, rsc[:, 4 + hp:5 + hp],
                ALU.mult, ALU.subtract),
                reads=[f"l1:qm{hp}", f"l1:dq{hp}"], writes=[f"l1:qb{hp}"])
            tk.op("s", lambda e, hp=hp, ps_td=ps_td: e.activation(
                exs[:, hp * 266:(hp + 1) * 266], ps_td[:, 0:266], AF.Exp,
                bias=rsc[:, 8 + hp:9 + hp], scale=DN),
                reads=[f"ps:{hp}", f"l1:qb{hp}"], writes=[f"l1:ex{hp}"])
            tk.op("v", lambda e, hp=hp: e.tensor_scalar(
                kpb[:, hp * 266:(hp + 1) * 266], exs[:, hp * 266:(hp + 1) * 266],
                RATIO, RATIO * EPS, ALU.mult, ALU.add),
                reads=[f"l1:ex{hp}"], writes=[f"l1:kp{hp}"])

            def mm_tr(e, hp=hp, ps_tr=ps_tr):
                ins = None
                for mc in range(3):
                    msz = 128 if mc < 2 else 10
                    ins = e.transpose(ps_tr[0:msz, mc * 128:mc * 128 + 128],
                                      kpb[:, hp * 266 + mc * 128: hp * 266 + mc * 128 + msz],
                                      cx.eye16[:, :])
                return ins
            tk.op("t", mm_tr, reads=[f"l1:kp{hp}", "c:eye16"], writes=[f"ps:b{hp}"])
            tk.op("s", lambda e, hp=hp, ps_tr=ps_tr: e.copy(
                qpT[:, hp * 384:(hp + 1) * 384], ps_tr[0:128, 0:384]),
                reads=[f"ps:b{hp}"], writes=[f"l1:qpT{hp}"])

            def mm_den(e, h=h, hp=hp, ps_dv=ps_dv):
                ins = None
                for mc in range(3):
                    msz = 128 if mc < 2 else 10
                    ins = e.matmul(ps_dv[0:1, 0:128], ksumb[0:msz, h * 3 + mc: h * 3 + mc + 1],
                                   qpT[0:msz, hp * 384 + mc * 128: hp * 384 + (mc + 1) * 128],
                                   start=(mc == 0), stop=(mc == 2))
                return ins
            tk.op("t", mm_den, reads=["l1:ksumb", f"l1:qpT{hp}"], writes=["ps:4"])
            tk.op("v", lambda e, ps_dv=ps_dv: e.tensor_copy(
                rrow[0:1, 304:432], ps_dv[0:1, 0:128]),
                reads=["ps:4"], writes=["l1:den"])
            tk.op("v", lambda e: e.reciprocal(rrow[0:1, 304:432], rrow[0:1, 304:432]),
                  reads=["l1:den"], writes=["l1:den"])
            tk.op("t", lambda e, ps_dv=ps_dv: e.matmul(
                ps_dv[0:128, 0:128], cx.ones_row[0:1, 0:128], rrow[0:1, 304:432],
                start=True, stop=True),
                reads=["l1:den", "c:ones_row"], writes=["ps:4"])
            for mc in range(3):
                msz = 128 if mc < 2 else 10
                tk.op("v", lambda e, hp=hp, mc=mc, msz=msz, ps_dv=ps_dv: e.tensor_mul(
                    qps[0:msz, hp * 384 + mc * 128: hp * 384 + (mc + 1) * 128],
                    qpT[0:msz, hp * 384 + mc * 128: hp * 384 + (mc + 1) * 128],
                    ps_dv[0:msz, 0:128]),
                    reads=[f"l1:qpT{hp}", "ps:4"], writes=[f"l1:qps{hp}"])

            def mm_pv(e, h=h, hp=hp, ps_o=ps_o):
                ins = None
                for mc in range(3):
                    msz = 128 if mc < 2 else 10
                    ins = e.matmul(ps_o[0:64, 0:128],
                                   ctxb[0:msz, h * 192 + mc * 64: h * 192 + (mc + 1) * 64],
                                   qps[0:msz, hp * 384 + mc * 128: hp * 384 + (mc + 1) * 128],
                                   start=(mc == 0), stop=(mc == 2))
                return ins
            tk.op("t", mm_pv, reads=["l1:ctxb", f"l1:qps{hp}"], writes=["ps:5"])
            tk.op("s", lambda e, h=h, base=base, par=par, ps_o=ps_o: e.copy(
                ath[base:base + 64, par * 1024 + (h // 2) * 128: par * 1024 + (h // 2) * 128 + 128],
                ps_o[0:64, 0:128]),
                reads=["ps:5"], writes=[f"l1:ath{par}"])
        for oc in range(NCH):
            pi = oc % 2
            ps = cx.pf[pi]

            def mm_wo(e, oc=oc, par=par, ps=ps):
                for kc in range(NCH):
                    e.matmul(ps[:, 0:128], wB[:, kc * 1024 + oc * 128: kc * 1024 + (oc + 1) * 128],
                             ath[:, par * 1024 + kc * 128: par * 1024 + (kc + 1) * 128],
                             start=(kc == 0), stop=False)
                return e.matmul(ps[:, 0:128], br[0:1, 1024 + oc * 128:1024 + (oc + 1) * 128],
                                cx.ones_row16[0:1, 0:128], start=False, stop=True)
            tk.op("t", mm_wo, reads=[f"l1:ath{par}", "w:B", "b:r1", "c:ones_row16"],
                  writes=[f"ps:{pi}"])
            tk.op("s", lambda e, oc=oc, tc0=tc0, ps=ps, pi=pi: e.copy(
                kbuf[:, oc * TPC + tc0: oc * TPC + tc0 + 128], ps[:, 0:128]),
                reads=[f"ps:{pi}"], writes=["kbuf"])

    tk.dma("g", lambda e: e.dma_start(
        out=wA[:, :], in_=cx.dram["l1_wfc_s"][:, :]), writes=["w:A"])
    tk.dma("g", lambda e: e.dma_start(out=br[0:1, 0:1024], in_=cx.dram["l1_bfc"][:, :]), writes=["b:r0"])
    for tt in range(4):
        c0 = tt * 512
        for kc in range(NCH):
            tk.dma("y", lambda e, kc=kc, c0=c0: e.dma_start(
                out=_chunk(xt, kc, 512), in_=x_in_ap[kc * 128:(kc + 1) * 128, c0:c0 + 512]),
                reads=list(x_dep), writes=["l1:xt"])
        for oc in range(NCH):
            pi = oc % 2
            ps = cx.pf[pi]

            def mm_fc(e, oc=oc, c0=c0, ps=ps):
                for kc in range(NCH):
                    e.matmul(ps[:, 0:512], wA[:, kc * 1024 + oc * 128: kc * 1024 + (oc + 1) * 128],
                             kbuf[:, kc * TPC + c0: kc * TPC + c0 + 512],
                             start=(kc == 0), stop=False)
                return e.matmul(ps[:, 0:512], br[0:1, oc * 128:(oc + 1) * 128],
                                cx.ones_row16[0:1, 0:512], start=False, stop=True)
            tk.op("t", mm_fc, reads=["kbuf", "w:A", "b:r0", "c:ones_row16"], writes=[f"ps:{pi}"])
            tk.op("v", lambda e, oc=oc, ps=ps, pi=pi: e.scalar_tensor_tensor(
                _chunk(xt, oc, 512), ps[:, 0:512], 0.0, _chunk(xt, oc, 512),
                ALU.max, ALU.add),
                reads=[f"ps:{pi}", "l1:xt"], writes=["l1:xt"])
        cx.l1_y0_store(tt)


def build_l1a():
    nc = bass.Bass("TRN2", target_bir_lowering=False)
    es = contextlib.ExitStack()
    with es:
        x1_in = nc.dram_tensor("x1_in", [DM, TPC], F32, kind="ExternalInput").ap()
        hsel_in = nc.dram_tensor("hsel", [1, 2], F32, kind="ExternalInput").ap()
        hselc_in = nc.dram_tensor("hselc", [2, 1], F32, kind="ExternalInput").ap()
        ccmb_in = nc.dram_tensor("ccmaxbias", [2, 16], F32, kind="ExternalInput").ap()
        xn_out = nc.dram_tensor("xn1_out", [DM, TPC], BF16, kind="ExternalOutput").ap()
        y0_out = nc.dram_tensor("y0_out", [DM, TPC], F32, kind="ExternalOutput").ap()

        block = es.enter_context(nc.Block())
        cx = _setup_common(nc, es, block, _SHARED_CACHE, L1A_SHARED)
        tk = cx.tk
        cx.ln_scr = cx.sb("ln_scr", [128, 2 * 512], F32)
        cx.ln_rows = cx.sb("ln_rows", [1, 4 * 512], F32)
        cx.ln_ones_col = cx.sb("ln_ones_col", [128, 1], F32)
        cx.ln_eps = cx.sb("ln_eps", [1, 1], F32)
        tk.op("v", lambda e: e.memset(cx.ln_ones_col[:, :], 1.0), writes=["ln:ones_col"])
        tk.op("v", lambda e: e.memset(cx.ln_eps[:, :], 1e-5), writes=["ln:eps"])
        cx.ln_init = True

        xn_sb = cx.sb("xn_sb", [128, NCH * TPC], BF16)

        def y0_store(tt):
            c0 = tt * 512
            for kc in range(NCH):
                tk.dma("y", lambda e, kc=kc, c0=c0: e.dma_start(
                    out=y0_out[kc * 128:(kc + 1) * 128, c0:c0 + 512],
                    in_=_chunk(cx.l1_xt, kc, 512)),
                    reads=["l1:xt"], writes=["o:y0"])
        cx.l1_y0_store = y0_store

        emit_l1_attn(cx, x1_in, xn_sb, "xn", TPC, 0, None, None,
                     {"hsel": hsel_in, "hselc": hselc_in, "ccmaxbias": ccmb_in})

        for kc in range(NCH):
            tk.dma("y", lambda e, kc=kc: e.dma_start(
                out=xn_out[kc * 128:(kc + 1) * 128, :], in_=_chunk(xn_sb, kc, TPC)),
                reads=["xn"], writes=["o:xn"])
        n = tk.dcnt["y"]
        tk.flush()
        block.sync(lambda eng: eng.wait_ge(tk.dsem["y"], 16 * n))
    return nc




L1B_SHARED = ["f_w1_s", "f_w2_s", "f_h1_s", "f_h2_s", "f_w3_s", "f_s1", "f_s2", "f_sh1", "f_sh2",
              "f_s3", "eye", "ones_row"]


def emit_l1_ffn(cx, xn_sb, xn_key, xnw, xc0, y0_in_ap, out_store, y_dep=()):
    nc, tk, sb = cx.nc, cx.tk, cx.sb

    def xch(kc, a, b):
        return xn_sb[:, kc * xnw + xc0 + a: kc * xnw + xc0 + b]

    hs = sb("f_h", [128, 32 * 1024], BF16)
    gs = sb("f_g", [128, 32 * 1024], BF16)
    wch = sb("f_wch", [128, 2 * 4096], BF16)
    yt = sb("f_yt", [128, NCH * 512], F32)
    cx.f_yt = yt
    usc = yt
    scl = sb("f_scl", [128, 4 * 32 + 8], F32)

    tk.dma("y", lambda e: e.dma_start(out=scl[:, 0:32], in_=cx.dram["f_s1"][:, :]), writes=["f:scl"])
    tk.dma("y", lambda e: e.dma_start(out=scl[:, 32:64], in_=cx.dram["f_s2"][:, :]), writes=["f:scl"])
    tk.dma("y", lambda e: e.dma_start(out=scl[:, 64:96], in_=cx.dram["f_sh1"][:, :]), writes=["f:scl"])
    tk.dma("y", lambda e: e.dma_start(out=scl[:, 96:128], in_=cx.dram["f_sh2"][:, :]), writes=["f:scl"])
    tk.dma("y", lambda e: e.dma_start(out=scl[:, 128:136], in_=cx.dram["f_s3"][:, :]), writes=["f:scl"])

    for half in range(2):
        hoff = half * 1024
        for oc in range(32):
            par = oc % 2
            tk.dma("g", lambda e, oc=oc, par=par: e.dma_start(
                out=wch[:, par * 4096:par * 4096 + 1024],
                in_=cx.dram["f_w1_s"][oc]),
                writes=[f"f:wh{par}"])
            tk.dma("g", lambda e, oc=oc, par=par: e.dma_start(
                out=wch[:, par * 4096 + 1024:par * 4096 + 2048],
                in_=cx.dram["f_w2_s"][oc]),
                writes=[f"f:wh{par}"])
            for tt in range(2):
                tc0 = hoff + tt * 512
                ppar = tt % 2
                ps_a, ps_b = cx.pf[ppar], cx.pf[2 + ppar]

                def mm_h(e, oc=oc, par=par, tc0=tc0, ps_a=ps_a, ps_b=ps_b):
                    for kc in range(NCH):
                        e.matmul(ps_a[:, 0:512], wch[:, par * 4096 + kc * 128: par * 4096 + (kc + 1) * 128],
                                 xch(kc, tc0, tc0 + 512), start=(kc == 0), stop=(kc == NCH - 1))
                    ins = None
                    for kc in range(NCH):
                        ins = e.matmul(ps_b[:, 0:512], wch[:, par * 4096 + 1024 + kc * 128: par * 4096 + 1024 + (kc + 1) * 128],
                                       xch(kc, tc0, tc0 + 512), start=(kc == 0), stop=(kc == NCH - 1))
                    return ins
                tk.op("t", mm_h, reads=[xn_key, f"f:wh{par}"],
                      writes=[f"ps:{ppar}", f"ps:{2 + ppar}"])
                tk.op("s", lambda e, oc=oc, ppar=ppar, ps_a=ps_a: e.activation(
                    usc[:, ppar * 512:(ppar + 1) * 512], ps_a[:, 0:512], AF.Silu,
                    scale=scl[:, oc:oc + 1]),
                    reads=[f"ps:{ppar}", "f:scl"], writes=["f:yt"])
                tk.op("v", lambda e, oc=oc, tt=tt, ppar=ppar, ps_b=ps_b: e.scalar_tensor_tensor(
                    hs[:, oc * 1024 + tt * 512: oc * 1024 + (tt + 1) * 512],
                    ps_b[:, 0:512], scl[:, 32 + oc:33 + oc], usc[:, ppar * 512:(ppar + 1) * 512],
                    ALU.mult, ALU.mult),
                    reads=[f"ps:{2 + ppar}", "f:scl", "f:yt"], writes=["f:h"])
        for stage, (wnm, snm, soff, src_key, sbuf, dst_key, dbuf) in enumerate([
                ("f_h1", "sh1", 64, "f:h", hs, "f:g", gs),
                ("f_h2", "sh2", 96, "f:g", gs, "f:h", hs)]):
            for oc in range(32):
                par = oc % 2
                tk.dma("g", lambda e, oc=oc, par=par, wnm=wnm: e.dma_start(
                    out=wch[:, par * 4096:(par + 1) * 4096],
                    in_=cx.dram[wnm + "_s"][oc]),
                    writes=[f"f:wh{par}"])
                for tt in range(2):
                    ppar = tt % 2
                    ps = cx.pf[4 + ppar]

                    def mm_g(e, oc=oc, par=par, tt=tt, sbuf=sbuf, ps=ps):
                        ins = None
                        for kc in range(32):
                            ins = e.matmul(ps[:, 0:512],
                                           wch[:, par * 4096 + kc * 128: par * 4096 + (kc + 1) * 128],
                                           sbuf[:, kc * 1024 + tt * 512: kc * 1024 + (tt + 1) * 512],
                                           start=(kc == 0), stop=(kc == 31))
                        return ins
                    tk.op("t", mm_g, reads=[src_key, f"f:wh{par}"], writes=[f"ps:{4 + ppar}"])
                    tk.op("s", lambda e, oc=oc, tt=tt, soff=soff, dbuf=dbuf, ps=ps: e.activation(
                        dbuf[:, oc * 1024 + tt * 512: oc * 1024 + (tt + 1) * 512],
                        ps[:, 0:512], AF.Gelu, scale=scl[:, soff + oc:soff + oc + 1]),
                        reads=[f"ps:{4 + ppar}", "f:scl"], writes=[dst_key])
        for tt in range(2):
            tc0 = hoff + tt * 512
            for kc in range(NCH):
                tk.dma("y", lambda e, kc=kc, tc0=tc0: e.dma_start(
                    out=_chunk(yt, kc, 512), in_=y0_in_ap[kc * 128:(kc + 1) * 128, tc0:tc0 + 512]),
                    reads=list(y_dep), writes=["f:yt"])
            for oc in range(NCH):
                par = oc % 2
                tk.dma("g", lambda e, oc=oc, par=par: e.dma_start(
                    out=wch[:, par * 4096:(par + 1) * 4096],
                    in_=cx.dram["f_w3_s"][oc]),
                    writes=[f"f:wh{par}"])
                ps = cx.pf[par]

                def mm_o(e, oc=oc, par=par, tt=tt, ps=ps):
                    ins = None
                    for kc in range(32):
                        ins = e.matmul(ps[:, 0:512],
                                       wch[:, par * 4096 + kc * 128: par * 4096 + (kc + 1) * 128],
                                       hs[:, kc * 1024 + tt * 512: kc * 1024 + (tt + 1) * 512],
                                       start=(kc == 0), stop=(kc == 31))
                    return ins
                tk.op("t", mm_o, reads=["f:h", f"f:wh{par}"], writes=[f"ps:{par}"])
                tk.op("v", lambda e, oc=oc, par=par, ps=ps: e.scalar_tensor_tensor(
                    _chunk(yt, oc, 512), ps[:, 0:512], scl[:, 128 + oc:129 + oc],
                    _chunk(yt, oc, 512), ALU.mult, ALU.add),
                    reads=[f"ps:{par}", "f:scl", "f:yt"], writes=["f:yt"])
            out_store(half, tt)


def build_l1b():
    nc = bass.Bass("TRN2", target_bir_lowering=False)
    es = contextlib.ExitStack()
    with es:
        xn_in = nc.dram_tensor("xn1_in", [DM, TPC], BF16, kind="ExternalInput").ap()
        y0_in = nc.dram_tensor("y0_in", [DM, TPC], F32, kind="ExternalInput").ap()
        y_out = nc.dram_tensor("y_out", [DM, TPC], F32, kind="ExternalOutput").ap()

        block = es.enter_context(nc.Block())
        cx = _setup_common(nc, es, block, _SHARED_CACHE, L1B_SHARED)
        tk = cx.tk

        xn_sb = cx.sb("xn_sb", [128, NCH * TPC], BF16)
        for kc in range(NCH):
            tk.dma("y", lambda e, kc=kc: e.dma_start(
                out=_chunk(xn_sb, kc, TPC), in_=xn_in[kc * 128:(kc + 1) * 128, :]), writes=["xn"])

        def out_store(half, tt):
            tc0 = half * 1024 + tt * 512
            for kc in range(NCH):
                tk.dma("y", lambda e, kc=kc, tc0=tc0: e.dma_start(
                    out=y_out[kc * 128:(kc + 1) * 128, tc0:tc0 + 512],
                    in_=_chunk(cx.f_yt, kc, 512)),
                    reads=["f:yt"], writes=["o:y"])

        emit_l1_ffn(cx, xn_sb, "xn", TPC, 0, y0_in, out_store)
        n = tk.dcnt["y"]
        tk.flush()
        block.sync(lambda eng: eng.wait_ge(tk.dsem["y"], 16 * n))
    return nc


_SHARED_CACHE = None


def _set_shared(shared):
    global _SHARED_CACHE
    _SHARED_CACHE = shared




MEGA_SHARED = sorted(set(L0A_SHARED + MOE_SHARED + L1A_SHARED + L1B_SHARED))


def build_mega(dbg=False, cc=True):
    nc = bass.Bass("TRN2", target_bir_lowering=False)
    es = contextlib.ExitStack()
    with es:
        x_in = nc.dram_tensor("x_fm", [DM, TH], F32, kind="ExternalInput").ap()
        kvm_in = nc.dram_tensor("kvmask", [1, TH], F32, kind="ExternalInput").ap()
        y_out = nc.dram_tensor("y_out", [DM, TPC], F32, kind="ExternalOutput").ap()
        kind = dict(kind="ExternalOutput") if dbg else {}
        x1s = nc.dram_tensor("x1s", [DM, TPC], F32, **kind)
        y0s = nc.dram_tensor("y0s", [DM, TPC], F32, **kind)
        if dbg:
            xn_dbg = nc.dram_tensor("xn_dbg", [DM, TPC], BF16, kind="ExternalOutput").ap()
            stg_dbg = nc.dram_tensor("stg_dbg", [128, STG_W], F32, kind="ExternalOutput").ap()
            gts_dbg = nc.dram_tensor("gts_dbg", [128, 256], F32, kind="ExternalOutput").ap()
            x1c_dbg = nc.dram_tensor("x1c_dbg", [DM, TPC], F32, kind="ExternalOutput").ap()
            xn2_dbg = nc.dram_tensor("xn2_dbg", [DM, TPC], BF16, kind="ExternalOutput").ap()

        block = es.enter_context(nc.Block())
        cx = _setup_common(nc, es, block, _SHARED_CACHE, MEGA_SHARED)
        cx.dram["kvmask"] = kvm_in
        tk = cx.tk

        awu = cx.sb("actwarm", [1, 16], F32)
        tk.op("v", lambda e: e.memset(awu[:, :], 1.0), writes=["actwarm"])
        for fn in (AF.Sqrt, AF.Exp, AF.Silu, AF.Gelu):
            tk.op("s", lambda e, fn=fn: e.activation(awu[:, :], awu[:, :], fn),
                  reads=["actwarm"], writes=["actwarm"])
            tk.barrier()

        ccw_in = nc.dram_tensor("ccw_in", [1, 16], F32)
        ccw_out = nc.dram_tensor("ccw_out", [1, 16], F32)
        wup = cx.sb("warmup", [1, 16], F32)
        tk.op("v", lambda e: e.memset(wup[:, :], 1.0), writes=["wup"])
        tk.dma("g", lambda e: e.dma_start(out=ccw_in.ap()[:, :], in_=wup[:, :]),
               reads=["wup"], writes=["d:ccw"])
        tk.op("g", lambda e: e.collective_compute(
            "AllReduce", ALU.add, replica_groups=[[0, 1, 2, 3], [4, 5, 6, 7]],
            ins=[ccw_in.ap().opt()], outs=[ccw_out.ap().opt()]),
            reads=["d:ccw"], writes=["d:ccwo"])
        tk.dma("g", lambda e: e.dma_start(out=wup[:, :], in_=ccw_out.ap()[:, :]),
               reads=["d:ccwo"], writes=["wup"])
        tk.barrier()

        xn_sb = cx.sb("xn_sb", [128, NCH * TH], BF16)
        cx.push_scope()
        cx.ln_scr = cx.sb("ln_scr", [128, 2 * 512], F32)
        cx.ln_rows = cx.sb("ln_rows", [1, 4 * 512], F32)
        cx.ln_ones_col = cx.sb("ln_ones_col", [128, 1], F32)
        cx.ln_eps = cx.sb("ln_eps", [1, 1], F32)
        tk.op("v", lambda e: e.memset(cx.ln_ones_col[:, :], 1.0), writes=["ln:ones_col"])
        tk.op("v", lambda e: e.memset(cx.ln_eps[:, :], 1e-5), writes=["ln:eps"])
        cx.ln_init = True

        cx.push_scope()
        x_sb = cx.sb("x_sb", [128, NCH * TPC], F32)
        cx.push_scope()
        xh_sb = cx.sb("xh_sb", [128, NCH * 256], F32)
        for kc in range(NCH):
            tk.dma("y", lambda e, kc=kc: e.dma_start(
                out=_chunk(x_sb, kc, TPC), in_=x_in[kc * 128:(kc + 1) * 128, HALO:HALO + TPC]),
                writes=["x"])
            tk.dma("y", lambda e, kc=kc: e.dma_start(
                out=_chunk(xh_sb, kc, 256, 0, 128), in_=x_in[kc * 128:(kc + 1) * 128, 0:HALO]),
                writes=["xh"])
            tk.dma("y", lambda e, kc=kc: e.dma_start(
                out=_chunk(xh_sb, kc, 256, 128, 256),
                in_=x_in[kc * 128:(kc + 1) * 128, TH - HALO:TH]),
                writes=["xh"])
        emit_ln(cx, x_sb, "x", TPC, 0, TPC, xn_sb, "xn", TH, HALO)
        emit_ln(cx, xh_sb, "xh", 256, 0, 128, xn_sb, "xn", TH, 0)
        emit_ln(cx, xh_sb, "xh", 256, 128, 128, xn_sb, "xn", TH, TH - HALO)
        emit_l0_attn(cx, xn_sb, "xn", x_sb, "x")
        cx.pop_scope()

        cx.push_scope()
        cx.moe_dbg = (gts_dbg,) if dbg else None
        emit_moe(cx, xn_sb, "xn", TH, HALO, x_sb, "x")
        if dbg:
            for kc in range(NCH):
                tk.dma("y", lambda e, kc=kc: e.dma_start(
                    out=xn2_dbg[kc * 128:(kc + 1) * 128, :],
                    in_=xn_sb[:, kc * TH + HALO: kc * TH + HALO + TPC]),
                    reads=["xn"], writes=["dbg"])
        for kc in range(NCH):
            tk.dma("y", lambda e, kc=kc: e.dma_start(
                out=x1s.ap()[kc * 128:(kc + 1) * 128, :], in_=_chunk(x_sb, kc, TPC)),
                reads=["x"], writes=["d:x1s"])
        if dbg:
            tk.dma("y", lambda e: e.dma_start(out=x1c_dbg[:, :], in_=x1s.ap()[:, :]),
                   reads=["d:x1s"], writes=["dbg"])
        cx.pop_scope()
        cx.pop_scope()

        cx.push_scope()

        def y0_store(tt):
            c0 = tt * 512
            for kc in range(NCH):
                tk.dma("y", lambda e, kc=kc, c0=c0: e.dma_start(
                    out=y0s.ap()[kc * 128:(kc + 1) * 128, c0:c0 + 512],
                    in_=_chunk(cx.l1_xt, kc, 512)),
                    reads=["l1:xt"], writes=["d:y0s"])
        cx.l1_y0_store = y0_store
        emit_l1_attn(cx, x1s.ap(), xn_sb, "xn", TH, HALO, None, None, {},
                     x_dep=("d:x1s",), cc=cc)
        if dbg:
            for kc in range(NCH):
                tk.dma("y", lambda e, kc=kc: e.dma_start(
                    out=xn_dbg[kc * 128:(kc + 1) * 128, :],
                    in_=xn_sb[:, kc * TH + HALO: kc * TH + HALO + TPC]),
                    reads=["xn"], writes=["dbg"])
            tk.dma("y", lambda e: e.dma_start(out=stg_dbg[:, :], in_=cx.l1_stg_dbg[:, :]),
                   reads=["l1:stg"], writes=["dbg"])
        cx.pop_scope()
        cx.pop_scope()

        cx.push_scope()

        def out_store(half, tt):
            tc0 = half * 1024 + tt * 512
            for kc in range(NCH):
                tk.dma("y", lambda e, kc=kc, tc0=tc0: e.dma_start(
                    out=y_out[kc * 128:(kc + 1) * 128, tc0:tc0 + 512],
                    in_=_chunk(cx.f_yt, kc, 512)),
                    reads=["f:yt"], writes=["o:y"])
        emit_l1_ffn(cx, xn_sb, "xn", TH, HALO, y0s.ap(), out_store, y_dep=("d:y0s",))
        cx.pop_scope()

        ny = tk.dcnt["y"]
        ng = tk.dcnt["g"]
        tk.flush()
        def fin(eng):
            eng.wait_ge(tk.dsem["y"], 16 * ny)
            eng.wait_ge(tk.dsem["g"], 16 * ng)
        block.sync(fin)
    return nc


def kernel(**inputs):
    shared = _prep_shared(inputs)
    _set_shared(shared)
    cores = list(range(NCORE))
    nc = _get_prog("MEGA", build_mega)
    maps = []
    for c in cores:
        pc = _prep_core(inputs, c)
        m = {nm: shared[nm] for nm in MEGA_SHARED}
        m["x_fm"] = pc["x_fm"]
        m["kvmask"] = pc["kvmask"]
        maps.append(m)
    res = run_bass_kernel_spmd(nc, maps, core_ids=cores)
    out = np.empty((B * S, DM), np.float32)
    for c in cores:
        out[c * TPC:(c + 1) * TPC] = np.asarray(res.results[c]["y_out"]).T
    return out.reshape(B, S, DM)




def build_mega_l0(**moekw):
    nc = bass.Bass("TRN2", target_bir_lowering=False)
    es = contextlib.ExitStack()
    with es:
        x_in = nc.dram_tensor("x_fm", [DM, TH], F32, kind="ExternalInput").ap()
        kvm_in = nc.dram_tensor("kvmask", [1, TH], F32, kind="ExternalInput").ap()
        y_out = nc.dram_tensor("y_out", [DM, TPC], F32, kind="ExternalOutput").ap()

        block = es.enter_context(nc.Block())
        cx = _setup_common(nc, es, block, _SHARED_CACHE, MEGA_SHARED)
        cx.dram["kvmask"] = kvm_in
        tk = cx.tk
        xn_sb = cx.sb("xn_sb", [128, NCH * TH], BF16)
        cx.push_scope()
        cx.ln_scr = cx.sb("ln_scr", [128, 2 * 512], F32)
        cx.ln_rows = cx.sb("ln_rows", [1, 4 * 512], F32)
        cx.ln_ones_col = cx.sb("ln_ones_col", [128, 1], F32)
        cx.ln_eps = cx.sb("ln_eps", [1, 1], F32)
        tk.op("v", lambda e: e.memset(cx.ln_ones_col[:, :], 1.0), writes=["ln:ones_col"])
        tk.op("v", lambda e: e.memset(cx.ln_eps[:, :], 1e-5), writes=["ln:eps"])
        cx.ln_init = True
        cx.push_scope()
        x_sb = cx.sb("x_sb", [128, NCH * TPC], F32)
        cx.push_scope()
        xh_sb = cx.sb("xh_sb", [128, NCH * 256], F32)
        for kc in range(NCH):
            tk.dma("y", lambda e, kc=kc: e.dma_start(
                out=_chunk(x_sb, kc, TPC), in_=x_in[kc * 128:(kc + 1) * 128, HALO:HALO + TPC]),
                writes=["x"])
            tk.dma("y", lambda e, kc=kc: e.dma_start(
                out=_chunk(xh_sb, kc, 256, 0, 128), in_=x_in[kc * 128:(kc + 1) * 128, 0:HALO]),
                writes=["xh"])
            tk.dma("y", lambda e, kc=kc: e.dma_start(
                out=_chunk(xh_sb, kc, 256, 128, 256),
                in_=x_in[kc * 128:(kc + 1) * 128, TH - HALO:TH]),
                writes=["xh"])
        emit_ln(cx, x_sb, "x", TPC, 0, TPC, xn_sb, "xn", TH, HALO)
        emit_ln(cx, xh_sb, "xh", 256, 0, 128, xn_sb, "xn", TH, 0)
        emit_ln(cx, xh_sb, "xh", 256, 128, 128, xn_sb, "xn", TH, TH - HALO)
        emit_l0_attn(cx, xn_sb, "xn", x_sb, "x")
        cx.pop_scope()
        cx.push_scope()
        emit_moe(cx, xn_sb, "xn", TH, HALO, x_sb, "x", **moekw)
        for kc in range(NCH):
            tk.dma("y", lambda e, kc=kc: e.dma_start(
                out=y_out[kc * 128:(kc + 1) * 128, :], in_=_chunk(x_sb, kc, TPC)),
                reads=["x"], writes=["o:y"])
        cx.pop_scope()
        cx.pop_scope()
        cx.pop_scope()
        ny = tk.dcnt["y"]
        ng = tk.dcnt["g"]
        tk.flush()
        def fin(eng):
            eng.wait_ge(tk.dsem["y"], 16 * ny)
            eng.wait_ge(tk.dsem["g"], 16 * ng)
        block.sync(fin)
    return nc


_PROGS = {}


def _get_prog(name, builder):
    if name not in _PROGS:
        _PROGS[name] = builder()
    return _PROGS[name]


def kernel_v1(**inputs):
    shared = _prep_shared(inputs)
    _set_shared(shared)
    cores = list(range(NCORE))
    pcs = [_prep_core(inputs, c) for c in cores]

    ncA = _get_prog("A", build_l0a)
    mapsA = []
    for c in cores:
        m = {nm: shared[nm] for nm in L0A_SHARED}
        m["x_fm"] = pcs[c]["x_fm"]
        m["kvmask"] = pcs[c]["kvmask"]
        mapsA.append(m)
    resA = run_bass_kernel_spmd(ncA, mapsA, core_ids=cores)

    ncB = _get_prog("B", build_l0b)
    mapsB = []
    for c in cores:
        m = {nm: shared[nm] for nm in MOE_SHARED}
        m["xn_in"] = np.ascontiguousarray(
            np.asarray(resA.results[c]["xn_out"])[:, HALO:HALO + TPC])
        m["x1p_in"] = np.ascontiguousarray(np.asarray(resA.results[c]["x1p_out"]))
        mapsB.append(m)
    resB = run_bass_kernel_spmd(ncB, mapsB, core_ids=cores)

    ncC = _get_prog("C", build_l1a)
    mapsC = []
    for c in cores:
        m = {nm: shared[nm] for nm in L1A_SHARED}
        m["x1_in"] = np.ascontiguousarray(np.asarray(resB.results[c]["x1_out"]))
        m["hsel"] = pcs[c]["hsel"]
        m["hselc"] = pcs[c]["hselc"]
        m["ccmaxbias"] = pcs[c]["ccmaxbias"]
        mapsC.append(m)
    resC = run_bass_kernel_spmd(ncC, mapsC, core_ids=cores)

    ncD = _get_prog("D", build_l1b)
    mapsD = []
    for c in cores:
        m = {nm: shared[nm] for nm in L1B_SHARED}
        m["xn1_in"] = np.ascontiguousarray(np.asarray(resC.results[c]["xn1_out"]))
        m["y0_in"] = np.ascontiguousarray(np.asarray(resC.results[c]["y0_out"]))
        mapsD.append(m)
    resD = run_bass_kernel_spmd(ncD, mapsD, core_ids=cores)

    out = np.empty((B * S, DM), np.float32)
    for c in cores:
        out[c * TPC:(c + 1) * TPC] = np.asarray(resD.results[c]["y_out"]).T
    return out.reshape(B, S, DM)


# version 06
# speedup vs baseline: 1.3010x; 1.0213x over previous
import contextlib
from collections import defaultdict

import numpy as np
import ml_dtypes

import concourse.bass as bass
import concourse.mybir as mybir
from concourse.bass_utils import run_bass_kernel_spmd

F32 = mybir.dt.float32
BF16 = mybir.dt.bfloat16
AF = mybir.ActivationFunctionType
ALU = mybir.AluOpType
AX = mybir.AxisListType.X

NCORE = 8
B, S, DM = 2, 8192, 1024
TPC = 2048
HALO = 128
TH = TPC + 2 * HALO
NCH = DM // 128

L0_H, L0_KV, L0_D, L0_W = 16, 2, 64, 256
HLIST = [(p % 2) * 8 + p // 2 for p in range(16)]

MOE_E, MOE_K = 16, 4
GLU_LIMIT, GLU_ALPHA = 7.0, 1.702

L1_M, L1_DH = 266, 64
L1_EPS = 1e-4
L1_RATIO = float(L1_M) ** -0.5
L1_DN = float(L1_DH) ** -0.25

ENG_NAME = {"v": "vector", "s": "scalar", "t": "tensor", "g": "gpsimd", "y": "sync"}


class TK:

    def __init__(self, nc, block, es):
        self.nc = nc
        self.block = block
        self.sem = {e: es.enter_context(nc.semaphore(f"tk_{e}")) for e in ENG_NAME}
        self.dsem = {q: es.enter_context(nc.semaphore(f"tkd_{q}")) for q in ("g", "y")}
        self.cnt = {e: 0 for e in ENG_NAME}
        self.dcnt = {q: 0 for q in ("g", "y")}
        self.known = {e: defaultdict(int) for e in ENG_NAME}
        self.writer = {}
        self.readers = defaultdict(list)
        self.pending = {e: [] for e in ENG_NAME}

    def flush(self):
        for e in ENG_NAME:
            bodies = self.pending[e]
            if not bodies:
                continue

            def run_all(eng, bodies=bodies):
                for b in bodies:
                    b(eng)
            getattr(self.block, ENG_NAME[e])(run_all)
            self.pending[e] = []

    def _needs(self, e, reads, writes):
        need = {}

        def add(tok):
            if tok is None:
                return
            kind, src, c = tok
            pk = (kind, src)
            if need.get(pk, 0) < c:
                need[pk] = c

        for k in reads:
            add(self.writer.get(k))
        for k in writes:
            add(self.writer.get(k))
            for tok in self.readers.get(k, ()):
                add(tok)
        out = []
        kn = self.known[e]
        for pk in sorted(need, key=lambda p: (p[0], str(p[1]))):
            c = need[pk]
            if kn[pk] >= c:
                continue
            out.append((pk, c))
        return out

    def _emit(self, e, fn, needs, inc_q):
        sem, dsem = self.sem, self.dsem

        def body(eng):
            for (kind, src), c in needs:
                if kind == "e":
                    eng.wait_ge(sem[src], c)
                else:
                    eng.wait_ge(dsem[src], 16 * c)
            ins = fn(eng)
            if inc_q is None:
                ins.then_inc(sem[e], 1)
            else:
                ins.then_inc(dsem[inc_q], 16)

        self.pending[e].append(body)
        kn = self.known[e]
        for pk, c in needs:
            if kn[pk] < c:
                kn[pk] = c

    def op(self, e, fn, reads=(), writes=()):
        needs = self._needs(e, reads, writes)
        self._emit(e, fn, needs, None)
        self.cnt[e] += 1
        tok = ("e", e, self.cnt[e])
        for k in reads:
            self.readers[k].append(tok)
        for k in writes:
            self.writer[k] = tok
            self.readers[k] = [tok]
        return tok

    DMA_WINDOW = 20

    def dma(self, q, fn, reads=(), writes=()):
        needs = self._needs(q, reads, writes)
        lag = self.dcnt[q] + 1 - self.DMA_WINDOW
        if lag > 0 and self.known[q][("d", q)] < lag:
            have = {pk: c for pk, c in needs}
            if have.get(("d", q), 0) < lag:
                have[("d", q)] = lag
            needs = sorted(have.items(), key=lambda kv: (kv[0][0], str(kv[0][1])))
        self._emit(q, fn, needs, q)
        self.dcnt[q] += 1
        tok = ("d", q, self.dcnt[q])
        for k in reads:
            self.readers[k].append(tok)
        for k in writes:
            self.writer[k] = tok
            self.readers[k] = [tok]
        return tok

    def barrier(self):
        snap_e = dict(self.cnt)
        snap_d = dict(self.dcnt)
        for e in ENG_NAME:
            needs = []
            for o in ENG_NAME:
                if snap_e[o] > self.known[e][("e", o)]:
                    needs.append((("e", o), snap_e[o]))
            for q in ("g", "y"):
                if snap_d[q] > self.known[e][("d", q)]:
                    needs.append((("d", q), snap_d[q]))
            if not needs:
                continue

            def wrapped(eng, needs=needs):
                for (kind, src), c in needs:
                    if kind == "e":
                        eng.wait_ge(self.sem[src], c)
                    else:
                        eng.wait_ge(self.dsem[src], 16 * c)

            self.pending[e].append(wrapped)
            for pk, c in needs:
                if self.known[e][pk] < c:
                    self.known[e][pk] = c



def _prep_shared(I):
    d = {}
    perm = np.concatenate([np.arange(h * 64, (h + 1) * 64) for h in HLIST])
    d["l0_wq"] = np.ascontiguousarray(np.asarray(I["L0_attn_Wq"], np.float32)[:, perm])
    d["l0_bq"] = np.ascontiguousarray(np.asarray(I["L0_attn_bq"], np.float32)[perm][None, :])
    d["l0_wk"] = np.ascontiguousarray(np.asarray(I["L0_attn_Wk"], np.float32))
    d["l0_bk"] = np.ascontiguousarray(np.asarray(I["L0_attn_bk"], np.float32)[None, :])
    d["l0_wv"] = np.ascontiguousarray(np.asarray(I["L0_attn_Wv"], np.float32))
    d["l0_bv"] = np.ascontiguousarray(np.asarray(I["L0_attn_bv"], np.float32)[None, :])
    d["l0_wo"] = np.ascontiguousarray(np.asarray(I["L0_attn_Wo"], np.float32)[perm, :])
    d["l0_bo"] = np.ascontiguousarray(np.asarray(I["L0_attn_bo"], np.float32)[None, :])

    d["moe_scale"] = np.ascontiguousarray(
        np.asarray(I["L0_ffn_norm_scale"], np.float32).reshape(NCH, 128).T)
    d["moe_wg"] = np.ascontiguousarray(np.asarray(I["L0_ffn_wg"], np.float32))
    d["moe_bg"] = np.ascontiguousarray(np.asarray(I["L0_ffn_bg"], np.float32)[None, :])
    w1 = np.asarray(I["L0_ffn_w1"], np.float32)
    b1 = np.asarray(I["L0_ffn_b1"], np.float32)
    w1p = np.empty((MOE_E, 1024, 2048), np.float32)
    b1p = np.empty((MOE_E, 1, 2048), np.float32)
    for e in range(MOE_E):
        a, bb = w1[e][:, 0::2], w1[e][:, 1::2]
        ba, bbv = b1[e][0::2], b1[e][1::2]
        for j in range(8):
            w1p[e][:, j * 256:j * 256 + 128] = a[:, j * 128:(j + 1) * 128]
            w1p[e][:, j * 256 + 128:(j + 1) * 256] = bb[:, j * 128:(j + 1) * 128]
            b1p[e][0, j * 256:j * 256 + 128] = ba[j * 128:(j + 1) * 128]
            b1p[e][0, j * 256 + 128:(j + 1) * 256] = bbv[j * 128:(j + 1) * 128]
    d["moe_w1"] = w1p
    d["moe_b1"] = b1p
    b1c = np.empty((MOE_E, 128, 16), np.float32)
    for e in range(MOE_E):
        for j in range(8):
            b1c[e, :, j * 2] = b1p[e, 0, j * 256:j * 256 + 128]
            b1c[e, :, j * 2 + 1] = b1p[e, 0, j * 256 + 128:(j + 1) * 256]
    d["moe_b1c"] = b1c
    d["moe_w2"] = np.ascontiguousarray(np.asarray(I["L0_ffn_w2"], np.float32))
    d["moe_b2"] = np.ascontiguousarray(np.asarray(I["L0_ffn_b2"], np.float32))

    for nm in ("Wq", "Wk", "Wv", "Wo", "Wfc"):
        d[f"l1_{nm.lower()}"] = np.ascontiguousarray(np.asarray(I[f"L1_attn_{nm}"], np.float32).T)
    for nm in ("bq", "bk", "bv", "bo", "bfc"):
        d[f"l1_{nm}"] = np.ascontiguousarray(np.asarray(I[f"L1_attn_{nm}"], np.float32)[None, :])
    projT = np.asarray(I["L1_attn_proj"], np.float32).T
    d["l1_projT"] = np.ascontiguousarray(np.concatenate([projT, projT], axis=0))

    d["f_w1"] = np.ascontiguousarray(np.asarray(I["L1_ffn_w1_w"], np.float32).T)
    d["f_w2"] = np.ascontiguousarray(np.asarray(I["L1_ffn_w2_w"], np.float32).T)
    d["f_h1"] = np.ascontiguousarray(np.asarray(I["L1_ffn_h1_w"], np.float32).T)
    d["f_h2"] = np.ascontiguousarray(np.asarray(I["L1_ffn_h2_w"], np.float32).T)
    d["f_w3"] = np.ascontiguousarray(np.asarray(I["L1_ffn_w3_w"], np.float32).T)
    d["f_s1"] = np.ascontiguousarray(np.asarray(I["L1_ffn_w1_s"], np.float32).reshape(-1, 128).T)
    d["f_s2"] = np.ascontiguousarray(np.asarray(I["L1_ffn_w2_s"], np.float32).reshape(-1, 128).T)
    d["f_sh1"] = np.ascontiguousarray(np.asarray(I["L1_ffn_h1_s"], np.float32).reshape(-1, 128).T)
    d["f_sh2"] = np.ascontiguousarray(np.asarray(I["L1_ffn_h2_s"], np.float32).reshape(-1, 128).T)
    d["f_s3"] = np.ascontiguousarray(np.asarray(I["L1_ffn_w3_s"], np.float32).reshape(-1, 128).T)

    def _pstream(w, kchunks):
        K, O = w.shape
        assert K == kchunks * 128
        return np.ascontiguousarray(w.reshape(kchunks, 128, O).transpose(1, 0, 2).reshape(128, kchunks * O))

    def _ostream(w, kchunks, osz=128):
        K, O = w.shape
        noc = O // osz
        a = w.reshape(kchunks, 128, noc, osz).transpose(2, 1, 0, 3)
        return np.ascontiguousarray(a.reshape(noc, 128, kchunks * osz))

    d["l0_wq_s"] = _pstream(d["l0_wq"], NCH)
    d["l0_wk_s"] = _pstream(d["l0_wk"], NCH)
    d["l0_wv_s"] = _pstream(d["l0_wv"], NCH)
    d["l0_wo_s"] = _pstream(d["l0_wo"], NCH)
    d["moe_wg_s"] = _pstream(d["moe_wg"], NCH)
    d["moe_w1_s"] = np.ascontiguousarray(np.stack(
        [np.stack([_pstream(d["moe_w1"][e][:, j * 256:(j + 1) * 256], NCH) for j in range(8)])
         for e in range(MOE_E)]))
    d["moe_w2_s"] = np.ascontiguousarray(np.stack(
        [_pstream(d["moe_w2"][e], NCH) for e in range(MOE_E)]))
    for nm in ("l1_wq", "l1_wk", "l1_wv", "l1_wo", "l1_wfc"):
        d[nm + "_s"] = _pstream(d[nm], NCH)
    d["f_w1_s"] = _ostream(d["f_w1"], NCH)
    d["f_w2_s"] = _ostream(d["f_w2"], NCH)
    d["f_h1_s"] = _ostream(d["f_h1"], 32)
    d["f_h2_s"] = _ostream(d["f_h2"], 32)
    d["f_w3_s"] = _ostream(d["f_w3"], 32)

    d["eye"] = np.eye(128, dtype=np.float32)
    d["ones_row"] = np.ones((1, 512), np.float32)
    r = np.arange(128)[:, None]
    u = np.arange(384)[None, :]
    d["l0mask"] = ((u >= r) & (u < r + 256)).astype(np.float32)
    return d


def _prep_core(I, c):
    xf = np.asarray(I["L0_attn_x"], np.float32).reshape(B * S, DM)
    t0 = c * TPC
    seq = t0 // S
    s_lo, s_hi = seq * S, (seq + 1) * S
    lo, hi = t0 - HALO, t0 + TPC + HALO
    pad_l, pad_r = max(0, s_lo - lo), max(0, hi - s_hi)
    sl = xf[max(lo, s_lo):min(hi, s_hi)]
    shard = np.concatenate(
        [np.zeros((pad_l, DM), np.float32), sl, np.zeros((pad_r, DM), np.float32)], axis=0)
    kvmask = np.ones((1, TH), np.float32)
    if pad_l:
        kvmask[0, :pad_l] = 0.0
    if pad_r:
        kvmask[0, TH - pad_r:] = 0.0
    hsel = np.zeros((1, 2), np.float32)
    hsel[0, seq] = 1.0
    ccmaxbias = np.full((2, 16), -1.0e30, np.float32)
    ccmaxbias[seq, :] = 0.0
    return {
        "x_fm": np.ascontiguousarray(shard.T),
        "kvmask": kvmask,
        "hsel": hsel,
        "hselc": np.ascontiguousarray(hsel.T),
        "ccmaxbias": ccmaxbias,
    }



class Ctx:
    pass


def _setup_common(nc, es, block, shared, want):
    cx = Ctx()
    cx.nc = nc
    cx.es = es
    cx.tk = TK(nc, block, es)
    cx.dram = {}
    for nm in want:
        arr = shared[nm]
        dt = BF16 if arr.dtype == ml_dtypes.bfloat16 else F32
        cx.dram[nm] = nc.dram_tensor(nm, list(arr.shape), dt, kind="ExternalInput").ap()

    cx._scopes = [es]
    cx.sb = lambda name, shape, dt: cx._scopes[-1].enter_context(nc.sbuf_tensor(name, shape, dt))

    def _push():
        cx._scopes.append(contextlib.ExitStack())

    def _pop():
        cx._scopes.pop().close()
        cx.tk.barrier()
    cx.push_scope = _push
    cx.pop_scope = _pop
    cx.eye = cx.sb("c_eye", [128, 128], F32)
    cx.eye16 = cx.sb("c_eye16", [128, 128], BF16)
    cx.ones_row = cx.sb("c_ones_row", [1, 512], F32)
    cx.ones_col16 = cx.sb("c_ones16", [128, 1], BF16)
    tk = cx.tk
    tk.dma("y", lambda e: e.dma_start(out=cx.eye[:, :], in_=cx.dram["eye"][:, :]), writes=["c:eye"])
    tk.dma("y", lambda e: e.dma_start(out=cx.ones_row[:, :], in_=cx.dram["ones_row"][:, :]), writes=["c:ones_row"])
    tk.op("v", lambda e: e.tensor_copy(cx.eye16[:, :], cx.eye[:, :]), reads=["c:eye"], writes=["c:eye16"])
    tk.op("v", lambda e: e.memset(cx.ones_col16[:, :], 1.0), writes=["c:ones_col16"])
    cx.ones_row16 = cx.sb("c_ones_row16", [1, 512], BF16)
    tk.op("v", lambda e: e.memset(cx.ones_row16[:, :], 1.0), writes=["c:ones_row16"])

    cx.pf = [es.enter_context(nc.psum_tensor(f"pf{i}", [128, 512], F32)) for i in range(6)]
    cx.pb = [es.enter_context(nc.psum_tensor(f"pb{i}", [128, 1024], BF16)) for i in range(2)]
    cx.ln_init = False
    return cx


def _chunk(t, kc, width, c0=0, c1=None):
    if c1 is None:
        c1 = width
    return t[:, kc * width + c0: kc * width + c1]



def emit_ln(cx, src_sb, src_key, src_width, src_c0, ntok, dst_sb, dst_key, dst_width, dst_c0):
    nc, tk = cx.nc, cx.tk
    if not cx.ln_init:
        cx.ln_scr = cx.sb("ln_scr", [128, 2 * 512], F32)
        cx.ln_rows = cx.sb("ln_rows", [1, 4 * 512], F32)
        cx.ln_ones_col = cx.sb("ln_ones_col", [128, 1], F32)
        cx.ln_eps = cx.sb("ln_eps", [1, 1], F32)
        tk.op("v", lambda e: e.memset(cx.ln_ones_col[:, :], 1.0), writes=["ln:ones_col"])
        tk.op("v", lambda e: e.memset(cx.ln_eps[:, :], 1e-5), writes=["ln:eps"])
        cx.ln_init = True
    scr, rows = cx.ln_scr, cx.ln_rows

    def sch(kc, a, b):
        return src_sb[:, kc * src_width + src_c0 + a: kc * src_width + src_c0 + b]

    def dch(kc, a, b):
        return dst_sb[:, kc * dst_width + dst_c0 + a: kc * dst_width + dst_c0 + b]

    for ti in range((ntok + 511) // 512):
        c0 = ti * 512
        w = min(512, ntok - c0)
        ps_s, ps_q, ps_b, ps_c = cx.pf[0], cx.pf[1], cx.pf[2], cx.pf[3]

        def mm_sum(e, c0=c0, w=w):
            ins = None
            for kc in range(NCH):
                ins = e.matmul(ps_s[0:1, 0:w], cx.ln_ones_col[:, :], sch(kc, c0, c0 + w),
                               start=(kc == 0), stop=(kc == NCH - 1))
            return ins
        tk.op("t", mm_sum, reads=[src_key, "ln:ones_col"], writes=["ps:0"])
        for kc in range(NCH):
            par = kc % 2
            tk.op("s", lambda e, kc=kc, par=par, c0=c0, w=w: e.activation(
                scr[:, par * 512:par * 512 + w], sch(kc, c0, c0 + w), AF.Square),
                reads=[src_key], writes=[f"ln:scr{par}"])
            tk.op("t", lambda e, kc=kc, par=par, w=w: e.matmul(
                ps_q[0:1, 0:w], cx.ln_ones_col[:, :], scr[:, par * 512:par * 512 + w],
                start=(kc == 0), stop=(kc == NCH - 1)),
                reads=[f"ln:scr{par}", "ln:ones_col"], writes=["ps:1"])
        tk.op("v", lambda e, w=w: e.tensor_scalar(
            rows[0:1, 0:w], ps_s[0:1, 0:w], 1.0 / DM, None, ALU.mult),
            reads=["ps:0"], writes=["ln:m"])
        tk.op("v", lambda e, w=w: e.tensor_mul(
            rows[0:1, 512:512 + w], rows[0:1, 0:w], rows[0:1, 0:w]),
            reads=["ln:m"], writes=["ln:msq"])
        tk.op("v", lambda e, w=w: e.tensor_scalar(
            rows[0:1, 1024:1024 + w], ps_q[0:1, 0:w], 1.0 / DM, None, ALU.mult),
            reads=["ps:1"], writes=["ln:var"])
        tk.op("v", lambda e, w=w: e.tensor_sub(
            rows[0:1, 1024:1024 + w], rows[0:1, 1024:1024 + w], rows[0:1, 512:512 + w]),
            reads=["ln:var", "ln:msq"], writes=["ln:var"])
        tk.op("s", lambda e, w=w: e.activation(
            rows[0:1, 1536:1536 + w], rows[0:1, 1024:1024 + w], AF.Sqrt,
            bias=cx.ln_eps[0:1, 0:1]),
            reads=["ln:var", "ln:eps"], writes=["ln:sd"])
        tk.op("v", lambda e, w=w: e.reciprocal(
            rows[0:1, 1536:1536 + w], rows[0:1, 1536:1536 + w]),
            reads=["ln:sd"], writes=["ln:sd"])
        tk.op("t", lambda e, w=w: e.matmul(
            ps_b[:, 0:w], cx.ones_row[0:1, 0:128], rows[0:1, 1536:1536 + w],
            start=True, stop=True), reads=["ln:sd", "c:ones_row"], writes=["ps:2"])
        tk.op("v", lambda e, w=w: e.tensor_mul(
            rows[0:1, 512:512 + w], rows[0:1, 0:w], rows[0:1, 1536:1536 + w]),
            reads=["ln:m", "ln:sd"], writes=["ln:msq"])
        tk.op("t", lambda e, w=w: e.matmul(
            ps_c[:, 0:w], cx.ones_row[0:1, 0:128], rows[0:1, 512:512 + w],
            start=True, stop=True), reads=["ln:msq", "c:ones_row"], writes=["ps:3"])
        for kc in range(NCH):
            par = kc % 2
            tk.op("v", lambda e, kc=kc, par=par, c0=c0, w=w: e.tensor_mul(
                scr[:, par * 512:par * 512 + w], sch(kc, c0, c0 + w), ps_b[:, 0:w]),
                reads=[src_key, "ps:2"], writes=[f"ln:scr{par}"])
            tk.op("v", lambda e, kc=kc, par=par, c0=c0, w=w: e.tensor_sub(
                dch(kc, c0, c0 + w), scr[:, par * 512:par * 512 + w], ps_c[:, 0:w]),
                reads=[f"ln:scr{par}", "ps:3"], writes=[dst_key])


def emit_l0_attn(cx, xn_sb, xn_key, x_sb, x_key, tiles=None, dbg=None):
    nc, tk, sb = cx.nc, cx.tk, cx.sb
    wq = sb("l0_wq_sb", [128, NCH * 1024], BF16)
    wo = sb("l0_wo_sb", [128, NCH * 1024], BF16)
    wk = sb("l0_wk_sb", [128, NCH * 128], BF16)
    wv = sb("l0_wv_sb", [128, NCH * 128], BF16)
    brow = sb("l0_brow", [1, 2304], F32)
    kvm = sb("l0_kvm", [1, TH], F32)
    msk = sb("l0_msk", [128, 384], F32)
    kbf = sb("l0_k", [128, TH], BF16)
    vtm = sb("l0_v", [128, 18 * 128], BF16)
    qt = sb("l0_qt", [128, 2 * 1024], BF16)
    at = sb("l0_at", [128, 2 * 1024], BF16)
    pr = sb("l0_pr", [128, 2 * 384], F32)
    prn = sb("l0_prn", [128, 2 * 384], BF16)
    prT = sb("l0_prT", [128, 2 * 384], BF16)
    rowm = sb("l0_rowm", [128, 8], F32)

    tk.dma("g", lambda e: e.dma_start(out=wq[:, :], in_=cx.dram["l0_wq_s"][:, :]), writes=["w:l0q"])
    tk.dma("g", lambda e: e.dma_start(out=wk[:, :], in_=cx.dram["l0_wk_s"][:, :]), writes=["w:l0k"])
    tk.dma("g", lambda e: e.dma_start(out=wv[:, :], in_=cx.dram["l0_wv_s"][:, :]), writes=["w:l0v"])
    tk.dma("g", lambda e: e.dma_start(out=wo[:, :], in_=cx.dram["l0_wo_s"][:, :]), writes=["w:l0o"])
    tk.dma("y", lambda e: e.dma_start(out=brow[0:1, 0:1024], in_=cx.dram["l0_bq"][:, :]), writes=["b:l0"])
    tk.dma("y", lambda e: e.dma_start(out=brow[0:1, 1024:1152], in_=cx.dram["l0_bk"][:, :]), writes=["b:l0"])
    tk.dma("y", lambda e: e.dma_start(out=brow[0:1, 1152:1280], in_=cx.dram["l0_bv"][:, :]), writes=["b:l0"])
    tk.dma("y", lambda e: e.dma_start(out=brow[0:1, 1280:2304], in_=cx.dram["l0_bo"][:, :]), writes=["b:l0"])
    tk.dma("y", lambda e: e.dma_start(out=kvm[:, :], in_=cx.dram["kvmask"][:, :]), writes=["b:kvm"])
    tk.dma("y", lambda e: e.dma_start(out=msk[:, :], in_=cx.dram["l0mask"][:, :]), writes=["b:msk"])

    for ti in range(5):
        c0 = ti * 512
        w = min(512, TH - c0)
        pi = ti % 2
        ps = cx.pf[pi]

        def mm_k(e, c0=c0, w=w, ps=ps):
            for kc in range(NCH):
                e.matmul(ps[:, 0:w], wk[:, kc * 128:(kc + 1) * 128],
                         _chunk(xn_sb, kc, TH, c0, c0 + w), start=(kc == 0), stop=False)
            return e.matmul(ps[:, 0:w], brow[0:1, 1024:1152], kvm[0:1, c0:c0 + w],
                            start=False, stop=True)
        tk.op("t", mm_k, reads=[xn_key, "w:l0k", "b:l0", "b:kvm"], writes=[f"ps:{pi}"])
        tk.op("s", lambda e, c0=c0, w=w, ps=ps: e.copy(kbf[:, c0:c0 + w], ps[:, 0:w]),
              reads=[f"ps:{pi}"], writes=["l0:k"])
    for j in range(18):
        c0 = j * 128
        pi = 2 + j % 2
        ps = cx.pf[pi]

        def mm_v(e, c0=c0, ps=ps):
            for kc in range(NCH):
                e.matmul(ps[:, 0:128], _chunk(xn_sb, kc, TH, c0, c0 + 128),
                         wv[:, kc * 128:(kc + 1) * 128], start=(kc == 0), stop=False)
            return e.matmul(ps[:, 0:128], kvm[0:1, c0:c0 + 128], brow[0:1, 1152:1280],
                            start=False, stop=True)
        tk.op("t", mm_v, reads=[xn_key, "w:l0v", "b:l0", "b:kvm"], writes=[f"ps:{pi}"])
        tk.op("s", lambda e, j=j, ps=ps: e.copy(vtm[:, j * 128:(j + 1) * 128], ps[:, 0:128]),
              reads=[f"ps:{pi}"], writes=["l0:v"])

    if tiles is None:
        tiles = range(16)
    if dbg is not None:
        tk.dma("y", lambda e: e.dma_start(out=dbg["k"][:, :], in_=kbf[:, :]), reads=["l0:k"], writes=["dbg"])
        tk.dma("y", lambda e: e.dma_start(out=dbg["v"][:, :], in_=vtm[:, :]), reads=["l0:v"], writes=["dbg"])
    for ti in tiles:
        par = ti % 2
        a = ti * 128
        ah = a + HALO
        for qc in range(NCH):
            pi = 4 + qc % 2
            ps = cx.pf[pi]

            def mm_qc(e, qc=qc, ah=ah, ps=ps):
                for kc in range(NCH):
                    e.matmul(ps[:, 0:128],
                             wq[:, kc * 1024 + qc * 128: kc * 1024 + (qc + 1) * 128],
                             _chunk(xn_sb, kc, TH, ah, ah + 128), start=(kc == 0), stop=False)
                return e.matmul(ps[:, 0:128], brow[0:1, qc * 128:(qc + 1) * 128],
                                cx.ones_row[0:1, 0:128], start=False, stop=True)
            tk.op("t", mm_qc, reads=[xn_key, "w:l0q", "b:l0", "c:ones_row"], writes=[f"ps:{pi}"])
            tk.op("s", lambda e, qc=qc, par=par, ps=ps: e.copy(
                qt[:, par * 1024 + qc * 128: par * 1024 + (qc + 1) * 128], ps[:, 0:128]),
                reads=[f"ps:{pi}"], writes=[f"l0:qt{par}"])
        for h in range(16):
            qc, base = h // 2, 64 * (h % 2)
            hp = h % 2
            ps_s = cx.pf[hp]
            ps_t = cx.pb[hp]
            ps_o = cx.pf[2 + hp]
            tk.op("t", lambda e, qc=qc, base=base, par=par, a=a, ps_s=ps_s: e.matmul(
                ps_s[0:128, 0:384],
                qt[base:base + 64, par * 1024 + qc * 128: par * 1024 + (qc + 1) * 128],
                kbf[base:base + 64, a:a + 384], start=True, stop=True),
                reads=[f"l0:qt{par}", "l0:k"], writes=[f"ps:{hp}"])
            if dbg is not None and ti == list(tiles)[0] and h == 0 and "sc" in dbg:
                tk.op("v", lambda e, ps_s=ps_s: e.tensor_copy(pr[:, 384:768], ps_s[:, 0:384]),
                      reads=["ps:0"], writes=["l0:scdump"])
                tk.dma("y", lambda e: e.dma_start(out=dbg["sc"][:, :], in_=pr[:, 384:768]),
                       reads=["l0:scdump"], writes=["dbg"])
            tk.op("v", lambda e, hp=hp, ps_s=ps_s: e.reduce_max(
                rowm[:, hp:hp + 1], ps_s[:, 0:384], axis=AX),
                reads=[f"ps:{hp}"], writes=[f"l0:rm{hp}"])
            tk.op("s", lambda e, hp=hp: e.mul(
                rowm[:, 2 + hp:3 + hp], rowm[:, hp:hp + 1], -1.0),
                reads=[f"l0:rm{hp}"], writes=[f"l0:nrm{hp}"])
            tk.op("s", lambda e, hp=hp, ps_s=ps_s: e.activation(
                pr[:, hp * 384:(hp + 1) * 384], ps_s[:, 0:384], AF.Exp,
                bias=rowm[:, 2 + hp:3 + hp]),
                reads=[f"ps:{hp}", f"l0:nrm{hp}"], writes=[f"l0:pr{hp}"])
            tk.op("v", lambda e, hp=hp: e.scalar_tensor_tensor(
                pr[:, hp * 384:(hp + 1) * 384], pr[:, hp * 384:(hp + 1) * 384],
                1.0, msk[:, 0:384], ALU.mult, ALU.mult,
                accum_out=rowm[:, 4 + hp:5 + hp]),
                reads=[f"l0:pr{hp}", "b:msk"], writes=[f"l0:pr{hp}", f"l0:rs{hp}"])
            tk.op("v", lambda e, hp=hp: e.reciprocal(
                rowm[:, 6 + hp:7 + hp], rowm[:, 4 + hp:5 + hp]),
                reads=[f"l0:rs{hp}"], writes=[f"l0:ri{hp}"])
            tk.op("v", lambda e, hp=hp: e.tensor_scalar(
                prn[:, hp * 384:(hp + 1) * 384], pr[:, hp * 384:(hp + 1) * 384],
                rowm[:, 6 + hp:7 + hp], None, ALU.mult),
                reads=[f"l0:pr{hp}", f"l0:ri{hp}"], writes=[f"l0:prn{hp}"])

            def mm_tr(e, hp=hp, ps_t=ps_t):
                ins = None
                for uc in range(3):
                    ins = e.transpose(ps_t[0:128, uc * 128:(uc + 1) * 128],
                                      prn[:, hp * 384 + uc * 128: hp * 384 + (uc + 1) * 128],
                                      cx.eye16[:, :])
                return ins
            tk.op("t", mm_tr, reads=[f"l0:prn{hp}", "c:eye16"], writes=[f"ps:b{hp}"])
            tk.op("s", lambda e, hp=hp, ps_t=ps_t: e.copy(
                prT[:, hp * 384:(hp + 1) * 384], ps_t[0:128, 0:384]),
                reads=[f"ps:b{hp}"], writes=[f"l0:prT{hp}"])

            def mm_pv(e, hp=hp, base=base, a=a, ps_o=ps_o):
                ins = None
                for uc in range(3):
                    t0 = a // 128 + uc
                    ins = e.matmul(
                        ps_o[0:64, 0:128],
                        vtm[:, t0 * 128 + base: t0 * 128 + base + 64],
                        prT[:, hp * 384 + uc * 128: hp * 384 + (uc + 1) * 128],
                        start=(uc == 0), stop=(uc == 2))
                return ins
            tk.op("t", mm_pv, reads=["l0:v", f"l0:prT{hp}"], writes=[f"ps:{2 + hp}"])
            tk.op("s", lambda e, qc=qc, base=base, par=par, hp=hp, ps_o=ps_o: e.copy(
                at[base:base + 64, par * 1024 + qc * 128: par * 1024 + (qc + 1) * 128],
                ps_o[0:64, 0:128]),
                reads=[f"ps:{2 + hp}"], writes=[f"l0:at{par}"])
        if dbg is not None and ti == list(tiles)[0]:
            tk.dma("y", lambda e, par=par: e.dma_start(out=dbg["qt"][:, :], in_=qt[:, par * 1024:(par + 1) * 1024]), reads=[f"l0:qt{par}"], writes=["dbg"])
            tk.dma("y", lambda e: e.dma_start(out=dbg["pr"][:, :], in_=pr[:, 0:384]), reads=["l0:pr0"], writes=["dbg"])
            tk.dma("y", lambda e: e.dma_start(out=dbg["prn"][:, :], in_=prn[:, 0:384]), reads=["l0:prn0"], writes=["dbg"])
            tk.dma("y", lambda e: e.dma_start(out=dbg["prT"][:, :], in_=prT[:, 0:384]), reads=["l0:prT0"], writes=["dbg"])
            tk.dma("y", lambda e, par=par: e.dma_start(out=dbg["at"][:, :], in_=at[:, par * 1024:(par + 1) * 1024]), reads=[f"l0:at{par}"], writes=["dbg"])
            tk.dma("y", lambda e: e.dma_start(out=dbg["rowm"][:, :], in_=rowm[:, :]), reads=["l0:rs0", "l0:ri0", "l0:rm0"], writes=["dbg"])
        for oc in range(NCH):
            pi = 4 + oc % 2
            ps = cx.pf[pi]

            def mm_wo(e, oc=oc, par=par, ps=ps):
                for kc in range(NCH):
                    e.matmul(ps[:, 0:128],
                             wo[:, kc * 1024 + oc * 128: kc * 1024 + (oc + 1) * 128],
                             at[:, par * 1024 + kc * 128: par * 1024 + (kc + 1) * 128],
                             start=(kc == 0), stop=False)
                return e.matmul(ps[:, 0:128], brow[0:1, 1280 + oc * 128:1280 + (oc + 1) * 128],
                                cx.ones_row[0:1, 0:128], start=False, stop=True)
            tk.op("t", mm_wo, reads=[f"l0:at{par}", "w:l0o", "b:l0", "c:ones_row"],
                  writes=[f"ps:{pi}"])
            tk.op("v", lambda e, oc=oc, a=a, ps=ps, pi=pi: e.tensor_add(
                _chunk(x_sb, oc, TPC, a, a + 128), _chunk(x_sb, oc, TPC, a, a + 128),
                ps[:, 0:128]),
                reads=[f"ps:{pi}", x_key], writes=[x_key])



L0A_SHARED = ["l0_wq_s", "l0_bq", "l0_wk_s", "l0_bk", "l0_wv_s", "l0_bv", "l0_wo_s", "l0_bo",
              "eye", "ones_row", "l0mask"]


def build_l0a():
    nc = bass.Bass("TRN2", target_bir_lowering=False)
    es = contextlib.ExitStack()
    with es:
        x_in = nc.dram_tensor("x_fm", [DM, TH], F32, kind="ExternalInput").ap()
        kvm_in = nc.dram_tensor("kvmask", [1, TH], F32, kind="ExternalInput").ap()
        xn_out = nc.dram_tensor("xn_out", [DM, TH], BF16, kind="ExternalOutput").ap()
        x1p_out = nc.dram_tensor("x1p_out", [DM, TPC], F32, kind="ExternalOutput").ap()

        block = es.enter_context(nc.Block())
        cx = _setup_common(nc, es, block, _SHARED_CACHE, L0A_SHARED)
        cx.dram["kvmask"] = kvm_in
        tk = cx.tk

        x_sb = cx.sb("x_sb", [128, NCH * TPC], F32)
        xh_sb = cx.sb("xh_sb", [128, NCH * 256], F32)
        xn_sb = cx.sb("xn_sb", [128, NCH * TH], BF16)

        for kc in range(NCH):
            tk.dma("y", lambda e, kc=kc: e.dma_start(
                out=_chunk(x_sb, kc, TPC), in_=x_in[kc * 128:(kc + 1) * 128, HALO:HALO + TPC]),
                writes=["x"])
            tk.dma("y", lambda e, kc=kc: e.dma_start(
                out=_chunk(xh_sb, kc, 256, 0, 128), in_=x_in[kc * 128:(kc + 1) * 128, 0:HALO]),
                writes=["xh"])
            tk.dma("y", lambda e, kc=kc: e.dma_start(
                out=_chunk(xh_sb, kc, 256, 128, 256),
                in_=x_in[kc * 128:(kc + 1) * 128, TH - HALO:TH]),
                writes=["xh"])

        emit_ln(cx, x_sb, "x", TPC, 0, TPC, xn_sb, "xn", TH, HALO)
        emit_ln(cx, xh_sb, "xh", 256, 0, 128, xn_sb, "xn", TH, 0)
        emit_ln(cx, xh_sb, "xh", 256, 128, 128, xn_sb, "xn", TH, TH - HALO)

        emit_l0_attn(cx, xn_sb, "xn", x_sb, "x")

        for kc in range(NCH):
            tk.dma("y", lambda e, kc=kc: e.dma_start(
                out=xn_out[kc * 128:(kc + 1) * 128, :], in_=_chunk(xn_sb, kc, TH)),
                reads=["xn"], writes=["o:xn"])
            tk.dma("y", lambda e, kc=kc: e.dma_start(
                out=x1p_out[kc * 128:(kc + 1) * 128, :], in_=_chunk(x_sb, kc, TPC)),
                reads=["x"], writes=["o:x1p"])

        n = tk.dcnt["y"]
        tk.flush()
        block.sync(lambda eng: eng.wait_ge(tk.dsem["y"], 16 * n))
    return nc




MOE_SHARED = ["moe_scale", "moe_wg_s", "moe_bg", "moe_w1_s", "moe_b1c", "moe_w2_s", "moe_b2",
              "eye", "ones_row"]


def emit_moe(cx, xn_sb, xn_key, xnw, xc0, x_sb, x_key, experts=range(MOE_E), do_h=True, do_w2=True):
    nc, tk, sb = cx.nc, cx.tk, cx.sb
    E = MOE_E

    def xch(kc, a, b):
        return xn_sb[:, kc * xnw + xc0 + a: kc * xnw + xc0 + b]

    wgb = sb("moe_wg_sb", [128, NCH * 16], BF16)
    bgr = sb("moe_bg_sb", [1, 16], F32)
    sclc = sb("moe_scl_sb", [128, NCH], F32)
    b2sb = sb("moe_b2_sb", [16, 1024], BF16)
    w1b = sb("moe_w1_sb", [128, 2 * NCH * 256], BF16)
    b1c = sb("moe_b1c_sb", [128, 16], F32)
    b1w = sb("moe_b1w_sb", [128, 16], F32)
    w2b = sb("moe_w2_sb", [128, NCH * 1024], BF16)
    ssb = sb("moe_s_sb", [128, 8 * 1024], BF16)
    acc = sb("moe_acc", [128, 8 * 1024], F32)
    gsc = sb("moe_gsc", [128, 64], F32)
    lgs = sb("moe_lg", [128, 16], F32)
    gts = sb("moe_gts", [128, 16 * 16], F32)
    gTb = sb("moe_gT", [16, 16 * 128], BF16)
    msc = sb("moe_msc", [128, 4 * 512], F32)
    c7 = sb("moe_c7", [128, 1], F32)
    tk.op("v", lambda e: e.memset(c7[:, :], 7.0), writes=["moe:c7"])

    tk.dma("g", lambda e: e.dma_start(out=wgb[:, :], in_=cx.dram["moe_wg_s"][:, :]), writes=["w:wg"])
    tk.dma("y", lambda e: e.dma_start(out=bgr[:, :], in_=cx.dram["moe_bg"][:, :]), writes=["b:bg"])
    tk.dma("y", lambda e: e.dma_start(out=sclc[:, :], in_=cx.dram["moe_scale"][:, :]), writes=["b:scl"])
    tk.dma("g", lambda e: e.dma_start(out=b2sb[:, :], in_=cx.dram["moe_b2"][:, :]), writes=["w:b2"])

    for kc in range(NCH):
        for ti in range(4):
            c0 = ti * 512
            tk.op("v", lambda e, kc=kc, c0=c0: e.tensor_add(
                _chunk(x_sb, kc, TPC, c0, c0 + 512), _chunk(x_sb, kc, TPC, c0, c0 + 512),
                xch(kc, c0, c0 + 512)),
                reads=[x_key, xn_key], writes=[x_key])

    scr, rows = cx.ln_scr, cx.ln_rows
    for ti in range(4):
        c0 = ti * 512
        w = 512
        ps_q, ps_b = cx.pf[1], cx.pf[2]
        for kc in range(NCH):
            par = kc % 2
            tk.op("s", lambda e, kc=kc, par=par, c0=c0, w=w: e.activation(
                scr[:, par * 512:par * 512 + w], xch(kc, c0, c0 + w), AF.Square),
                reads=[xn_key], writes=[f"ln:scr{par}"])
            tk.op("t", lambda e, kc=kc, par=par, w=w: e.matmul(
                ps_q[0:1, 0:w], cx.ln_ones_col[:, :], scr[:, par * 512:par * 512 + w],
                start=(kc == 0), stop=(kc == NCH - 1)),
                reads=[f"ln:scr{par}", "ln:ones_col"], writes=["ps:1"])
        tk.op("v", lambda e, w=w: e.tensor_scalar(
            rows[0:1, 0:w], ps_q[0:1, 0:w], 1.0 / DM, None, ALU.mult),
            reads=["ps:1"], writes=["ln:m"])
        tk.op("s", lambda e, w=w: e.activation(
            rows[0:1, 512:512 + w], rows[0:1, 0:w], AF.Sqrt, bias=cx.ln_eps[0:1, 0:1]),
            reads=["ln:m", "ln:eps"], writes=["ln:sd"])
        tk.op("v", lambda e, w=w: e.reciprocal(
            rows[0:1, 512:512 + w], rows[0:1, 512:512 + w]),
            reads=["ln:sd"], writes=["ln:sd"])
        tk.op("t", lambda e, w=w: e.matmul(
            ps_b[:, 0:w], cx.ones_row[0:1, 0:128], rows[0:1, 512:512 + w],
            start=True, stop=True), reads=["ln:sd", "c:ones_row"], writes=["ps:2"])
        for kc in range(NCH):
            par = kc % 2
            tk.op("v", lambda e, kc=kc, par=par, c0=c0, w=w: e.tensor_mul(
                scr[:, par * 512:par * 512 + w], xch(kc, c0, c0 + w), ps_b[:, 0:w]),
                reads=[xn_key, "ps:2"], writes=[f"ln:scr{par}"])
            tk.op("v", lambda e, kc=kc, par=par, c0=c0, w=w: e.tensor_scalar(
                xch(kc, c0, c0 + w), scr[:, par * 512:par * 512 + w],
                sclc[:, kc:kc + 1], None, ALU.mult),
                reads=[f"ln:scr{par}", "b:scl"], writes=[xn_key])

    for tg in range(16):
        t0 = tg * 128
        ps = cx.pf[3]

        def mm_lg(e, t0=t0, ps=ps):
            for kc in range(NCH):
                e.matmul(ps[:, 0:16], xch(kc, t0, t0 + 128), wgb[:, kc * 16:(kc + 1) * 16],
                         start=(kc == 0), stop=False)
            return e.matmul(ps[:, 0:16], cx.ones_row[0:1, 0:128], bgr[0:1, 0:16],
                            start=False, stop=True)
        tk.op("t", mm_lg, reads=[xn_key, "w:wg", "b:bg", "c:ones_row"], writes=["ps:3"])
        tk.op("v", lambda e, ps=ps: e.tensor_copy(lgs[:, :], ps[:, 0:16]),
              reads=["ps:3"], writes=["moe:lg"])
        tk.op("v", lambda e: e.tensor_copy(gsc[:, 0:16], lgs[:, :]),
              reads=["moe:lg"], writes=["moe:cur"])
        tk.op("v", lambda e: e.reduce_max(gsc[:, 32:33], gsc[:, 0:16], axis=AX),
              reads=["moe:cur"], writes=["moe:m1"])
        tk.op("s", lambda e: e.mul(gsc[:, 34:35], gsc[:, 32:33], -1.0),
              reads=["moe:m1"], writes=["moe:nm1"])
        for it in range(3):
            mkc = 32 if it == 0 else 33
            tk.op("v", lambda e, mkc=mkc: e.tensor_scalar(
                gsc[:, 16:32], gsc[:, 0:16], gsc[:, mkc:mkc + 1], None, ALU.is_ge),
                reads=["moe:cur", "moe:m1" if mkc == 32 else "moe:mk"], writes=["moe:mask"])
            tk.op("v", lambda e: e.scalar_tensor_tensor(
                gsc[:, 0:16], gsc[:, 16:32], -1.0e9, gsc[:, 0:16], ALU.mult, ALU.add),
                reads=["moe:mask", "moe:cur"], writes=["moe:cur"])
            tk.op("v", lambda e: e.reduce_max(gsc[:, 33:34], gsc[:, 0:16], axis=AX),
                  reads=["moe:cur"], writes=["moe:mk"])
        tk.op("v", lambda e: e.tensor_scalar(
            gsc[:, 16:32], lgs[:, :], gsc[:, 33:34], None, ALU.is_ge),
            reads=["moe:lg", "moe:mk"], writes=["moe:mask"])
        tk.op("s", lambda e: e.activation(
            gsc[:, 0:16], lgs[:, :], AF.Exp, bias=gsc[:, 34:35]),
            reads=["moe:lg", "moe:nm1"], writes=["moe:cur"])
        tk.op("v", lambda e: e.scalar_tensor_tensor(
            gsc[:, 0:16], gsc[:, 0:16], 1.0, gsc[:, 16:32], ALU.mult, ALU.mult,
            accum_out=gsc[:, 35:36]),
            reads=["moe:cur", "moe:mask"], writes=["moe:cur", "moe:zs"])
        tk.op("v", lambda e: e.reciprocal(gsc[:, 36:37], gsc[:, 35:36]),
              reads=["moe:zs"], writes=["moe:zi"])
        tk.op("v", lambda e, tg=tg: e.tensor_scalar(
            gts[:, tg * 16:(tg + 1) * 16], gsc[:, 0:16], gsc[:, 36:37], None, ALU.mult),
            reads=["moe:cur", "moe:zi"], writes=["moe:gts"])
        tk.op("t", lambda e, tg=tg, ps=ps: e.transpose(
            ps[0:16, 0:128], gts[:, tg * 16:(tg + 1) * 16], cx.eye[:, :]),
            reads=["moe:gts", "c:eye"], writes=["ps:3"])
        tk.op("s", lambda e, tg=tg, ps=ps: e.copy(
            gTb[0:16, tg * 128:(tg + 1) * 128], ps[0:16, 0:128]),
            reads=["ps:3"], writes=["moe:gT"])

    if getattr(cx, "moe_dbg", None) is not None:
        tk.dma("y", lambda e: e.dma_start(out=cx.moe_dbg[0][:, :], in_=gts[:, :]),
               reads=["moe:gts"], writes=["dbg"])
    for half in range(2):
        hoff = half * 1024
        for tsl in range(8):
            tg = half * 8 + tsl
            for h2 in range(2):
                ps = cx.pf[4 + h2 % 2]
                tk.op("t", lambda e, tg=tg, h2=h2, ps=ps: e.matmul(
                    ps[:, 0:512], gTb[0:16, tg * 128:(tg + 1) * 128],
                    b2sb[0:16, h2 * 512:(h2 + 1) * 512], start=True, stop=True),
                    reads=["moe:gT", "w:b2"], writes=[f"ps:{4 + h2 % 2}"])
                tk.op("v", lambda e, tsl=tsl, h2=h2, ps=ps: e.tensor_copy(
                    acc[:, tsl * 1024 + h2 * 512: tsl * 1024 + (h2 + 1) * 512],
                    ps[:, 0:512]),
                    reads=[f"ps:{4 + h2 % 2}"], writes=["moe:acc"])
        for ex in experts:
            wpar = ex % 2
            tk.dma("g", lambda e, ex=ex: e.dma_start(
                out=w2b[:, :], in_=cx.dram["moe_w2_s"][ex]),
                writes=["w:w2"])
            tk.dma("y", lambda e, ex=ex: e.dma_start(
                out=b1c[:, :], in_=cx.dram["moe_b1c"][ex]),
                writes=["b:b1"])
            tk.op("v", lambda e: e.tensor_scalar(
                b1w[:, :], b1c[:, :], -1.0, 7.0, ALU.mult, ALU.add),
                reads=["b:b1"], writes=["b:b1w"])
            for j in (range(8) if do_h else []):
                jpar = (ex * 8 + j) % 2
                tk.dma("g", lambda e, ex=ex, j=j, jpar=jpar: e.dma_start(
                    out=w1b[:, jpar * 2048:(jpar + 1) * 2048],
                    in_=cx.dram["moe_w1_s"][ex, j]),
                    writes=[f"w:w1{jpar}"])
                for ttl in range(2):
                    tc0 = hoff + ttl * 512
                    ppar = ttl % 2
                    ps_a, ps_b2 = cx.pf[ppar], cx.pf[2 + ppar]

                    def mm_h(e, j=j, jpar=jpar, wpar=wpar, tc0=tc0, ps_a=ps_a, ps_b2=ps_b2):
                        for kc in range(NCH):
                            e.matmul(ps_a[:, 0:512],
                                     w1b[:, jpar * 2048 + kc * 256: jpar * 2048 + kc * 256 + 128],
                                     xch(kc, tc0, tc0 + 512), start=(kc == 0), stop=(kc == NCH - 1))
                        ins = None
                        for kc in range(NCH):
                            ins = e.matmul(ps_b2[:, 0:512],
                                           w1b[:, jpar * 2048 + kc * 256 + 128: jpar * 2048 + (kc + 1) * 256],
                                           xch(kc, tc0, tc0 + 512), start=(kc == 0), stop=(kc == NCH - 1))
                        return ins
                    tk.op("t", mm_h, reads=[xn_key, f"w:w1{jpar}"],
                          writes=[f"ps:{ppar}", f"ps:{2 + ppar}"])
                    sa = 1024 * ppar
                    tk.op("v", lambda e, sa=sa, j=j, ps_a=ps_a: e.tensor_scalar(
                        msc[:, sa:sa + 512], ps_a[:, 0:512], b1c[:, 2 * j:2 * j + 1], 7.0,
                        ALU.add, ALU.min),
                        reads=[f"ps:{ppar}", "b:b1"], writes=[f"moe:s0{ppar}"])
                    tk.op("s", lambda e, sa=sa: e.activation(
                        msc[:, sa + 512:sa + 1024], msc[:, sa:sa + 512], AF.Silu, scale=GLU_ALPHA),
                        reads=[f"moe:s0{ppar}"], writes=[f"moe:s1{ppar}"])
                    tk.op("s", lambda e, sa=sa, j=j, ps_b2=ps_b2: e.activation(
                        msc[:, sa:sa + 512], ps_b2[:, 0:512], AF.Relu,
                        bias=b1w[:, 2 * j + 1:2 * j + 2], scale=-1.0),
                        reads=[f"ps:{2 + ppar}", "b:b1w", f"moe:s0{ppar}"], writes=[f"moe:s0{ppar}"])
                    tk.op("v", lambda e, sa=sa: e.tensor_scalar(
                        msc[:, sa:sa + 512], msc[:, sa:sa + 512],
                        14.0, -1.0 / GLU_ALPHA, ALU.min, ALU.mult),
                        reads=[f"moe:s0{ppar}"], writes=[f"moe:s0{ppar}"])
                    tk.op("v", lambda e, sa=sa, j=j, tc0=tc0, ttl=ttl: e.scalar_tensor_tensor(
                        ssb[:, j * 1024 + ttl * 512: j * 1024 + (ttl + 1) * 512],
                        msc[:, sa:sa + 512], 8.0 / GLU_ALPHA,
                        msc[:, sa + 512:sa + 1024], ALU.add, ALU.mult),
                        reads=[f"moe:s0{ppar}", f"moe:s1{ppar}"], writes=["moe:s"])
            for ttl in (range(2) if do_w2 else []):
                for tsub in range(4):
                    tsl = ttl * 4 + tsub
                    tg = half * 8 + tsl
                    for h2 in range(2):
                        ppar = (tsl * 2 + h2) % 2
                        ps = cx.pf[4 + ppar]

                        def mm_y(e, ttl=ttl, tsub=tsub, h2=h2, wpar=wpar, ps=ps):
                            ins = None
                            for j in range(8):
                                ins = e.matmul(
                                    ps[:, 0:512],
                                    ssb[:, j * 1024 + ttl * 512 + tsub * 128: j * 1024 + ttl * 512 + (tsub + 1) * 128],
                                    w2b[:, j * 1024 + h2 * 512: j * 1024 + (h2 + 1) * 512],
                                    start=(j == 0), stop=(j == 7))
                            return ins
                        tk.op("t", mm_y, reads=["moe:s", "w:w2"], writes=[f"ps:{4 + ppar}"])
                        tk.op("v", lambda e, tsl=tsl, tg=tg, h2=h2, ex=ex, ps=ps: e.scalar_tensor_tensor(
                            acc[:, tsl * 1024 + h2 * 512: tsl * 1024 + (h2 + 1) * 512],
                            ps[:, 0:512], gts[:, tg * 16 + ex: tg * 16 + ex + 1],
                            acc[:, tsl * 1024 + h2 * 512: tsl * 1024 + (h2 + 1) * 512],
                            ALU.mult, ALU.add),
                            reads=[f"ps:{4 + ppar}", "moe:gts", "moe:acc"], writes=["moe:acc"])
        for tsl in range(8):
            for oc in range(NCH):
                ppar = (tsl * 8 + oc) % 2
                ps = cx.pf[ppar]
                tk.op("t", lambda e, tsl=tsl, oc=oc, ps=ps: e.transpose(
                    ps[0:128, 0:128], acc[:, tsl * 1024 + oc * 128: tsl * 1024 + (oc + 1) * 128],
                    cx.eye[:, :]),
                    reads=["moe:acc", "c:eye"], writes=[f"ps:{ppar}"])
                tk.op("v", lambda e, tsl=tsl, oc=oc, hoff=hoff, ps=ps, ppar=ppar: e.tensor_add(
                    _chunk(x_sb, oc, TPC, hoff + tsl * 128, hoff + (tsl + 1) * 128),
                    _chunk(x_sb, oc, TPC, hoff + tsl * 128, hoff + (tsl + 1) * 128),
                    ps[0:128, 0:128]),
                    reads=[f"ps:{ppar}", x_key], writes=[x_key])


def build_l0b():
    nc = bass.Bass("TRN2", target_bir_lowering=False)
    es = contextlib.ExitStack()
    with es:
        xn_in = nc.dram_tensor("xn_in", [DM, TPC], BF16, kind="ExternalInput").ap()
        x1p_in = nc.dram_tensor("x1p_in", [DM, TPC], F32, kind="ExternalInput").ap()
        x1_out = nc.dram_tensor("x1_out", [DM, TPC], F32, kind="ExternalOutput").ap()

        block = es.enter_context(nc.Block())
        cx = _setup_common(nc, es, block, _SHARED_CACHE, MOE_SHARED)
        tk = cx.tk
        cx.ln_scr = cx.sb("ln_scr", [128, 2 * 512], F32)
        cx.ln_rows = cx.sb("ln_rows", [1, 2 * 512], F32)
        cx.ln_ones_col = cx.sb("ln_ones_col", [128, 1], F32)
        cx.ln_eps = cx.sb("ln_eps", [1, 1], F32)
        tk.op("v", lambda e: e.memset(cx.ln_ones_col[:, :], 1.0), writes=["ln:ones_col"])
        tk.op("v", lambda e: e.memset(cx.ln_eps[:, :], 1e-5), writes=["ln:eps"])
        cx.ln_init = True

        x_sb = cx.sb("x_sb", [128, NCH * TPC], F32)
        xn_sb = cx.sb("xn_sb", [128, NCH * TPC], BF16)
        for kc in range(NCH):
            tk.dma("y", lambda e, kc=kc: e.dma_start(
                out=_chunk(x_sb, kc, TPC), in_=x1p_in[kc * 128:(kc + 1) * 128, :]), writes=["x"])
            tk.dma("y", lambda e, kc=kc: e.dma_start(
                out=_chunk(xn_sb, kc, TPC), in_=xn_in[kc * 128:(kc + 1) * 128, :]), writes=["xn"])

        emit_moe(cx, xn_sb, "xn", TPC, 0, x_sb, "x")

        for kc in range(NCH):
            tk.dma("y", lambda e, kc=kc: e.dma_start(
                out=x1_out[kc * 128:(kc + 1) * 128, :], in_=_chunk(x_sb, kc, TPC)),
                reads=["x"], writes=["o:x1"])
        n = tk.dcnt["y"]
        tk.flush()
        block.sync(lambda eng: eng.wait_ge(tk.dsem["y"], 16 * n))
    return nc




L1A_SHARED = ["l1_wq_s", "l1_bq", "l1_wk_s", "l1_bk", "l1_wv_s", "l1_bv", "l1_wo_s", "l1_bo",
              "l1_wfc_s", "l1_bfc", "l1_projT", "eye", "ones_row"]

CTXW = 3 * 65
STG_W = 16 * CTXW


def emit_l1_attn(cx, x_in_ap, xn_sb, xn_key, xnw, xc0, y0_sb, y0_key, core_dram, x_dep=(), cc=True):
    nc, tk, sb = cx.nc, cx.tk, cx.sb
    DN, RATIO, EPS = L1_DN, L1_RATIO, L1_EPS

    def xch(kc, a, b):
        return xn_sb[:, kc * xnw + xc0 + a: kc * xnw + xc0 + b]

    RG = [[0, 1, 2, 3], [4, 5, 6, 7]]
    ccm_in = nc.dram_tensor("ccm_in", [1, 16], F32)
    ccm_out = nc.dram_tensor("ccm_out", [1, 16], F32)
    ccs_in = nc.dram_tensor("ccs_in", [128, STG_W], F32)
    ccs_out = nc.dram_tensor("ccs_out", [128, STG_W], F32)

    wA = sb("l1_wA", [128, NCH * 1024], BF16)
    wB = sb("l1_wB", [128, NCH * 1024], BF16)
    br = sb("l1_br", [1, 2048], BF16)
    projT = sb("l1_projT_sb", [128, 266], BF16)
    vtm = sb("l1_v", [128, 16 * 1024], BF16)
    kbuf = sb("l1_kbuf", [128, NCH * TPC], BF16)
    xt = sb("l1_xt", [128, NCH * 512], F32)
    cx.l1_xt = xt
    diagk = sb("l1_diagk", [128, 256], F32)
    colmax = sb("l1_colmax", [128, 16], F32)
    stg = sb("l1_stg", [128, STG_W], F32)
    cx.l1_stg_dbg = stg
    stg2 = sb("l1_stg2", [16, 256], F32)
    ctxb = sb("l1_ctx", [128, 16 * 192], BF16)
    ksumb = sb("l1_ksum", [128, 48], BF16)
    mxb = sb("l1_mxb", [128, 16], F32)
    rsc = sb("l1_rsc", [128, 16], F32)
    rrow = sb("l1_rrow", [1, 512], F32)
    exs = sb("l1_exs", [128, 2 * 266], F32)
    kpb = sb("l1_kpb", [128, 2 * 266], BF16)
    qpT = sb("l1_qpT", [128, 2 * 384], BF16)
    qps = sb("l1_qps", [128, 2 * 384], BF16)
    sq64 = sb("l1_sq64", [128, 2 * 128], F32)
    qt = sb("l1_qt", [128, 2 * 1024], BF16)
    ath = sb("l1_ath", [128, 2 * 1024], BF16)
    msel = sb("l1_msel", [1, 16], F32)

    tk.dma("g", lambda e: e.dma_start(out=projT[:, :], in_=cx.dram["l1_projT"][:, :]), writes=["l1:projT"])

    for tt in range(4):
        c0 = tt * 512
        for kc in range(NCH):
            tk.dma("y", lambda e, kc=kc, c0=c0: e.dma_start(
                out=_chunk(xt, kc, 512), in_=x_in_ap[kc * 128:(kc + 1) * 128, c0:c0 + 512]),
                reads=list(x_dep), writes=["l1:xt"])
        emit_ln(cx, xt, "l1:xt", 512, 0, 512, xn_sb, xn_key, xnw, xc0 + c0)

    tk.dma("g", lambda e: e.dma_start(
        out=wA[:, :], in_=cx.dram["l1_wv_s"][:, :]), writes=["w:A"])
    tk.dma("g", lambda e: e.dma_start(out=br[0:1, 0:1024], in_=cx.dram["l1_bv"][:, :]), writes=["b:r0"])
    for j in range(16):
        c0 = j * 128
        for vh in range(2):
            pi = vh % 2
            ps = cx.pf[pi]

            def mm_v(e, c0=c0, vh=vh, ps=ps):
                for kc in range(NCH):
                    e.matmul(ps[:, 0:512], xch(kc, c0, c0 + 128),
                             wA[:, kc * 1024 + vh * 512: kc * 1024 + (vh + 1) * 512],
                             start=(kc == 0), stop=False)
                return e.matmul(ps[:, 0:512], cx.ones_row16[0:1, 0:128],
                                br[0:1, vh * 512:(vh + 1) * 512], start=False, stop=True)
            tk.op("t", mm_v, reads=[xn_key, "w:A", "b:r0", "c:ones_row16"], writes=[f"ps:{pi}"])
            tk.op("s", lambda e, j=j, vh=vh, ps=ps: e.copy(
                vtm[:, j * 1024 + vh * 512: j * 1024 + (vh + 1) * 512], ps[:, 0:512]),
                reads=[f"ps:{pi}"], writes=["l1:v"])

    tk.dma("g", lambda e: e.dma_start(
        out=wA[:, :], in_=cx.dram["l1_wk_s"][:, :]), writes=["w:A"])
    tk.dma("g", lambda e: e.dma_start(out=br[0:1, 0:1024], in_=cx.dram["l1_bk"][:, :]), writes=["b:r0"])
    for ti4 in range(4):
        c0 = ti4 * 512
        for oc in range(NCH):
            pi = 2 + oc % 2
            ps = cx.pf[pi]

            def mm_k(e, c0=c0, oc=oc, ps=ps):
                for kc in range(NCH):
                    e.matmul(ps[:, 0:512], wA[:, kc * 1024 + oc * 128: kc * 1024 + (oc + 1) * 128],
                             xch(kc, c0, c0 + 512), start=(kc == 0), stop=False)
                return e.matmul(ps[:, 0:512], br[0:1, oc * 128:(oc + 1) * 128],
                                cx.ones_row16[0:1, 0:512], start=False, stop=True)
            tk.op("t", mm_k, reads=[xn_key, "w:A", "b:r0", "c:ones_row16"], writes=[f"ps:{pi}"])
            tk.op("s", lambda e, c0=c0, oc=oc, ps=ps: e.copy(
                kbuf[:, oc * TPC + c0: oc * TPC + c0 + 512], ps[:, 0:512]),
                reads=[f"ps:{pi}"], writes=["kbuf"])

    tk.op("v", lambda e: e.memset(colmax[:, :], -3.0e38), writes=["l1:colmax"])
    tk.op("v", lambda e: e.memset(stg[:, :], 0.0), writes=["l1:stg"])
    for ti in range(16):
        for h in range(16):
            base = 64 * (h % 2)
            kcol = (h // 2) * TPC + ti * 128
            pi = (ti * 16 + h) % 2
            ps_td = cx.pf[pi]
            ps_d = cx.pf[2 + pi]
            tk.op("t", lambda e, base=base, kcol=kcol, ps_td=ps_td: e.matmul(
                ps_td[0:128, 0:266], kbuf[base:base + 64, kcol:kcol + 128],
                projT[base:base + 64, 0:266], start=True, stop=True),
                reads=["kbuf", "l1:projT"], writes=[f"ps:{pi}"])
            tk.op("v", lambda e, pi=pi, ps_td=ps_td: e.reduce_max(
                rsc[:, pi:pi + 1], ps_td[:, 0:266], axis=AX),
                reads=[f"ps:{pi}"], writes=[f"l1:rm{pi}"])
            tk.op("v", lambda e, h=h, pi=pi: e.tensor_max(
                colmax[:, h:h + 1], colmax[:, h:h + 1], rsc[:, pi:pi + 1]),
                reads=[f"l1:rm{pi}", "l1:colmax"], writes=["l1:colmax"])
            tk.op("s", lambda e, base=base, kcol=kcol, pi=pi: e.activation(
                sq64[0:64, pi * 128:pi * 128 + 128], kbuf[base:base + 64, kcol:kcol + 128],
                AF.Square),
                reads=["kbuf"], writes=[f"l1:sq{pi}"])
            tk.op("t", lambda e, pi=pi, ps_d=ps_d: e.matmul(
                ps_d[0:128, 0:1], sq64[0:64, pi * 128:pi * 128 + 128], cx.ln_ones_col[0:64, 0:1],
                start=True, stop=True),
                reads=[f"l1:sq{pi}", "ln:ones_col"], writes=[f"ps:{2 + pi}"])
            tk.op("v", lambda e, ti=ti, h=h, pi=pi, ps_d=ps_d: e.tensor_scalar(
                diagk[:, ti * 16 + h: ti * 16 + h + 1], ps_d[:, 0:1], 0.5 * DN * DN, None,
                ALU.mult),
                reads=[f"ps:{2 + pi}"], writes=["l1:diagk"])

    tk.op("t", lambda e: e.transpose(cx.pf[4][0:16, 0:128], colmax[:, :], cx.eye[:, :]),
          reads=["l1:colmax", "c:eye"], writes=["ps:4"])
    tk.op("v", lambda e: e.tensor_copy(stg2[0:16, 0:128], cx.pf[4][0:16, 0:128]),
          reads=["ps:4"], writes=["l1:stg2"])
    tk.op("v", lambda e: e.reduce_max(stg2[0:16, 128:129], stg2[0:16, 0:128], axis=AX),
          reads=["l1:stg2"], writes=["l1:stg2"])
    tk.op("t", lambda e: e.transpose(cx.pf[4][0:1, 0:16], stg2[0:16, 128:129], cx.eye[0:16, 0:16]),
          reads=["l1:stg2", "c:eye"], writes=["ps:4"])
    tk.op("v", lambda e: e.tensor_copy(msel[0:1, 0:16], cx.pf[4][0:1, 0:16]),
          reads=["ps:4"], writes=["l1:msel"])
    if cc:
        tk.dma("g", lambda e: e.dma_start(out=ccm_in.ap()[:, :], in_=msel[0:1, 0:16]),
               reads=["l1:msel"], writes=["d:ccm_in"])
        tk.barrier()
        tk.op("g", lambda e: e.collective_compute(
            "AllReduce", ALU.max, replica_groups=RG,
            ins=[ccm_in.ap().opt()], outs=[ccm_out.ap().opt()]),
            reads=["d:ccm_in"], writes=["d:ccm_out"])
        tk.dma("g", lambda e: e.dma_start(out=msel[0:1, 0:16], in_=ccm_out.ap()[0:1, :]),
               reads=["d:ccm_out"], writes=["l1:msel"])
    for h in range(16):
        tk.op("t", lambda e, h=h: e.matmul(
            cx.pf[4][0:128, 0:1], cx.ones_row[0:1, 0:128], msel[0:1, h:h + 1],
            start=True, stop=True),
            reads=["l1:msel", "c:ones_row"], writes=["ps:4"])
        tk.op("v", lambda e, h=h: e.tensor_scalar(
            mxb[:, h:h + 1], cx.pf[4][0:128, 0:1], DN, None, ALU.mult),
            reads=["ps:4"], writes=["l1:mxb"])

    for h in range(16):
        base = 64 * (h % 2)
        ps_ks = cx.pf[4]
        for ti in range(16):
            kcol = (h // 2) * TPC + ti * 128
            pi = ti % 2
            ps_td = cx.pf[pi]
            tk.op("t", lambda e, base=base, kcol=kcol, ps_td=ps_td: e.matmul(
                ps_td[0:128, 0:266], kbuf[base:base + 64, kcol:kcol + 128],
                projT[base:base + 64, 0:266], start=True, stop=True),
                reads=["kbuf", "l1:projT"], writes=[f"ps:{pi}"])
            tk.op("v", lambda e, ti=ti, h=h, pi=pi: e.scalar_tensor_tensor(
                rsc[:, 2 + pi:3 + pi], diagk[:, ti * 16 + h: ti * 16 + h + 1], -1.0,
                mxb[:, h:h + 1], ALU.mult, ALU.subtract),
                reads=["l1:diagk", "l1:mxb"], writes=[f"l1:bc{pi}"])
            tk.op("s", lambda e, pi=pi, ps_td=ps_td: e.activation(
                exs[:, pi * 266:(pi + 1) * 266], ps_td[:, 0:266], AF.Exp,
                bias=rsc[:, 2 + pi:3 + pi], scale=DN),
                reads=[f"ps:{pi}", f"l1:bc{pi}"], writes=[f"l1:ex{pi}"])
            tk.op("v", lambda e, pi=pi: e.tensor_scalar(
                kpb[:, pi * 266:(pi + 1) * 266], exs[:, pi * 266:(pi + 1) * 266],
                RATIO, RATIO * EPS, ALU.mult, ALU.add),
                reads=[f"l1:ex{pi}"], writes=[f"l1:kp{pi}"])

            CTX_BANK = [2, 3, 5]

            def mm_acc2(e, ti=ti, h=h, pi=pi, ps_ks=ps_ks):
                st = (ti == 0)
                sp = (ti == 15)
                e.matmul(ps_ks[0:1, 0:266], cx.ones_col16[:, 0:1],
                         kpb[:, pi * 266:(pi + 1) * 266], start=st, stop=sp)
                ins = None
                for mc in range(3):
                    msz = 128 if mc < 2 else 10
                    ins = e.matmul(cx.pf[CTX_BANK[mc]][0:msz, 0:64],
                                   kpb[:, pi * 266 + mc * 128: pi * 266 + mc * 128 + msz],
                                   vtm[:, ti * 1024 + h * 64: ti * 1024 + (h + 1) * 64],
                                   start=st, stop=sp)
                return ins
            tk.op("t", mm_acc2,
                  reads=[f"l1:kp{pi}", "l1:v", "c:ones_col16", "ps:4", "ps:2", "ps:3", "ps:5"],
                  writes=["ps:4", "ps:2", "ps:3", "ps:5"])
        tk.op("v", lambda e, h=h: e.tensor_copy(rrow[0:1, 32:298], ps_ks[0:1, 0:266]),
              reads=["ps:4"], writes=["l1:ksrow"])
        for mc in range(3):
            msz = 128 if mc < 2 else 10
            col = h * CTXW + mc * 65
            bk = [2, 3, 5][mc]
            tk.op("v", lambda e, mc=mc, msz=msz, col=col, bk=bk: e.tensor_copy(
                stg[0:msz, col:col + 64], cx.pf[bk][0:msz, 0:64]),
                reads=[f"ps:{bk}"], writes=["l1:stg"])
            tk.op("t", lambda e, mc=mc, msz=msz: e.matmul(
                cx.pf[0][0:msz, 0:1], rrow[0:1, 32 + mc * 128: 32 + mc * 128 + msz],
                cx.ones_row[0:1, 0:1], start=True, stop=True),
                reads=["l1:ksrow", "c:ones_row"], writes=["ps:0"])
            tk.op("v", lambda e, mc=mc, msz=msz, col=col: e.tensor_copy(
                stg[0:msz, col + 64:col + 65], cx.pf[0][0:msz, 0:1]),
                reads=["ps:0"], writes=["l1:stg"])

    if cc:
        tk.dma("g", lambda e: e.dma_start(out=ccs_in.ap()[:, :], in_=stg[:, :]),
               reads=["l1:stg"], writes=["d:ccs_in"])
        tk.barrier()
        tk.op("g", lambda e: e.collective_compute(
            "AllReduce", ALU.add, replica_groups=RG,
            ins=[ccs_in.ap().opt()], outs=[ccs_out.ap().opt()]),
            reads=["d:ccs_in"], writes=["d:ccs_out"])
        tk.dma("g", lambda e: e.dma_start(out=stg[:, :], in_=ccs_out.ap()[:, :]),
               reads=["d:ccs_out"], writes=["l1:stg"])
    for h in range(16):
        for mc in range(3):
            msz = 128 if mc < 2 else 10
            col = h * CTXW + mc * 65
            tk.op("v", lambda e, h=h, mc=mc, msz=msz, col=col: e.tensor_copy(
                ctxb[0:msz, h * 192 + mc * 64: h * 192 + (mc + 1) * 64],
                stg[0:msz, col:col + 64]),
                reads=["l1:stg"], writes=["l1:ctxb"])
            tk.op("v", lambda e, h=h, mc=mc, msz=msz, col=col: e.tensor_copy(
                ksumb[0:msz, h * 3 + mc: h * 3 + mc + 1], stg[0:msz, col + 64:col + 65]),
                reads=["l1:stg"], writes=["l1:ksumb"])

    tk.dma("g", lambda e: e.dma_start(
        out=wA[:, :], in_=cx.dram["l1_wq_s"][:, :]), writes=["w:A"])
    tk.dma("g", lambda e: e.dma_start(
        out=wB[:, :], in_=cx.dram["l1_wo_s"][:, :]), writes=["w:B"])
    tk.dma("g", lambda e: e.dma_start(out=br[0:1, 0:1024], in_=cx.dram["l1_bq"][:, :]), writes=["b:r0"])
    tk.dma("g", lambda e: e.dma_start(out=br[0:1, 1024:2048], in_=cx.dram["l1_bo"][:, :]), writes=["b:r1"])
    for ti in range(16):
        par = ti % 2
        tc0 = ti * 128
        for qc in range(NCH):
            pi = qc % 2
            ps = cx.pf[pi]

            def mm_q(e, qc=qc, tc0=tc0, ps=ps):
                for kc in range(NCH):
                    e.matmul(ps[:, 0:128], wA[:, kc * 1024 + qc * 128: kc * 1024 + (qc + 1) * 128],
                             xch(kc, tc0, tc0 + 128), start=(kc == 0), stop=False)
                return e.matmul(ps[:, 0:128], br[0:1, qc * 128:(qc + 1) * 128],
                                cx.ones_row16[0:1, 0:128], start=False, stop=True)
            tk.op("t", mm_q, reads=[xn_key, "w:A", "b:r0", "c:ones_row16"], writes=[f"ps:{pi}"])
            tk.op("s", lambda e, qc=qc, par=par, ps=ps, pi=pi: e.copy(
                qt[:, par * 1024 + qc * 128: par * 1024 + (qc + 1) * 128], ps[:, 0:128]),
                reads=[f"ps:{pi}"], writes=[f"l1:qt{par}"])
        for h in range(16):
            base = 64 * (h % 2)
            qcol = par * 1024 + (h // 2) * 128
            hp = h % 2
            ps_td = cx.pf[hp]
            ps_d = cx.pf[2 + hp]
            ps_dv = cx.pf[4]
            ps_o = cx.pf[5]
            ps_tr = cx.pb[hp]
            tk.op("t", lambda e, base=base, qcol=qcol, ps_td=ps_td: e.matmul(
                ps_td[0:128, 0:266], qt[base:base + 64, qcol:qcol + 128],
                projT[base:base + 64, 0:266], start=True, stop=True),
                reads=[f"l1:qt{par}", "l1:projT"], writes=[f"ps:{hp}"])
            tk.op("s", lambda e, base=base, qcol=qcol, hp=hp: e.activation(
                sq64[0:64, hp * 128:hp * 128 + 128], qt[base:base + 64, qcol:qcol + 128],
                AF.Square),
                reads=[f"l1:qt{par}"], writes=[f"l1:sq{hp}"])
            tk.op("t", lambda e, hp=hp, ps_d=ps_d: e.matmul(
                ps_d[0:128, 0:1], sq64[0:64, hp * 128:hp * 128 + 128], cx.ln_ones_col[0:64, 0:1],
                start=True, stop=True),
                reads=[f"l1:sq{hp}", "ln:ones_col"], writes=[f"ps:{2 + hp}"])
            tk.op("v", lambda e, hp=hp, ps_d=ps_d: e.tensor_scalar(
                rsc[:, 4 + hp:5 + hp], ps_d[:, 0:1], 0.5 * DN * DN, None, ALU.mult),
                reads=[f"ps:{2 + hp}"], writes=[f"l1:dq{hp}"])
            tk.op("v", lambda e, hp=hp, ps_td=ps_td: e.reduce_max(
                rsc[:, 6 + hp:7 + hp], ps_td[:, 0:266], axis=AX),
                reads=[f"ps:{hp}"], writes=[f"l1:qm{hp}"])
            tk.op("v", lambda e, hp=hp: e.scalar_tensor_tensor(
                rsc[:, 8 + hp:9 + hp], rsc[:, 6 + hp:7 + hp], -DN, rsc[:, 4 + hp:5 + hp],
                ALU.mult, ALU.subtract),
                reads=[f"l1:qm{hp}", f"l1:dq{hp}"], writes=[f"l1:qb{hp}"])
            tk.op("s", lambda e, hp=hp, ps_td=ps_td: e.activation(
                exs[:, hp * 266:(hp + 1) * 266], ps_td[:, 0:266], AF.Exp,
                bias=rsc[:, 8 + hp:9 + hp], scale=DN),
                reads=[f"ps:{hp}", f"l1:qb{hp}"], writes=[f"l1:ex{hp}"])
            tk.op("v", lambda e, hp=hp: e.tensor_scalar(
                kpb[:, hp * 266:(hp + 1) * 266], exs[:, hp * 266:(hp + 1) * 266],
                RATIO, RATIO * EPS, ALU.mult, ALU.add),
                reads=[f"l1:ex{hp}"], writes=[f"l1:kp{hp}"])

            def mm_tr(e, hp=hp, ps_tr=ps_tr):
                ins = None
                for mc in range(3):
                    msz = 128 if mc < 2 else 10
                    ins = e.transpose(ps_tr[0:msz, mc * 128:mc * 128 + 128],
                                      kpb[:, hp * 266 + mc * 128: hp * 266 + mc * 128 + msz],
                                      cx.eye16[:, :])
                return ins
            tk.op("t", mm_tr, reads=[f"l1:kp{hp}", "c:eye16"], writes=[f"ps:b{hp}"])
            tk.op("s", lambda e, hp=hp, ps_tr=ps_tr: e.copy(
                qpT[:, hp * 384:(hp + 1) * 384], ps_tr[0:128, 0:384]),
                reads=[f"ps:b{hp}"], writes=[f"l1:qpT{hp}"])

            def mm_den(e, h=h, hp=hp, ps_dv=ps_dv):
                ins = None
                for mc in range(3):
                    msz = 128 if mc < 2 else 10
                    ins = e.matmul(ps_dv[0:1, 0:128], ksumb[0:msz, h * 3 + mc: h * 3 + mc + 1],
                                   qpT[0:msz, hp * 384 + mc * 128: hp * 384 + (mc + 1) * 128],
                                   start=(mc == 0), stop=(mc == 2))
                return ins
            tk.op("t", mm_den, reads=["l1:ksumb", f"l1:qpT{hp}"], writes=["ps:4"])
            tk.op("v", lambda e, ps_dv=ps_dv: e.tensor_copy(
                rrow[0:1, 304:432], ps_dv[0:1, 0:128]),
                reads=["ps:4"], writes=["l1:den"])
            tk.op("v", lambda e: e.reciprocal(rrow[0:1, 304:432], rrow[0:1, 304:432]),
                  reads=["l1:den"], writes=["l1:den"])
            tk.op("t", lambda e, ps_dv=ps_dv: e.matmul(
                ps_dv[0:128, 0:128], cx.ones_row[0:1, 0:128], rrow[0:1, 304:432],
                start=True, stop=True),
                reads=["l1:den", "c:ones_row"], writes=["ps:4"])
            for mc in range(3):
                msz = 128 if mc < 2 else 10
                tk.op("v", lambda e, hp=hp, mc=mc, msz=msz, ps_dv=ps_dv: e.tensor_mul(
                    qps[0:msz, hp * 384 + mc * 128: hp * 384 + (mc + 1) * 128],
                    qpT[0:msz, hp * 384 + mc * 128: hp * 384 + (mc + 1) * 128],
                    ps_dv[0:msz, 0:128]),
                    reads=[f"l1:qpT{hp}", "ps:4"], writes=[f"l1:qps{hp}"])

            def mm_pv(e, h=h, hp=hp, ps_o=ps_o):
                ins = None
                for mc in range(3):
                    msz = 128 if mc < 2 else 10
                    ins = e.matmul(ps_o[0:64, 0:128],
                                   ctxb[0:msz, h * 192 + mc * 64: h * 192 + (mc + 1) * 64],
                                   qps[0:msz, hp * 384 + mc * 128: hp * 384 + (mc + 1) * 128],
                                   start=(mc == 0), stop=(mc == 2))
                return ins
            tk.op("t", mm_pv, reads=["l1:ctxb", f"l1:qps{hp}"], writes=["ps:5"])
            tk.op("s", lambda e, h=h, base=base, par=par, ps_o=ps_o: e.copy(
                ath[base:base + 64, par * 1024 + (h // 2) * 128: par * 1024 + (h // 2) * 128 + 128],
                ps_o[0:64, 0:128]),
                reads=["ps:5"], writes=[f"l1:ath{par}"])
        for oc in range(NCH):
            pi = oc % 2
            ps = cx.pf[pi]

            def mm_wo(e, oc=oc, par=par, ps=ps):
                for kc in range(NCH):
                    e.matmul(ps[:, 0:128], wB[:, kc * 1024 + oc * 128: kc * 1024 + (oc + 1) * 128],
                             ath[:, par * 1024 + kc * 128: par * 1024 + (kc + 1) * 128],
                             start=(kc == 0), stop=False)
                return e.matmul(ps[:, 0:128], br[0:1, 1024 + oc * 128:1024 + (oc + 1) * 128],
                                cx.ones_row16[0:1, 0:128], start=False, stop=True)
            tk.op("t", mm_wo, reads=[f"l1:ath{par}", "w:B", "b:r1", "c:ones_row16"],
                  writes=[f"ps:{pi}"])
            tk.op("s", lambda e, oc=oc, tc0=tc0, ps=ps, pi=pi: e.copy(
                kbuf[:, oc * TPC + tc0: oc * TPC + tc0 + 128], ps[:, 0:128]),
                reads=[f"ps:{pi}"], writes=["kbuf"])

    tk.dma("g", lambda e: e.dma_start(
        out=wA[:, :], in_=cx.dram["l1_wfc_s"][:, :]), writes=["w:A"])
    tk.dma("g", lambda e: e.dma_start(out=br[0:1, 0:1024], in_=cx.dram["l1_bfc"][:, :]), writes=["b:r0"])
    for tt in range(4):
        c0 = tt * 512
        for kc in range(NCH):
            tk.dma("y", lambda e, kc=kc, c0=c0: e.dma_start(
                out=_chunk(xt, kc, 512), in_=x_in_ap[kc * 128:(kc + 1) * 128, c0:c0 + 512]),
                reads=list(x_dep), writes=["l1:xt"])
        for oc in range(NCH):
            pi = oc % 2
            ps = cx.pf[pi]

            def mm_fc(e, oc=oc, c0=c0, ps=ps):
                for kc in range(NCH):
                    e.matmul(ps[:, 0:512], wA[:, kc * 1024 + oc * 128: kc * 1024 + (oc + 1) * 128],
                             kbuf[:, kc * TPC + c0: kc * TPC + c0 + 512],
                             start=(kc == 0), stop=False)
                return e.matmul(ps[:, 0:512], br[0:1, oc * 128:(oc + 1) * 128],
                                cx.ones_row16[0:1, 0:512], start=False, stop=True)
            tk.op("t", mm_fc, reads=["kbuf", "w:A", "b:r0", "c:ones_row16"], writes=[f"ps:{pi}"])
            tk.op("v", lambda e, oc=oc, ps=ps, pi=pi: e.scalar_tensor_tensor(
                _chunk(xt, oc, 512), ps[:, 0:512], 0.0, _chunk(xt, oc, 512),
                ALU.max, ALU.add),
                reads=[f"ps:{pi}", "l1:xt"], writes=["l1:xt"])
        cx.l1_y0_store(tt)


def build_l1a():
    nc = bass.Bass("TRN2", target_bir_lowering=False)
    es = contextlib.ExitStack()
    with es:
        x1_in = nc.dram_tensor("x1_in", [DM, TPC], F32, kind="ExternalInput").ap()
        hsel_in = nc.dram_tensor("hsel", [1, 2], F32, kind="ExternalInput").ap()
        hselc_in = nc.dram_tensor("hselc", [2, 1], F32, kind="ExternalInput").ap()
        ccmb_in = nc.dram_tensor("ccmaxbias", [2, 16], F32, kind="ExternalInput").ap()
        xn_out = nc.dram_tensor("xn1_out", [DM, TPC], BF16, kind="ExternalOutput").ap()
        y0_out = nc.dram_tensor("y0_out", [DM, TPC], F32, kind="ExternalOutput").ap()

        block = es.enter_context(nc.Block())
        cx = _setup_common(nc, es, block, _SHARED_CACHE, L1A_SHARED)
        tk = cx.tk
        cx.ln_scr = cx.sb("ln_scr", [128, 2 * 512], F32)
        cx.ln_rows = cx.sb("ln_rows", [1, 4 * 512], F32)
        cx.ln_ones_col = cx.sb("ln_ones_col", [128, 1], F32)
        cx.ln_eps = cx.sb("ln_eps", [1, 1], F32)
        tk.op("v", lambda e: e.memset(cx.ln_ones_col[:, :], 1.0), writes=["ln:ones_col"])
        tk.op("v", lambda e: e.memset(cx.ln_eps[:, :], 1e-5), writes=["ln:eps"])
        cx.ln_init = True

        xn_sb = cx.sb("xn_sb", [128, NCH * TPC], BF16)

        def y0_store(tt):
            c0 = tt * 512
            for kc in range(NCH):
                tk.dma("y", lambda e, kc=kc, c0=c0: e.dma_start(
                    out=y0_out[kc * 128:(kc + 1) * 128, c0:c0 + 512],
                    in_=_chunk(cx.l1_xt, kc, 512)),
                    reads=["l1:xt"], writes=["o:y0"])
        cx.l1_y0_store = y0_store

        emit_l1_attn(cx, x1_in, xn_sb, "xn", TPC, 0, None, None,
                     {"hsel": hsel_in, "hselc": hselc_in, "ccmaxbias": ccmb_in})

        for kc in range(NCH):
            tk.dma("y", lambda e, kc=kc: e.dma_start(
                out=xn_out[kc * 128:(kc + 1) * 128, :], in_=_chunk(xn_sb, kc, TPC)),
                reads=["xn"], writes=["o:xn"])
        n = tk.dcnt["y"]
        tk.flush()
        block.sync(lambda eng: eng.wait_ge(tk.dsem["y"], 16 * n))
    return nc




L1B_SHARED = ["f_w1_s", "f_w2_s", "f_h1_s", "f_h2_s", "f_w3_s", "f_s1", "f_s2", "f_sh1", "f_sh2",
              "f_s3", "eye", "ones_row"]


def emit_l1_ffn(cx, xn_sb, xn_key, xnw, xc0, y0_in_ap, out_store, y_dep=()):
    nc, tk, sb = cx.nc, cx.tk, cx.sb

    def xch(kc, a, b):
        return xn_sb[:, kc * xnw + xc0 + a: kc * xnw + xc0 + b]

    hs = sb("f_h", [128, 32 * 1024], BF16)
    gs = sb("f_g", [128, 32 * 1024], BF16)
    wch = sb("f_wch", [128, 2 * 4096], BF16)
    yt = sb("f_yt", [128, NCH * 512], F32)
    cx.f_yt = yt
    usc = yt
    scl = sb("f_scl", [128, 4 * 32 + 8], F32)

    tk.dma("y", lambda e: e.dma_start(out=scl[:, 0:32], in_=cx.dram["f_s1"][:, :]), writes=["f:scl"])
    tk.dma("y", lambda e: e.dma_start(out=scl[:, 32:64], in_=cx.dram["f_s2"][:, :]), writes=["f:scl"])
    tk.dma("y", lambda e: e.dma_start(out=scl[:, 64:96], in_=cx.dram["f_sh1"][:, :]), writes=["f:scl"])
    tk.dma("y", lambda e: e.dma_start(out=scl[:, 96:128], in_=cx.dram["f_sh2"][:, :]), writes=["f:scl"])
    tk.dma("y", lambda e: e.dma_start(out=scl[:, 128:136], in_=cx.dram["f_s3"][:, :]), writes=["f:scl"])

    for half in range(2):
        hoff = half * 1024
        for oc in range(32):
            par = oc % 2
            tk.dma("g", lambda e, oc=oc, par=par: e.dma_start(
                out=wch[:, par * 4096:par * 4096 + 1024],
                in_=cx.dram["f_w1_s"][oc]),
                writes=[f"f:wh{par}"])
            tk.dma("g", lambda e, oc=oc, par=par: e.dma_start(
                out=wch[:, par * 4096 + 1024:par * 4096 + 2048],
                in_=cx.dram["f_w2_s"][oc]),
                writes=[f"f:wh{par}"])
            for tt in range(2):
                tc0 = hoff + tt * 512
                ppar = tt % 2
                ps_a, ps_b = cx.pf[ppar], cx.pf[2 + ppar]

                def mm_h(e, oc=oc, par=par, tc0=tc0, ps_a=ps_a, ps_b=ps_b):
                    for kc in range(NCH):
                        e.matmul(ps_a[:, 0:512], wch[:, par * 4096 + kc * 128: par * 4096 + (kc + 1) * 128],
                                 xch(kc, tc0, tc0 + 512), start=(kc == 0), stop=(kc == NCH - 1))
                    ins = None
                    for kc in range(NCH):
                        ins = e.matmul(ps_b[:, 0:512], wch[:, par * 4096 + 1024 + kc * 128: par * 4096 + 1024 + (kc + 1) * 128],
                                       xch(kc, tc0, tc0 + 512), start=(kc == 0), stop=(kc == NCH - 1))
                    return ins
                tk.op("t", mm_h, reads=[xn_key, f"f:wh{par}"],
                      writes=[f"ps:{ppar}", f"ps:{2 + ppar}"])
                tk.op("s", lambda e, oc=oc, ppar=ppar, ps_a=ps_a: e.activation(
                    usc[:, ppar * 512:(ppar + 1) * 512], ps_a[:, 0:512], AF.Silu,
                    scale=scl[:, oc:oc + 1]),
                    reads=[f"ps:{ppar}", "f:scl"], writes=["f:yt"])
                tk.op("v", lambda e, oc=oc, tt=tt, ppar=ppar, ps_b=ps_b: e.scalar_tensor_tensor(
                    hs[:, oc * 1024 + tt * 512: oc * 1024 + (tt + 1) * 512],
                    ps_b[:, 0:512], scl[:, 32 + oc:33 + oc], usc[:, ppar * 512:(ppar + 1) * 512],
                    ALU.mult, ALU.mult),
                    reads=[f"ps:{2 + ppar}", "f:scl", "f:yt"], writes=["f:h"])
        for stage, (wnm, snm, soff, src_key, sbuf, dst_key, dbuf) in enumerate([
                ("f_h1", "sh1", 64, "f:h", hs, "f:g", gs),
                ("f_h2", "sh2", 96, "f:g", gs, "f:h", hs)]):
            for oc in range(32):
                par = oc % 2
                tk.dma("g", lambda e, oc=oc, par=par, wnm=wnm: e.dma_start(
                    out=wch[:, par * 4096:(par + 1) * 4096],
                    in_=cx.dram[wnm + "_s"][oc]),
                    writes=[f"f:wh{par}"])
                for tt in range(2):
                    ppar = tt % 2
                    ps = cx.pf[4 + ppar]

                    def mm_g(e, oc=oc, par=par, tt=tt, sbuf=sbuf, ps=ps):
                        ins = None
                        for kc in range(32):
                            ins = e.matmul(ps[:, 0:512],
                                           wch[:, par * 4096 + kc * 128: par * 4096 + (kc + 1) * 128],
                                           sbuf[:, kc * 1024 + tt * 512: kc * 1024 + (tt + 1) * 512],
                                           start=(kc == 0), stop=(kc == 31))
                        return ins
                    tk.op("t", mm_g, reads=[src_key, f"f:wh{par}"], writes=[f"ps:{4 + ppar}"])
                    tk.op("s", lambda e, oc=oc, tt=tt, soff=soff, dbuf=dbuf, ps=ps: e.activation(
                        dbuf[:, oc * 1024 + tt * 512: oc * 1024 + (tt + 1) * 512],
                        ps[:, 0:512], AF.Gelu, scale=scl[:, soff + oc:soff + oc + 1]),
                        reads=[f"ps:{4 + ppar}", "f:scl"], writes=[dst_key])
        for tt in range(2):
            tc0 = hoff + tt * 512
            for kc in range(NCH):
                tk.dma("y", lambda e, kc=kc, tc0=tc0: e.dma_start(
                    out=_chunk(yt, kc, 512), in_=y0_in_ap[kc * 128:(kc + 1) * 128, tc0:tc0 + 512]),
                    reads=list(y_dep), writes=["f:yt"])
            for oc in range(NCH):
                par = oc % 2
                tk.dma("g", lambda e, oc=oc, par=par: e.dma_start(
                    out=wch[:, par * 4096:(par + 1) * 4096],
                    in_=cx.dram["f_w3_s"][oc]),
                    writes=[f"f:wh{par}"])
                ps = cx.pf[par]

                def mm_o(e, oc=oc, par=par, tt=tt, ps=ps):
                    ins = None
                    for kc in range(32):
                        ins = e.matmul(ps[:, 0:512],
                                       wch[:, par * 4096 + kc * 128: par * 4096 + (kc + 1) * 128],
                                       hs[:, kc * 1024 + tt * 512: kc * 1024 + (tt + 1) * 512],
                                       start=(kc == 0), stop=(kc == 31))
                    return ins
                tk.op("t", mm_o, reads=["f:h", f"f:wh{par}"], writes=[f"ps:{par}"])
                tk.op("v", lambda e, oc=oc, par=par, ps=ps: e.scalar_tensor_tensor(
                    _chunk(yt, oc, 512), ps[:, 0:512], scl[:, 128 + oc:129 + oc],
                    _chunk(yt, oc, 512), ALU.mult, ALU.add),
                    reads=[f"ps:{par}", "f:scl", "f:yt"], writes=["f:yt"])
            out_store(half, tt)


def build_l1b():
    nc = bass.Bass("TRN2", target_bir_lowering=False)
    es = contextlib.ExitStack()
    with es:
        xn_in = nc.dram_tensor("xn1_in", [DM, TPC], BF16, kind="ExternalInput").ap()
        y0_in = nc.dram_tensor("y0_in", [DM, TPC], F32, kind="ExternalInput").ap()
        y_out = nc.dram_tensor("y_out", [DM, TPC], F32, kind="ExternalOutput").ap()

        block = es.enter_context(nc.Block())
        cx = _setup_common(nc, es, block, _SHARED_CACHE, L1B_SHARED)
        tk = cx.tk

        xn_sb = cx.sb("xn_sb", [128, NCH * TPC], BF16)
        for kc in range(NCH):
            tk.dma("y", lambda e, kc=kc: e.dma_start(
                out=_chunk(xn_sb, kc, TPC), in_=xn_in[kc * 128:(kc + 1) * 128, :]), writes=["xn"])

        def out_store(half, tt):
            tc0 = half * 1024 + tt * 512
            for kc in range(NCH):
                tk.dma("y", lambda e, kc=kc, tc0=tc0: e.dma_start(
                    out=y_out[kc * 128:(kc + 1) * 128, tc0:tc0 + 512],
                    in_=_chunk(cx.f_yt, kc, 512)),
                    reads=["f:yt"], writes=["o:y"])

        emit_l1_ffn(cx, xn_sb, "xn", TPC, 0, y0_in, out_store)
        n = tk.dcnt["y"]
        tk.flush()
        block.sync(lambda eng: eng.wait_ge(tk.dsem["y"], 16 * n))
    return nc


_SHARED_CACHE = None


def _set_shared(shared):
    global _SHARED_CACHE
    _SHARED_CACHE = shared




MEGA_SHARED = sorted(set(L0A_SHARED + MOE_SHARED + L1A_SHARED + L1B_SHARED))


def build_mega(dbg=False, cc=True):
    nc = bass.Bass("TRN2", target_bir_lowering=False)
    es = contextlib.ExitStack()
    with es:
        x_in = nc.dram_tensor("x_fm", [DM, TH], F32, kind="ExternalInput").ap()
        kvm_in = nc.dram_tensor("kvmask", [1, TH], F32, kind="ExternalInput").ap()
        y_out = nc.dram_tensor("y_out", [DM, TPC], F32, kind="ExternalOutput").ap()
        kind = dict(kind="ExternalOutput") if dbg else {}
        x1s = nc.dram_tensor("x1s", [DM, TPC], F32, **kind)
        y0s = nc.dram_tensor("y0s", [DM, TPC], F32, **kind)
        if dbg:
            xn_dbg = nc.dram_tensor("xn_dbg", [DM, TPC], BF16, kind="ExternalOutput").ap()
            stg_dbg = nc.dram_tensor("stg_dbg", [128, STG_W], F32, kind="ExternalOutput").ap()
            gts_dbg = nc.dram_tensor("gts_dbg", [128, 256], F32, kind="ExternalOutput").ap()
            x1c_dbg = nc.dram_tensor("x1c_dbg", [DM, TPC], F32, kind="ExternalOutput").ap()
            xn2_dbg = nc.dram_tensor("xn2_dbg", [DM, TPC], BF16, kind="ExternalOutput").ap()

        block = es.enter_context(nc.Block())
        cx = _setup_common(nc, es, block, _SHARED_CACHE, MEGA_SHARED)
        cx.dram["kvmask"] = kvm_in
        tk = cx.tk

        awu = cx.sb("actwarm", [1, 16], F32)
        tk.op("v", lambda e: e.memset(awu[:, :], 1.0), writes=["actwarm"])
        for fn in (AF.Sqrt, AF.Exp, AF.Silu, AF.Gelu):
            tk.op("s", lambda e, fn=fn: e.activation(awu[:, :], awu[:, :], fn),
                  reads=["actwarm"], writes=["actwarm"])
            tk.barrier()

        ccw_in = nc.dram_tensor("ccw_in", [1, 16], F32)
        ccw_out = nc.dram_tensor("ccw_out", [1, 16], F32)
        wup = cx.sb("warmup", [1, 16], F32)
        tk.op("v", lambda e: e.memset(wup[:, :], 1.0), writes=["wup"])
        tk.dma("g", lambda e: e.dma_start(out=ccw_in.ap()[:, :], in_=wup[:, :]),
               reads=["wup"], writes=["d:ccw"])
        tk.op("g", lambda e: e.collective_compute(
            "AllReduce", ALU.add, replica_groups=[[0, 1, 2, 3], [4, 5, 6, 7]],
            ins=[ccw_in.ap().opt()], outs=[ccw_out.ap().opt()]),
            reads=["d:ccw"], writes=["d:ccwo"])
        tk.dma("g", lambda e: e.dma_start(out=wup[:, :], in_=ccw_out.ap()[:, :]),
               reads=["d:ccwo"], writes=["wup"])
        tk.barrier()

        xn_sb = cx.sb("xn_sb", [128, NCH * TH], BF16)
        cx.push_scope()
        cx.ln_scr = cx.sb("ln_scr", [128, 2 * 512], F32)
        cx.ln_rows = cx.sb("ln_rows", [1, 4 * 512], F32)
        cx.ln_ones_col = cx.sb("ln_ones_col", [128, 1], F32)
        cx.ln_eps = cx.sb("ln_eps", [1, 1], F32)
        tk.op("v", lambda e: e.memset(cx.ln_ones_col[:, :], 1.0), writes=["ln:ones_col"])
        tk.op("v", lambda e: e.memset(cx.ln_eps[:, :], 1e-5), writes=["ln:eps"])
        cx.ln_init = True

        cx.push_scope()
        x_sb = cx.sb("x_sb", [128, NCH * TPC], F32)
        cx.push_scope()
        xh_sb = cx.sb("xh_sb", [128, NCH * 256], F32)
        for kc in range(NCH):
            tk.dma("y", lambda e, kc=kc: e.dma_start(
                out=_chunk(x_sb, kc, TPC), in_=x_in[kc * 128:(kc + 1) * 128, HALO:HALO + TPC]),
                writes=["x"])
            tk.dma("y", lambda e, kc=kc: e.dma_start(
                out=_chunk(xh_sb, kc, 256, 0, 128), in_=x_in[kc * 128:(kc + 1) * 128, 0:HALO]),
                writes=["xh"])
            tk.dma("y", lambda e, kc=kc: e.dma_start(
                out=_chunk(xh_sb, kc, 256, 128, 256),
                in_=x_in[kc * 128:(kc + 1) * 128, TH - HALO:TH]),
                writes=["xh"])
        emit_ln(cx, x_sb, "x", TPC, 0, TPC, xn_sb, "xn", TH, HALO)
        emit_ln(cx, xh_sb, "xh", 256, 0, 128, xn_sb, "xn", TH, 0)
        emit_ln(cx, xh_sb, "xh", 256, 128, 128, xn_sb, "xn", TH, TH - HALO)
        emit_l0_attn(cx, xn_sb, "xn", x_sb, "x")
        cx.pop_scope()

        cx.push_scope()
        cx.moe_dbg = (gts_dbg,) if dbg else None
        emit_moe(cx, xn_sb, "xn", TH, HALO, x_sb, "x")
        if dbg:
            for kc in range(NCH):
                tk.dma("y", lambda e, kc=kc: e.dma_start(
                    out=xn2_dbg[kc * 128:(kc + 1) * 128, :],
                    in_=xn_sb[:, kc * TH + HALO: kc * TH + HALO + TPC]),
                    reads=["xn"], writes=["dbg"])
        for kc in range(NCH):
            tk.dma("y", lambda e, kc=kc: e.dma_start(
                out=x1s.ap()[kc * 128:(kc + 1) * 128, :], in_=_chunk(x_sb, kc, TPC)),
                reads=["x"], writes=["d:x1s"])
        if dbg:
            tk.dma("y", lambda e: e.dma_start(out=x1c_dbg[:, :], in_=x1s.ap()[:, :]),
                   reads=["d:x1s"], writes=["dbg"])
        cx.pop_scope()
        cx.pop_scope()

        cx.push_scope()

        def y0_store(tt):
            c0 = tt * 512
            for kc in range(NCH):
                tk.dma("y", lambda e, kc=kc, c0=c0: e.dma_start(
                    out=y0s.ap()[kc * 128:(kc + 1) * 128, c0:c0 + 512],
                    in_=_chunk(cx.l1_xt, kc, 512)),
                    reads=["l1:xt"], writes=["d:y0s"])
        cx.l1_y0_store = y0_store
        emit_l1_attn(cx, x1s.ap(), xn_sb, "xn", TH, HALO, None, None, {},
                     x_dep=("d:x1s",), cc=cc)
        if dbg:
            for kc in range(NCH):
                tk.dma("y", lambda e, kc=kc: e.dma_start(
                    out=xn_dbg[kc * 128:(kc + 1) * 128, :],
                    in_=xn_sb[:, kc * TH + HALO: kc * TH + HALO + TPC]),
                    reads=["xn"], writes=["dbg"])
            tk.dma("y", lambda e: e.dma_start(out=stg_dbg[:, :], in_=cx.l1_stg_dbg[:, :]),
                   reads=["l1:stg"], writes=["dbg"])
        cx.pop_scope()
        cx.pop_scope()

        cx.push_scope()

        def out_store(half, tt):
            tc0 = half * 1024 + tt * 512
            for kc in range(NCH):
                tk.dma("y", lambda e, kc=kc, tc0=tc0: e.dma_start(
                    out=y_out[kc * 128:(kc + 1) * 128, tc0:tc0 + 512],
                    in_=_chunk(cx.f_yt, kc, 512)),
                    reads=["f:yt"], writes=["o:y"])
        emit_l1_ffn(cx, xn_sb, "xn", TH, HALO, y0s.ap(), out_store, y_dep=("d:y0s",))
        cx.pop_scope()

        ny = tk.dcnt["y"]
        ng = tk.dcnt["g"]
        tk.flush()
        def fin(eng):
            eng.wait_ge(tk.dsem["y"], 16 * ny)
            eng.wait_ge(tk.dsem["g"], 16 * ng)
        block.sync(fin)
    return nc


def kernel(**inputs):
    shared = _prep_shared(inputs)
    _set_shared(shared)
    cores = list(range(NCORE))
    nc = _get_prog("MEGA", build_mega)
    maps = []
    for c in cores:
        pc = _prep_core(inputs, c)
        m = {nm: shared[nm] for nm in MEGA_SHARED}
        m["x_fm"] = pc["x_fm"]
        m["kvmask"] = pc["kvmask"]
        maps.append(m)
    res = run_bass_kernel_spmd(nc, maps, core_ids=cores)
    out = np.empty((B * S, DM), np.float32)
    for c in cores:
        out[c * TPC:(c + 1) * TPC] = np.asarray(res.results[c]["y_out"]).T
    return out.reshape(B, S, DM)




def build_mega_l0(**moekw):
    nc = bass.Bass("TRN2", target_bir_lowering=False)
    es = contextlib.ExitStack()
    with es:
        x_in = nc.dram_tensor("x_fm", [DM, TH], F32, kind="ExternalInput").ap()
        kvm_in = nc.dram_tensor("kvmask", [1, TH], F32, kind="ExternalInput").ap()
        y_out = nc.dram_tensor("y_out", [DM, TPC], F32, kind="ExternalOutput").ap()

        block = es.enter_context(nc.Block())
        cx = _setup_common(nc, es, block, _SHARED_CACHE, MEGA_SHARED)
        cx.dram["kvmask"] = kvm_in
        tk = cx.tk
        xn_sb = cx.sb("xn_sb", [128, NCH * TH], BF16)
        cx.push_scope()
        cx.ln_scr = cx.sb("ln_scr", [128, 2 * 512], F32)
        cx.ln_rows = cx.sb("ln_rows", [1, 4 * 512], F32)
        cx.ln_ones_col = cx.sb("ln_ones_col", [128, 1], F32)
        cx.ln_eps = cx.sb("ln_eps", [1, 1], F32)
        tk.op("v", lambda e: e.memset(cx.ln_ones_col[:, :], 1.0), writes=["ln:ones_col"])
        tk.op("v", lambda e: e.memset(cx.ln_eps[:, :], 1e-5), writes=["ln:eps"])
        cx.ln_init = True
        cx.push_scope()
        x_sb = cx.sb("x_sb", [128, NCH * TPC], F32)
        cx.push_scope()
        xh_sb = cx.sb("xh_sb", [128, NCH * 256], F32)
        for kc in range(NCH):
            tk.dma("y", lambda e, kc=kc: e.dma_start(
                out=_chunk(x_sb, kc, TPC), in_=x_in[kc * 128:(kc + 1) * 128, HALO:HALO + TPC]),
                writes=["x"])
            tk.dma("y", lambda e, kc=kc: e.dma_start(
                out=_chunk(xh_sb, kc, 256, 0, 128), in_=x_in[kc * 128:(kc + 1) * 128, 0:HALO]),
                writes=["xh"])
            tk.dma("y", lambda e, kc=kc: e.dma_start(
                out=_chunk(xh_sb, kc, 256, 128, 256),
                in_=x_in[kc * 128:(kc + 1) * 128, TH - HALO:TH]),
                writes=["xh"])
        emit_ln(cx, x_sb, "x", TPC, 0, TPC, xn_sb, "xn", TH, HALO)
        emit_ln(cx, xh_sb, "xh", 256, 0, 128, xn_sb, "xn", TH, 0)
        emit_ln(cx, xh_sb, "xh", 256, 128, 128, xn_sb, "xn", TH, TH - HALO)
        emit_l0_attn(cx, xn_sb, "xn", x_sb, "x")
        cx.pop_scope()
        cx.push_scope()
        emit_moe(cx, xn_sb, "xn", TH, HALO, x_sb, "x", **moekw)
        for kc in range(NCH):
            tk.dma("y", lambda e, kc=kc: e.dma_start(
                out=y_out[kc * 128:(kc + 1) * 128, :], in_=_chunk(x_sb, kc, TPC)),
                reads=["x"], writes=["o:y"])
        cx.pop_scope()
        cx.pop_scope()
        cx.pop_scope()
        ny = tk.dcnt["y"]
        ng = tk.dcnt["g"]
        tk.flush()
        def fin(eng):
            eng.wait_ge(tk.dsem["y"], 16 * ny)
            eng.wait_ge(tk.dsem["g"], 16 * ng)
        block.sync(fin)
    return nc


_PROGS = {}


def _get_prog(name, builder):
    if name not in _PROGS:
        _PROGS[name] = builder()
    return _PROGS[name]


def kernel_v1(**inputs):
    shared = _prep_shared(inputs)
    _set_shared(shared)
    cores = list(range(NCORE))
    pcs = [_prep_core(inputs, c) for c in cores]

    ncA = _get_prog("A", build_l0a)
    mapsA = []
    for c in cores:
        m = {nm: shared[nm] for nm in L0A_SHARED}
        m["x_fm"] = pcs[c]["x_fm"]
        m["kvmask"] = pcs[c]["kvmask"]
        mapsA.append(m)
    resA = run_bass_kernel_spmd(ncA, mapsA, core_ids=cores)

    ncB = _get_prog("B", build_l0b)
    mapsB = []
    for c in cores:
        m = {nm: shared[nm] for nm in MOE_SHARED}
        m["xn_in"] = np.ascontiguousarray(
            np.asarray(resA.results[c]["xn_out"])[:, HALO:HALO + TPC])
        m["x1p_in"] = np.ascontiguousarray(np.asarray(resA.results[c]["x1p_out"]))
        mapsB.append(m)
    resB = run_bass_kernel_spmd(ncB, mapsB, core_ids=cores)

    ncC = _get_prog("C", build_l1a)
    mapsC = []
    for c in cores:
        m = {nm: shared[nm] for nm in L1A_SHARED}
        m["x1_in"] = np.ascontiguousarray(np.asarray(resB.results[c]["x1_out"]))
        m["hsel"] = pcs[c]["hsel"]
        m["hselc"] = pcs[c]["hselc"]
        m["ccmaxbias"] = pcs[c]["ccmaxbias"]
        mapsC.append(m)
    resC = run_bass_kernel_spmd(ncC, mapsC, core_ids=cores)

    ncD = _get_prog("D", build_l1b)
    mapsD = []
    for c in cores:
        m = {nm: shared[nm] for nm in L1B_SHARED}
        m["xn1_in"] = np.ascontiguousarray(np.asarray(resC.results[c]["xn1_out"]))
        m["y0_in"] = np.ascontiguousarray(np.asarray(resC.results[c]["y0_out"]))
        mapsD.append(m)
    resD = run_bass_kernel_spmd(ncD, mapsD, core_ids=cores)

    out = np.empty((B * S, DM), np.float32)
    for c in cores:
        out[c * TPC:(c + 1) * TPC] = np.asarray(resD.results[c]["y_out"]).T
    return out.reshape(B, S, DM)


# version 07 (final)
# speedup vs baseline: 1.3162x; 1.0117x over previous
import contextlib
from collections import defaultdict

import numpy as np
import ml_dtypes

import concourse.bass as bass
import concourse.mybir as mybir
from concourse.bass_utils import run_bass_kernel_spmd

F32 = mybir.dt.float32
BF16 = mybir.dt.bfloat16
AF = mybir.ActivationFunctionType
ALU = mybir.AluOpType
AX = mybir.AxisListType.X

NCORE = 8
B, S, DM = 2, 8192, 1024
TPC = 2048
HALO = 128
TH = TPC + 2 * HALO
NCH = DM // 128

L0_H, L0_KV, L0_D, L0_W = 16, 2, 64, 256
HLIST = [(p % 2) * 8 + p // 2 for p in range(16)]

MOE_E, MOE_K = 16, 4
GLU_LIMIT, GLU_ALPHA = 7.0, 1.702

L1_M, L1_DH = 266, 64
L1_EPS = 1e-4
L1_RATIO = float(L1_M) ** -0.5
L1_DN = float(L1_DH) ** -0.25

ENG_NAME = {"v": "vector", "s": "scalar", "t": "tensor", "g": "gpsimd", "y": "sync"}


class TK:

    def __init__(self, nc, block, es):
        self.nc = nc
        self.block = block
        self.sem = {e: es.enter_context(nc.semaphore(f"tk_{e}")) for e in ENG_NAME}
        self.NDSEM = {"g": 16, "y": 8}
        self.dsem = {q: [es.enter_context(nc.semaphore(f"tkd_{q}{i}")) for i in range(n)]
                     for q, n in self.NDSEM.items()}
        self.cnt = {e: 0 for e in ENG_NAME}
        self.dcnt = {q: 0 for q in ("g", "y")}
        self.dscnt = {q: [0] * n for q, n in self.NDSEM.items()}
        self.known = {e: defaultdict(int) for e in ENG_NAME}
        self.writer = {}
        self.readers = defaultdict(list)
        self.pending = {e: [] for e in ENG_NAME}

    def flush(self):
        for e in ENG_NAME:
            bodies = self.pending[e]
            if not bodies:
                continue

            def run_all(eng, bodies=bodies):
                for b in bodies:
                    b(eng)
            getattr(self.block, ENG_NAME[e])(run_all)
            self.pending[e] = []

    def _needs(self, e, reads, writes):
        need = {}

        def add(tok):
            if tok is None:
                return
            kind, src, c = tok
            pk = (kind, src)
            if need.get(pk, 0) < c:
                need[pk] = c

        for k in reads:
            add(self.writer.get(k))
        for k in writes:
            add(self.writer.get(k))
            for tok in self.readers.get(k, ()):
                add(tok)
        out = []
        kn = self.known[e]
        for pk in sorted(need, key=lambda p: (p[0], str(p[1]))):
            c = need[pk]
            if kn[pk] >= c:
                continue
            out.append((pk, c))
        return out

    def _emit(self, e, fn, needs, inc_q):
        sem, dsem = self.sem, self.dsem

        def body(eng):
            for (kind, src), c in needs:
                if kind == "e":
                    eng.wait_ge(sem[src], c)
                else:
                    eng.wait_ge(dsem[src[0]][src[1]], 16 * c)
            ins = fn(eng)
            if inc_q is None:
                ins.then_inc(sem[e], 1)
            else:
                ins.then_inc(dsem[inc_q[0]][inc_q[1]], 16)

        self.pending[e].append(body)
        kn = self.known[e]
        for pk, c in needs:
            if kn[pk] < c:
                kn[pk] = c

    def op(self, e, fn, reads=(), writes=()):
        needs = self._needs(e, reads, writes)
        self._emit(e, fn, needs, None)
        self.cnt[e] += 1
        tok = ("e", e, self.cnt[e])
        for k in reads:
            self.readers[k].append(tok)
        for k in writes:
            self.writer[k] = tok
            self.readers[k] = [tok]
        return tok

    def dma(self, q, fn, reads=(), writes=()):
        s = self.dcnt[q] % self.NDSEM[q]
        needs = self._needs(q, reads, writes)
        prev = self.dscnt[q][s]
        if prev > self.known[q][("d", (q, s))]:
            have = {pk: c for pk, c in needs}
            if have.get(("d", (q, s)), 0) < prev:
                have[("d", (q, s))] = prev
            needs = sorted(have.items(), key=lambda kv: (kv[0][0], str(kv[0][1])))
        self._emit(q, fn, needs, (q, s))
        self.dcnt[q] += 1
        self.dscnt[q][s] += 1
        tok = ("d", (q, s), self.dscnt[q][s])
        for k in reads:
            self.readers[k].append(tok)
        for k in writes:
            self.writer[k] = tok
            self.readers[k] = [tok]
        return tok

    def final_wait(self, eng):
        for q in ("g", "y"):
            for s in range(self.NDSEM[q]):
                if self.dscnt[q][s]:
                    eng.wait_ge(self.dsem[q][s], 16 * self.dscnt[q][s])

    def barrier(self):
        snap_e = dict(self.cnt)
        snap_ds = {q: list(v) for q, v in self.dscnt.items()}
        for e in ENG_NAME:
            needs = []
            for o in ENG_NAME:
                if snap_e[o] > self.known[e][("e", o)]:
                    needs.append((("e", o), snap_e[o]))
            for q in ("g", "y"):
                for s in range(self.NDSEM[q]):
                    if snap_ds[q][s] > self.known[e][("d", (q, s))]:
                        needs.append((("d", (q, s)), snap_ds[q][s]))
            if not needs:
                continue

            def wrapped(eng, needs=needs):
                for (kind, src), c in needs:
                    if kind == "e":
                        eng.wait_ge(self.sem[src], c)
                    else:
                        eng.wait_ge(self.dsem[src[0]][src[1]], 16 * c)

            self.pending[e].append(wrapped)
            for pk, c in needs:
                if self.known[e][pk] < c:
                    self.known[e][pk] = c



def _prep_shared(I):
    d = {}
    perm = np.concatenate([np.arange(h * 64, (h + 1) * 64) for h in HLIST])
    d["l0_wq"] = np.ascontiguousarray(np.asarray(I["L0_attn_Wq"], np.float32)[:, perm])
    d["l0_bq"] = np.ascontiguousarray(np.asarray(I["L0_attn_bq"], np.float32)[perm][None, :])
    d["l0_wk"] = np.ascontiguousarray(np.asarray(I["L0_attn_Wk"], np.float32))
    d["l0_bk"] = np.ascontiguousarray(np.asarray(I["L0_attn_bk"], np.float32)[None, :])
    d["l0_wv"] = np.ascontiguousarray(np.asarray(I["L0_attn_Wv"], np.float32))
    d["l0_bv"] = np.ascontiguousarray(np.asarray(I["L0_attn_bv"], np.float32)[None, :])
    d["l0_wo"] = np.ascontiguousarray(np.asarray(I["L0_attn_Wo"], np.float32)[perm, :])
    d["l0_bo"] = np.ascontiguousarray(np.asarray(I["L0_attn_bo"], np.float32)[None, :])

    d["moe_scale"] = np.ascontiguousarray(
        np.asarray(I["L0_ffn_norm_scale"], np.float32).reshape(NCH, 128).T)
    d["moe_wg"] = np.ascontiguousarray(np.asarray(I["L0_ffn_wg"], np.float32))
    d["moe_bg"] = np.ascontiguousarray(np.asarray(I["L0_ffn_bg"], np.float32)[None, :])
    w1 = np.asarray(I["L0_ffn_w1"], np.float32)
    b1 = np.asarray(I["L0_ffn_b1"], np.float32)
    w1p = np.empty((MOE_E, 1024, 2048), np.float32)
    b1p = np.empty((MOE_E, 1, 2048), np.float32)
    for e in range(MOE_E):
        a, bb = w1[e][:, 0::2], w1[e][:, 1::2]
        ba, bbv = b1[e][0::2], b1[e][1::2]
        for j in range(8):
            w1p[e][:, j * 256:j * 256 + 128] = a[:, j * 128:(j + 1) * 128]
            w1p[e][:, j * 256 + 128:(j + 1) * 256] = bb[:, j * 128:(j + 1) * 128]
            b1p[e][0, j * 256:j * 256 + 128] = ba[j * 128:(j + 1) * 128]
            b1p[e][0, j * 256 + 128:(j + 1) * 256] = bbv[j * 128:(j + 1) * 128]
    d["moe_w1"] = w1p
    d["moe_b1"] = b1p
    b1c = np.empty((MOE_E, 128, 16), np.float32)
    for e in range(MOE_E):
        for j in range(8):
            b1c[e, :, j * 2] = b1p[e, 0, j * 256:j * 256 + 128]
            b1c[e, :, j * 2 + 1] = b1p[e, 0, j * 256 + 128:(j + 1) * 256]
    d["moe_b1c"] = b1c
    d["moe_w2"] = np.ascontiguousarray(np.asarray(I["L0_ffn_w2"], np.float32))
    d["moe_b2"] = np.ascontiguousarray(np.asarray(I["L0_ffn_b2"], np.float32))

    for nm in ("Wq", "Wk", "Wv", "Wo", "Wfc"):
        d[f"l1_{nm.lower()}"] = np.ascontiguousarray(np.asarray(I[f"L1_attn_{nm}"], np.float32).T)
    for nm in ("bq", "bk", "bv", "bo", "bfc"):
        d[f"l1_{nm}"] = np.ascontiguousarray(np.asarray(I[f"L1_attn_{nm}"], np.float32)[None, :])
    projT = np.asarray(I["L1_attn_proj"], np.float32).T
    d["l1_projT"] = np.ascontiguousarray(np.concatenate([projT, projT], axis=0))

    d["f_w1"] = np.ascontiguousarray(np.asarray(I["L1_ffn_w1_w"], np.float32).T)
    d["f_w2"] = np.ascontiguousarray(np.asarray(I["L1_ffn_w2_w"], np.float32).T)
    d["f_h1"] = np.ascontiguousarray(np.asarray(I["L1_ffn_h1_w"], np.float32).T)
    d["f_h2"] = np.ascontiguousarray(np.asarray(I["L1_ffn_h2_w"], np.float32).T)
    d["f_w3"] = np.ascontiguousarray(np.asarray(I["L1_ffn_w3_w"], np.float32).T)
    d["f_s1"] = np.ascontiguousarray(np.asarray(I["L1_ffn_w1_s"], np.float32).reshape(-1, 128).T)
    d["f_s2"] = np.ascontiguousarray(np.asarray(I["L1_ffn_w2_s"], np.float32).reshape(-1, 128).T)
    d["f_sh1"] = np.ascontiguousarray(np.asarray(I["L1_ffn_h1_s"], np.float32).reshape(-1, 128).T)
    d["f_sh2"] = np.ascontiguousarray(np.asarray(I["L1_ffn_h2_s"], np.float32).reshape(-1, 128).T)
    d["f_s3"] = np.ascontiguousarray(np.asarray(I["L1_ffn_w3_s"], np.float32).reshape(-1, 128).T)

    def _pstream(w, kchunks):
        K, O = w.shape
        assert K == kchunks * 128
        return np.ascontiguousarray(w.reshape(kchunks, 128, O).transpose(1, 0, 2).reshape(128, kchunks * O))

    def _ostream(w, kchunks, osz=128):
        K, O = w.shape
        noc = O // osz
        a = w.reshape(kchunks, 128, noc, osz).transpose(2, 1, 0, 3)
        return np.ascontiguousarray(a.reshape(noc, 128, kchunks * osz))

    d["l0_wq_s"] = _pstream(d["l0_wq"], NCH)
    d["l0_wk_s"] = _pstream(d["l0_wk"], NCH)
    d["l0_wv_s"] = _pstream(d["l0_wv"], NCH)
    d["l0_wo_s"] = _pstream(d["l0_wo"], NCH)
    d["moe_wg_s"] = _pstream(d["moe_wg"], NCH)
    d["moe_w1_s"] = np.ascontiguousarray(np.stack(
        [np.stack([_pstream(d["moe_w1"][e][:, j * 256:(j + 1) * 256], NCH) for j in range(8)])
         for e in range(MOE_E)]))
    d["moe_w2_s"] = np.ascontiguousarray(np.stack(
        [_pstream(d["moe_w2"][e], NCH) for e in range(MOE_E)]))
    for nm in ("l1_wq", "l1_wk", "l1_wv", "l1_wo", "l1_wfc"):
        d[nm + "_s"] = _pstream(d[nm], NCH)
    d["f_w1_s"] = _ostream(d["f_w1"], NCH)
    d["f_w2_s"] = _ostream(d["f_w2"], NCH)
    d["f_h1_s"] = _ostream(d["f_h1"], 32)
    d["f_h2_s"] = _ostream(d["f_h2"], 32)
    d["f_w3_s"] = _ostream(d["f_w3"], 32)

    d["eye"] = np.eye(128, dtype=np.float32)
    d["ones_row"] = np.ones((1, 512), np.float32)
    r = np.arange(128)[:, None]
    u = np.arange(384)[None, :]
    d["l0mask"] = ((u >= r) & (u < r + 256)).astype(np.float32)
    return d


def _prep_core(I, c):
    xf = np.asarray(I["L0_attn_x"], np.float32).reshape(B * S, DM)
    t0 = c * TPC
    seq = t0 // S
    s_lo, s_hi = seq * S, (seq + 1) * S
    lo, hi = t0 - HALO, t0 + TPC + HALO
    pad_l, pad_r = max(0, s_lo - lo), max(0, hi - s_hi)
    sl = xf[max(lo, s_lo):min(hi, s_hi)]
    shard = np.concatenate(
        [np.zeros((pad_l, DM), np.float32), sl, np.zeros((pad_r, DM), np.float32)], axis=0)
    kvmask = np.ones((1, TH), np.float32)
    if pad_l:
        kvmask[0, :pad_l] = 0.0
    if pad_r:
        kvmask[0, TH - pad_r:] = 0.0
    hsel = np.zeros((1, 2), np.float32)
    hsel[0, seq] = 1.0
    ccmaxbias = np.full((2, 16), -1.0e30, np.float32)
    ccmaxbias[seq, :] = 0.0
    return {
        "x_fm": np.ascontiguousarray(shard.T),
        "kvmask": kvmask,
        "hsel": hsel,
        "hselc": np.ascontiguousarray(hsel.T),
        "ccmaxbias": ccmaxbias,
    }



class Ctx:
    pass


def _setup_common(nc, es, block, shared, want):
    cx = Ctx()
    cx.nc = nc
    cx.es = es
    cx.tk = TK(nc, block, es)
    cx.dram = {}
    for nm in want:
        arr = shared[nm]
        dt = BF16 if arr.dtype == ml_dtypes.bfloat16 else F32
        cx.dram[nm] = nc.dram_tensor(nm, list(arr.shape), dt, kind="ExternalInput").ap()

    cx._scopes = [es]
    cx.sb = lambda name, shape, dt: cx._scopes[-1].enter_context(nc.sbuf_tensor(name, shape, dt))

    def _push():
        cx._scopes.append(contextlib.ExitStack())

    def _pop():
        cx._scopes.pop().close()
        cx.tk.barrier()
    cx.push_scope = _push
    cx.pop_scope = _pop
    cx.eye = cx.sb("c_eye", [128, 128], F32)
    cx.eye16 = cx.sb("c_eye16", [128, 128], BF16)
    cx.ones_row = cx.sb("c_ones_row", [1, 512], F32)
    cx.ones_col16 = cx.sb("c_ones16", [128, 1], BF16)
    tk = cx.tk
    tk.dma("y", lambda e: e.dma_start(out=cx.eye[:, :], in_=cx.dram["eye"][:, :]), writes=["c:eye"])
    tk.dma("y", lambda e: e.dma_start(out=cx.ones_row[:, :], in_=cx.dram["ones_row"][:, :]), writes=["c:ones_row"])
    tk.op("v", lambda e: e.tensor_copy(cx.eye16[:, :], cx.eye[:, :]), reads=["c:eye"], writes=["c:eye16"])
    tk.op("v", lambda e: e.memset(cx.ones_col16[:, :], 1.0), writes=["c:ones_col16"])
    cx.ones_row16 = cx.sb("c_ones_row16", [1, 512], BF16)
    tk.op("v", lambda e: e.memset(cx.ones_row16[:, :], 1.0), writes=["c:ones_row16"])

    cx.pf = [es.enter_context(nc.psum_tensor(f"pf{i}", [128, 512], F32)) for i in range(6)]
    cx.pb = [es.enter_context(nc.psum_tensor(f"pb{i}", [128, 1024], BF16)) for i in range(2)]
    cx.ln_init = False
    return cx


def _chunk(t, kc, width, c0=0, c1=None):
    if c1 is None:
        c1 = width
    return t[:, kc * width + c0: kc * width + c1]



def emit_ln(cx, src_sb, src_key, src_width, src_c0, ntok, dst_sb, dst_key, dst_width, dst_c0):
    nc, tk = cx.nc, cx.tk
    if not cx.ln_init:
        cx.ln_scr = cx.sb("ln_scr", [128, 2 * 512], F32)
        cx.ln_rows = cx.sb("ln_rows", [1, 4 * 512], F32)
        cx.ln_ones_col = cx.sb("ln_ones_col", [128, 1], F32)
        cx.ln_eps = cx.sb("ln_eps", [1, 1], F32)
        tk.op("v", lambda e: e.memset(cx.ln_ones_col[:, :], 1.0), writes=["ln:ones_col"])
        tk.op("v", lambda e: e.memset(cx.ln_eps[:, :], 1e-5), writes=["ln:eps"])
        cx.ln_init = True
    scr, rows = cx.ln_scr, cx.ln_rows

    def sch(kc, a, b):
        return src_sb[:, kc * src_width + src_c0 + a: kc * src_width + src_c0 + b]

    def dch(kc, a, b):
        return dst_sb[:, kc * dst_width + dst_c0 + a: kc * dst_width + dst_c0 + b]

    for ti in range((ntok + 511) // 512):
        c0 = ti * 512
        w = min(512, ntok - c0)
        ps_s, ps_q, ps_b, ps_c = cx.pf[0], cx.pf[1], cx.pf[2], cx.pf[3]

        def mm_sum(e, c0=c0, w=w):
            ins = None
            for kc in range(NCH):
                ins = e.matmul(ps_s[0:1, 0:w], cx.ln_ones_col[:, :], sch(kc, c0, c0 + w),
                               start=(kc == 0), stop=(kc == NCH - 1))
            return ins
        tk.op("t", mm_sum, reads=[src_key, "ln:ones_col"], writes=["ps:0"])
        for kc in range(NCH):
            par = kc % 2
            tk.op("s", lambda e, kc=kc, par=par, c0=c0, w=w: e.activation(
                scr[:, par * 512:par * 512 + w], sch(kc, c0, c0 + w), AF.Square),
                reads=[src_key], writes=[f"ln:scr{par}"])
            tk.op("t", lambda e, kc=kc, par=par, w=w: e.matmul(
                ps_q[0:1, 0:w], cx.ln_ones_col[:, :], scr[:, par * 512:par * 512 + w],
                start=(kc == 0), stop=(kc == NCH - 1)),
                reads=[f"ln:scr{par}", "ln:ones_col"], writes=["ps:1"])
        tk.op("v", lambda e, w=w: e.tensor_scalar(
            rows[0:1, 0:w], ps_s[0:1, 0:w], 1.0 / DM, None, ALU.mult),
            reads=["ps:0"], writes=["ln:m"])
        tk.op("v", lambda e, w=w: e.tensor_mul(
            rows[0:1, 512:512 + w], rows[0:1, 0:w], rows[0:1, 0:w]),
            reads=["ln:m"], writes=["ln:msq"])
        tk.op("v", lambda e, w=w: e.tensor_scalar(
            rows[0:1, 1024:1024 + w], ps_q[0:1, 0:w], 1.0 / DM, None, ALU.mult),
            reads=["ps:1"], writes=["ln:var"])
        tk.op("v", lambda e, w=w: e.tensor_sub(
            rows[0:1, 1024:1024 + w], rows[0:1, 1024:1024 + w], rows[0:1, 512:512 + w]),
            reads=["ln:var", "ln:msq"], writes=["ln:var"])
        tk.op("s", lambda e, w=w: e.activation(
            rows[0:1, 1536:1536 + w], rows[0:1, 1024:1024 + w], AF.Sqrt,
            bias=cx.ln_eps[0:1, 0:1]),
            reads=["ln:var", "ln:eps"], writes=["ln:sd"])
        tk.op("v", lambda e, w=w: e.reciprocal(
            rows[0:1, 1536:1536 + w], rows[0:1, 1536:1536 + w]),
            reads=["ln:sd"], writes=["ln:sd"])
        tk.op("t", lambda e, w=w: e.matmul(
            ps_b[:, 0:w], cx.ones_row[0:1, 0:128], rows[0:1, 1536:1536 + w],
            start=True, stop=True), reads=["ln:sd", "c:ones_row"], writes=["ps:2"])
        tk.op("v", lambda e, w=w: e.tensor_mul(
            rows[0:1, 512:512 + w], rows[0:1, 0:w], rows[0:1, 1536:1536 + w]),
            reads=["ln:m", "ln:sd"], writes=["ln:msq"])
        tk.op("t", lambda e, w=w: e.matmul(
            ps_c[:, 0:w], cx.ones_row[0:1, 0:128], rows[0:1, 512:512 + w],
            start=True, stop=True), reads=["ln:msq", "c:ones_row"], writes=["ps:3"])
        for kc in range(NCH):
            par = kc % 2
            tk.op("v", lambda e, kc=kc, par=par, c0=c0, w=w: e.tensor_mul(
                scr[:, par * 512:par * 512 + w], sch(kc, c0, c0 + w), ps_b[:, 0:w]),
                reads=[src_key, "ps:2"], writes=[f"ln:scr{par}"])
            tk.op("v", lambda e, kc=kc, par=par, c0=c0, w=w: e.tensor_sub(
                dch(kc, c0, c0 + w), scr[:, par * 512:par * 512 + w], ps_c[:, 0:w]),
                reads=[f"ln:scr{par}", "ps:3"], writes=[dst_key])


def emit_l0_attn(cx, xn_sb, xn_key, x_sb, x_key, tiles=None, dbg=None):
    nc, tk, sb = cx.nc, cx.tk, cx.sb
    wq = sb("l0_wq_sb", [128, NCH * 1024], BF16)
    wo = sb("l0_wo_sb", [128, NCH * 1024], BF16)
    wk = sb("l0_wk_sb", [128, NCH * 128], BF16)
    wv = sb("l0_wv_sb", [128, NCH * 128], BF16)
    brow = sb("l0_brow", [1, 2304], F32)
    kvm = sb("l0_kvm", [1, TH], F32)
    msk = sb("l0_msk", [128, 384], F32)
    kbf = sb("l0_k", [128, TH], BF16)
    vtm = sb("l0_v", [128, 18 * 128], BF16)
    qt = sb("l0_qt", [128, 2 * 1024], BF16)
    at = sb("l0_at", [128, 2 * 1024], BF16)
    pr = sb("l0_pr", [128, 2 * 384], F32)
    prn = sb("l0_prn", [128, 2 * 384], BF16)
    prT = sb("l0_prT", [128, 2 * 384], BF16)
    rowm = sb("l0_rowm", [128, 8], F32)

    tk.dma("g", lambda e: e.dma_start(out=wq[:, :], in_=cx.dram["l0_wq_s"][:, :]), writes=["w:l0q"])
    tk.dma("g", lambda e: e.dma_start(out=wk[:, :], in_=cx.dram["l0_wk_s"][:, :]), writes=["w:l0k"])
    tk.dma("g", lambda e: e.dma_start(out=wv[:, :], in_=cx.dram["l0_wv_s"][:, :]), writes=["w:l0v"])
    tk.dma("g", lambda e: e.dma_start(out=wo[:, :], in_=cx.dram["l0_wo_s"][:, :]), writes=["w:l0o"])
    tk.dma("y", lambda e: e.dma_start(out=brow[0:1, 0:1024], in_=cx.dram["l0_bq"][:, :]), writes=["b:l0"])
    tk.dma("y", lambda e: e.dma_start(out=brow[0:1, 1024:1152], in_=cx.dram["l0_bk"][:, :]), writes=["b:l0"])
    tk.dma("y", lambda e: e.dma_start(out=brow[0:1, 1152:1280], in_=cx.dram["l0_bv"][:, :]), writes=["b:l0"])
    tk.dma("y", lambda e: e.dma_start(out=brow[0:1, 1280:2304], in_=cx.dram["l0_bo"][:, :]), writes=["b:l0"])
    tk.dma("y", lambda e: e.dma_start(out=kvm[:, :], in_=cx.dram["kvmask"][:, :]), writes=["b:kvm"])
    tk.dma("y", lambda e: e.dma_start(out=msk[:, :], in_=cx.dram["l0mask"][:, :]), writes=["b:msk"])

    for ti in range(5):
        c0 = ti * 512
        w = min(512, TH - c0)
        pi = ti % 2
        ps = cx.pf[pi]

        def mm_k(e, c0=c0, w=w, ps=ps):
            for kc in range(NCH):
                e.matmul(ps[:, 0:w], wk[:, kc * 128:(kc + 1) * 128],
                         _chunk(xn_sb, kc, TH, c0, c0 + w), start=(kc == 0), stop=False)
            return e.matmul(ps[:, 0:w], brow[0:1, 1024:1152], kvm[0:1, c0:c0 + w],
                            start=False, stop=True)
        tk.op("t", mm_k, reads=[xn_key, "w:l0k", "b:l0", "b:kvm"], writes=[f"ps:{pi}"])
        tk.op("s", lambda e, c0=c0, w=w, ps=ps: e.copy(kbf[:, c0:c0 + w], ps[:, 0:w]),
              reads=[f"ps:{pi}"], writes=["l0:k"])
    for j in range(18):
        c0 = j * 128
        pi = 2 + j % 2
        ps = cx.pf[pi]

        def mm_v(e, c0=c0, ps=ps):
            for kc in range(NCH):
                e.matmul(ps[:, 0:128], _chunk(xn_sb, kc, TH, c0, c0 + 128),
                         wv[:, kc * 128:(kc + 1) * 128], start=(kc == 0), stop=False)
            return e.matmul(ps[:, 0:128], kvm[0:1, c0:c0 + 128], brow[0:1, 1152:1280],
                            start=False, stop=True)
        tk.op("t", mm_v, reads=[xn_key, "w:l0v", "b:l0", "b:kvm"], writes=[f"ps:{pi}"])
        tk.op("s", lambda e, j=j, ps=ps: e.copy(vtm[:, j * 128:(j + 1) * 128], ps[:, 0:128]),
              reads=[f"ps:{pi}"], writes=["l0:v"])

    if tiles is None:
        tiles = range(16)
    if dbg is not None:
        tk.dma("y", lambda e: e.dma_start(out=dbg["k"][:, :], in_=kbf[:, :]), reads=["l0:k"], writes=["dbg"])
        tk.dma("y", lambda e: e.dma_start(out=dbg["v"][:, :], in_=vtm[:, :]), reads=["l0:v"], writes=["dbg"])
    for ti in tiles:
        par = ti % 2
        a = ti * 128
        ah = a + HALO
        for qc in range(NCH):
            pi = 4 + qc % 2
            ps = cx.pf[pi]

            def mm_qc(e, qc=qc, ah=ah, ps=ps):
                for kc in range(NCH):
                    e.matmul(ps[:, 0:128],
                             wq[:, kc * 1024 + qc * 128: kc * 1024 + (qc + 1) * 128],
                             _chunk(xn_sb, kc, TH, ah, ah + 128), start=(kc == 0), stop=False)
                return e.matmul(ps[:, 0:128], brow[0:1, qc * 128:(qc + 1) * 128],
                                cx.ones_row[0:1, 0:128], start=False, stop=True)
            tk.op("t", mm_qc, reads=[xn_key, "w:l0q", "b:l0", "c:ones_row"], writes=[f"ps:{pi}"])
            tk.op("s", lambda e, qc=qc, par=par, ps=ps: e.copy(
                qt[:, par * 1024 + qc * 128: par * 1024 + (qc + 1) * 128], ps[:, 0:128]),
                reads=[f"ps:{pi}"], writes=[f"l0:qt{par}"])
        for h in range(16):
            qc, base = h // 2, 64 * (h % 2)
            hp = h % 2
            ps_s = cx.pf[hp]
            ps_t = cx.pb[hp]
            ps_o = cx.pf[2 + hp]
            tk.op("t", lambda e, qc=qc, base=base, par=par, a=a, ps_s=ps_s: e.matmul(
                ps_s[0:128, 0:384],
                qt[base:base + 64, par * 1024 + qc * 128: par * 1024 + (qc + 1) * 128],
                kbf[base:base + 64, a:a + 384], start=True, stop=True),
                reads=[f"l0:qt{par}", "l0:k"], writes=[f"ps:{hp}"])
            if dbg is not None and ti == list(tiles)[0] and h == 0 and "sc" in dbg:
                tk.op("v", lambda e, ps_s=ps_s: e.tensor_copy(pr[:, 384:768], ps_s[:, 0:384]),
                      reads=["ps:0"], writes=["l0:scdump"])
                tk.dma("y", lambda e: e.dma_start(out=dbg["sc"][:, :], in_=pr[:, 384:768]),
                       reads=["l0:scdump"], writes=["dbg"])
            tk.op("v", lambda e, hp=hp, ps_s=ps_s: e.reduce_max(
                rowm[:, hp:hp + 1], ps_s[:, 0:384], axis=AX),
                reads=[f"ps:{hp}"], writes=[f"l0:rm{hp}"])
            tk.op("s", lambda e, hp=hp: e.mul(
                rowm[:, 2 + hp:3 + hp], rowm[:, hp:hp + 1], -1.0),
                reads=[f"l0:rm{hp}"], writes=[f"l0:nrm{hp}"])
            tk.op("s", lambda e, hp=hp, ps_s=ps_s: e.activation(
                pr[:, hp * 384:(hp + 1) * 384], ps_s[:, 0:384], AF.Exp,
                bias=rowm[:, 2 + hp:3 + hp]),
                reads=[f"ps:{hp}", f"l0:nrm{hp}"], writes=[f"l0:pr{hp}"])
            tk.op("v", lambda e, hp=hp: e.scalar_tensor_tensor(
                pr[:, hp * 384:(hp + 1) * 384], pr[:, hp * 384:(hp + 1) * 384],
                1.0, msk[:, 0:384], ALU.mult, ALU.mult,
                accum_out=rowm[:, 4 + hp:5 + hp]),
                reads=[f"l0:pr{hp}", "b:msk"], writes=[f"l0:pr{hp}", f"l0:rs{hp}"])
            tk.op("v", lambda e, hp=hp: e.reciprocal(
                rowm[:, 6 + hp:7 + hp], rowm[:, 4 + hp:5 + hp]),
                reads=[f"l0:rs{hp}"], writes=[f"l0:ri{hp}"])
            tk.op("v", lambda e, hp=hp: e.tensor_scalar(
                prn[:, hp * 384:(hp + 1) * 384], pr[:, hp * 384:(hp + 1) * 384],
                rowm[:, 6 + hp:7 + hp], None, ALU.mult),
                reads=[f"l0:pr{hp}", f"l0:ri{hp}"], writes=[f"l0:prn{hp}"])

            def mm_tr(e, hp=hp, ps_t=ps_t):
                ins = None
                for uc in range(3):
                    ins = e.transpose(ps_t[0:128, uc * 128:(uc + 1) * 128],
                                      prn[:, hp * 384 + uc * 128: hp * 384 + (uc + 1) * 128],
                                      cx.eye16[:, :])
                return ins
            tk.op("t", mm_tr, reads=[f"l0:prn{hp}", "c:eye16"], writes=[f"ps:b{hp}"])
            tk.op("s", lambda e, hp=hp, ps_t=ps_t: e.copy(
                prT[:, hp * 384:(hp + 1) * 384], ps_t[0:128, 0:384]),
                reads=[f"ps:b{hp}"], writes=[f"l0:prT{hp}"])

            def mm_pv(e, hp=hp, base=base, a=a, ps_o=ps_o):
                ins = None
                for uc in range(3):
                    t0 = a // 128 + uc
                    ins = e.matmul(
                        ps_o[0:64, 0:128],
                        vtm[:, t0 * 128 + base: t0 * 128 + base + 64],
                        prT[:, hp * 384 + uc * 128: hp * 384 + (uc + 1) * 128],
                        start=(uc == 0), stop=(uc == 2))
                return ins
            tk.op("t", mm_pv, reads=["l0:v", f"l0:prT{hp}"], writes=[f"ps:{2 + hp}"])
            tk.op("s", lambda e, qc=qc, base=base, par=par, hp=hp, ps_o=ps_o: e.copy(
                at[base:base + 64, par * 1024 + qc * 128: par * 1024 + (qc + 1) * 128],
                ps_o[0:64, 0:128]),
                reads=[f"ps:{2 + hp}"], writes=[f"l0:at{par}"])
        if dbg is not None and ti == list(tiles)[0]:
            tk.dma("y", lambda e, par=par: e.dma_start(out=dbg["qt"][:, :], in_=qt[:, par * 1024:(par + 1) * 1024]), reads=[f"l0:qt{par}"], writes=["dbg"])
            tk.dma("y", lambda e: e.dma_start(out=dbg["pr"][:, :], in_=pr[:, 0:384]), reads=["l0:pr0"], writes=["dbg"])
            tk.dma("y", lambda e: e.dma_start(out=dbg["prn"][:, :], in_=prn[:, 0:384]), reads=["l0:prn0"], writes=["dbg"])
            tk.dma("y", lambda e: e.dma_start(out=dbg["prT"][:, :], in_=prT[:, 0:384]), reads=["l0:prT0"], writes=["dbg"])
            tk.dma("y", lambda e, par=par: e.dma_start(out=dbg["at"][:, :], in_=at[:, par * 1024:(par + 1) * 1024]), reads=[f"l0:at{par}"], writes=["dbg"])
            tk.dma("y", lambda e: e.dma_start(out=dbg["rowm"][:, :], in_=rowm[:, :]), reads=["l0:rs0", "l0:ri0", "l0:rm0"], writes=["dbg"])
        for oc in range(NCH):
            pi = 4 + oc % 2
            ps = cx.pf[pi]

            def mm_wo(e, oc=oc, par=par, ps=ps):
                for kc in range(NCH):
                    e.matmul(ps[:, 0:128],
                             wo[:, kc * 1024 + oc * 128: kc * 1024 + (oc + 1) * 128],
                             at[:, par * 1024 + kc * 128: par * 1024 + (kc + 1) * 128],
                             start=(kc == 0), stop=False)
                return e.matmul(ps[:, 0:128], brow[0:1, 1280 + oc * 128:1280 + (oc + 1) * 128],
                                cx.ones_row[0:1, 0:128], start=False, stop=True)
            tk.op("t", mm_wo, reads=[f"l0:at{par}", "w:l0o", "b:l0", "c:ones_row"],
                  writes=[f"ps:{pi}"])
            tk.op("v", lambda e, oc=oc, a=a, ps=ps, pi=pi: e.tensor_add(
                _chunk(x_sb, oc, TPC, a, a + 128), _chunk(x_sb, oc, TPC, a, a + 128),
                ps[:, 0:128]),
                reads=[f"ps:{pi}", x_key], writes=[x_key])



L0A_SHARED = ["l0_wq_s", "l0_bq", "l0_wk_s", "l0_bk", "l0_wv_s", "l0_bv", "l0_wo_s", "l0_bo",
              "eye", "ones_row", "l0mask"]


def build_l0a():
    nc = bass.Bass("TRN2", target_bir_lowering=False)
    es = contextlib.ExitStack()
    with es:
        x_in = nc.dram_tensor("x_fm", [DM, TH], F32, kind="ExternalInput").ap()
        kvm_in = nc.dram_tensor("kvmask", [1, TH], F32, kind="ExternalInput").ap()
        xn_out = nc.dram_tensor("xn_out", [DM, TH], BF16, kind="ExternalOutput").ap()
        x1p_out = nc.dram_tensor("x1p_out", [DM, TPC], F32, kind="ExternalOutput").ap()

        block = es.enter_context(nc.Block())
        cx = _setup_common(nc, es, block, _SHARED_CACHE, L0A_SHARED)
        cx.dram["kvmask"] = kvm_in
        tk = cx.tk

        x_sb = cx.sb("x_sb", [128, NCH * TPC], F32)
        xh_sb = cx.sb("xh_sb", [128, NCH * 256], F32)
        xn_sb = cx.sb("xn_sb", [128, NCH * TH], BF16)

        for kc in range(NCH):
            tk.dma("y", lambda e, kc=kc: e.dma_start(
                out=_chunk(x_sb, kc, TPC), in_=x_in[kc * 128:(kc + 1) * 128, HALO:HALO + TPC]),
                writes=["x"])
            tk.dma("y", lambda e, kc=kc: e.dma_start(
                out=_chunk(xh_sb, kc, 256, 0, 128), in_=x_in[kc * 128:(kc + 1) * 128, 0:HALO]),
                writes=["xh"])
            tk.dma("y", lambda e, kc=kc: e.dma_start(
                out=_chunk(xh_sb, kc, 256, 128, 256),
                in_=x_in[kc * 128:(kc + 1) * 128, TH - HALO:TH]),
                writes=["xh"])

        emit_ln(cx, x_sb, "x", TPC, 0, TPC, xn_sb, "xn", TH, HALO)
        emit_ln(cx, xh_sb, "xh", 256, 0, 128, xn_sb, "xn", TH, 0)
        emit_ln(cx, xh_sb, "xh", 256, 128, 128, xn_sb, "xn", TH, TH - HALO)

        emit_l0_attn(cx, xn_sb, "xn", x_sb, "x")

        for kc in range(NCH):
            tk.dma("y", lambda e, kc=kc: e.dma_start(
                out=xn_out[kc * 128:(kc + 1) * 128, :], in_=_chunk(xn_sb, kc, TH)),
                reads=["xn"], writes=["o:xn"])
            tk.dma("y", lambda e, kc=kc: e.dma_start(
                out=x1p_out[kc * 128:(kc + 1) * 128, :], in_=_chunk(x_sb, kc, TPC)),
                reads=["x"], writes=["o:x1p"])

        tk.flush()
        block.sync(lambda eng: tk.final_wait(eng))
    return nc




MOE_SHARED = ["moe_scale", "moe_wg_s", "moe_bg", "moe_w1_s", "moe_b1c", "moe_w2_s", "moe_b2",
              "eye", "ones_row"]


def emit_moe(cx, xn_sb, xn_key, xnw, xc0, x_sb, x_key, experts=range(MOE_E), do_h=True, do_w2=True):
    nc, tk, sb = cx.nc, cx.tk, cx.sb
    E = MOE_E

    def xch(kc, a, b):
        return xn_sb[:, kc * xnw + xc0 + a: kc * xnw + xc0 + b]

    wgb = sb("moe_wg_sb", [128, NCH * 16], BF16)
    bgr = sb("moe_bg_sb", [1, 16], F32)
    sclc = sb("moe_scl_sb", [128, NCH], F32)
    b2sb = sb("moe_b2_sb", [16, 1024], BF16)
    w1b = sb("moe_w1_sb", [128, 2 * NCH * 256], BF16)
    b1c = sb("moe_b1c_sb", [128, 16], F32)
    b1w = sb("moe_b1w_sb", [128, 16], F32)
    w2b = sb("moe_w2_sb", [128, NCH * 1024], BF16)
    ssb = sb("moe_s_sb", [128, 8 * 1024], BF16)
    acc = sb("moe_acc", [128, 8 * 1024], F32)
    gsc = sb("moe_gsc", [128, 64], F32)
    lgs = sb("moe_lg", [128, 16], F32)
    gts = sb("moe_gts", [128, 16 * 16], F32)
    gTb = sb("moe_gT", [16, 16 * 128], BF16)
    msc = sb("moe_msc", [128, 4 * 512], F32)
    c7 = sb("moe_c7", [128, 1], F32)
    tk.op("v", lambda e: e.memset(c7[:, :], 7.0), writes=["moe:c7"])

    tk.dma("g", lambda e: e.dma_start(out=wgb[:, :], in_=cx.dram["moe_wg_s"][:, :]), writes=["w:wg"])
    tk.dma("y", lambda e: e.dma_start(out=bgr[:, :], in_=cx.dram["moe_bg"][:, :]), writes=["b:bg"])
    tk.dma("y", lambda e: e.dma_start(out=sclc[:, :], in_=cx.dram["moe_scale"][:, :]), writes=["b:scl"])
    tk.dma("g", lambda e: e.dma_start(out=b2sb[:, :], in_=cx.dram["moe_b2"][:, :]), writes=["w:b2"])

    for kc in range(NCH):
        for ti in range(4):
            c0 = ti * 512
            tk.op("v", lambda e, kc=kc, c0=c0: e.tensor_add(
                _chunk(x_sb, kc, TPC, c0, c0 + 512), _chunk(x_sb, kc, TPC, c0, c0 + 512),
                xch(kc, c0, c0 + 512)),
                reads=[x_key, xn_key], writes=[x_key])

    scr, rows = cx.ln_scr, cx.ln_rows
    for ti in range(4):
        c0 = ti * 512
        w = 512
        ps_q, ps_b = cx.pf[1], cx.pf[2]
        for kc in range(NCH):
            par = kc % 2
            tk.op("s", lambda e, kc=kc, par=par, c0=c0, w=w: e.activation(
                scr[:, par * 512:par * 512 + w], xch(kc, c0, c0 + w), AF.Square),
                reads=[xn_key], writes=[f"ln:scr{par}"])
            tk.op("t", lambda e, kc=kc, par=par, w=w: e.matmul(
                ps_q[0:1, 0:w], cx.ln_ones_col[:, :], scr[:, par * 512:par * 512 + w],
                start=(kc == 0), stop=(kc == NCH - 1)),
                reads=[f"ln:scr{par}", "ln:ones_col"], writes=["ps:1"])
        tk.op("v", lambda e, w=w: e.tensor_scalar(
            rows[0:1, 0:w], ps_q[0:1, 0:w], 1.0 / DM, None, ALU.mult),
            reads=["ps:1"], writes=["ln:m"])
        tk.op("s", lambda e, w=w: e.activation(
            rows[0:1, 512:512 + w], rows[0:1, 0:w], AF.Sqrt, bias=cx.ln_eps[0:1, 0:1]),
            reads=["ln:m", "ln:eps"], writes=["ln:sd"])
        tk.op("v", lambda e, w=w: e.reciprocal(
            rows[0:1, 512:512 + w], rows[0:1, 512:512 + w]),
            reads=["ln:sd"], writes=["ln:sd"])
        tk.op("t", lambda e, w=w: e.matmul(
            ps_b[:, 0:w], cx.ones_row[0:1, 0:128], rows[0:1, 512:512 + w],
            start=True, stop=True), reads=["ln:sd", "c:ones_row"], writes=["ps:2"])
        for kc in range(NCH):
            par = kc % 2
            tk.op("v", lambda e, kc=kc, par=par, c0=c0, w=w: e.tensor_mul(
                scr[:, par * 512:par * 512 + w], xch(kc, c0, c0 + w), ps_b[:, 0:w]),
                reads=[xn_key, "ps:2"], writes=[f"ln:scr{par}"])
            tk.op("v", lambda e, kc=kc, par=par, c0=c0, w=w: e.tensor_scalar(
                xch(kc, c0, c0 + w), scr[:, par * 512:par * 512 + w],
                sclc[:, kc:kc + 1], None, ALU.mult),
                reads=[f"ln:scr{par}", "b:scl"], writes=[xn_key])

    for tg in range(16):
        t0 = tg * 128
        ps = cx.pf[3]

        def mm_lg(e, t0=t0, ps=ps):
            for kc in range(NCH):
                e.matmul(ps[:, 0:16], xch(kc, t0, t0 + 128), wgb[:, kc * 16:(kc + 1) * 16],
                         start=(kc == 0), stop=False)
            return e.matmul(ps[:, 0:16], cx.ones_row[0:1, 0:128], bgr[0:1, 0:16],
                            start=False, stop=True)
        tk.op("t", mm_lg, reads=[xn_key, "w:wg", "b:bg", "c:ones_row"], writes=["ps:3"])
        tk.op("v", lambda e, ps=ps: e.tensor_copy(lgs[:, :], ps[:, 0:16]),
              reads=["ps:3"], writes=["moe:lg"])
        tk.op("v", lambda e: e.tensor_copy(gsc[:, 0:16], lgs[:, :]),
              reads=["moe:lg"], writes=["moe:cur"])
        tk.op("v", lambda e: e.reduce_max(gsc[:, 32:33], gsc[:, 0:16], axis=AX),
              reads=["moe:cur"], writes=["moe:m1"])
        tk.op("s", lambda e: e.mul(gsc[:, 34:35], gsc[:, 32:33], -1.0),
              reads=["moe:m1"], writes=["moe:nm1"])
        for it in range(3):
            mkc = 32 if it == 0 else 33
            tk.op("v", lambda e, mkc=mkc: e.tensor_scalar(
                gsc[:, 16:32], gsc[:, 0:16], gsc[:, mkc:mkc + 1], None, ALU.is_ge),
                reads=["moe:cur", "moe:m1" if mkc == 32 else "moe:mk"], writes=["moe:mask"])
            tk.op("v", lambda e: e.scalar_tensor_tensor(
                gsc[:, 0:16], gsc[:, 16:32], -1.0e9, gsc[:, 0:16], ALU.mult, ALU.add),
                reads=["moe:mask", "moe:cur"], writes=["moe:cur"])
            tk.op("v", lambda e: e.reduce_max(gsc[:, 33:34], gsc[:, 0:16], axis=AX),
                  reads=["moe:cur"], writes=["moe:mk"])
        tk.op("v", lambda e: e.tensor_scalar(
            gsc[:, 16:32], lgs[:, :], gsc[:, 33:34], None, ALU.is_ge),
            reads=["moe:lg", "moe:mk"], writes=["moe:mask"])
        tk.op("s", lambda e: e.activation(
            gsc[:, 0:16], lgs[:, :], AF.Exp, bias=gsc[:, 34:35]),
            reads=["moe:lg", "moe:nm1"], writes=["moe:cur"])
        tk.op("v", lambda e: e.scalar_tensor_tensor(
            gsc[:, 0:16], gsc[:, 0:16], 1.0, gsc[:, 16:32], ALU.mult, ALU.mult,
            accum_out=gsc[:, 35:36]),
            reads=["moe:cur", "moe:mask"], writes=["moe:cur", "moe:zs"])
        tk.op("v", lambda e: e.reciprocal(gsc[:, 36:37], gsc[:, 35:36]),
              reads=["moe:zs"], writes=["moe:zi"])
        tk.op("v", lambda e, tg=tg: e.tensor_scalar(
            gts[:, tg * 16:(tg + 1) * 16], gsc[:, 0:16], gsc[:, 36:37], None, ALU.mult),
            reads=["moe:cur", "moe:zi"], writes=["moe:gts"])
        tk.op("t", lambda e, tg=tg, ps=ps: e.transpose(
            ps[0:16, 0:128], gts[:, tg * 16:(tg + 1) * 16], cx.eye[:, :]),
            reads=["moe:gts", "c:eye"], writes=["ps:3"])
        tk.op("s", lambda e, tg=tg, ps=ps: e.copy(
            gTb[0:16, tg * 128:(tg + 1) * 128], ps[0:16, 0:128]),
            reads=["ps:3"], writes=["moe:gT"])

    if getattr(cx, "moe_dbg", None) is not None:
        tk.dma("y", lambda e: e.dma_start(out=cx.moe_dbg[0][:, :], in_=gts[:, :]),
               reads=["moe:gts"], writes=["dbg"])
    for half in range(2):
        hoff = half * 1024
        for tsl in range(8):
            tg = half * 8 + tsl
            for h2 in range(2):
                ps = cx.pf[4 + h2 % 2]
                tk.op("t", lambda e, tg=tg, h2=h2, ps=ps: e.matmul(
                    ps[:, 0:512], gTb[0:16, tg * 128:(tg + 1) * 128],
                    b2sb[0:16, h2 * 512:(h2 + 1) * 512], start=True, stop=True),
                    reads=["moe:gT", "w:b2"], writes=[f"ps:{4 + h2 % 2}"])
                tk.op("v", lambda e, tsl=tsl, h2=h2, ps=ps: e.tensor_copy(
                    acc[:, tsl * 1024 + h2 * 512: tsl * 1024 + (h2 + 1) * 512],
                    ps[:, 0:512]),
                    reads=[f"ps:{4 + h2 % 2}"], writes=["moe:acc"])
        for ex in experts:
            wpar = ex % 2
            tk.dma("g", lambda e, ex=ex: e.dma_start(
                out=w2b[:, :], in_=cx.dram["moe_w2_s"][ex]),
                writes=["w:w2"])
            tk.dma("y", lambda e, ex=ex: e.dma_start(
                out=b1c[:, :], in_=cx.dram["moe_b1c"][ex]),
                writes=["b:b1"])
            tk.op("v", lambda e: e.tensor_scalar(
                b1w[:, :], b1c[:, :], -1.0, 7.0, ALU.mult, ALU.add),
                reads=["b:b1"], writes=["b:b1w"])
            for j in (range(8) if do_h else []):
                jpar = (ex * 8 + j) % 2
                tk.dma("g", lambda e, ex=ex, j=j, jpar=jpar: e.dma_start(
                    out=w1b[:, jpar * 2048:(jpar + 1) * 2048],
                    in_=cx.dram["moe_w1_s"][ex, j]),
                    writes=[f"w:w1{jpar}"])
                for ttl in range(2):
                    tc0 = hoff + ttl * 512
                    ppar = ttl % 2
                    ps_a, ps_b2 = cx.pf[ppar], cx.pf[2 + ppar]

                    def mm_h(e, j=j, jpar=jpar, wpar=wpar, tc0=tc0, ps_a=ps_a, ps_b2=ps_b2):
                        for kc in range(NCH):
                            e.matmul(ps_a[:, 0:512],
                                     w1b[:, jpar * 2048 + kc * 256: jpar * 2048 + kc * 256 + 128],
                                     xch(kc, tc0, tc0 + 512), start=(kc == 0), stop=(kc == NCH - 1))
                        ins = None
                        for kc in range(NCH):
                            ins = e.matmul(ps_b2[:, 0:512],
                                           w1b[:, jpar * 2048 + kc * 256 + 128: jpar * 2048 + (kc + 1) * 256],
                                           xch(kc, tc0, tc0 + 512), start=(kc == 0), stop=(kc == NCH - 1))
                        return ins
                    tk.op("t", mm_h, reads=[xn_key, f"w:w1{jpar}"],
                          writes=[f"ps:{ppar}", f"ps:{2 + ppar}"])
                    sa = 1024 * ppar
                    tk.op("v", lambda e, sa=sa, j=j, ps_a=ps_a: e.tensor_scalar(
                        msc[:, sa:sa + 512], ps_a[:, 0:512], b1c[:, 2 * j:2 * j + 1], 7.0,
                        ALU.add, ALU.min),
                        reads=[f"ps:{ppar}", "b:b1"], writes=[f"moe:s0{ppar}"])
                    tk.op("s", lambda e, sa=sa: e.activation(
                        msc[:, sa + 512:sa + 1024], msc[:, sa:sa + 512], AF.Silu, scale=GLU_ALPHA),
                        reads=[f"moe:s0{ppar}"], writes=[f"moe:s1{ppar}"])
                    tk.op("s", lambda e, sa=sa, j=j, ps_b2=ps_b2: e.activation(
                        msc[:, sa:sa + 512], ps_b2[:, 0:512], AF.Relu,
                        bias=b1w[:, 2 * j + 1:2 * j + 2], scale=-1.0),
                        reads=[f"ps:{2 + ppar}", "b:b1w", f"moe:s0{ppar}"], writes=[f"moe:s0{ppar}"])
                    tk.op("v", lambda e, sa=sa: e.tensor_scalar(
                        msc[:, sa:sa + 512], msc[:, sa:sa + 512],
                        14.0, -1.0 / GLU_ALPHA, ALU.min, ALU.mult),
                        reads=[f"moe:s0{ppar}"], writes=[f"moe:s0{ppar}"])
                    tk.op("v", lambda e, sa=sa, j=j, tc0=tc0, ttl=ttl: e.scalar_tensor_tensor(
                        ssb[:, j * 1024 + ttl * 512: j * 1024 + (ttl + 1) * 512],
                        msc[:, sa:sa + 512], 8.0 / GLU_ALPHA,
                        msc[:, sa + 512:sa + 1024], ALU.add, ALU.mult),
                        reads=[f"moe:s0{ppar}", f"moe:s1{ppar}"], writes=["moe:s"])
            for ttl in (range(2) if do_w2 else []):
                for tsub in range(4):
                    tsl = ttl * 4 + tsub
                    tg = half * 8 + tsl
                    for h2 in range(2):
                        ppar = (tsl * 2 + h2) % 2
                        ps = cx.pf[4 + ppar]

                        def mm_y(e, ttl=ttl, tsub=tsub, h2=h2, wpar=wpar, ps=ps):
                            ins = None
                            for j in range(8):
                                ins = e.matmul(
                                    ps[:, 0:512],
                                    ssb[:, j * 1024 + ttl * 512 + tsub * 128: j * 1024 + ttl * 512 + (tsub + 1) * 128],
                                    w2b[:, j * 1024 + h2 * 512: j * 1024 + (h2 + 1) * 512],
                                    start=(j == 0), stop=(j == 7))
                            return ins
                        tk.op("t", mm_y, reads=["moe:s", "w:w2"], writes=[f"ps:{4 + ppar}"])
                        tk.op("v", lambda e, tsl=tsl, tg=tg, h2=h2, ex=ex, ps=ps: e.scalar_tensor_tensor(
                            acc[:, tsl * 1024 + h2 * 512: tsl * 1024 + (h2 + 1) * 512],
                            ps[:, 0:512], gts[:, tg * 16 + ex: tg * 16 + ex + 1],
                            acc[:, tsl * 1024 + h2 * 512: tsl * 1024 + (h2 + 1) * 512],
                            ALU.mult, ALU.add),
                            reads=[f"ps:{4 + ppar}", "moe:gts", "moe:acc"], writes=["moe:acc"])
        for tsl in range(8):
            for oc in range(NCH):
                ppar = (tsl * 8 + oc) % 2
                ps = cx.pf[ppar]
                tk.op("t", lambda e, tsl=tsl, oc=oc, ps=ps: e.transpose(
                    ps[0:128, 0:128], acc[:, tsl * 1024 + oc * 128: tsl * 1024 + (oc + 1) * 128],
                    cx.eye[:, :]),
                    reads=["moe:acc", "c:eye"], writes=[f"ps:{ppar}"])
                tk.op("v", lambda e, tsl=tsl, oc=oc, hoff=hoff, ps=ps, ppar=ppar: e.tensor_add(
                    _chunk(x_sb, oc, TPC, hoff + tsl * 128, hoff + (tsl + 1) * 128),
                    _chunk(x_sb, oc, TPC, hoff + tsl * 128, hoff + (tsl + 1) * 128),
                    ps[0:128, 0:128]),
                    reads=[f"ps:{ppar}", x_key], writes=[x_key])


def build_l0b():
    nc = bass.Bass("TRN2", target_bir_lowering=False)
    es = contextlib.ExitStack()
    with es:
        xn_in = nc.dram_tensor("xn_in", [DM, TPC], BF16, kind="ExternalInput").ap()
        x1p_in = nc.dram_tensor("x1p_in", [DM, TPC], F32, kind="ExternalInput").ap()
        x1_out = nc.dram_tensor("x1_out", [DM, TPC], F32, kind="ExternalOutput").ap()

        block = es.enter_context(nc.Block())
        cx = _setup_common(nc, es, block, _SHARED_CACHE, MOE_SHARED)
        tk = cx.tk
        cx.ln_scr = cx.sb("ln_scr", [128, 2 * 512], F32)
        cx.ln_rows = cx.sb("ln_rows", [1, 2 * 512], F32)
        cx.ln_ones_col = cx.sb("ln_ones_col", [128, 1], F32)
        cx.ln_eps = cx.sb("ln_eps", [1, 1], F32)
        tk.op("v", lambda e: e.memset(cx.ln_ones_col[:, :], 1.0), writes=["ln:ones_col"])
        tk.op("v", lambda e: e.memset(cx.ln_eps[:, :], 1e-5), writes=["ln:eps"])
        cx.ln_init = True

        x_sb = cx.sb("x_sb", [128, NCH * TPC], F32)
        xn_sb = cx.sb("xn_sb", [128, NCH * TPC], BF16)
        for kc in range(NCH):
            tk.dma("y", lambda e, kc=kc: e.dma_start(
                out=_chunk(x_sb, kc, TPC), in_=x1p_in[kc * 128:(kc + 1) * 128, :]), writes=["x"])
            tk.dma("y", lambda e, kc=kc: e.dma_start(
                out=_chunk(xn_sb, kc, TPC), in_=xn_in[kc * 128:(kc + 1) * 128, :]), writes=["xn"])

        emit_moe(cx, xn_sb, "xn", TPC, 0, x_sb, "x")

        for kc in range(NCH):
            tk.dma("y", lambda e, kc=kc: e.dma_start(
                out=x1_out[kc * 128:(kc + 1) * 128, :], in_=_chunk(x_sb, kc, TPC)),
                reads=["x"], writes=["o:x1"])
        tk.flush()
        block.sync(lambda eng: tk.final_wait(eng))
    return nc




L1A_SHARED = ["l1_wq_s", "l1_bq", "l1_wk_s", "l1_bk", "l1_wv_s", "l1_bv", "l1_wo_s", "l1_bo",
              "l1_wfc_s", "l1_bfc", "l1_projT", "eye", "ones_row"]

CTXW = 3 * 65
STG_W = 16 * CTXW


def emit_l1_attn(cx, x_in_ap, xn_sb, xn_key, xnw, xc0, y0_sb, y0_key, core_dram, x_dep=(), cc=True):
    nc, tk, sb = cx.nc, cx.tk, cx.sb
    DN, RATIO, EPS = L1_DN, L1_RATIO, L1_EPS

    def xch(kc, a, b):
        return xn_sb[:, kc * xnw + xc0 + a: kc * xnw + xc0 + b]

    RG = [[0, 1, 2, 3], [4, 5, 6, 7]]
    ccm_in = nc.dram_tensor("ccm_in", [1, 16], F32)
    ccm_out = nc.dram_tensor("ccm_out", [1, 16], F32)
    ccs_in = nc.dram_tensor("ccs_in", [128, STG_W], F32)
    ccs_out = nc.dram_tensor("ccs_out", [128, STG_W], F32)

    wA = sb("l1_wA", [128, NCH * 1024], BF16)
    wB = sb("l1_wB", [128, NCH * 1024], BF16)
    br = sb("l1_br", [1, 2048], BF16)
    projT = sb("l1_projT_sb", [128, 266], BF16)
    vtm = sb("l1_v", [128, 16 * 1024], BF16)
    kbuf = sb("l1_kbuf", [128, NCH * TPC], BF16)
    xt = sb("l1_xt", [128, NCH * 512], F32)
    cx.l1_xt = xt
    diagk = sb("l1_diagk", [128, 256], F32)
    colmax = sb("l1_colmax", [128, 16], F32)
    stg = sb("l1_stg", [128, STG_W], F32)
    cx.l1_stg_dbg = stg
    stg2 = sb("l1_stg2", [16, 256], F32)
    ctxb = sb("l1_ctx", [128, 16 * 192], BF16)
    ksumb = sb("l1_ksum", [128, 48], BF16)
    mxb = sb("l1_mxb", [128, 16], F32)
    rsc = sb("l1_rsc", [128, 16], F32)
    rrow = sb("l1_rrow", [1, 512], F32)
    exs = sb("l1_exs", [128, 2 * 266], F32)
    kpb = sb("l1_kpb", [128, 2 * 266], BF16)
    qpT = sb("l1_qpT", [128, 2 * 384], BF16)
    qps = sb("l1_qps", [128, 2 * 384], BF16)
    sq64 = sb("l1_sq64", [128, 2 * 128], F32)
    qt = sb("l1_qt", [128, 2 * 1024], BF16)
    ath = sb("l1_ath", [128, 2 * 1024], BF16)
    msel = sb("l1_msel", [1, 16], F32)

    tk.dma("g", lambda e: e.dma_start(out=projT[:, :], in_=cx.dram["l1_projT"][:, :]), writes=["l1:projT"])

    for tt in range(4):
        c0 = tt * 512
        for kc in range(NCH):
            tk.dma("y", lambda e, kc=kc, c0=c0: e.dma_start(
                out=_chunk(xt, kc, 512), in_=x_in_ap[kc * 128:(kc + 1) * 128, c0:c0 + 512]),
                reads=list(x_dep), writes=["l1:xt"])
        emit_ln(cx, xt, "l1:xt", 512, 0, 512, xn_sb, xn_key, xnw, xc0 + c0)

    tk.dma("g", lambda e: e.dma_start(
        out=wA[:, :], in_=cx.dram["l1_wv_s"][:, :]), writes=["w:A"])
    tk.dma("g", lambda e: e.dma_start(out=br[0:1, 0:1024], in_=cx.dram["l1_bv"][:, :]), writes=["b:r0"])
    for j in range(16):
        c0 = j * 128
        for vh in range(2):
            pi = vh % 2
            ps = cx.pf[pi]

            def mm_v(e, c0=c0, vh=vh, ps=ps):
                for kc in range(NCH):
                    e.matmul(ps[:, 0:512], xch(kc, c0, c0 + 128),
                             wA[:, kc * 1024 + vh * 512: kc * 1024 + (vh + 1) * 512],
                             start=(kc == 0), stop=False)
                return e.matmul(ps[:, 0:512], cx.ones_row16[0:1, 0:128],
                                br[0:1, vh * 512:(vh + 1) * 512], start=False, stop=True)
            tk.op("t", mm_v, reads=[xn_key, "w:A", "b:r0", "c:ones_row16"], writes=[f"ps:{pi}"])
            tk.op("s", lambda e, j=j, vh=vh, ps=ps: e.copy(
                vtm[:, j * 1024 + vh * 512: j * 1024 + (vh + 1) * 512], ps[:, 0:512]),
                reads=[f"ps:{pi}"], writes=["l1:v"])

    tk.dma("g", lambda e: e.dma_start(
        out=wA[:, :], in_=cx.dram["l1_wk_s"][:, :]), writes=["w:A"])
    tk.dma("g", lambda e: e.dma_start(out=br[0:1, 0:1024], in_=cx.dram["l1_bk"][:, :]), writes=["b:r0"])
    for ti4 in range(4):
        c0 = ti4 * 512
        for oc in range(NCH):
            pi = 2 + oc % 2
            ps = cx.pf[pi]

            def mm_k(e, c0=c0, oc=oc, ps=ps):
                for kc in range(NCH):
                    e.matmul(ps[:, 0:512], wA[:, kc * 1024 + oc * 128: kc * 1024 + (oc + 1) * 128],
                             xch(kc, c0, c0 + 512), start=(kc == 0), stop=False)
                return e.matmul(ps[:, 0:512], br[0:1, oc * 128:(oc + 1) * 128],
                                cx.ones_row16[0:1, 0:512], start=False, stop=True)
            tk.op("t", mm_k, reads=[xn_key, "w:A", "b:r0", "c:ones_row16"], writes=[f"ps:{pi}"])
            tk.op("s", lambda e, c0=c0, oc=oc, ps=ps: e.copy(
                kbuf[:, oc * TPC + c0: oc * TPC + c0 + 512], ps[:, 0:512]),
                reads=[f"ps:{pi}"], writes=["kbuf"])

    tk.op("v", lambda e: e.memset(colmax[:, :], -3.0e38), writes=["l1:colmax"])
    tk.op("v", lambda e: e.memset(stg[:, :], 0.0), writes=["l1:stg"])
    for ti in range(16):
        for h in range(16):
            base = 64 * (h % 2)
            kcol = (h // 2) * TPC + ti * 128
            pi = (ti * 16 + h) % 2
            ps_td = cx.pf[pi]
            ps_d = cx.pf[2 + pi]
            tk.op("t", lambda e, base=base, kcol=kcol, ps_td=ps_td: e.matmul(
                ps_td[0:128, 0:266], kbuf[base:base + 64, kcol:kcol + 128],
                projT[base:base + 64, 0:266], start=True, stop=True),
                reads=["kbuf", "l1:projT"], writes=[f"ps:{pi}"])
            tk.op("v", lambda e, pi=pi, ps_td=ps_td: e.reduce_max(
                rsc[:, pi:pi + 1], ps_td[:, 0:266], axis=AX),
                reads=[f"ps:{pi}"], writes=[f"l1:rm{pi}"])
            tk.op("v", lambda e, h=h, pi=pi: e.tensor_max(
                colmax[:, h:h + 1], colmax[:, h:h + 1], rsc[:, pi:pi + 1]),
                reads=[f"l1:rm{pi}", "l1:colmax"], writes=["l1:colmax"])
            tk.op("s", lambda e, base=base, kcol=kcol, pi=pi: e.activation(
                sq64[0:64, pi * 128:pi * 128 + 128], kbuf[base:base + 64, kcol:kcol + 128],
                AF.Square),
                reads=["kbuf"], writes=[f"l1:sq{pi}"])
            tk.op("t", lambda e, pi=pi, ps_d=ps_d: e.matmul(
                ps_d[0:128, 0:1], sq64[0:64, pi * 128:pi * 128 + 128], cx.ln_ones_col[0:64, 0:1],
                start=True, stop=True),
                reads=[f"l1:sq{pi}", "ln:ones_col"], writes=[f"ps:{2 + pi}"])
            tk.op("v", lambda e, ti=ti, h=h, pi=pi, ps_d=ps_d: e.tensor_scalar(
                diagk[:, ti * 16 + h: ti * 16 + h + 1], ps_d[:, 0:1], 0.5 * DN * DN, None,
                ALU.mult),
                reads=[f"ps:{2 + pi}"], writes=["l1:diagk"])

    tk.op("t", lambda e: e.transpose(cx.pf[4][0:16, 0:128], colmax[:, :], cx.eye[:, :]),
          reads=["l1:colmax", "c:eye"], writes=["ps:4"])
    tk.op("v", lambda e: e.tensor_copy(stg2[0:16, 0:128], cx.pf[4][0:16, 0:128]),
          reads=["ps:4"], writes=["l1:stg2"])
    tk.op("v", lambda e: e.reduce_max(stg2[0:16, 128:129], stg2[0:16, 0:128], axis=AX),
          reads=["l1:stg2"], writes=["l1:stg2"])
    tk.op("t", lambda e: e.transpose(cx.pf[4][0:1, 0:16], stg2[0:16, 128:129], cx.eye[0:16, 0:16]),
          reads=["l1:stg2", "c:eye"], writes=["ps:4"])
    tk.op("v", lambda e: e.tensor_copy(msel[0:1, 0:16], cx.pf[4][0:1, 0:16]),
          reads=["ps:4"], writes=["l1:msel"])
    if cc:
        tk.dma("g", lambda e: e.dma_start(out=ccm_in.ap()[:, :], in_=msel[0:1, 0:16]),
               reads=["l1:msel"], writes=["d:ccm_in"])
        tk.barrier()
        tk.op("g", lambda e: e.collective_compute(
            "AllReduce", ALU.max, replica_groups=RG,
            ins=[ccm_in.ap().opt()], outs=[ccm_out.ap().opt()]),
            reads=["d:ccm_in"], writes=["d:ccm_out"])
        tk.dma("g", lambda e: e.dma_start(out=msel[0:1, 0:16], in_=ccm_out.ap()[0:1, :]),
               reads=["d:ccm_out"], writes=["l1:msel"])
    for h in range(16):
        tk.op("t", lambda e, h=h: e.matmul(
            cx.pf[4][0:128, 0:1], cx.ones_row[0:1, 0:128], msel[0:1, h:h + 1],
            start=True, stop=True),
            reads=["l1:msel", "c:ones_row"], writes=["ps:4"])
        tk.op("v", lambda e, h=h: e.tensor_scalar(
            mxb[:, h:h + 1], cx.pf[4][0:128, 0:1], DN, None, ALU.mult),
            reads=["ps:4"], writes=["l1:mxb"])

    for h in range(16):
        base = 64 * (h % 2)
        ps_ks = cx.pf[4]
        for ti in range(16):
            kcol = (h // 2) * TPC + ti * 128
            pi = ti % 2
            ps_td = cx.pf[pi]
            tk.op("t", lambda e, base=base, kcol=kcol, ps_td=ps_td: e.matmul(
                ps_td[0:128, 0:266], kbuf[base:base + 64, kcol:kcol + 128],
                projT[base:base + 64, 0:266], start=True, stop=True),
                reads=["kbuf", "l1:projT"], writes=[f"ps:{pi}"])
            tk.op("v", lambda e, ti=ti, h=h, pi=pi: e.scalar_tensor_tensor(
                rsc[:, 2 + pi:3 + pi], diagk[:, ti * 16 + h: ti * 16 + h + 1], -1.0,
                mxb[:, h:h + 1], ALU.mult, ALU.subtract),
                reads=["l1:diagk", "l1:mxb"], writes=[f"l1:bc{pi}"])
            tk.op("s", lambda e, pi=pi, ps_td=ps_td: e.activation(
                exs[:, pi * 266:(pi + 1) * 266], ps_td[:, 0:266], AF.Exp,
                bias=rsc[:, 2 + pi:3 + pi], scale=DN),
                reads=[f"ps:{pi}", f"l1:bc{pi}"], writes=[f"l1:ex{pi}"])
            tk.op("v", lambda e, pi=pi: e.tensor_scalar(
                kpb[:, pi * 266:(pi + 1) * 266], exs[:, pi * 266:(pi + 1) * 266],
                RATIO, RATIO * EPS, ALU.mult, ALU.add),
                reads=[f"l1:ex{pi}"], writes=[f"l1:kp{pi}"])

            CTX_BANK = [2, 3, 5]

            def mm_acc2(e, ti=ti, h=h, pi=pi, ps_ks=ps_ks):
                st = (ti == 0)
                sp = (ti == 15)
                e.matmul(ps_ks[0:1, 0:266], cx.ones_col16[:, 0:1],
                         kpb[:, pi * 266:(pi + 1) * 266], start=st, stop=sp)
                ins = None
                for mc in range(3):
                    msz = 128 if mc < 2 else 10
                    ins = e.matmul(cx.pf[CTX_BANK[mc]][0:msz, 0:64],
                                   kpb[:, pi * 266 + mc * 128: pi * 266 + mc * 128 + msz],
                                   vtm[:, ti * 1024 + h * 64: ti * 1024 + (h + 1) * 64],
                                   start=st, stop=sp)
                return ins
            tk.op("t", mm_acc2,
                  reads=[f"l1:kp{pi}", "l1:v", "c:ones_col16", "ps:4", "ps:2", "ps:3", "ps:5"],
                  writes=["ps:4", "ps:2", "ps:3", "ps:5"])
        tk.op("v", lambda e, h=h: e.tensor_copy(rrow[0:1, 32:298], ps_ks[0:1, 0:266]),
              reads=["ps:4"], writes=["l1:ksrow"])
        for mc in range(3):
            msz = 128 if mc < 2 else 10
            col = h * CTXW + mc * 65
            bk = [2, 3, 5][mc]
            tk.op("v", lambda e, mc=mc, msz=msz, col=col, bk=bk: e.tensor_copy(
                stg[0:msz, col:col + 64], cx.pf[bk][0:msz, 0:64]),
                reads=[f"ps:{bk}"], writes=["l1:stg"])
            tk.op("t", lambda e, mc=mc, msz=msz: e.matmul(
                cx.pf[0][0:msz, 0:1], rrow[0:1, 32 + mc * 128: 32 + mc * 128 + msz],
                cx.ones_row[0:1, 0:1], start=True, stop=True),
                reads=["l1:ksrow", "c:ones_row"], writes=["ps:0"])
            tk.op("v", lambda e, mc=mc, msz=msz, col=col: e.tensor_copy(
                stg[0:msz, col + 64:col + 65], cx.pf[0][0:msz, 0:1]),
                reads=["ps:0"], writes=["l1:stg"])

    if cc:
        tk.dma("g", lambda e: e.dma_start(out=ccs_in.ap()[:, :], in_=stg[:, :]),
               reads=["l1:stg"], writes=["d:ccs_in"])
        tk.barrier()
        tk.op("g", lambda e: e.collective_compute(
            "AllReduce", ALU.add, replica_groups=RG,
            ins=[ccs_in.ap().opt()], outs=[ccs_out.ap().opt()]),
            reads=["d:ccs_in"], writes=["d:ccs_out"])
        tk.dma("g", lambda e: e.dma_start(out=stg[:, :], in_=ccs_out.ap()[:, :]),
               reads=["d:ccs_out"], writes=["l1:stg"])
    for h in range(16):
        for mc in range(3):
            msz = 128 if mc < 2 else 10
            col = h * CTXW + mc * 65
            tk.op("v", lambda e, h=h, mc=mc, msz=msz, col=col: e.tensor_copy(
                ctxb[0:msz, h * 192 + mc * 64: h * 192 + (mc + 1) * 64],
                stg[0:msz, col:col + 64]),
                reads=["l1:stg"], writes=["l1:ctxb"])
            tk.op("v", lambda e, h=h, mc=mc, msz=msz, col=col: e.tensor_copy(
                ksumb[0:msz, h * 3 + mc: h * 3 + mc + 1], stg[0:msz, col + 64:col + 65]),
                reads=["l1:stg"], writes=["l1:ksumb"])

    tk.dma("g", lambda e: e.dma_start(
        out=wA[:, :], in_=cx.dram["l1_wq_s"][:, :]), writes=["w:A"])
    tk.dma("g", lambda e: e.dma_start(
        out=wB[:, :], in_=cx.dram["l1_wo_s"][:, :]), writes=["w:B"])
    tk.dma("g", lambda e: e.dma_start(out=br[0:1, 0:1024], in_=cx.dram["l1_bq"][:, :]), writes=["b:r0"])
    tk.dma("g", lambda e: e.dma_start(out=br[0:1, 1024:2048], in_=cx.dram["l1_bo"][:, :]), writes=["b:r1"])
    for ti in range(16):
        par = ti % 2
        tc0 = ti * 128
        for qc in range(NCH):
            pi = qc % 2
            ps = cx.pf[pi]

            def mm_q(e, qc=qc, tc0=tc0, ps=ps):
                for kc in range(NCH):
                    e.matmul(ps[:, 0:128], wA[:, kc * 1024 + qc * 128: kc * 1024 + (qc + 1) * 128],
                             xch(kc, tc0, tc0 + 128), start=(kc == 0), stop=False)
                return e.matmul(ps[:, 0:128], br[0:1, qc * 128:(qc + 1) * 128],
                                cx.ones_row16[0:1, 0:128], start=False, stop=True)
            tk.op("t", mm_q, reads=[xn_key, "w:A", "b:r0", "c:ones_row16"], writes=[f"ps:{pi}"])
            tk.op("s", lambda e, qc=qc, par=par, ps=ps, pi=pi: e.copy(
                qt[:, par * 1024 + qc * 128: par * 1024 + (qc + 1) * 128], ps[:, 0:128]),
                reads=[f"ps:{pi}"], writes=[f"l1:qt{par}"])
        for h in range(16):
            base = 64 * (h % 2)
            qcol = par * 1024 + (h // 2) * 128
            hp = h % 2
            ps_td = cx.pf[hp]
            ps_d = cx.pf[2 + hp]
            ps_dv = cx.pf[4]
            ps_o = cx.pf[5]
            ps_tr = cx.pb[hp]
            tk.op("t", lambda e, base=base, qcol=qcol, ps_td=ps_td: e.matmul(
                ps_td[0:128, 0:266], qt[base:base + 64, qcol:qcol + 128],
                projT[base:base + 64, 0:266], start=True, stop=True),
                reads=[f"l1:qt{par}", "l1:projT"], writes=[f"ps:{hp}"])
            tk.op("s", lambda e, base=base, qcol=qcol, hp=hp: e.activation(
                sq64[0:64, hp * 128:hp * 128 + 128], qt[base:base + 64, qcol:qcol + 128],
                AF.Square),
                reads=[f"l1:qt{par}"], writes=[f"l1:sq{hp}"])
            tk.op("t", lambda e, hp=hp, ps_d=ps_d: e.matmul(
                ps_d[0:128, 0:1], sq64[0:64, hp * 128:hp * 128 + 128], cx.ln_ones_col[0:64, 0:1],
                start=True, stop=True),
                reads=[f"l1:sq{hp}", "ln:ones_col"], writes=[f"ps:{2 + hp}"])
            tk.op("v", lambda e, hp=hp, ps_d=ps_d: e.tensor_scalar(
                rsc[:, 4 + hp:5 + hp], ps_d[:, 0:1], 0.5 * DN * DN, None, ALU.mult),
                reads=[f"ps:{2 + hp}"], writes=[f"l1:dq{hp}"])
            tk.op("v", lambda e, hp=hp, ps_td=ps_td: e.reduce_max(
                rsc[:, 6 + hp:7 + hp], ps_td[:, 0:266], axis=AX),
                reads=[f"ps:{hp}"], writes=[f"l1:qm{hp}"])
            tk.op("v", lambda e, hp=hp: e.scalar_tensor_tensor(
                rsc[:, 8 + hp:9 + hp], rsc[:, 6 + hp:7 + hp], -DN, rsc[:, 4 + hp:5 + hp],
                ALU.mult, ALU.subtract),
                reads=[f"l1:qm{hp}", f"l1:dq{hp}"], writes=[f"l1:qb{hp}"])
            tk.op("s", lambda e, hp=hp, ps_td=ps_td: e.activation(
                exs[:, hp * 266:(hp + 1) * 266], ps_td[:, 0:266], AF.Exp,
                bias=rsc[:, 8 + hp:9 + hp], scale=DN),
                reads=[f"ps:{hp}", f"l1:qb{hp}"], writes=[f"l1:ex{hp}"])
            tk.op("v", lambda e, hp=hp: e.tensor_scalar(
                kpb[:, hp * 266:(hp + 1) * 266], exs[:, hp * 266:(hp + 1) * 266],
                RATIO, RATIO * EPS, ALU.mult, ALU.add),
                reads=[f"l1:ex{hp}"], writes=[f"l1:kp{hp}"])

            def mm_tr(e, hp=hp, ps_tr=ps_tr):
                ins = None
                for mc in range(3):
                    msz = 128 if mc < 2 else 10
                    ins = e.transpose(ps_tr[0:msz, mc * 128:mc * 128 + 128],
                                      kpb[:, hp * 266 + mc * 128: hp * 266 + mc * 128 + msz],
                                      cx.eye16[:, :])
                return ins
            tk.op("t", mm_tr, reads=[f"l1:kp{hp}", "c:eye16"], writes=[f"ps:b{hp}"])
            tk.op("s", lambda e, hp=hp, ps_tr=ps_tr: e.copy(
                qpT[:, hp * 384:(hp + 1) * 384], ps_tr[0:128, 0:384]),
                reads=[f"ps:b{hp}"], writes=[f"l1:qpT{hp}"])

            def mm_den(e, h=h, hp=hp, ps_dv=ps_dv):
                ins = None
                for mc in range(3):
                    msz = 128 if mc < 2 else 10
                    ins = e.matmul(ps_dv[0:1, 0:128], ksumb[0:msz, h * 3 + mc: h * 3 + mc + 1],
                                   qpT[0:msz, hp * 384 + mc * 128: hp * 384 + (mc + 1) * 128],
                                   start=(mc == 0), stop=(mc == 2))
                return ins
            tk.op("t", mm_den, reads=["l1:ksumb", f"l1:qpT{hp}"], writes=["ps:4"])
            tk.op("v", lambda e, ps_dv=ps_dv: e.tensor_copy(
                rrow[0:1, 304:432], ps_dv[0:1, 0:128]),
                reads=["ps:4"], writes=["l1:den"])
            tk.op("v", lambda e: e.reciprocal(rrow[0:1, 304:432], rrow[0:1, 304:432]),
                  reads=["l1:den"], writes=["l1:den"])
            tk.op("t", lambda e, ps_dv=ps_dv: e.matmul(
                ps_dv[0:128, 0:128], cx.ones_row[0:1, 0:128], rrow[0:1, 304:432],
                start=True, stop=True),
                reads=["l1:den", "c:ones_row"], writes=["ps:4"])
            for mc in range(3):
                msz = 128 if mc < 2 else 10
                tk.op("v", lambda e, hp=hp, mc=mc, msz=msz, ps_dv=ps_dv: e.tensor_mul(
                    qps[0:msz, hp * 384 + mc * 128: hp * 384 + (mc + 1) * 128],
                    qpT[0:msz, hp * 384 + mc * 128: hp * 384 + (mc + 1) * 128],
                    ps_dv[0:msz, 0:128]),
                    reads=[f"l1:qpT{hp}", "ps:4"], writes=[f"l1:qps{hp}"])

            def mm_pv(e, h=h, hp=hp, ps_o=ps_o):
                ins = None
                for mc in range(3):
                    msz = 128 if mc < 2 else 10
                    ins = e.matmul(ps_o[0:64, 0:128],
                                   ctxb[0:msz, h * 192 + mc * 64: h * 192 + (mc + 1) * 64],
                                   qps[0:msz, hp * 384 + mc * 128: hp * 384 + (mc + 1) * 128],
                                   start=(mc == 0), stop=(mc == 2))
                return ins
            tk.op("t", mm_pv, reads=["l1:ctxb", f"l1:qps{hp}"], writes=["ps:5"])
            tk.op("s", lambda e, h=h, base=base, par=par, ps_o=ps_o: e.copy(
                ath[base:base + 64, par * 1024 + (h // 2) * 128: par * 1024 + (h // 2) * 128 + 128],
                ps_o[0:64, 0:128]),
                reads=["ps:5"], writes=[f"l1:ath{par}"])
        for oc in range(NCH):
            pi = oc % 2
            ps = cx.pf[pi]

            def mm_wo(e, oc=oc, par=par, ps=ps):
                for kc in range(NCH):
                    e.matmul(ps[:, 0:128], wB[:, kc * 1024 + oc * 128: kc * 1024 + (oc + 1) * 128],
                             ath[:, par * 1024 + kc * 128: par * 1024 + (kc + 1) * 128],
                             start=(kc == 0), stop=False)
                return e.matmul(ps[:, 0:128], br[0:1, 1024 + oc * 128:1024 + (oc + 1) * 128],
                                cx.ones_row16[0:1, 0:128], start=False, stop=True)
            tk.op("t", mm_wo, reads=[f"l1:ath{par}", "w:B", "b:r1", "c:ones_row16"],
                  writes=[f"ps:{pi}"])
            tk.op("s", lambda e, oc=oc, tc0=tc0, ps=ps, pi=pi: e.copy(
                kbuf[:, oc * TPC + tc0: oc * TPC + tc0 + 128], ps[:, 0:128]),
                reads=[f"ps:{pi}"], writes=["kbuf"])

    tk.dma("g", lambda e: e.dma_start(
        out=wA[:, :], in_=cx.dram["l1_wfc_s"][:, :]), writes=["w:A"])
    tk.dma("g", lambda e: e.dma_start(out=br[0:1, 0:1024], in_=cx.dram["l1_bfc"][:, :]), writes=["b:r0"])
    for tt in range(4):
        c0 = tt * 512
        for kc in range(NCH):
            tk.dma("y", lambda e, kc=kc, c0=c0: e.dma_start(
                out=_chunk(xt, kc, 512), in_=x_in_ap[kc * 128:(kc + 1) * 128, c0:c0 + 512]),
                reads=list(x_dep), writes=["l1:xt"])
        for oc in range(NCH):
            pi = oc % 2
            ps = cx.pf[pi]

            def mm_fc(e, oc=oc, c0=c0, ps=ps):
                for kc in range(NCH):
                    e.matmul(ps[:, 0:512], wA[:, kc * 1024 + oc * 128: kc * 1024 + (oc + 1) * 128],
                             kbuf[:, kc * TPC + c0: kc * TPC + c0 + 512],
                             start=(kc == 0), stop=False)
                return e.matmul(ps[:, 0:512], br[0:1, oc * 128:(oc + 1) * 128],
                                cx.ones_row16[0:1, 0:512], start=False, stop=True)
            tk.op("t", mm_fc, reads=["kbuf", "w:A", "b:r0", "c:ones_row16"], writes=[f"ps:{pi}"])
            tk.op("v", lambda e, oc=oc, ps=ps, pi=pi: e.scalar_tensor_tensor(
                _chunk(xt, oc, 512), ps[:, 0:512], 0.0, _chunk(xt, oc, 512),
                ALU.max, ALU.add),
                reads=[f"ps:{pi}", "l1:xt"], writes=["l1:xt"])
        cx.l1_y0_store(tt)


def build_l1a():
    nc = bass.Bass("TRN2", target_bir_lowering=False)
    es = contextlib.ExitStack()
    with es:
        x1_in = nc.dram_tensor("x1_in", [DM, TPC], F32, kind="ExternalInput").ap()
        hsel_in = nc.dram_tensor("hsel", [1, 2], F32, kind="ExternalInput").ap()
        hselc_in = nc.dram_tensor("hselc", [2, 1], F32, kind="ExternalInput").ap()
        ccmb_in = nc.dram_tensor("ccmaxbias", [2, 16], F32, kind="ExternalInput").ap()
        xn_out = nc.dram_tensor("xn1_out", [DM, TPC], BF16, kind="ExternalOutput").ap()
        y0_out = nc.dram_tensor("y0_out", [DM, TPC], F32, kind="ExternalOutput").ap()

        block = es.enter_context(nc.Block())
        cx = _setup_common(nc, es, block, _SHARED_CACHE, L1A_SHARED)
        tk = cx.tk
        cx.ln_scr = cx.sb("ln_scr", [128, 2 * 512], F32)
        cx.ln_rows = cx.sb("ln_rows", [1, 4 * 512], F32)
        cx.ln_ones_col = cx.sb("ln_ones_col", [128, 1], F32)
        cx.ln_eps = cx.sb("ln_eps", [1, 1], F32)
        tk.op("v", lambda e: e.memset(cx.ln_ones_col[:, :], 1.0), writes=["ln:ones_col"])
        tk.op("v", lambda e: e.memset(cx.ln_eps[:, :], 1e-5), writes=["ln:eps"])
        cx.ln_init = True

        xn_sb = cx.sb("xn_sb", [128, NCH * TPC], BF16)

        def y0_store(tt):
            c0 = tt * 512
            for kc in range(NCH):
                tk.dma("y", lambda e, kc=kc, c0=c0: e.dma_start(
                    out=y0_out[kc * 128:(kc + 1) * 128, c0:c0 + 512],
                    in_=_chunk(cx.l1_xt, kc, 512)),
                    reads=["l1:xt"], writes=["o:y0"])
        cx.l1_y0_store = y0_store

        emit_l1_attn(cx, x1_in, xn_sb, "xn", TPC, 0, None, None,
                     {"hsel": hsel_in, "hselc": hselc_in, "ccmaxbias": ccmb_in})

        for kc in range(NCH):
            tk.dma("y", lambda e, kc=kc: e.dma_start(
                out=xn_out[kc * 128:(kc + 1) * 128, :], in_=_chunk(xn_sb, kc, TPC)),
                reads=["xn"], writes=["o:xn"])
        tk.flush()
        block.sync(lambda eng: tk.final_wait(eng))
    return nc




L1B_SHARED = ["f_w1_s", "f_w2_s", "f_h1_s", "f_h2_s", "f_w3_s", "f_s1", "f_s2", "f_sh1", "f_sh2",
              "f_s3", "eye", "ones_row"]


def emit_l1_ffn(cx, xn_sb, xn_key, xnw, xc0, y0_in_ap, out_store, y_dep=()):
    nc, tk, sb = cx.nc, cx.tk, cx.sb

    def xch(kc, a, b):
        return xn_sb[:, kc * xnw + xc0 + a: kc * xnw + xc0 + b]

    hs = sb("f_h", [128, 32 * 1024], BF16)
    gs = sb("f_g", [128, 32 * 1024], BF16)
    wch = sb("f_wch", [128, 2 * 4096], BF16)
    yt = sb("f_yt", [128, NCH * 512], F32)
    cx.f_yt = yt
    usc = yt
    scl = sb("f_scl", [128, 4 * 32 + 8], F32)

    tk.dma("y", lambda e: e.dma_start(out=scl[:, 0:32], in_=cx.dram["f_s1"][:, :]), writes=["f:scl"])
    tk.dma("y", lambda e: e.dma_start(out=scl[:, 32:64], in_=cx.dram["f_s2"][:, :]), writes=["f:scl"])
    tk.dma("y", lambda e: e.dma_start(out=scl[:, 64:96], in_=cx.dram["f_sh1"][:, :]), writes=["f:scl"])
    tk.dma("y", lambda e: e.dma_start(out=scl[:, 96:128], in_=cx.dram["f_sh2"][:, :]), writes=["f:scl"])
    tk.dma("y", lambda e: e.dma_start(out=scl[:, 128:136], in_=cx.dram["f_s3"][:, :]), writes=["f:scl"])

    for half in range(2):
        hoff = half * 1024
        for oc in range(32):
            par = oc % 2
            tk.dma("g", lambda e, oc=oc, par=par: e.dma_start(
                out=wch[:, par * 4096:par * 4096 + 1024],
                in_=cx.dram["f_w1_s"][oc]),
                writes=[f"f:wh{par}"])
            tk.dma("g", lambda e, oc=oc, par=par: e.dma_start(
                out=wch[:, par * 4096 + 1024:par * 4096 + 2048],
                in_=cx.dram["f_w2_s"][oc]),
                writes=[f"f:wh{par}"])
            for tt in range(2):
                tc0 = hoff + tt * 512
                ppar = tt % 2
                ps_a, ps_b = cx.pf[ppar], cx.pf[2 + ppar]

                def mm_h(e, oc=oc, par=par, tc0=tc0, ps_a=ps_a, ps_b=ps_b):
                    for kc in range(NCH):
                        e.matmul(ps_a[:, 0:512], wch[:, par * 4096 + kc * 128: par * 4096 + (kc + 1) * 128],
                                 xch(kc, tc0, tc0 + 512), start=(kc == 0), stop=(kc == NCH - 1))
                    ins = None
                    for kc in range(NCH):
                        ins = e.matmul(ps_b[:, 0:512], wch[:, par * 4096 + 1024 + kc * 128: par * 4096 + 1024 + (kc + 1) * 128],
                                       xch(kc, tc0, tc0 + 512), start=(kc == 0), stop=(kc == NCH - 1))
                    return ins
                tk.op("t", mm_h, reads=[xn_key, f"f:wh{par}"],
                      writes=[f"ps:{ppar}", f"ps:{2 + ppar}"])
                tk.op("s", lambda e, oc=oc, ppar=ppar, ps_a=ps_a: e.activation(
                    usc[:, ppar * 512:(ppar + 1) * 512], ps_a[:, 0:512], AF.Silu,
                    scale=scl[:, oc:oc + 1]),
                    reads=[f"ps:{ppar}", "f:scl"], writes=["f:yt"])
                tk.op("v", lambda e, oc=oc, tt=tt, ppar=ppar, ps_b=ps_b: e.scalar_tensor_tensor(
                    hs[:, oc * 1024 + tt * 512: oc * 1024 + (tt + 1) * 512],
                    ps_b[:, 0:512], scl[:, 32 + oc:33 + oc], usc[:, ppar * 512:(ppar + 1) * 512],
                    ALU.mult, ALU.mult),
                    reads=[f"ps:{2 + ppar}", "f:scl", "f:yt"], writes=["f:h"])
        for stage, (wnm, snm, soff, src_key, sbuf, dst_key, dbuf) in enumerate([
                ("f_h1", "sh1", 64, "f:h", hs, "f:g", gs),
                ("f_h2", "sh2", 96, "f:g", gs, "f:h", hs)]):
            for oc in range(32):
                par = oc % 2
                tk.dma("g", lambda e, oc=oc, par=par, wnm=wnm: e.dma_start(
                    out=wch[:, par * 4096:(par + 1) * 4096],
                    in_=cx.dram[wnm + "_s"][oc]),
                    writes=[f"f:wh{par}"])
                for tt in range(2):
                    ppar = tt % 2
                    ps = cx.pf[4 + ppar]

                    def mm_g(e, oc=oc, par=par, tt=tt, sbuf=sbuf, ps=ps):
                        ins = None
                        for kc in range(32):
                            ins = e.matmul(ps[:, 0:512],
                                           wch[:, par * 4096 + kc * 128: par * 4096 + (kc + 1) * 128],
                                           sbuf[:, kc * 1024 + tt * 512: kc * 1024 + (tt + 1) * 512],
                                           start=(kc == 0), stop=(kc == 31))
                        return ins
                    tk.op("t", mm_g, reads=[src_key, f"f:wh{par}"], writes=[f"ps:{4 + ppar}"])
                    tk.op("s", lambda e, oc=oc, tt=tt, soff=soff, dbuf=dbuf, ps=ps: e.activation(
                        dbuf[:, oc * 1024 + tt * 512: oc * 1024 + (tt + 1) * 512],
                        ps[:, 0:512], AF.Gelu, scale=scl[:, soff + oc:soff + oc + 1]),
                        reads=[f"ps:{4 + ppar}", "f:scl"], writes=[dst_key])
        for tt in range(2):
            tc0 = hoff + tt * 512
            for kc in range(NCH):
                tk.dma("y", lambda e, kc=kc, tc0=tc0: e.dma_start(
                    out=_chunk(yt, kc, 512), in_=y0_in_ap[kc * 128:(kc + 1) * 128, tc0:tc0 + 512]),
                    reads=list(y_dep), writes=["f:yt"])
            for oc in range(NCH):
                par = oc % 2
                tk.dma("g", lambda e, oc=oc, par=par: e.dma_start(
                    out=wch[:, par * 4096:(par + 1) * 4096],
                    in_=cx.dram["f_w3_s"][oc]),
                    writes=[f"f:wh{par}"])
                ps = cx.pf[par]

                def mm_o(e, oc=oc, par=par, tt=tt, ps=ps):
                    ins = None
                    for kc in range(32):
                        ins = e.matmul(ps[:, 0:512],
                                       wch[:, par * 4096 + kc * 128: par * 4096 + (kc + 1) * 128],
                                       hs[:, kc * 1024 + tt * 512: kc * 1024 + (tt + 1) * 512],
                                       start=(kc == 0), stop=(kc == 31))
                    return ins
                tk.op("t", mm_o, reads=["f:h", f"f:wh{par}"], writes=[f"ps:{par}"])
                tk.op("v", lambda e, oc=oc, par=par, ps=ps: e.scalar_tensor_tensor(
                    _chunk(yt, oc, 512), ps[:, 0:512], scl[:, 128 + oc:129 + oc],
                    _chunk(yt, oc, 512), ALU.mult, ALU.add),
                    reads=[f"ps:{par}", "f:scl", "f:yt"], writes=["f:yt"])
            out_store(half, tt)


def build_l1b():
    nc = bass.Bass("TRN2", target_bir_lowering=False)
    es = contextlib.ExitStack()
    with es:
        xn_in = nc.dram_tensor("xn1_in", [DM, TPC], BF16, kind="ExternalInput").ap()
        y0_in = nc.dram_tensor("y0_in", [DM, TPC], F32, kind="ExternalInput").ap()
        y_out = nc.dram_tensor("y_out", [DM, TPC], F32, kind="ExternalOutput").ap()

        block = es.enter_context(nc.Block())
        cx = _setup_common(nc, es, block, _SHARED_CACHE, L1B_SHARED)
        tk = cx.tk

        xn_sb = cx.sb("xn_sb", [128, NCH * TPC], BF16)
        for kc in range(NCH):
            tk.dma("y", lambda e, kc=kc: e.dma_start(
                out=_chunk(xn_sb, kc, TPC), in_=xn_in[kc * 128:(kc + 1) * 128, :]), writes=["xn"])

        def out_store(half, tt):
            tc0 = half * 1024 + tt * 512
            for kc in range(NCH):
                tk.dma("y", lambda e, kc=kc, tc0=tc0: e.dma_start(
                    out=y_out[kc * 128:(kc + 1) * 128, tc0:tc0 + 512],
                    in_=_chunk(cx.f_yt, kc, 512)),
                    reads=["f:yt"], writes=["o:y"])

        emit_l1_ffn(cx, xn_sb, "xn", TPC, 0, y0_in, out_store)
        tk.flush()
        block.sync(lambda eng: tk.final_wait(eng))
    return nc


_SHARED_CACHE = None


def _set_shared(shared):
    global _SHARED_CACHE
    _SHARED_CACHE = shared




MEGA_SHARED = sorted(set(L0A_SHARED + MOE_SHARED + L1A_SHARED + L1B_SHARED))


def build_mega(dbg=False, cc=True):
    nc = bass.Bass("TRN2", target_bir_lowering=False)
    es = contextlib.ExitStack()
    with es:
        x_in = nc.dram_tensor("x_fm", [DM, TH], F32, kind="ExternalInput").ap()
        kvm_in = nc.dram_tensor("kvmask", [1, TH], F32, kind="ExternalInput").ap()
        y_out = nc.dram_tensor("y_out", [DM, TPC], F32, kind="ExternalOutput").ap()
        kind = dict(kind="ExternalOutput") if dbg else {}
        x1s = nc.dram_tensor("x1s", [DM, TPC], F32, **kind)
        y0s = nc.dram_tensor("y0s", [DM, TPC], F32, **kind)
        if dbg:
            xn_dbg = nc.dram_tensor("xn_dbg", [DM, TPC], BF16, kind="ExternalOutput").ap()
            stg_dbg = nc.dram_tensor("stg_dbg", [128, STG_W], F32, kind="ExternalOutput").ap()
            gts_dbg = nc.dram_tensor("gts_dbg", [128, 256], F32, kind="ExternalOutput").ap()
            x1c_dbg = nc.dram_tensor("x1c_dbg", [DM, TPC], F32, kind="ExternalOutput").ap()
            xn2_dbg = nc.dram_tensor("xn2_dbg", [DM, TPC], BF16, kind="ExternalOutput").ap()

        block = es.enter_context(nc.Block())
        cx = _setup_common(nc, es, block, _SHARED_CACHE, MEGA_SHARED)
        cx.dram["kvmask"] = kvm_in
        tk = cx.tk

        awu = cx.sb("actwarm", [1, 16], F32)
        tk.op("v", lambda e: e.memset(awu[:, :], 1.0), writes=["actwarm"])
        for fn in (AF.Sqrt, AF.Exp, AF.Silu, AF.Gelu):
            tk.op("s", lambda e, fn=fn: e.activation(awu[:, :], awu[:, :], fn),
                  reads=["actwarm"], writes=["actwarm"])
            tk.barrier()

        ccw_in = nc.dram_tensor("ccw_in", [1, 16], F32)
        ccw_out = nc.dram_tensor("ccw_out", [1, 16], F32)
        wup = cx.sb("warmup", [1, 16], F32)
        tk.op("v", lambda e: e.memset(wup[:, :], 1.0), writes=["wup"])
        tk.dma("g", lambda e: e.dma_start(out=ccw_in.ap()[:, :], in_=wup[:, :]),
               reads=["wup"], writes=["d:ccw"])
        tk.op("g", lambda e: e.collective_compute(
            "AllReduce", ALU.add, replica_groups=[[0, 1, 2, 3], [4, 5, 6, 7]],
            ins=[ccw_in.ap().opt()], outs=[ccw_out.ap().opt()]),
            reads=["d:ccw"], writes=["d:ccwo"])
        tk.dma("g", lambda e: e.dma_start(out=wup[:, :], in_=ccw_out.ap()[:, :]),
               reads=["d:ccwo"], writes=["wup"])
        tk.barrier()

        xn_sb = cx.sb("xn_sb", [128, NCH * TH], BF16)
        cx.push_scope()
        cx.ln_scr = cx.sb("ln_scr", [128, 2 * 512], F32)
        cx.ln_rows = cx.sb("ln_rows", [1, 4 * 512], F32)
        cx.ln_ones_col = cx.sb("ln_ones_col", [128, 1], F32)
        cx.ln_eps = cx.sb("ln_eps", [1, 1], F32)
        tk.op("v", lambda e: e.memset(cx.ln_ones_col[:, :], 1.0), writes=["ln:ones_col"])
        tk.op("v", lambda e: e.memset(cx.ln_eps[:, :], 1e-5), writes=["ln:eps"])
        cx.ln_init = True

        cx.push_scope()
        x_sb = cx.sb("x_sb", [128, NCH * TPC], F32)
        cx.push_scope()
        xh_sb = cx.sb("xh_sb", [128, NCH * 256], F32)
        for kc in range(NCH):
            tk.dma("y", lambda e, kc=kc: e.dma_start(
                out=_chunk(x_sb, kc, TPC), in_=x_in[kc * 128:(kc + 1) * 128, HALO:HALO + TPC]),
                writes=["x"])
            tk.dma("y", lambda e, kc=kc: e.dma_start(
                out=_chunk(xh_sb, kc, 256, 0, 128), in_=x_in[kc * 128:(kc + 1) * 128, 0:HALO]),
                writes=["xh"])
            tk.dma("y", lambda e, kc=kc: e.dma_start(
                out=_chunk(xh_sb, kc, 256, 128, 256),
                in_=x_in[kc * 128:(kc + 1) * 128, TH - HALO:TH]),
                writes=["xh"])
        emit_ln(cx, x_sb, "x", TPC, 0, TPC, xn_sb, "xn", TH, HALO)
        emit_ln(cx, xh_sb, "xh", 256, 0, 128, xn_sb, "xn", TH, 0)
        emit_ln(cx, xh_sb, "xh", 256, 128, 128, xn_sb, "xn", TH, TH - HALO)
        emit_l0_attn(cx, xn_sb, "xn", x_sb, "x")
        cx.pop_scope()

        cx.push_scope()
        cx.moe_dbg = (gts_dbg,) if dbg else None
        emit_moe(cx, xn_sb, "xn", TH, HALO, x_sb, "x")
        if dbg:
            for kc in range(NCH):
                tk.dma("y", lambda e, kc=kc: e.dma_start(
                    out=xn2_dbg[kc * 128:(kc + 1) * 128, :],
                    in_=xn_sb[:, kc * TH + HALO: kc * TH + HALO + TPC]),
                    reads=["xn"], writes=["dbg"])
        for kc in range(NCH):
            tk.dma("y", lambda e, kc=kc: e.dma_start(
                out=x1s.ap()[kc * 128:(kc + 1) * 128, :], in_=_chunk(x_sb, kc, TPC)),
                reads=["x"], writes=["d:x1s"])
        if dbg:
            tk.dma("y", lambda e: e.dma_start(out=x1c_dbg[:, :], in_=x1s.ap()[:, :]),
                   reads=["d:x1s"], writes=["dbg"])
        cx.pop_scope()
        cx.pop_scope()

        cx.push_scope()

        def y0_store(tt):
            c0 = tt * 512
            for kc in range(NCH):
                tk.dma("y", lambda e, kc=kc, c0=c0: e.dma_start(
                    out=y0s.ap()[kc * 128:(kc + 1) * 128, c0:c0 + 512],
                    in_=_chunk(cx.l1_xt, kc, 512)),
                    reads=["l1:xt"], writes=["d:y0s"])
        cx.l1_y0_store = y0_store
        emit_l1_attn(cx, x1s.ap(), xn_sb, "xn", TH, HALO, None, None, {},
                     x_dep=("d:x1s",), cc=cc)
        if dbg:
            for kc in range(NCH):
                tk.dma("y", lambda e, kc=kc: e.dma_start(
                    out=xn_dbg[kc * 128:(kc + 1) * 128, :],
                    in_=xn_sb[:, kc * TH + HALO: kc * TH + HALO + TPC]),
                    reads=["xn"], writes=["dbg"])
            tk.dma("y", lambda e: e.dma_start(out=stg_dbg[:, :], in_=cx.l1_stg_dbg[:, :]),
                   reads=["l1:stg"], writes=["dbg"])
        cx.pop_scope()
        cx.pop_scope()

        cx.push_scope()

        def out_store(half, tt):
            tc0 = half * 1024 + tt * 512
            for kc in range(NCH):
                tk.dma("y", lambda e, kc=kc, tc0=tc0: e.dma_start(
                    out=y_out[kc * 128:(kc + 1) * 128, tc0:tc0 + 512],
                    in_=_chunk(cx.f_yt, kc, 512)),
                    reads=["f:yt"], writes=["o:y"])
        emit_l1_ffn(cx, xn_sb, "xn", TH, HALO, y0s.ap(), out_store, y_dep=("d:y0s",))
        cx.pop_scope()

        tk.flush()
        block.sync(lambda eng: tk.final_wait(eng))
    return nc


def kernel(**inputs):
    shared = _prep_shared(inputs)
    _set_shared(shared)
    cores = list(range(NCORE))
    nc = _get_prog("MEGA", build_mega)
    maps = []
    for c in cores:
        pc = _prep_core(inputs, c)
        m = {nm: shared[nm] for nm in MEGA_SHARED}
        m["x_fm"] = pc["x_fm"]
        m["kvmask"] = pc["kvmask"]
        maps.append(m)
    res = run_bass_kernel_spmd(nc, maps, core_ids=cores)
    out = np.empty((B * S, DM), np.float32)
    for c in cores:
        out[c * TPC:(c + 1) * TPC] = np.asarray(res.results[c]["y_out"]).T
    return out.reshape(B, S, DM)




def build_mega_l0(**moekw):
    nc = bass.Bass("TRN2", target_bir_lowering=False)
    es = contextlib.ExitStack()
    with es:
        x_in = nc.dram_tensor("x_fm", [DM, TH], F32, kind="ExternalInput").ap()
        kvm_in = nc.dram_tensor("kvmask", [1, TH], F32, kind="ExternalInput").ap()
        y_out = nc.dram_tensor("y_out", [DM, TPC], F32, kind="ExternalOutput").ap()

        block = es.enter_context(nc.Block())
        cx = _setup_common(nc, es, block, _SHARED_CACHE, MEGA_SHARED)
        cx.dram["kvmask"] = kvm_in
        tk = cx.tk
        xn_sb = cx.sb("xn_sb", [128, NCH * TH], BF16)
        cx.push_scope()
        cx.ln_scr = cx.sb("ln_scr", [128, 2 * 512], F32)
        cx.ln_rows = cx.sb("ln_rows", [1, 4 * 512], F32)
        cx.ln_ones_col = cx.sb("ln_ones_col", [128, 1], F32)
        cx.ln_eps = cx.sb("ln_eps", [1, 1], F32)
        tk.op("v", lambda e: e.memset(cx.ln_ones_col[:, :], 1.0), writes=["ln:ones_col"])
        tk.op("v", lambda e: e.memset(cx.ln_eps[:, :], 1e-5), writes=["ln:eps"])
        cx.ln_init = True
        cx.push_scope()
        x_sb = cx.sb("x_sb", [128, NCH * TPC], F32)
        cx.push_scope()
        xh_sb = cx.sb("xh_sb", [128, NCH * 256], F32)
        for kc in range(NCH):
            tk.dma("y", lambda e, kc=kc: e.dma_start(
                out=_chunk(x_sb, kc, TPC), in_=x_in[kc * 128:(kc + 1) * 128, HALO:HALO + TPC]),
                writes=["x"])
            tk.dma("y", lambda e, kc=kc: e.dma_start(
                out=_chunk(xh_sb, kc, 256, 0, 128), in_=x_in[kc * 128:(kc + 1) * 128, 0:HALO]),
                writes=["xh"])
            tk.dma("y", lambda e, kc=kc: e.dma_start(
                out=_chunk(xh_sb, kc, 256, 128, 256),
                in_=x_in[kc * 128:(kc + 1) * 128, TH - HALO:TH]),
                writes=["xh"])
        emit_ln(cx, x_sb, "x", TPC, 0, TPC, xn_sb, "xn", TH, HALO)
        emit_ln(cx, xh_sb, "xh", 256, 0, 128, xn_sb, "xn", TH, 0)
        emit_ln(cx, xh_sb, "xh", 256, 128, 128, xn_sb, "xn", TH, TH - HALO)
        emit_l0_attn(cx, xn_sb, "xn", x_sb, "x")
        cx.pop_scope()
        cx.push_scope()
        emit_moe(cx, xn_sb, "xn", TH, HALO, x_sb, "x", **moekw)
        for kc in range(NCH):
            tk.dma("y", lambda e, kc=kc: e.dma_start(
                out=y_out[kc * 128:(kc + 1) * 128, :], in_=_chunk(x_sb, kc, TPC)),
                reads=["x"], writes=["o:y"])
        cx.pop_scope()
        cx.pop_scope()
        cx.pop_scope()
        tk.flush()
        block.sync(lambda eng: tk.final_wait(eng))
    return nc


_PROGS = {}


def _get_prog(name, builder):
    if name not in _PROGS:
        _PROGS[name] = builder()
    return _PROGS[name]


def kernel_v1(**inputs):
    shared = _prep_shared(inputs)
    _set_shared(shared)
    cores = list(range(NCORE))
    pcs = [_prep_core(inputs, c) for c in cores]

    ncA = _get_prog("A", build_l0a)
    mapsA = []
    for c in cores:
        m = {nm: shared[nm] for nm in L0A_SHARED}
        m["x_fm"] = pcs[c]["x_fm"]
        m["kvmask"] = pcs[c]["kvmask"]
        mapsA.append(m)
    resA = run_bass_kernel_spmd(ncA, mapsA, core_ids=cores)

    ncB = _get_prog("B", build_l0b)
    mapsB = []
    for c in cores:
        m = {nm: shared[nm] for nm in MOE_SHARED}
        m["xn_in"] = np.ascontiguousarray(
            np.asarray(resA.results[c]["xn_out"])[:, HALO:HALO + TPC])
        m["x1p_in"] = np.ascontiguousarray(np.asarray(resA.results[c]["x1p_out"]))
        mapsB.append(m)
    resB = run_bass_kernel_spmd(ncB, mapsB, core_ids=cores)

    ncC = _get_prog("C", build_l1a)
    mapsC = []
    for c in cores:
        m = {nm: shared[nm] for nm in L1A_SHARED}
        m["x1_in"] = np.ascontiguousarray(np.asarray(resB.results[c]["x1_out"]))
        m["hsel"] = pcs[c]["hsel"]
        m["hselc"] = pcs[c]["hselc"]
        m["ccmaxbias"] = pcs[c]["ccmaxbias"]
        mapsC.append(m)
    resC = run_bass_kernel_spmd(ncC, mapsC, core_ids=cores)

    ncD = _get_prog("D", build_l1b)
    mapsD = []
    for c in cores:
        m = {nm: shared[nm] for nm in L1B_SHARED}
        m["xn1_in"] = np.ascontiguousarray(np.asarray(resC.results[c]["xn1_out"]))
        m["y0_in"] = np.ascontiguousarray(np.asarray(resC.results[c]["y0_out"]))
        mapsD.append(m)
    resD = run_bass_kernel_spmd(ncD, mapsD, core_ids=cores)

    out = np.empty((B * S, DM), np.float32)
    for c in cores:
        out[c * TPC:(c + 1) * TPC] = np.asarray(resD.results[c]["y_out"]).T
    return out.reshape(B, S, DM)
